# version 04
# baseline (speedup 1.0000x reference)
import numpy as np
import ml_dtypes
import concourse.bass as bass
import concourse.mybir as mybir
from concourse.bass_utils import run_bass_kernel_spmd

F32 = mybir.dt.float32
BF16 = mybir.dt.bfloat16
AF = mybir.ActivationFunctionType
ALU = mybir.AluOpType
AX = mybir.AxisListType
bfnp = ml_dtypes.bfloat16

NCORES = 8
D = 2048
T = 4096
B = 2
STRIP = 256
TOK = 1024
KC = D // 128
LN_EPS = 1e-5


class Sem:
    __slots__ = ("h", "n")

    def __init__(self, nc, name):
        self.h = nc.alloc_semaphore(name)
        self.n = 0

    def inc(self, instr, k=1):
        instr.then_inc(self.h, k)
        self.n += k
        return self.n

    def wait(self, eng, n=None):
        eng.wait_ge(self.h, self.n if n is None else n)


class Arena:

    def __init__(self, nc, name, cols_bf16):
        self.t = nc.alloc_sbuf_tensor(name, [128, cols_bf16], BF16)
        self.view_f = self.t[:, :].bitcast(F32)
        self.cols = cols_bf16
        self.off = 0

    def reset(self):
        self.off = 0

    def alloc(self, shape, dtype):
        p, n = shape
        esz = 4 if dtype == F32 else 2
        a = (self.off + 31) // 32 * 32
        self.off = a + n * esz
        assert self.off <= self.cols * 2, f"arena overflow: {self.off} > {self.cols*2}"
        if dtype == F32:
            return self.view_f[0:p, a // 4:a // 4 + n]
        return self.t[0:p, a // 2:a // 2 + n]


def emit_boundary(nc, blk, C):
    vals = [(C[k].h, C[k].n) for k in
            ["s_ld", "s_gw", "s_pe", "s_act", "s_dve", "s_out", "s_ag", "s_pool"]]
    for eng in ["sync", "gpsimd", "tensor", "scalar", "vector"]:
        def f(e, vals=vals):
            for h, n in vals:
                if n:
                    e.wait_ge(h, n)
        getattr(blk, eng)(f)



def shard_x_fm(x):
    x = np.ascontiguousarray(np.asarray(x), dtype=np.float32)
    shards = []
    for c in range(NCORES):
        cols = np.concatenate([x[0, c::8], x[1, c::8]], axis=0)
        shards.append(np.ascontiguousarray(cols.T))
    return shards


def unshard_x_fm(shards):
    x = np.empty((B, T, D), np.float32)
    for c in range(NCORES):
        cols = np.asarray(shards[c]).T
        x[0, c::8] = cols[0:512]
        x[1, c::8] = cols[512:1024]
    return x


def prep_moe_weights(d):
    w = {}
    w["router_T"] = np.ascontiguousarray(d["L0_ffn_router_weight"].T)
    gup = np.asarray(d["L0_ffn_gate_up_proj"])
    g = gup[:, :1408, :]
    u = gup[:, 1408:, :]
    g = g.reshape(8, 11, 128, 2048)
    u = u.reshape(8, 11, 128, 2048)
    inter = np.stack([g, u], axis=2)
    inter = inter.reshape(8, 2816, 2048)
    w["gup_T"] = np.ascontiguousarray(inter.transpose(0, 2, 1))
    w["down_T"] = np.ascontiguousarray(np.asarray(d["L0_ffn_down_proj"]).transpose(0, 2, 1))
    w["shg_T"] = np.ascontiguousarray(d["L0_ffn_sh_gate_w"].T)
    w["shu_T"] = np.ascontiguousarray(d["L0_ffn_sh_up_w"].T)
    w["shd_T"] = np.ascontiguousarray(d["L0_ffn_sh_down_w"].T)
    w["sgw_T"] = np.ascontiguousarray(
        np.asarray(d["L0_ffn_shared_gate_w"]).reshape(KC, 128).T)
    return w



def common_setup(nc):
    C = {}
    for nm in ["s_ld", "s_gw", "s_pe", "s_act", "s_dve", "s_pool", "s_out", "s_ag"]:
        C[nm] = Sem(nc, nm)
    C["t_ones_k"] = nc.inline_tensor(np.ones((128, 1), np.float32), "t_ones_k")
    C["t_ones_m"] = nc.inline_tensor(np.ones((1, 128), np.float32), "t_ones_m")
    C["t_idn_f"] = nc.inline_tensor(np.eye(128, dtype=np.float32), "t_idn_f")
    C["t_idn_b"] = nc.inline_tensor(np.eye(128, dtype=bfnp), "t_idn_b")
    C["t_eps"] = nc.inline_tensor(np.full((128, 1), LN_EPS, np.float32), "t_eps")
    C["t_onesb"] = nc.inline_tensor(np.ones((128, 1), bfnp), "t_onesb")
    return C


def emit_load_consts(nc, blk, C):
    sb = {
        "ones_k": nc.alloc_sbuf_tensor("sb_ones_k", [128, 1], F32),
        "ones_m": nc.alloc_sbuf_tensor("sb_ones_m", [1, 128], F32),
        "idn_f": nc.alloc_sbuf_tensor("sb_idn_f", [128, 128], F32),
        "eps": nc.alloc_sbuf_tensor("sb_eps", [128, 1], F32),
    }
    s_ld = C["s_ld"]

    def f(sy):
        s_ld.inc(sy.dma_start(out=sb["ones_k"][:], in_=C["t_ones_k"].ap()), 16)
        s_ld.inc(sy.dma_start(out=sb["ones_m"][:], in_=C["t_ones_m"].ap()), 16)
        s_ld.inc(sy.dma_start(out=sb["idn_f"][:], in_=C["t_idn_f"].ap()), 16)
        s_ld.inc(sy.dma_start(out=sb["eps"][:], in_=C["t_eps"].ap()), 16)
    blk.sync(f)
    return sb


def emit_load_master(nc, blk, C, X, x_fm_ap, ntok=TOK):
    s_ld = C["s_ld"]

    def f(sy):
        src = x_fm_ap.rearrange("(kc p) t -> p kc t", p=128)
        dst = X[:, :].rearrange("p (kc t) -> p kc t", t=ntok)
        s_ld.inc(sy.dma_start(out=dst, in_=src), 16)
    blk.sync(f)
    return s_ld.n


def emit_store_master(nc, blk, C, X, y_fm_ap, waits, ntok=TOK):
    s_out = C["s_out"]

    def f(sy):
        for s, v in waits:
            s.wait(sy, v)
        dst = y_fm_ap.rearrange("(kc p) t -> p kc t", p=128)
        src = X[:, :].rearrange("p (kc t) -> p kc t", t=ntok)
        s_out.inc(sy.dma_start(out=dst, in_=src), 16)
        s_out.wait(sy)
    blk.sync(f)


def emit_ln(nc, blk, C, sb, X, XB, pm, sc, ntok=TOK, x_ready=None, gamma_ap=None,
           beta_ap=None, XF=None):
    s_pe, s_act, s_dve, s_ld = C["s_pe"], C["s_act"], C["s_dve"], C["s_ld"]
    nh = ntok // 512
    SQ = [sc["SQ0"], sc["SQ1"]]
    rows = sc["rows"]
    r_sum = rows[:, 0 * ntok:1 * ntok]
    r_sq = rows[:, 1 * ntok:2 * ntok]
    r_mean = rows[:, 2 * ntok:3 * ntok]
    r_var = rows[:, 3 * ntok:4 * ntok]
    r_rstd = rows[:, 4 * ntok:5 * ntok]
    r_tmp = rows[:, 5 * ntok:6 * ntok]
    MB, RB, TMP = sc["MB"], sc["RB"], sc["TMP"]
    ones_k, ones_m = sb["ones_k"], sb["ones_m"]
    if x_ready is None:
        x_ready = [(s_ld, s_ld.n)]

    gb = None
    if gamma_ap is not None:
        gb = sc.get("gb")
        if gb is None:
            gb = nc.alloc_sbuf_tensor(None, [128, 2 * KC], F32)[:, :]

        def f_gb(sy):
            s_ld.inc(sy.dma_start(out=gb[:, 0:KC], in_=gamma_ap), 16)
            s_ld.inc(sy.dma_start(out=gb[:, KC:2 * KC], in_=beta_ap), 16)
        blk.sync(f_gb)
        gb_ready = s_ld.n

    pe_m = {}

    def f_pe_a(t):
        for s, v in x_ready:
            s.wait(t, v)
        for th in range(nh):
            for k in range(KC):
                ins = t.matmul(pm[th][0:1, :], ones_k[:, :],
                               X[:, k * ntok + th * 512: k * ntok + (th + 1) * 512],
                               start=(k == 0), stop=(k == KC - 1))
            pe_m[("sum", th)] = s_pe.inc(ins)
    blk.tensor(f_pe_a)

    def f_dve_a(v):
        for th in range(nh):
            s_pe.wait(v, pe_m[("sum", th)])
            ins = v.tensor_copy(r_sum[:, th * 512:(th + 1) * 512], pm[th][0:1, :])
        pe_m["sum_evac"] = s_dve.inc(ins)
    blk.vector(f_dve_a)

    act_m = {}
    for k in range(KC):
        def f_act_b(a, k=k):
            if k < 2:
                for s, v in x_ready:
                    s.wait(a, v)
            else:
                s_pe.wait(a, pe_m[("sq", k - 2)])
            if k == 0:
                s_dve.wait(a, pe_m["sum_evac"])
            act_m[k] = s_act.inc(a.activation(SQ[k % 2][:, :],
                                              X[:, k * ntok:(k + 1) * ntok], AF.Square))
        blk.scalar(f_act_b)

        def f_pe_b(t, k=k):
            s_act.wait(t, act_m[k])
            if k == 0:
                s_dve.wait(t, pe_m["sum_evac"])
            for th in range(nh):
                ins = t.matmul(pm[th][0:1, :], ones_k[:, :],
                               SQ[k % 2][:, th * 512:(th + 1) * 512],
                               start=(k == 0), stop=(k == KC - 1))
                if k == KC - 1:
                    pe_m[("sumsq", th)] = s_pe.inc(ins)
            if k < KC - 1:
                pe_m[("sq", k)] = s_pe.inc(ins)
            else:
                pe_m[("sq", k)] = pe_m[("sumsq", nh - 1)]
        blk.tensor(f_pe_b)

    dve_m = {}

    def f_dve_c(v):
        for th in range(nh):
            s_pe.wait(v, pe_m[("sumsq", th)])
            v.tensor_copy(r_sq[:, th * 512:(th + 1) * 512], pm[th][0:1, :])
        v.drain()
        v.tensor_scalar_mul(r_mean[:, :], r_sum[:, :], 1.0 / D)
        v.tensor_scalar_mul(r_var[:, :], r_sq[:, :], 1.0 / D)
        v.drain()
        v.tensor_tensor(r_tmp[:, :], r_mean[:, :], r_mean[:, :], ALU.mult)
        v.drain()
        ins = v.tensor_tensor(r_var[:, :], r_var[:, :], r_tmp[:, :], ALU.subtract)
        dve_m["var"] = s_dve.inc(ins)
    blk.vector(f_dve_c)

    def f_act_c(a):
        s_dve.wait(a, dve_m["var"])
        act_m["sd"] = s_act.inc(a.activation(r_var[:, :], r_var[:, :], AF.Sqrt,
                                             bias=sb["eps"][0:1, :]))
    blk.scalar(f_act_c)

    def f_dve_c2(v):
        s_act.wait(v, act_m["sd"])
        dve_m["rstd"] = s_dve.inc(v.reciprocal(r_rstd[:, :], r_var[:, :]))
    blk.vector(f_dve_c2)

    for th in range(nh):
        def f_pe_d(t, th=th):
            if th == 0:
                s_dve.wait(t, dve_m["rstd"])
            else:
                s_dve.wait(t, dve_m[("bc", th - 1)])
            t.matmul(pm[0][:, :], ones_m[:, :], r_mean[:, th * 512:(th + 1) * 512],
                     start=True, stop=True)
            ins = t.matmul(pm[1][:, :], ones_m[:, :], r_rstd[:, th * 512:(th + 1) * 512],
                           start=True, stop=True)
            pe_m[("bc", th)] = s_pe.inc(ins)
        blk.tensor(f_pe_d)

        def f_dve_d(v, th=th):
            s_pe.wait(v, pe_m[("bc", th)])
            v.tensor_copy(MB[:, th * 512:(th + 1) * 512], pm[0][:, :])
            ins = v.tensor_copy(RB[:, th * 512:(th + 1) * 512], pm[1][:, :])
            dve_m[("bc", th)] = s_dve.inc(ins)
        blk.vector(f_dve_d)

    def f_dve_d_all(v):
        dve_m["bc"] = dve_m[("bc", nh - 1)]
        if gamma_ap is not None:
            v.wait_ge(s_ld.h, gb_ready)
        for k in range(KC):
            v.tensor_tensor(TMP[:, :], X[:, k * ntok:(k + 1) * ntok], MB[:, :],
                            ALU.subtract)
            v.drain()
            if gamma_ap is None:
                if XF is not None:
                    ins = v.tensor_tensor(XF[:, k * ntok:(k + 1) * ntok], TMP[:, :],
                                          RB[:, :], ALU.mult)
                    if XB is not None:
                        v.drain()
                        ins = v.tensor_copy(XB[:, k * ntok:(k + 1) * ntok],
                                            XF[:, k * ntok:(k + 1) * ntok])
                else:
                    ins = v.tensor_tensor(XB[:, k * ntok:(k + 1) * ntok], TMP[:, :],
                                          RB[:, :], ALU.mult)
            else:
                v.tensor_tensor(TMP[:, :], TMP[:, :], RB[:, :], ALU.mult)
                v.drain()
                v.tensor_scalar(TMP[:, :], TMP[:, :], gb[:, k:k + 1],
                                gb[:, KC + k:KC + k + 1], ALU.mult, ALU.add)
                v.drain()
                if XF is not None:
                    v.tensor_copy(XF[:, k * ntok:(k + 1) * ntok], TMP[:, :])
                ins = v.tensor_copy(XB[:, k * ntok:(k + 1) * ntok], TMP[:, :])
        dve_m["xb"] = s_dve.inc(ins)
    blk.vector(f_dve_d_all)

    return dve_m["xb"], dve_m["bc"]


def emit_moe(nc, blk, C, sb, ar, pb, io, sfx, scale=0.5, debug=False):
    x_in = io["x_in"]
    y_out = io["y_out"]
    if debug:
        dbg_xb = io["dbg_xb"]
        dbg_cf = io["dbg_cf"]
        dbg_bg = io["dbg_bg"]
        dbg_a = io["dbg_a"]
        dbg_rs = io["dbg_rs"]
    rT = io["router_T"]
    gupT = io["gup_T"]
    dwnT = io["down_T"]
    shgT = io["shg_T"]
    shuT = io["shu_T"]
    shdT = io["shd_T"]
    sgwT = io["sgw_T"]

    s_wgu = [Sem(nc, f"s_wgu{i}{sfx}") for i in range(2)]
    s_wd = [Sem(nc, f"s_wd{i}{sfx}") for i in range(2)]
    s_crow = Sem(nc, f"s_crow{sfx}")

    X = ar.alloc([128, KC * TOK], F32)
    XB = ar.alloc([128, KC * TOK], BF16)
    A = ar.alloc([128, 11 * TOK], BF16)
    Wgu = [ar.alloc([128, KC * 256], BF16) for i in range(2)]
    Wd = [ar.alloc([128, 11 * 512], BF16) for i in range(2)]
    RTsb = ar.alloc([128, KC * 8], BF16)
    SGWsb = ar.alloc([128, KC], BF16)
    SCR = ar.alloc([128, 11264], F32)
    sc = {
        "SQ0": SCR[:, 0 * TOK:1 * TOK],
        "SQ1": SCR[:, 1 * TOK:2 * TOK],
        "MB": SCR[:, 2 * TOK:3 * TOK],
        "RB": SCR[:, 3 * TOK:4 * TOK],
        "TMP": SCR[:, 4 * TOK:5 * TOK],
        "rows": SCR[0:1, 5 * TOK:11 * TOK],
    }
    SG = SCR[:, 0 * TOK:1 * TOK]
    UB = SCR[:, 1 * TOK:2 * TOK]
    BE = SCR[:, 5 * TOK:6 * TOK]
    BG = SCR[:, 6 * TOK:7 * TOK]
    CF = SCR[0:8, 7 * TOK:8 * TOK]
    CROW = SCR[0:1, 8 * TOK:9 * TOK]
    GR = SCR[0:1, 9 * TOK:10 * TOK]
    RS = SCR[:, 10 * TOK:10 * TOK + 64]
    RCOL = SCR[:, 10 * TOK + 64:10 * TOK + 72]

    pg = pb[0:2]
    pu = pb[2:4]
    pd = pb[4:6]
    pm = pb[6:8]

    s_ld, s_gw, s_pe = C["s_ld"], C["s_gw"], C["s_pe"]
    s_act, s_dve, s_pool = C["s_act"], C["s_dve"], C["s_pool"]

    if True:
        emit_load_master(nc, blk, C, X, x_in)

        def f_rt(sy):
            s_ld.inc(sy.dma_start(out=RTsb[:, :].rearrange("p (kc o) -> p kc o", o=8),
                                  in_=rT.rearrange("(kc p) o -> p kc o", p=128)), 16)
            s_ld.inc(sy.dma_start(out=SGWsb[:, :], in_=sgwT), 16)
        blk.gpsimd(f_rt)
        rt_mark = s_ld.n

        xb_mark, ln_bank_mark = emit_ln(nc, blk, C, sb, X, XB, pm, sc)

        pe_m = {}
        dve_m = {}
        act_m = {}
        for tc in range(8):
            def f_pe_r(t, tc=tc):
                if tc == 0:
                    s_dve.wait(t, ln_bank_mark)
                    s_dve.wait(t, xb_mark)
                    t.wait_ge(s_ld.h, rt_mark)
                else:
                    s_dve.wait(t, dve_m[("rdone", tc - 1)])
                for k in range(KC):
                    ins = t.matmul(pm[tc % 2][:, 0:8],
                                   XB[:, k * TOK + tc * 128: k * TOK + (tc + 1) * 128],
                                   RTsb[:, k * 8:(k + 1) * 8],
                                   start=(k == 0), stop=(k == KC - 1))
                pe_m[("rl", tc)] = s_pe.inc(ins)
            blk.tensor(f_pe_r)

            def f_dve_r(v, tc=tc):
                if debug and tc == 1:
                    v.wait_ge(C["s_out"].h, rs_dump_mark)
                s_pe.wait(v, pe_m[("rl", tc)])
                l8 = RS[:, 0:8]
                eq = RS[:, 8:16]
                lm = RS[:, 16:24]
                ge = RS[:, 24:32]
                ex = RS[:, 32:40]
                gate = RS[:, 40:48]
                cmb = RS[:, 48:56]
                m1 = RCOL[:, 0:1]
                m2 = RCOL[:, 1:2]
                dn = RCOL[:, 2:3]
                di = RCOL[:, 3:4]
                v.tensor_copy(l8, pm[tc % 2][:, 0:8])
                v.drain()
                v.reduce_max(m1, l8, axis=AX.X)
                v.drain()
                v.tensor_scalar(eq, l8, m1, None, ALU.is_ge)
                v.drain()
                v.scalar_tensor_tensor(lm, eq, -1e30, l8, ALU.mult, ALU.add)
                v.drain()
                v.reduce_max(m2, lm, axis=AX.X)
                v.drain()
                v.tensor_scalar(ge, l8, m2, None, ALU.is_ge)
                v.tensor_scalar(ex, l8, m1, None, ALU.subtract)
                ins = v.drain()
                dve_m[("rsub", tc)] = s_dve.inc(ins)
            blk.vector(f_dve_r)

            def f_act_r(a, tc=tc):
                s_dve.wait(a, dve_m[("rsub", tc)])
                act_m[("rexp", tc)] = s_act.inc(
                    a.activation(RS[:, 32:40], RS[:, 32:40], AF.Exp))
            blk.scalar(f_act_r)

            def f_dve_r2(v, tc=tc):
                s_act.wait(v, act_m[("rexp", tc)])
                ex = RS[:, 32:40]
                ge = RS[:, 24:32]
                gate = RS[:, 40:48]
                cmb = RS[:, 48:56]
                dn = RCOL[:, 2:3]
                di = RCOL[:, 3:4]
                v.tensor_tensor(gate, ex, ge, ALU.mult)
                v.drain()
                v.reduce_sum(dn, gate, axis=AX.X)
                v.drain()
                v.reciprocal(di, dn)
                v.drain()
                v.tensor_scalar_mul(di, di, scale)
                v.drain()
                ins = v.tensor_scalar_mul(cmb, gate, di)
                dve_m[("cmb", tc)] = s_dve.inc(ins)
            blk.vector(f_dve_r2)

            if debug and tc == 0:
                def f_dbg_rs(sy):
                    s_dve.wait(sy, dve_m[("cmb", 0)])
                    dve_m["dbg_rs"] = True
                    C["s_out"].inc(sy.dma_start(out=dbg_rs[:, 0:64], in_=RS[:, :]), 16)
                    C["s_out"].inc(sy.dma_start(out=dbg_rs[:, 64:72], in_=RCOL[:, :]), 16)
                blk.sync(f_dbg_rs)
                rs_dump_mark = C["s_out"].n

            def f_pe_r2(t, tc=tc):
                s_dve.wait(t, dve_m[("cmb", tc)])
                ins = t.transpose(pm[tc % 2][0:8, 0:128], RS[:, 48:56], sb["idn_f"][:, :])
                pe_m[("rt", tc)] = s_pe.inc(ins)
            blk.tensor(f_pe_r2)

            def f_dve_r3(v, tc=tc):
                s_pe.wait(v, pe_m[("rt", tc)])
                ins = v.tensor_copy(CF[0:8, tc * 128:(tc + 1) * 128], pm[tc % 2][0:8, 0:128])
                dve_m[("rdone", tc)] = s_dve.inc(ins)
            blk.vector(f_dve_r3)

        def f_pe_g(t):
            s_dve.wait(t, dve_m[("rdone", 7)])
            for th in range(2):
                for k in range(KC):
                    ins = t.matmul(pm[th][0:1, :], SGWsb[:, k:k + 1],
                                   XB[:, k * TOK + th * 512: k * TOK + (th + 1) * 512],
                                   start=(k == 0), stop=(k == KC - 1))
                pe_m[("sg", th)] = s_pe.inc(ins)
        blk.tensor(f_pe_g)

        def f_act_g(a):
            for th in range(2):
                s_pe.wait(a, pe_m[("sg", th)])
                ins = a.activation(GR[:, th * 512:(th + 1) * 512], pm[th][0:1, :],
                                   AF.Sigmoid)
            act_m["sg"] = s_act.inc(ins)
        blk.scalar(f_act_g)

        def f_dve_g(v):
            s_act.wait(v, act_m["sg"])
            ins = v.tensor_scalar_mul(GR[:, :], GR[:, :], scale)
            dve_m["gr"] = s_dve.inc(ins)
        blk.vector(f_dve_g)

        def f_pe_g2(t):
            s_dve.wait(t, dve_m["gr"])
            for th in range(2):
                ins = t.matmul(pm[th][:, :], sb["ones_m"][:, :],
                               GR[:, th * 512:(th + 1) * 512], start=True, stop=True)
            pe_m["gb"] = s_pe.inc(ins)
        blk.tensor(f_pe_g2)

        def f_dve_g2(v):
            s_pe.wait(v, pe_m["gb"])
            for th in range(2):
                ins = v.tensor_copy(BG[:, th * 512:(th + 1) * 512], pm[th][:, :])
            dve_m["bg"] = s_dve.inc(ins)
        blk.vector(f_dve_g2)

        gw_mark = {}
        gu_consumed = {}
        dn_consumed = {}
        pe_gu = {}
        pe_dn = {}
        act_su = {}
        dve_ub = {}
        dve_a = {}
        dve_x = {}
        be_ready = {}
        pd_free = {}
        pd_use = 0

        dbg_a_mark = [0]
        for u in range(12):
            is_sh = u >= 8
            g = u - 8
            if debug and u == 1:
                def f_dbg_a(sy):
                    s_dve.wait(sy, dve_a[(0, 10, 1)])
                    dbg_a_mark[0] = C["s_out"].inc(
                        sy.dma_start(out=dbg_a.rearrange("(jc p) t -> p jc t", p=128),
                                     in_=A[:, :].rearrange("p (jc t) -> p jc t", t=TOK)),
                        16)
                blk.sync(f_dbg_a)
            if not is_sh:
                def f_pool_be(gp, u=u):
                    s_dve.wait(gp, dve_m[("rdone", 7)])
                    if u > 0:
                        s_dve.wait(gp, dve_a[(u - 1, 10, 1)])
                    gw_mark[("crow", u)] = s_crow.inc(
                        gp.dma_start(out=CROW[:, :], in_=CF[u:u + 1, :]), 16)
                blk.gpsimd(f_pool_be)

                def f_pe_be(t, u=u):
                    t.wait_ge(s_crow.h, gw_mark[("crow", u)])
                    if u > 0:
                        s_dve.wait(t, dve_m[("be", u - 1)])
                    else:
                        s_dve.wait(t, dve_m["bg"])
                    for th in range(2):
                        ins = t.matmul(pm[th][:, :], sb["ones_m"][:, :],
                                       CROW[:, th * 512:(th + 1) * 512],
                                       start=True, stop=True)
                    pe_m[("be", u)] = s_pe.inc(ins)
                blk.tensor(f_pe_be)

                def f_dve_be(v, u=u):
                    s_pe.wait(v, pe_m[("be", u)])
                    if u > 0:
                        pass
                    for th in range(2):
                        ins = v.tensor_copy(BE[:, th * 512:(th + 1) * 512], pm[th][:, :])
                    dve_m[("be", u)] = s_dve.inc(ins)
                    be_ready[u] = dve_m[("be", u)]
                blk.vector(f_dve_be)
            else:
                if u == 8:
                    be_ready[u] = dve_m["bg"]
                else:
                    be_ready[u] = be_ready[8]
            BEu = BG if is_sh else BE

            for m in range(11):
                gi = u * 11 + m
                par = gi % 2

                def f_gw(gp, u=u, m=m, gi=gi, par=par, is_sh=is_sh, g=g):
                    if gi >= 2:
                        s_pe.wait(gp, gu_consumed[gi - 2])
                    dst = Wgu[par][:, :].rearrange("p (kc j) -> p kc j", j=256)
                    sw = s_wgu[par]
                    if not is_sh:
                        src = gupT[u, :, m * 256:(m + 1) * 256] \
                            .rearrange("(kc p) j -> p kc j", p=128)
                        ins = gp.dma_start(out=dst, in_=src)
                    else:
                        jm = g * 11 + m
                        srcg = shgT[:, jm * 128:(jm + 1) * 128] \
                            .rearrange("(kc p) j -> p kc j", p=128)
                        srcu = shuT[:, jm * 128:(jm + 1) * 128] \
                            .rearrange("(kc p) j -> p kc j", p=128)
                        sw.inc(gp.dma_start(out=dst[:, :, 0:128], in_=srcg), 16)
                        ins = gp.dma_start(out=dst[:, :, 128:256], in_=srcu)
                    gw_mark[("gu", gi)] = sw.inc(ins, 16)
                blk.gpsimd(f_gw)

                def f_pe_gu(t, u=u, m=m, gi=gi, par=par):
                    t.wait_ge(s_wgu[par].h, gw_mark[("gu", gi)])
                    if u == 0 and m == 0:
                        s_dve.wait(t, xb_mark)
                    if gi >= 1:
                        s_act.wait(t, act_su[(gi - 1, 1)])
                        s_dve.wait(t, dve_ub[(gi - 1, 1)])
                    for th in range(2):
                        for k in range(KC):
                            ins = t.matmul(pg[th][:, :],
                                           Wgu[par][:, k * 256:k * 256 + 128],
                                           XB[:, k * TOK + th * 512:k * TOK + (th + 1) * 512],
                                           start=(k == 0), stop=(k == KC - 1))
                        pe_gu[(gi, "g", th)] = s_pe.inc(ins)
                    for th in range(2):
                        for k in range(KC):
                            ins = t.matmul(pu[th][:, :],
                                           Wgu[par][:, k * 256 + 128:k * 256 + 256],
                                           XB[:, k * TOK + th * 512:k * TOK + (th + 1) * 512],
                                           start=(k == 0), stop=(k == KC - 1))
                        pe_gu[(gi, "u", th)] = s_pe.inc(ins)
                    gu_consumed[gi] = pe_gu[(gi, "u", 1)]
                blk.tensor(f_pe_gu)

                def f_act_su(a, u=u, m=m, gi=gi):
                    for th in range(2):
                        s_pe.wait(a, pe_gu[(gi, "g", th)])
                        if gi >= 1:
                            s_dve.wait(a, dve_a[(u, m - 1, 1)] if m >= 1
                                       else dve_a[(u - 1, 10, 1)] if u >= 1 else 0)
                        ins = a.activation(SG[:, th * 512:(th + 1) * 512], pg[th][:, :],
                                           AF.Silu)
                        act_su[(gi, th)] = s_act.inc(ins)
                blk.scalar(f_act_su)

                def f_dve_gu(v, u=u, m=m, gi=gi):
                    if debug and u == 1 and m == 0:
                        v.wait_ge(C["s_out"].h, dbg_a_mark[0])
                    for th in range(2):
                        s_pe.wait(v, pe_gu[(gi, "u", th)])
                        ins = v.tensor_tensor(UB[:, th * 512:(th + 1) * 512], pu[th][:, :],
                                              BEu[:, th * 512:(th + 1) * 512], ALU.mult)
                        dve_ub[(gi, th)] = s_dve.inc(ins)
                        v.drain()
                        s_act.wait(v, act_su[(gi, th)])
                        ins = v.tensor_tensor(
                            A[:, m * TOK + th * 512:m * TOK + (th + 1) * 512],
                            SG[:, th * 512:(th + 1) * 512],
                            UB[:, th * 512:(th + 1) * 512], ALU.mult)
                        dve_a[(u, m, th)] = s_dve.inc(ins)
                blk.vector(f_dve_gu)

            for s4 in range(4):
                di = u * 4 + s4
                par = di % 2

                def f_gw_d(gp, u=u, s4=s4, di=di, par=par, is_sh=is_sh, g=g):
                    if di >= 2:
                        s_pe.wait(gp, dn_consumed[di - 2])
                    dst = Wd[par][:, :].rearrange("p (jc h) -> p jc h", h=512)
                    if not is_sh:
                        src = dwnT[u, :, s4 * 512:(s4 + 1) * 512] \
                            .rearrange("(jc p) h -> p jc h", p=128)
                    else:
                        src = shdT[g * 1408:(g + 1) * 1408, s4 * 512:(s4 + 1) * 512] \
                            .rearrange("(jc p) h -> p jc h", p=128)
                    gw_mark[("d", di)] = s_wd[par].inc(gp.dma_start(out=dst, in_=src), 16)
                blk.gpsimd(f_gw_d)

                for hl in range(4):
                    hc = s4 * 4 + hl
                    for th in range(2):
                        nonlocal_pd = pd_use
                        pdb = pd[pd_use % 2]
                        pd_use += 1

                        def f_pe_d(t, u=u, s4=s4, di=di, hl=hl, th=th, hc=hc,
                                   pdb=pdb, slot=nonlocal_pd, par=par):
                            if hl == 0 and th == 0:
                                t.wait_ge(s_wd[par].h, gw_mark[("d", di)])
                                if s4 == 0:
                                    s_dve.wait(t, dve_a[(u, 10, 1)])
                            if slot >= 2:
                                s_dve.wait(t, pd_free[slot - 2])
                            for jc in range(11):
                                ins = t.matmul(pdb[:, :],
                                               Wd[par][:, jc * 512 + hl * 128: jc * 512 + (hl + 1) * 128],
                                               A[:, jc * TOK + th * 512: jc * TOK + (th + 1) * 512],
                                               start=(jc == 0), stop=(jc == 10))
                            pe_dn[slot] = s_pe.inc(ins)
                            if hl == 3 and th == 1:
                                dn_consumed[di] = pe_dn[slot]
                        blk.tensor(f_pe_d)

                        def f_dve_d(v, u=u, hc=hc, th=th, pdb=pdb, slot=nonlocal_pd):
                            s_pe.wait(v, pe_dn[slot])
                            ins = v.scalar_tensor_tensor(
                                X[:, hc * TOK + th * 512: hc * TOK + (th + 1) * 512],
                                pdb[:, :], 1.0,
                                X[:, hc * TOK + th * 512: hc * TOK + (th + 1) * 512],
                                ALU.mult, ALU.add)
                            pd_free[slot] = s_dve.inc(ins)
                        blk.vector(f_dve_d)

        if debug:
            def f_dbg(sy):
                s_dve.wait(sy, xb_mark)
                C["s_out"].inc(sy.dma_start(
                    out=dbg_xb.rearrange("(kc p) t -> p kc t", p=128),
                    in_=XB[:, :].rearrange("p (kc t) -> p kc t", t=TOK)), 16)
                s_dve.wait(sy, dve_m[("rdone", 7)])
                C["s_out"].inc(sy.dma_start(out=dbg_cf, in_=CF), 16)
                s_dve.wait(sy, dve_m["bg"])
                C["s_out"].inc(sy.dma_start(out=dbg_bg, in_=BG[:, :]), 16)
            blk.sync(f_dbg)

        last_x = s_dve.n
        emit_store_master(nc, blk, C, X, y_out, [(s_dve, last_x)])


def moe_io(nc, debug=False):
    io = {
        "x_in": nc.dram_tensor("x_fm", [D, TOK], F32, kind="ExternalInput").ap(),
        "y_out": nc.dram_tensor("y_fm", [D, TOK], F32, kind="ExternalOutput").ap(),
        "router_T": nc.dram_tensor("router_T", [D, 8], F32, kind="ExternalInput").ap(),
        "gup_T": nc.dram_tensor("gup_T", [8, D, 2816], F32, kind="ExternalInput").ap(),
        "down_T": nc.dram_tensor("down_T", [8, 1408, D], F32, kind="ExternalInput").ap(),
        "shg_T": nc.dram_tensor("shg_T", [D, 5632], F32, kind="ExternalInput").ap(),
        "shu_T": nc.dram_tensor("shu_T", [D, 5632], F32, kind="ExternalInput").ap(),
        "shd_T": nc.dram_tensor("shd_T", [5632, D], F32, kind="ExternalInput").ap(),
        "sgw_T": nc.dram_tensor("sgw_T", [128, KC], F32, kind="ExternalInput").ap(),
    }
    if debug:
        io["dbg_xb"] = nc.dram_tensor("dbg_xb", [D, TOK], BF16, kind="ExternalOutput").ap()
        io["dbg_cf"] = nc.dram_tensor("dbg_cf", [8, TOK], F32, kind="ExternalOutput").ap()
        io["dbg_bg"] = nc.dram_tensor("dbg_bg", [128, TOK], F32, kind="ExternalOutput").ap()
        io["dbg_a"] = nc.dram_tensor("dbg_a", [1408, TOK], BF16, kind="ExternalOutput").ap()
        io["dbg_rs"] = nc.dram_tensor("dbg_rs", [128, 72], F32, kind="ExternalOutput").ap()
    return io


def new_arena(nc):
    return Arena(nc, "AR", (nc.sbuf_bytes_remaining - 64) // 2)


def build_moe_program(scale=0.5, debug=False):
    nc = bass.Bass("TRN2", target_bir_lowering=False)
    io = moe_io(nc, debug)
    C = common_setup(nc)
    pbk = [nc.alloc_psum_tensor(f"pbk{i}", [128, 512], F32) for i in range(8)]
    with nc.Block() as blk:
        sb = emit_load_consts(nc, blk, C)
        ar = new_arena(nc)
        emit_moe(nc, blk, C, sb, ar, pbk, io, "_m", scale=scale, debug=debug)
    return nc





NQH = 32
NKV = 8
HD = 64


def prep_attn0_weights(d):
    return {
        "Wq": np.ascontiguousarray(np.asarray(d["L0_attn_Wq"])),
        "Wk": np.ascontiguousarray(np.asarray(d["L0_attn_Wk"])),
        "Wv": np.ascontiguousarray(np.asarray(d["L0_attn_Wv"])),
        "Wo": np.ascontiguousarray(np.asarray(d["L0_attn_Wo"])),
    }


def rope_tables_for_core(c):
    pos = (np.arange(512) * 8 + c).astype(np.float32)
    pos = np.concatenate([pos, pos])
    theta = 1.0 / 10000.0 ** (np.arange(0, HD, 2, dtype=np.float32) / HD)
    fr = np.outer(theta, pos)
    co, si = np.cos(fr).astype(np.float32), np.sin(fr).astype(np.float32)
    cs1 = np.concatenate([co, si, co, si], axis=0)
    cs2 = np.concatenate([si, co, si, co], axis=0)
    return np.concatenate([cs1, cs2], axis=1)


def attn_masks_for_core(c):
    out = np.zeros((8, 128, 512), np.float32)
    io = np.arange(128)
    for r in range(8):
        m = (io[:, None] + (1 if r > c else 0) <= io[None, :]).astype(np.float32)
        out[r] = np.tile(m, (1, 4))
    return out.astype(bfnp)


def _selk_table():
    t = np.zeros((128, 16, 32), np.float32)
    for k in range(16):
        for dd in range(128):
            t[dd, k, (128 * k + dd) // 64 % 32] = 1.0
    return np.ascontiguousarray(t.reshape(128, 16 * 32))


def _exp32_table():
    t = np.zeros((8, 32), np.float32)
    for h in range(32):
        t[h // 4, h] = 1.0
    return t


def emit_attn0(nc, blk, C, sb, ar, pb, io, sfx, debug=False):
    if "t_selk" not in C:
        C["t_selk"] = nc.inline_tensor(_selk_table(), "t_selk")
        C["t_exp32"] = nc.inline_tensor(_exp32_table(), "t_exp32")
    x_in = io["x_in"]
    y_out = io["y_out"]
    Wq = io["Wq"]
    Wk = io["Wk"]
    Wv = io["Wv"]
    Wo = io["Wo"]
    cs_in = io["cs_t"]
    mk_in = io["masks"]

    kag_i = nc.dram_tensor("kag_i" + sfx, [521, TOK], BF16, kind="Internal").ap()
    kag_o = nc.dram_tensor("kag_o" + sfx, [8 * 521, TOK], BF16, kind="Internal").ap()
    vag_i = nc.dram_tensor("vag_i" + sfx, [TOK, 520], BF16, kind="Internal").ap()
    vag_o = nc.dram_tensor("vag_o" + sfx, [8 * TOK, 520], BF16, kind="Internal").ap()
    mub_d = nc.dram_tensor("mub_d" + sfx, [NQH, TOK], BF16, kind="Internal").ap()
    if debug:
        dbg_qp = io["dbg_qp"]
        dbg_wq = io["dbg_wq"]
        dbg_xb = io["dbg_xb"]
        dbg_kt = io["dbg_kt"]
        dbg_vt = io["dbg_vt"]
        dbg_qr = io["dbg_qr"]
        dbg_ctx = io["dbg_ctx"]

    s_wq2 = [Sem(nc, f"s_wq2{i}{sfx}") for i in range(2)]
    s_wo2 = [Sem(nc, f"s_wo2{i}{sfx}") for i in range(2)]
    s_kl2 = [Sem(nc, f"s_kl2{i}{sfx}") for i in range(2)]
    s_vl2 = Sem(nc, "s_vl2" + sfx)
    s_mk2 = Sem(nc, "s_mk2" + sfx)
    s_mub2 = Sem(nc, "s_mub2" + sfx)
    s_kv2 = Sem(nc, "s_kv2" + sfx)
    s_x2 = [Sem(nc, f"s_x2{i}{sfx}") for i in range(2)]
    s_y2 = [Sem(nc, f"s_y2{i}{sfx}") for i in range(2)]
    s_agk = Sem(nc, "s_agk" + sfx)
    s_agv = Sem(nc, "s_agv" + sfx)

    BIGA = ar.alloc([128, KC * TOK], F32)
    Xm = BIGA[:, :]
    BAb = BIGA[:, :].bitcast(BF16)
    XB = BAb[:, 0:16384]
    VT = BAb[:, 16384:16384 + 8 * 520]
    KT = BAb[0:65, 20544:20544 + 8 * TOK]
    CTX = BAb[:, 0:16384]
    KL = [BAb[0:65, 16384 + i * 4096: 16384 + (i + 1) * 4096] for i in range(2)]
    PT = [BAb[:, 24576 + i * 512: 24576 + (i + 1) * 512] for i in range(2)]

    WBUF = ar.alloc([128, 20736], BF16)
    Wk_sb = WBUF[:, 0:8192]
    Wv_sb = WBUF[:, 8192:16384]
    Wqs = [WBUF[:, 16384 + i * 2048:16384 + (i + 1) * 2048] for i in range(2)]
    VL = WBUF[:, 0:16640]
    Wos = [WBUF[:, 16640 + i * 2048:16640 + (i + 1) * 2048] for i in range(2)]

    QR = ar.alloc([128, NQH * TOK], BF16)[0:65, :]
    QRf = QR[:, :].bitcast(F32)
    rows_scr = QRf[0:1, 0:6 * TOK]

    MK = ar.alloc([128, 8 * 512], BF16)
    SCR2 = ar.alloc([128, 3072], F32)
    sc = {"MB": SCR2[:, 0:1024], "RB": SCR2[:, 1024:2048], "TMP": SCR2[:, 2048:3072],
          "rows": rows_scr}
    ctmp = SCR2[0:64, 2048:2560]
    zr = SCR2[0:1, 1024:1536]
    zr2 = SCR2[0:1, 1536:2048]
    SQ0 = ar.alloc([128, 1024], F32)
    SQ1 = ar.alloc([128, 1024], F32)
    sc["SQ0"] = SQ0[:, :]
    sc["SQ1"] = SQ1[:, :]
    QP = ar.alloc([128, 1024], F32)
    QN = ar.alloc([128, 1024], F32)[0:32, :]
    M32 = SQ0[0:32, :]
    M32b = SQ1[0:32, :].bitcast(BF16)[:, 0:1024]
    CS = MK[:, 0:4096].bitcast(F32)
    K8 = SCR2[0:8, 1024:1032]
    KV8 = SCR2[0:8, 1032:1033]
    KV32 = SCR2[0:32, 1040:1041]
    EXP32 = SCR2[0:8, 1056:1088]
    NRMX = SCR2[0:1, 1088:1096]
    SELK = ar.alloc([128, 512], F32)
    KRW = SCR2[0:1, 2048:3072]

    s_ld, s_gw, s_pe = C["s_ld"], C["s_gw"], C["s_pe"]
    s_act, s_dve, s_pool, s_ag = C["s_act"], C["s_dve"], C["s_pool"], C["s_ag"]
    s_out = C["s_out"]

    if True:
        emit_load_master(nc, blk, C, Xm, x_in)

        def f_ld2(sy):
            s_ld.inc(sy.dma_start(out=CS[:, :], in_=cs_in), 16)
            s_ld.inc(sy.dma_start(out=SELK[:, :], in_=C["t_selk"].ap()), 16)
            s_ld.inc(sy.dma_start(out=EXP32[:, :], in_=C["t_exp32"].ap()), 16)
        blk.sync(f_ld2)

        def f_ld3(gp):
            s_gw.inc(gp.dma_start(
                out=Wk_sb.rearrange("p (kc v) -> p kc v", v=512),
                in_=Wk.rearrange("(kc p) v -> p kc v", p=128)), 16)
            s_gw.inc(gp.dma_start(
                out=Wv_sb.rearrange("p (kc v) -> p kc v", v=512),
                in_=Wv.rearrange("(kc p) v -> p kc v", p=128)), 16)
        blk.gpsimd(f_ld3)
        wkv_mark = s_gw.n

        xb_mark, ln_bank = emit_ln(nc, blk, C, sb, Xm, XB, [pb[0], pb[1]], sc)

        pe_m = {}
        act_m = {}
        dve_m = {}

        def f_v_init(v):
            s_dve.wait(v, xb_mark)
            ins = v.memset(
                VT.rearrange("p (tc g e) -> p tc g e", g=8, e=65)[:, :, :, 64:65], 1.0)
            dve_m["vt_ones"] = s_dve.inc(ins)
        blk.vector(f_v_init)

        for tc in range(8):
            def f_pe_v(t, tc=tc):
                if tc == 0:
                    s_dve.wait(t, ln_bank)
                    t.wait_ge(s_gw.h, wkv_mark)
                    s_dve.wait(t, xb_mark)
                else:
                    s_dve.wait(t, dve_m[("vte", tc - 1)])
                for k in range(KC):
                    ins = t.matmul(pb[2 + tc % 2][:, :],
                                   XB[:, k * TOK + tc * 128:k * TOK + (tc + 1) * 128],
                                   Wv_sb[:, k * 512:(k + 1) * 512],
                                   start=(k == 0), stop=(k == KC - 1))
                pe_m[("v", tc)] = s_pe.inc(ins)
            blk.tensor(f_pe_v)

            def f_dve_v(v, tc=tc):
                s_pe.wait(v, pe_m[("v", tc)])
                if tc == 0:
                    pass
                dst = VT[:, tc * 520:(tc + 1) * 520] \
                    .rearrange("p (g e) -> p g e", e=65)[:, :, 0:64]
                src = pb[2 + tc % 2][:, :].rearrange("p (g e) -> p g e", e=64)
                ins = v.tensor_copy(dst, src)
                dve_m[("vte", tc)] = s_dve.inc(ins)
            blk.vector(f_dve_v)

        def f_vout(sy):
            s_dve.wait(sy, dve_m[("vte", 7)])
            s_dve.wait(sy, dve_m["vt_ones"])
            s_kv2.inc(sy.dma_start(
                out=vag_i.rearrange("(tc p) ge -> p tc ge", p=128),
                in_=VT.rearrange("p (tc ge) -> p tc ge", ge=520)), 16)
        blk.sync(f_vout)
        vout_mark = s_kv2.n

        def f_k_init(v):
            ins = v.memset(KT[64:65, :].bitcast(BF16), 1.0)
            dve_m["kt_ones"] = s_dve.inc(ins)
        blk.vector(f_k_init)

        for kc2 in range(4):
            def f_pe_k(t, kc2=kc2):
                if kc2 > 0:
                    s_act.wait(t, act_m[("kqp", kc2 - 1)])
                for th in range(2):
                    for k in range(KC):
                        ins = t.matmul(pb[4 + th][:, :],
                                       Wk_sb[:, k * 512 + kc2 * 128:k * 512 + kc2 * 128 + 128],
                                       XB[:, k * TOK + th * 512:k * TOK + (th + 1) * 512],
                                       start=(k == 0), stop=(k == KC - 1))
                    pe_m[("k", kc2, th)] = s_pe.inc(ins)
            blk.tensor(f_pe_k)

            def f_act_k(a, kc2=kc2):
                for th in range(2):
                    s_pe.wait(a, pe_m[("k", kc2, th)])
                    if kc2 > 0:
                        s_dve.wait(a, dve_m[("krope", kc2 - 1)])
                        s_act.wait(a, act_m[("ksq", kc2 - 1)])
                    ins = a.activation(QP[:, th * 512:(th + 1) * 512],
                                       pb[4 + th][:, :], AF.Copy)
                act_m[("kqp", kc2)] = s_act.inc(ins)
            blk.scalar(f_act_k)

            def f_dve_k(v, kc2=kc2):
                s_act.wait(v, act_m[("kqp", kc2)])
                t1 = sc["TMP"][0:32, :]
                t2 = SCR2[0:32, 0:1024]
                for e in range(2):
                    gidx = 2 * kc2 + e
                    qa = QP[64 * e:64 * e + 32, :]
                    qb = QP[64 * e + 32:64 * e + 64, :]
                    cs_c = CS[64 * e:64 * e + 32, 0:1024]
                    cs_s = CS[64 * e + 32:64 * e + 64, 0:1024]
                    cs_s2 = CS[64 * e:64 * e + 32, 1024:2048]
                    cs_c2 = CS[64 * e + 32:64 * e + 64, 1024:2048]
                    kt_a = KT[0:32, gidx * TOK:(gidx + 1) * TOK]
                    kt_b = KT[32:64, gidx * TOK:(gidx + 1) * TOK]
                    v.tensor_tensor(t1, qa, cs_c, ALU.mult)
                    v.tensor_tensor(t2, qb, cs_s, ALU.mult)
                    v.drain()
                    v.tensor_tensor(kt_a, t1, t2, ALU.subtract)
                    v.tensor_tensor(t1, qb, cs_c2, ALU.mult)
                    v.tensor_tensor(t2, qa, cs_s2, ALU.mult)
                    v.drain()
                    ins = v.tensor_tensor(kt_b, t1, t2, ALU.add)
                dve_m[("krope", kc2)] = s_dve.inc(ins)
            blk.vector(f_dve_k)

            def f_act_k2(a, kc2=kc2):
                s_act.wait(a, act_m[("kqp", kc2)])
                if kc2 > 0:
                    s_pe.wait(a, pe_m[("knrm", kc2 - 1, 1, 1)])
                ins = a.activation(SQ0[:, :], QP[:, :], AF.Square)
                act_m[("ksq", kc2)] = s_act.inc(ins)
            blk.scalar(f_act_k2)

            for e in range(2):
                def f_pe_k2(t, kc2=kc2, e=e):
                    s_act.wait(t, act_m[("ksq", kc2)])
                    if (kc2, e) != (0, 0):
                        prev = (kc2, e - 1) if e == 1 else (kc2 - 1, 1)
                        s_dve.wait(t, dve_m[("knc", prev[0], prev[1])])
                    ins = t.matmul(pb[6][0:1, :], SELK[:, e:e + 1],
                                   SQ0[:, 0:512], start=True, stop=True)
                    pe_m[("knrm", kc2, e, 0)] = s_pe.inc(ins)
                    ins = t.matmul(pb[7][0:1, :], SELK[:, e:e + 1],
                                   SQ0[:, 512:1024], start=True, stop=True)
                    pe_m[("knrm", kc2, e, 1)] = s_pe.inc(ins)
                blk.tensor(f_pe_k2)

                def f_dve_k2(v, kc2=kc2, e=e):
                    gidx = 2 * kc2 + e
                    s_pe.wait(v, pe_m[("knrm", kc2, e, 0)])
                    v.tensor_copy(KRW[:, 0:512], pb[6][0:1, :])
                    s_pe.wait(v, pe_m[("knrm", kc2, e, 1)])
                    v.tensor_copy(KRW[:, 512:1024], pb[7][0:1, :])
                    v.drain()
                    ins = v.reduce_max(NRMX[:, gidx:gidx + 1], KRW[:, :], axis=AX.X)
                    dve_m[("knc", kc2, e)] = s_dve.inc(ins)
                blk.vector(f_dve_k2)

        def f_kout(sy):
            s_dve.wait(sy, dve_m[("krope", 3)])
            s_dve.wait(sy, dve_m["kt_ones"])
            s_kv2.inc(sy.dma_start(
                out=kag_i[0:520, :].rearrange("(g dd) t -> dd g t", dd=65),
                in_=KT.rearrange("dd (g t) -> dd g t", t=TOK)), 16)
        blk.sync(f_kout)
        kout_mark = s_kv2.n

        def f_nout(gp):
            s_dve.wait(gp, dve_m[("knc", 3, 1)])
            s_gw.inc(gp.dma_start(out=kag_i[520:521, 0:8], in_=NRMX[:, :]), 16)
        blk.gpsimd(f_nout)
        nout_mark = s_gw.n

        dbg_mark = [0]
        dbg_m2 = [0]
        if debug:
            def f_dbg1(sy):
                s_dve.wait(sy, dve_m[("krope", 3)])
                s_dve.wait(sy, dve_m[("vte", 7)])
                s_out.inc(sy.dma_start(out=dbg_xb.rearrange("(kc p) t -> p kc t", p=128),
                                       in_=XB.rearrange("p (kc t) -> p kc t", t=TOK)), 16)
                s_out.inc(sy.dma_start(out=dbg_kt, in_=KT), 16)
                dbg_mark[0] = s_out.inc(sy.dma_start(out=dbg_vt, in_=VT), 16)
            blk.sync(f_dbg1)

        def f_ag(gp):
            gp.wait_ge(s_kv2.h, kout_mark)
            gp.wait_ge(s_gw.h, nout_mark)
            ins = gp.collective_compute(
                "AllGather", ALU.bypass, replica_groups=[list(range(8))],
                ins=[kag_i], outs=[kag_o])
            s_agk.inc(ins, 1)
            ins = gp.collective_compute(
                "AllGather", ALU.bypass, replica_groups=[list(range(8))],
                ins=[vag_i], outs=[vag_o])
            s_agv.inc(ins, 1)
        blk.gpsimd(f_ag)

        for qc in range(16):
            par = qc % 2

            def f_gw_q(gp, qc=qc, par=par):
                if qc >= 2:
                    s_pe.wait(gp, pe_m[("q", qc - 2, 1)])
                    if debug and qc == 2:
                        gp.wait_ge(s_out.h, dbg_m2[0])
                src = Wq[:, qc * 128:(qc + 1) * 128].rearrange(
                    "(kc p) o -> p kc o", p=128)
                dst = Wqs[par].rearrange("p (kc o) -> p kc o", o=128)
                pe_m[("qw", qc)] = s_wq2[par].inc(gp.dma_start(out=dst, in_=src), 16)
            blk.gpsimd(f_gw_q)

            def f_pe_q(t, qc=qc, par=par):
                t.wait_ge(s_wq2[par].h, pe_m[("qw", qc)])
                if qc == 0:
                    s_dve.wait(t, dve_m[("vte", 7)])
                else:
                    s_act.wait(t, act_m[("qqp", qc - 1)])
                for th in range(2):
                    for k in range(KC):
                        ins = t.matmul(pb[2 + th][:, :],
                                       Wqs[par][:, k * 128:(k + 1) * 128],
                                       XB[:, k * TOK + th * 512:k * TOK + (th + 1) * 512],
                                       start=(k == 0), stop=(k == KC - 1))
                    pe_m[("q", qc, th)] = s_pe.inc(ins)
            blk.tensor(f_pe_q)

            def f_act_q(a, qc=qc):
                for th in range(2):
                    s_pe.wait(a, pe_m[("q", qc, th)])
                    if qc > 0:
                        s_dve.wait(a, dve_m[("qrope", qc - 1)])
                        s_act.wait(a, act_m[("qsq", qc - 1)])
                    ins = a.activation(QP[:, th * 512:(th + 1) * 512],
                                       pb[2 + th][:, :], AF.Copy, scale=0.125)
                act_m[("qqp", qc)] = s_act.inc(ins)
            blk.scalar(f_act_q)

            if debug and qc == 0:
                def f_dbg_qp(sy):
                    s_act.wait(sy, act_m[("qqp", 0)])
                    s_out.inc(sy.dma_start(out=dbg_wq, in_=Wqs[0][:, :]), 16)
                    dbg_m2[0] = s_out.inc(sy.dma_start(out=dbg_qp, in_=QP[:, :]), 16)
                blk.sync(f_dbg_qp)

            def f_dve_q(v, qc=qc):
                if debug and qc == 0:
                    v.wait_ge(s_out.h, dbg_m2[0])
                s_act.wait(v, act_m[("qqp", qc)])
                t1 = sc["TMP"][0:32, :]
                t2 = SCR2[0:32, 0:1024]
                for e in range(2):
                    h = 2 * qc + e
                    qa = QP[64 * e:64 * e + 32, :]
                    qb = QP[64 * e + 32:64 * e + 64, :]
                    cs_c = CS[64 * e:64 * e + 32, 0:1024]
                    cs_s = CS[64 * e + 32:64 * e + 64, 0:1024]
                    cs_s2 = CS[64 * e:64 * e + 32, 1024:2048]
                    cs_c2 = CS[64 * e + 32:64 * e + 64, 1024:2048]
                    qt_a = QR[0:32, h * TOK:(h + 1) * TOK]
                    qt_b = QR[32:64, h * TOK:(h + 1) * TOK]
                    v.tensor_tensor(t1, qa, cs_c, ALU.mult)
                    v.tensor_tensor(t2, qb, cs_s, ALU.mult)
                    v.drain()
                    v.tensor_tensor(qt_a, t1, t2, ALU.subtract)
                    v.tensor_tensor(t1, qb, cs_c2, ALU.mult)
                    v.tensor_tensor(t2, qa, cs_s2, ALU.mult)
                    v.drain()
                    ins = v.tensor_tensor(qt_b, t1, t2, ALU.add)
                dve_m[("qrope", qc)] = s_dve.inc(ins)
            blk.vector(f_dve_q)

            def f_act_q2(a, qc=qc):
                s_act.wait(a, act_m[("qqp", qc)])
                if qc >= 2:
                    s_pe.wait(a, pe_m[("qn", qc - 2)])
                SQx = SQ0 if qc % 2 == 0 else SQ1
                ins = a.activation(SQx[:, :], QP[:, :], AF.Square)
                act_m[("qsq", qc)] = s_act.inc(ins)
            blk.scalar(f_act_q2)

            def f_pe_q2(t, qc=qc):
                s_act.wait(t, act_m[("qsq", qc)])
                if qc == 0:
                    s_dve.wait(t, dve_m[("knc", 3, 1)])
                SQx = SQ0 if qc % 2 == 0 else SQ1
                for th in range(2):
                    ins = t.matmul(pb[6 + th][0:32, :],
                                   SELK[:, qc * 32:(qc + 1) * 32],
                                   SQx[:, th * 512:(th + 1) * 512],
                                   start=(qc == 0), stop=(qc == KC - 1))
                pe_m[("qn", qc)] = s_pe.inc(ins)
            blk.tensor(f_pe_q2)

        qproj_done = pe_m[("q", 15, 1)]

        def f_dve_qn(v):
            s_pe.wait(v, pe_m[("qn", 15)])
            v.tensor_copy(QN[:, 0:512], pb[6][0:32, :])
            ins = v.tensor_copy(QN[:, 512:1024], pb[7][0:32, :])
            dve_m["qn"] = s_dve.inc(ins)
        blk.vector(f_dve_qn)

        def f_mkld(sy):
            s_dve.wait(sy, dve_m[("qrope", 15)])
            s_mk2.inc(sy.dma_start(out=MK[:, :].rearrange("p (r q) -> p r q", q=512),
                                   in_=mk_in.rearrange("r p q -> p r q")), 16)
        blk.sync(f_mkld)
        mk_mark = s_mk2.n

        def f_km1(gp):
            gp.wait_ge(s_agk.h, 1)
            src = kag_o[:, 0:8].rearrange("(r q) h -> r q h", q=521)[:, 520, :]
            s_gw.inc(gp.dma_start(out=K8[:, :], in_=src), 16)
        blk.gpsimd(f_km1)
        km1_mark = s_gw.n

        def f_pe_km(t):
            t.wait_ge(s_gw.h, km1_mark)
            ins = t.transpose(pb[4][0:8, 0:8], K8[:, :], sb["idn_f"][0:8, 0:8])
            pe_m["k8t"] = s_pe.inc(ins)
        blk.tensor(f_pe_km)

        def f_dve_km(v):
            s_pe.wait(v, pe_m["k8t"])
            v.tensor_copy(K8[:, :], pb[4][0:8, 0:8])
            v.drain()
            ins = v.reduce_max(KV8[:, :], K8[:, :], axis=AX.X)
            dve_m["kv8"] = s_dve.inc(ins)
        blk.vector(f_dve_km)

        def f_pe_km2(t):
            s_dve.wait(t, dve_m["kv8"])
            ins = t.matmul(pb[5][0:32, 0:1], EXP32[:, :], KV8[:, :],
                           start=True, stop=True)
            pe_m["kv32"] = s_pe.inc(ins)
        blk.tensor(f_pe_km2)

        def f_dve_km2(v):
            s_pe.wait(v, pe_m["kv32"])
            v.tensor_copy(KV32[:, :], pb[5][0:32, 0:1])
            s_dve.wait(v, dve_m["qn"])
            s_pe.wait(v, pe_m[("qn", 15)])
            v.drain()
            ins = v.tensor_scalar_mul(M32[:, :], QN[:, :], KV32[:, :])
            dve_m["m32a"] = s_dve.inc(ins)
        blk.vector(f_dve_km2)

        def f_act_km(a):
            s_dve.wait(a, dve_m["m32a"])
            ins = a.activation(M32[:, :], M32[:, :], AF.Sqrt)
            act_m["m32s"] = s_act.inc(ins)
        blk.scalar(f_act_km)

        def f_dve_km3(v):
            s_act.wait(v, act_m["m32s"])
            v.tensor_scalar_mul(M32[:, :], M32[:, :], -1.0)
            v.drain()
            ins = v.tensor_copy(M32b[:, :], M32[:, :])
            dve_m["m32b"] = s_dve.inc(ins)
        blk.vector(f_dve_km3)

        def f_mub(sy):
            s_dve.wait(sy, dve_m["m32b"])
            s_mub2.inc(sy.dma_start(out=mub_d, in_=M32b[:, :]), 16)
            sy.wait_ge(s_mub2.h, s_mub2.n)
            s_mub2.inc(sy.dma_start(out=QR[64:65, :],
                                    in_=mub_d.rearrange("h t -> (h t)")), 16)
        blk.sync(f_mub)
        mub_mark = s_mub2.n

        kl_mark = {}
        vl_mark = {}
        cnt = [0]
        pbO = [pb[2], pb[6]]
        pbZ = [pb[3], pb[7]]

        for b in range(2):
            def f_vl(sy, b=b):
                sy.wait_ge(s_agv.h, 1)
                s_pe.wait(sy, qproj_done)
                if b > 0:
                    s_pe.wait(sy, pe_m[("avl", b - 1)])
                for aa in range(4):
                    src = vag_o.rearrange("(r aa p) e -> p r aa e", p=128, aa=8)[
                        :, :, 4 * b + aa, :]
                    dst = VL.rearrange("p (r aa e) -> p r aa e", r=8, e=520)[
                        :, :, aa, :]
                    vl_mark[b] = s_vl2.inc(sy.dma_start(out=dst, in_=src), 16)
            blk.sync(f_vl)

            for g in range(8):
                par = (b * 8 + g) % 2

                def f_kl(sy, b=b, g=g, par=par):
                    sy.wait_ge(s_agk.h, 1)
                    bg = b * 8 + g
                    if bg < 2:
                        sy.wait_ge(s_mub2.h, mub_mark)
                        if debug:
                            sy.wait_ge(s_out.h, dbg_mark[0])
                    if bg >= 2:
                        s_pe.wait(sy, pe_m[("kl_done", bg - 2)])
                    src = kag_o.rearrange("(r z) i -> z r i", z=521)[
                        65 * g:65 * g + 65, :, b * 512:(b + 1) * 512]
                    dst = KL[par].rearrange("dd (r i) -> dd r i", i=512)
                    kl_mark[(b, g)] = s_kl2[par].inc(sy.dma_start(out=dst, in_=src), 16)
                blk.sync(f_kl)

                for qt in range(4):
                    qoff = 4 * g * TOK + b * 512 + qt * 128
                    nblk = (qt + 1) * 8
                    qi = (b * 8 + g) * 4 + qt

                    first = True
                    for a_ in range(qt + 1):
                        for r in range(8):
                            ci = cnt[0]
                            cnt[0] += 1
                            sb_ = 4 + (ci % 2)

                            def f_pe_s(t, b=b, g=g, qt=qt, a_=a_, r=r, ci=ci,
                                       sb_=sb_, par=par, qoff=qoff):
                                if ci == 0:
                                    t.wait_ge(s_mub2.h, mub_mark)
                                t.wait_ge(s_kl2[par].h, kl_mark[(b, g)])
                                t.wait_ge(s_vl2.h, vl_mark[b])
                                if ci >= 2:
                                    s_act.wait(t, act_m[("exp", ci - 2)])
                                rhs = QR[0:65, :].rearrange(
                                    "dd (h i) -> dd h i", i=TOK)[
                                    :, 4 * g:4 * g + 4,
                                    b * 512 + qt * 128: b * 512 + (qt + 1) * 128]
                                ins = t.matmul(pb[sb_][:, :],
                                               KL[par][:, r * 512 + a_ * 128:
                                                       r * 512 + (a_ + 1) * 128],
                                               rhs, start=True, stop=True)
                                pe_m[("s", ci)] = s_pe.inc(ins)
                            blk.tensor(f_pe_s)

                            def f_act_e(a, ci=ci, sb_=sb_):
                                s_pe.wait(a, pe_m[("s", ci)])
                                if ci >= 2:
                                    s_pe.wait(a, pe_m[("av", ci - 2)])
                                ins = a.activation(PT[ci % 2][:, :], pb[sb_][:, :],
                                                   AF.Exp)
                                act_m[("exp", ci)] = s_act.inc(ins)
                            blk.scalar(f_act_e)

                            masked = (a_ == qt)
                            if masked:
                                def f_dve_mk(v, ci=ci, r=r):
                                    v.wait_ge(s_mk2.h, mk_mark)
                                    s_act.wait(v, act_m[("exp", ci)])
                                    ins = v.tensor_tensor(
                                        PT[ci % 2][:, :], PT[ci % 2][:, :],
                                        MK[:, r * 512:(r + 1) * 512], ALU.mult)
                                    dve_m[("mk", ci)] = s_dve.inc(ins)
                                blk.vector(f_dve_mk)

                            def f_pe_av(t, g=g, a_=a_, r=r, ci=ci, qt=qt,
                                        first=first, masked=masked, qi=qi):
                                if masked:
                                    s_dve.wait(t, dve_m[("mk", ci)])
                                else:
                                    s_act.wait(t, act_m[("exp", ci)])
                                if first and qi >= 2:
                                    s_dve.wait(t, dve_m[("zi", qi - 2)])
                                last = (a_ == qt and r == 7)
                                ins = t.matmul(pbO[qi % 2][0:65, :],
                                               VL[:, (r * 4 + a_) * 520 + 65 * g:
                                                  (r * 4 + a_) * 520 + 65 * g + 65],
                                               PT[ci % 2][:, :],
                                               start=first, stop=last)
                                pe_m[("av", ci)] = s_pe.inc(ins)
                                if last:
                                    pe_m[("av_last", qi)] = pe_m[("av", ci)]
                            blk.tensor(f_pe_av)
                            first = False

                    def f_dve_ev(v, qi=qi):
                        s_pe.wait(v, pe_m[("av_last", qi)])
                        v.tensor_copy(ctmp[:, :], pbO[qi % 2][0:64, :])
                        v.tensor_copy(zr[:, :], pbO[qi % 2][64:65, :])
                        v.drain()
                        ins = v.reciprocal(zr2[:, :], zr[:, :])
                        dve_m[("zi", qi)] = s_dve.inc(ins)
                    blk.vector(f_dve_ev)

                    def f_pe_z(t, qi=qi):
                        s_dve.wait(t, dve_m[("zi", qi)])
                        if qi >= 2:
                            s_dve.wait(t, dve_m[("ctxw", qi - 2)])
                        ins = t.matmul(pbZ[qi % 2][0:64, :], sb["ones_m"][:, 0:64],
                                       zr2[:, :], start=True, stop=True)
                        pe_m[("z", qi)] = s_pe.inc(ins)
                    blk.tensor(f_pe_z)

                    def f_dve_ev2(v, b=b, g=g, qt=qt, qi=qi):
                        s_pe.wait(v, pe_m[("z", qi)])
                        for hp in range(4):
                            h = 4 * g + hp
                            dst = CTX[64 * (h % 2):64 * (h % 2) + 64,
                                      (h // 2) * TOK + b * 512 + qt * 128:
                                      (h // 2) * TOK + b * 512 + (qt + 1) * 128]
                            ins = v.tensor_tensor(
                                dst, ctmp[:, hp * 128:(hp + 1) * 128],
                                pbZ[qi % 2][0:64, hp * 128:(hp + 1) * 128], ALU.mult)
                        dve_m[("ctxw", qi)] = s_dve.inc(ins)
                    blk.vector(f_dve_ev2)

                pe_m[("kl_done", b * 8 + g)] = pe_m[("s", cnt[0] - 1)]
            pe_m[("avl", b)] = pe_m[("av_last", (b * 8 + 7) * 4 + 3)]

        ctx_all = dve_m[("ctxw", 63)]

        oc_mark = {}
        for hc in range(16):
            par = hc % 2

            def f_gw_o(gp, hc=hc, par=par):
                s_pe.wait(gp, qproj_done)
                if hc >= 2:
                    s_pe.wait(gp, pe_m[("o", hc - 2, 1)])
                src = Wo[:, hc * 128:(hc + 1) * 128].rearrange(
                    "(ec p) o -> p ec o", p=128)
                dst = Wos[par].rearrange("p (ec o) -> p ec o", o=128)
                oc_mark[hc] = s_wo2[par].inc(gp.dma_start(out=dst, in_=src), 16)
            blk.gpsimd(f_gw_o)

            def f_xr(sy, hc=hc):
                if hc >= 1:
                    s_dve.wait(sy, dve_m.get(("yt", hc - 1, 1), 0))
                for th in range(2):
                    src = x_in.rearrange("(kc p) t -> p kc t", p=128)[
                        :, hc, th * 512:(th + 1) * 512]
                    oc_mark[("x", hc, th)] = s_x2[th].inc(
                        sy.dma_start(out=(SQ0 if th == 0 else SQ1)[:, 0:512],
                                     in_=src), 16)
            blk.sync(f_xr)

            for th in range(2):
                def f_pe_o(t, hc=hc, th=th, par=par):
                    t.wait_ge(s_wo2[par].h, oc_mark[hc])
                    if hc == 0 and th == 0:
                        s_dve.wait(t, ctx_all)
                    prev = (hc, th - 1) if th == 1 else (hc - 1, 1)
                    if hc > 0 or th > 0:
                        s_dve.wait(t, dve_m.get(("yt",) + prev, 0))
                    for ec in range(KC):
                        ins = t.matmul(pb[4 + th][:, :],
                                       Wos[par][:, ec * 128:(ec + 1) * 128],
                                       CTX[:, ec * TOK + th * 512:
                                           ec * TOK + (th + 1) * 512],
                                       start=(ec == 0), stop=(ec == KC - 1))
                    pe_m[("o", hc, th)] = s_pe.inc(ins)
                blk.tensor(f_pe_o)

                def f_dve_o(v, hc=hc, th=th):
                    s_pe.wait(v, pe_m[("o", hc, th)])
                    v.wait_ge(s_x2[th].h, oc_mark[("x", hc, th)])
                    if hc > 0:
                        v.wait_ge(s_y2[th].h, oc_mark.get(("y", hc - 1, th), 0))
                    xr = (SQ0 if th == 0 else SQ1)[:, 0:512]
                    yt = (SQ0 if th == 0 else SQ1)[:, 512:1024]
                    ins = v.scalar_tensor_tensor(yt, pb[4 + th][:, :], 1.0, xr,
                                                 ALU.mult, ALU.add)
                    dve_m[("yt", hc, th)] = s_dve.inc(ins)
                blk.vector(f_dve_o)

                def f_yo(sy, hc=hc, th=th):
                    s_dve.wait(sy, dve_m[("yt", hc, th)])
                    yt = (SQ0 if th == 0 else SQ1)[:, 512:1024]
                    dst = y_out.rearrange("(kc p) t -> p kc t", p=128)[
                        :, hc, th * 512:(th + 1) * 512]
                    oc_mark[("y", hc, th)] = s_y2[th].inc(sy.dma_start(out=dst, in_=yt), 16)
                    s_out.inc(sy.nop(), 0) if False else None
                blk.sync(f_yo)

        if debug:
            def f_dbg2(sy):
                sy.wait_ge(s_mub2.h, mub_mark)
                s_out.inc(sy.dma_start(out=dbg_qr, in_=QR[:, :]), 16)
                s_dve.wait(sy, ctx_all)
                s_out.inc(sy.dma_start(out=dbg_ctx, in_=CTX), 16)
            blk.sync(f_dbg2)

        def f_fin(sy):
            s_y2[0].wait(sy)
            s_y2[1].wait(sy)
            s_out.wait(sy)
        blk.sync(f_fin)


def attn0_io(nc, debug=False):
    io = {
        "x_in": nc.dram_tensor("x_fm", [D, TOK], F32, kind="ExternalInput").ap(),
        "y_out": nc.dram_tensor("y_fm", [D, TOK], F32, kind="ExternalOutput").ap(),
        "Wq": nc.dram_tensor("Wq", [D, D], F32, kind="ExternalInput").ap(),
        "Wk": nc.dram_tensor("Wk", [D, 512], F32, kind="ExternalInput").ap(),
        "Wv": nc.dram_tensor("Wv", [D, 512], F32, kind="ExternalInput").ap(),
        "Wo": nc.dram_tensor("Wo", [D, D], F32, kind="ExternalInput").ap(),
        "cs_t": nc.dram_tensor("cs_t", [128, 2048], F32, kind="ExternalInput").ap(),
        "masks": nc.dram_tensor("masks", [8, 128, 512], BF16, kind="ExternalInput").ap(),
    }
    if debug:
        io["dbg_qp"] = nc.dram_tensor("dbg_qp", [128, TOK], F32, kind="ExternalOutput").ap()
        io["dbg_wq"] = nc.dram_tensor("dbg_wq", [128, 2048], BF16, kind="ExternalOutput").ap()
        io["dbg_xb"] = nc.dram_tensor("dbg_xb", [D, TOK], BF16, kind="ExternalOutput").ap()
        io["dbg_kt"] = nc.dram_tensor("dbg_kt", [65, 8 * TOK], BF16, kind="ExternalOutput").ap()
        io["dbg_vt"] = nc.dram_tensor("dbg_vt", [128, 4160], BF16, kind="ExternalOutput").ap()
        io["dbg_qr"] = nc.dram_tensor("dbg_qr", [65, NQH * TOK], BF16, kind="ExternalOutput").ap()
        io["dbg_ctx"] = nc.dram_tensor("dbg_ctx", [128, 16384], BF16, kind="ExternalOutput").ap()
    return io


def build_attn0_program(debug=False):
    nc = bass.Bass("TRN2", target_bir_lowering=False)
    io = attn0_io(nc, debug)
    C = common_setup(nc)
    C["t_selk"] = nc.inline_tensor(_selk_table(), "t_selk")
    C["t_exp32"] = nc.inline_tensor(_exp32_table(), "t_exp32")
    pbk = [nc.alloc_psum_tensor(f"pbk{i}", [128, 512], F32) for i in range(8)]
    with nc.Block() as blk:
        sb = emit_load_consts(nc, blk, C)
        ar = new_arena(nc)
        emit_attn0(nc, blk, C, sb, ar, pbk, io, "_a0", debug=debug)
    return nc



def prep_ffn1_weights(d, c):
    mu = np.ascontiguousarray(np.asarray(d["L1_ffn_mu"]).reshape(D, 2048))
    w = {
        "gamma": np.ascontiguousarray(np.asarray(d["L1_ffn_gamma"]).reshape(KC, 128).T),
        "beta": np.ascontiguousarray(np.asarray(d["L1_ffn_beta"]).reshape(KC, 128).T),
        "mu_f": mu,
        "mu_sl": np.ascontiguousarray(mu[:, 256 * c:256 * (c + 1)]),
        "scale1": np.asarray(d["L1_ffn_scale"]).reshape(1, 1).astype(np.float32),
        "W1_sl": np.ascontiguousarray(np.asarray(d["L1_ffn_W1"])[2 * c:2 * c + 2]),
        "b1_sl": np.ascontiguousarray(
            np.asarray(d["L1_ffn_b1"])[2 * c:2 * c + 2].reshape(2, 4, 128).transpose(2, 0, 1).reshape(128, 8)),
        "W2_sl": np.ascontiguousarray(np.asarray(d["L1_ffn_W2"])[2 * c:2 * c + 2]),
        "b2_sl": np.ascontiguousarray(np.asarray(d["L1_ffn_b2"])[2 * c:2 * c + 2]),
    }
    return w


def emit_ffn1(nc, blk, C, sb, ar, pb, io, sfx, scale=0.5, debug=False):
    x_in = io["x_in"]
    y_out = io["y_out"]
    gam = io["gamma"]
    bet = io["beta"]
    muf = io["mu_f"]
    musl = io["mu_sl"]
    sc1 = io["scale1"]
    W1s = io["W1_sl"]
    b1s = io["b1_sl"]
    W2s = io["W2_sl"]
    b2s = io["b2_sl"]

    xag_i = nc.dram_tensor("xag_i" + sfx, [2 * D, TOK], BF16, kind="Internal").ap()
    xag_o = nc.dram_tensor("xag_o" + sfx, [8 * 2 * D, TOK], BF16, kind="Internal").ap()
    slag_i = nc.dram_tensor("slag_i" + sfx, [512, D], BF16, kind="Internal").ap()
    slag_o = nc.dram_tensor("slag_o" + sfx, [8 * 512, D], BF16, kind="Internal").ap()

    s_xl = [Sem(nc, f"s_xl{i}{sfx}") for i in range(2)]
    s_sl = [Sem(nc, f"s_sl{i}{sfx}") for i in range(2)]
    s_mu = [Sem(nc, f"s_mu{i}{sfx}") for i in range(2)]
    s_x2 = [Sem(nc, f"s_x2{i}{sfx}") for i in range(2)]
    s_y2 = [Sem(nc, f"s_y2{i}{sfx}") for i in range(2)]
    s_so = Sem(nc, "s_so" + sfx)
    s_agx = Sem(nc, "s_agx" + sfx)
    s_agsl = Sem(nc, "s_agsl" + sfx)

    BIGA = ar.alloc([128, KC * TOK], F32)
    Xm = BIGA[:, :]
    BAb = BIGA[:, :].bitcast(BF16)
    XB2 = BAb[:, 0:16384]
    XN = ar.alloc([128, KC * TOK], BF16)
    EXA = ar.alloc([128, 32 * 256], BF16)
    XLT = [ar.alloc([128, 2048], BF16) for i in range(2)]
    MUNS = ar.alloc([128, KC * 256], BF16)
    W1B = BAb[:, 0:8192]
    W2B = BAb[:, 8192:16384]
    B2B = BIGA[:, 8192:10240]
    PALL = BAb[:, 0:16384]
    SLT = [BAb[:, 16384 + i * 2048:16384 + (i + 1) * 2048] for i in range(2)]
    MUC = [BAb[:, 20480 + i * 2048:20480 + (i + 1) * 2048] for i in range(2)]
    ZIB = BIGA[:, 11264 + 1024:11264 + 2048]
    SCR2 = ar.alloc([128, 3072], F32)
    sc = {"MB": SCR2[:, 0:1024], "RB": SCR2[:, 1024:2048], "TMP": SCR2[:, 2048:3072]}
    SQ0 = ar.alloc([128, 1024], F32)
    SQ1 = ar.alloc([128, 1024], F32)
    sc["SQ0"] = SQ0[:, :]
    sc["SQ1"] = SQ1[:, :]
    XNf = XN[:, :].bitcast(F32)
    sc["rows"] = XNf[0:1, 0:6 * TOK]
    r_ss = SCR2[0:1, 1024:2048]
    r_z = SCR2[0:1, 2048:3072]
    cn_row = SCR2[0:1, 0:2048]
    zrow = SCR2[0:1, 2048:3072]
    SMALL = ar.alloc([128, 96], F32)
    sc_gb = SMALL[:, 32:64]
    s_scale = SMALL[0:1, 0:1]
    s_negabs = SMALL[0:1, 1:2]
    s_nb = SMALL[:, 2:3]
    ZC = SMALL[:, 4:6]
    ZBT = SCR2[:, 0:256]
    b2row = SCR2[0:1, 256:2304]
    B1C = SMALL[:, 8:16]
    RNC = SMALL[:, 16:32]
    HFM = ar.alloc([128, 1024], BF16)
    SLIN = ar.alloc([128, KC * 256], BF16)
    STG = ar.alloc([128, 2 * 512], BF16)

    s_ld, s_gw, s_pe = C["s_ld"], C["s_gw"], C["s_pe"]
    s_act, s_dve, s_out = C["s_act"], C["s_dve"], C["s_out"]

    if True:
        emit_load_master(nc, blk, C, Xm, x_in)

        def f_ld2(gp):
            s_gw.inc(gp.dma_start(
                out=MUNS[:, :].rearrange("p (kc e) -> p kc e", e=256),
                in_=musl.rearrange("(kc p) e -> p kc e", p=128)), 16)
            s_gw.inc(gp.dma_start(out=SMALL[0:1, 0:1], in_=sc1), 16)
            s_gw.inc(gp.dma_start(out=B1C[:, :], in_=b1s), 16)
        blk.gpsimd(f_ld2)
        ld2_mark = s_gw.n

        xb_mark, bank1 = emit_ln(nc, blk, C, sb, Xm, None, [pb[0], pb[1]], sc,
                                 XF=Xm)
        sc["gb"] = sc_gb
        xb2_mark, bank2 = emit_ln(nc, blk, C, sb, Xm, XB2, [pb[0], pb[1]], sc,
                                  x_ready=[(C["s_dve"], xb_mark)],
                                  gamma_ap=gam, beta_ap=bet, XF=None)

        pe_m = {}
        act_m = {}
        dve_m = {}

        for k in range(KC):
            def f_act_l2(a, k=k):
                if k == 0:
                    s_dve.wait(a, xb2_mark)
                else:
                    s_pe.wait(a, pe_m[("ss", k - 1)])
                act_m[("sq", k)] = C["s_act"].inc(
                    a.activation(SQ0[:, :], XB2[:, k * TOK:(k + 1) * TOK], AF.Square))
            blk.scalar(f_act_l2)

            def f_pe_l2(t, k=k):
                s_act.wait(t, act_m[("sq", k)])
                if k == 0:
                    s_dve.wait(t, bank2)
                for th in range(2):
                    ins = t.matmul(pb[th][0:1, :], sb["ones_k"][:, :],
                                   SQ0[:, th * 512:(th + 1) * 512],
                                   start=(k == 0), stop=(k == KC - 1))
                pe_m[("ss", k)] = s_pe.inc(ins)
            blk.tensor(f_pe_l2)

        def f_dve_l2(v):
            s_pe.wait(v, pe_m[("ss", KC - 1)])
            v.tensor_copy(r_ss[:, 0:512], pb[0][0:1, :])
            v.tensor_copy(r_ss[:, 512:1024], pb[1][0:1, :])
            v.drain()
            dve_m["ss"] = C["s_dve"].inc(v.engine_nop())
        blk.vector(f_dve_l2)

        def f_act_l2b(a):
            s_dve.wait(a, dve_m["ss"])
            act_m["sd"] = C["s_act"].inc(
                a.activation(r_ss[:, :], r_ss[:, :], AF.Sqrt))
        blk.scalar(f_act_l2b)

        def f_dve_l2b(v):
            s_act.wait(v, act_m["sd"])
            v.wait_ge(s_gw.h, ld2_mark)
            v.tensor_scalar_max(r_ss[:, :], r_ss[:, :], 1e-12)
            v.drain()
            v.reciprocal(r_z[:, :], r_ss[:, :])
            v.drain()
            ins = v.tensor_scalar_mul(r_z[:, :], r_z[:, :], s_scale)
            dve_m["rsc"] = C["s_dve"].inc(ins)
        blk.vector(f_dve_l2b)

        def f_pe_rb(t):
            s_dve.wait(t, dve_m["rsc"])
            for th in range(2):
                ins = t.matmul(pb[2 + th][:, :], sb["ones_m"][:, :],
                               r_z[:, th * 512:(th + 1) * 512], start=True, stop=True)
            pe_m["rb"] = s_pe.inc(ins)
        blk.tensor(f_pe_rb)

        def f_dve_rb(v):
            s_pe.wait(v, pe_m["rb"])
            for th in range(2):
                v.tensor_copy(sc["RB"][:, th * 512:(th + 1) * 512], pb[2 + th][:, :])
            v.drain()
            for k in range(KC):
                ins = v.tensor_tensor(XN[:, k * TOK:(k + 1) * TOK],
                                      XB2[:, k * TOK:(k + 1) * TOK], sc["RB"][:, :],
                                      ALU.mult)
            dve_m["xn"] = C["s_dve"].inc(ins)
        blk.vector(f_dve_rb)

        def f_act_na(a):
            a.wait_ge(s_gw.h, ld2_mark)
            act_m["abs"] = C["s_act"].inc(
                a.activation(s_negabs, s_scale, AF.Abs))
        blk.scalar(f_act_na)

        def f_dve_na(v):
            s_act.wait(v, act_m["abs"])
            ins = v.tensor_scalar_mul(s_negabs, s_negabs, -1.0)
            dve_m["na"] = C["s_dve"].inc(ins)
        blk.vector(f_dve_na)

        def f_pe_na(t):
            s_dve.wait(t, dve_m["na"])
            ins = t.matmul(pb[4][:, 0:1], sb["ones_m"][:, :], s_negabs,
                           start=True, stop=True)
            pe_m["nab"] = s_pe.inc(ins)
        blk.tensor(f_pe_na)

        def f_dve_nab(v):
            s_pe.wait(v, pe_m["nab"])
            ins = v.tensor_copy(s_nb, pb[4][:, 0:1])
            dve_m["nab"] = C["s_dve"].inc(ins)
        blk.vector(f_dve_nab)

        def f_xout(sy):
            s_dve.wait(sy, dve_m["xn"])
            s_ld.inc(sy.dma_start(
                out=xag_i[0:D, :].rearrange("(kc p) t -> p kc t", p=128),
                in_=XB2.rearrange("p (kc t) -> p kc t", t=TOK)), 16)
            s_ld.inc(sy.dma_start(
                out=xag_i[D:2 * D, :].rearrange("(kc p) t -> p kc t", p=128),
                in_=XN[:, :].rearrange("p (kc t) -> p kc t", t=TOK)), 16)
        blk.sync(f_xout)
        xout_mark = s_ld.n

        def f_ag(gp):
            gp.wait_ge(s_ld.h, xout_mark)
            ins = gp.collective_compute(
                "AllGather", ALU.bypass, replica_groups=[list(range(8))],
                ins=[xag_i], outs=[xag_o])
            s_agx.inc(ins, 1)
        blk.gpsimd(f_ag)

        for k in range(KC):
            def f_act_mn(a, k=k):
                a.wait_ge(s_gw.h, ld2_mark)
                if k > 0:
                    s_pe.wait(a, pe_m[("mn", k - 1)])
                s_act.wait(a, act_m.get(("mnp", k - 2), 0))
                act_m[("mnp", k)] = C["s_act"].inc(
                    a.activation(SQ1[:, 0:256], MUNS[:, k * 256:(k + 1) * 256],
                                 AF.Square))
            blk.scalar(f_act_mn)

            def f_pe_mn(t, k=k):
                s_act.wait(t, act_m[("mnp", k)])
                if k == 0:
                    s_dve.wait(t, dve_m["nab"])
                ins = t.matmul(pb[4][0:1, 0:256], sb["ones_k"][:, :],
                               SQ1[:, 0:256], start=(k == 0), stop=(k == KC - 1))
                pe_m[("mn", k)] = s_pe.inc(ins)
            blk.tensor(f_pe_mn)

        def f_dve_mn(v):
            s_pe.wait(v, pe_m[("mn", KC - 1)])
            v.tensor_copy(SQ1[0:1, 0:256], pb[4][0:1, 0:256])
            v.drain()
            v.tensor_scalar_max(SQ1[0:1, 0:256], SQ1[0:1, 0:256], 1e-24)
            dve_m["mnc"] = C["s_dve"].inc(v.drain())
        blk.vector(f_dve_mn)

        def f_act_mn2(a):
            s_dve.wait(a, dve_m["mnc"])
            act_m["mns"] = C["s_act"].inc(
                a.activation(SQ1[0:1, 0:256], SQ1[0:1, 0:256], AF.Sqrt))
        blk.scalar(f_act_mn2)

        def f_dve_mn2(v):
            s_act.wait(v, act_m["mns"])
            ins = v.reciprocal(SQ1[0:1, 0:256], SQ1[0:1, 0:256])
            dve_m["mnr"] = C["s_dve"].inc(ins)
        blk.vector(f_dve_mn2)

        def f_pe_mn2(t):
            s_dve.wait(t, dve_m["mnr"])
            ins = t.matmul(pb[4][:, 0:256], sb["ones_m"][:, :], SQ1[0:1, 0:256],
                           start=True, stop=True)
            pe_m["mnb"] = s_pe.inc(ins)
        blk.tensor(f_pe_mn2)

        def f_dve_mn3(v):
            s_pe.wait(v, pe_m["mnb"])
            v.tensor_copy(SQ1[:, 0:256], pb[4][:, 0:256])
            v.drain()
            for k in range(KC):
                ins = v.tensor_tensor(MUNS[:, k * 256:(k + 1) * 256],
                                      MUNS[:, k * 256:(k + 1) * 256],
                                      SQ1[:, 0:256], ALU.mult)
            dve_m["muns"] = C["s_dve"].inc(ins)
        blk.vector(f_dve_mn3)

        onekb = ar.alloc([128, 1], BF16)

        s_okb = Sem(nc, "s_okb" + sfx)

        def f_okb(sy):
            s_okb.inc(sy.dma_start(out=onekb[:], in_=C["t_onesb"].ap()), 16)
        blk.sync(f_okb)
        okb_mark = s_okb.n

        so_marks = {}
        for bq in range(2):
            for nb in range(32):
                r_, a_ = nb // 4, nb % 4
                par = nb % 2

                def f_xnt(gp, bq=bq, nb=nb, r_=r_, a_=a_, par=par):
                    gp.wait_ge(s_agx.h, 1)
                    if bq == 1 and nb < 2:
                        s_pe.wait(gp, pe_m[("si", 0, 30 + nb)])
                    if nb >= 2:
                        s_pe.wait(gp, pe_m[("lg", bq, nb - 2)])
                    src = xag_o.rearrange("(r z) t -> z r t", z=2 * D)[
                        D + 0:2 * D, r_, bq * 512 + a_ * 128: bq * 512 + (a_ + 1) * 128]
                    src = src.rearrange("(kc p) i -> p kc i", p=128)
                    dst = XLT[par][:, :].rearrange("p (kc i) -> p kc i", i=128)
                    pe_m[("xnt", bq, nb)] = s_xl[par].inc(
                        gp.dma_start(out=dst, in_=src), 16)
                blk.gpsimd(f_xnt)

                def f_pe_lg(t, bq=bq, nb=nb, par=par):
                    t.wait_ge(s_xl[par].h, pe_m[("xnt", bq, nb)])
                    if bq == 0 and nb == 0:
                        s_dve.wait(t, dve_m["muns"])
                        s_dve.wait(t, dve_m["nab"])
                    if bq == 1 and nb in (0, 1):
                        s_dve.wait(t, dve_m[("sov", 0, 1, 3)])
                    if (bq, nb) != (0, 0) and (bq, nb) != (0, 1):
                        pnb = (bq, nb - 2) if nb >= 2 else (bq - 1, 30 + nb)
                        s_act.wait(t, act_m[("exp",) + pnb])
                    for kc in range(KC):
                        ins = t.matmul(pb[6 + par][:, 0:256],
                                       XLT[par][:, kc * 128:(kc + 1) * 128],
                                       MUNS[:, kc * 256:(kc + 1) * 256],
                                       start=(kc == 0), stop=(kc == KC - 1))
                    pe_m[("lg", bq, nb)] = s_pe.inc(ins)
                blk.tensor(f_pe_lg)

                def f_act_exp(a, bq=bq, nb=nb, par=par):
                    s_pe.wait(a, pe_m[("lg", bq, nb)])
                    if bq == 1 and nb == 0:
                        s_pe.wait(a, pe_m[("si", 0, 31)])
                    ins = a.activation(EXA[:, nb * 256:(nb + 1) * 256],
                                       pb[6 + par][:, 0:256], AF.Exp, bias=s_nb)
                    act_m[("exp", bq, nb)] = C["s_act"].inc(ins)
                blk.scalar(f_act_exp)

                def f_pe_z(t, bq=bq, nb=nb):
                    s_act.wait(t, act_m[("exp", bq, nb)])
                    t.wait_ge(s_okb.h, okb_mark)
                    if bq == 1 and nb == 0:
                        s_dve.wait(t, dve_m[("zin", 0)])
                    ins = t.matmul(pb[5][0:1, 0:256], onekb[:, :],
                                   EXA[:, nb * 256:(nb + 1) * 256],
                                   start=(nb == 0), stop=(nb == 31))
                    pe_m[("zz", bq, nb)] = s_pe.inc(ins)
                blk.tensor(f_pe_z)

            def f_dve_zi(v, bq=bq):
                s_pe.wait(v, pe_m[("zz", bq, 31)])
                v.tensor_copy(SQ0[0:1, 0:256], pb[5][0:1, 0:256])
                v.drain()
                ins = v.reciprocal(SQ0[0:1, 0:256], SQ0[0:1, 0:256])
                dve_m[("zin", bq)] = C["s_dve"].inc(ins)
            blk.vector(f_dve_zi)

            def f_pe_zb(t, bq=bq):
                s_dve.wait(t, dve_m[("zin", bq)])
                ins = t.matmul(pb[4][:, 0:256], sb["ones_m"][:, :],
                               SQ0[0:1, 0:256], start=True, stop=True)
                pe_m[("zb", bq)] = s_pe.inc(ins)
            blk.tensor(f_pe_zb)

            def f_dve_zb(v, bq=bq):
                s_pe.wait(v, pe_m[("zb", bq)])
                ins = v.tensor_copy(ZBT[:, :], pb[4][:, 0:256])
                dve_m[("zc", bq)] = C["s_dve"].inc(ins)
            blk.vector(f_dve_zb)

            for nb in range(32):
                r_, a_ = nb // 4, nb % 4
                par = nb % 2

                def f_xlt(sy, bq=bq, nb=nb, r_=r_, a_=a_, par=par):
                    sy.wait_ge(s_agx.h, 1)
                    if nb < 2:
                        s_pe.wait(sy, pe_m[("lg", bq, 30 + nb)])
                    if (bq, nb) >= (0, 2):
                        prev = ("si", bq, nb - 2) if nb >= 2 else ("si", bq - 1, 30 + nb)
                        s_pe.wait(sy, pe_m.get(prev, 0))
                    src = xag_o[r_ * 2 * D: r_ * 2 * D + D,
                                bq * 512 + a_ * 128: bq * 512 + (a_ + 1) * 128]
                    pe_m[("xlt", bq, nb)] = s_xl[par].inc(
                        sy.dma_start(out=XLT[par][:, :], in_=src, transpose=True), 16)
                blk.sync(f_xlt)

                def f_pe_si(t, bq=bq, nb=nb, par=par):
                    t.wait_ge(s_xl[par].h, pe_m[("xlt", bq, nb)])
                    if nb == 0:
                        s_dve.wait(t, dve_m[("zc", bq)])
                        if bq == 1:
                            s_dve.wait(t, dve_m[("sov", 0, 1, 3)])
                    for kc in range(KC):
                        bank = kc // 2
                        half = (kc % 2) * 256
                        ins = t.matmul(pb[bank][:, half:half + 256],
                                       XLT[par][:, kc * 128:(kc + 1) * 128],
                                       EXA[:, nb * 256:(nb + 1) * 256],
                                       start=(nb == 0 and kc % 2 == 0),
                                       stop=(nb == 31))
                    pe_m[("si", bq, nb)] = s_pe.inc(ins)
                blk.tensor(f_pe_si)

            for kc in range(KC):
                def f_dve_sev(v, bq=bq, kc=kc):
                    if kc == 0:
                        s_pe.wait(v, pe_m[("si", bq, 31)])
                    bank = kc // 2
                    half = (kc % 2) * 256
                    ins = v.tensor_tensor(SLIN[:, kc * 256:(kc + 1) * 256],
                                          pb[bank][:, half:half + 256],
                                          ZBT[:, :], ALU.mult)
                    dve_m[("sev", bq, kc)] = C["s_dve"].inc(ins)
                blk.vector(f_dve_sev)

            for e in range(2):
                def f_w1(gp, bq=bq, e=e):
                    if bq == 0 and e == 0:
                        s_dve.wait(gp, dve_m["xn"])
                        gp.wait_ge(s_agx.h, 1)
                    else:
                        s_pe.wait(gp, pe_m.get(("w1", bq, e - 1),
                                               pe_m.get(("w1", bq - 1, 1), 0)))
                    src = W1s[e].rearrange("(kc p) h -> p kc h", p=128)
                    dst = W1B.rearrange("p (kc h) -> p kc h", h=512)
                    pe_m[("w1l", bq, e)] = s_mu[0].inc(gp.dma_start(out=dst, in_=src), 16)
                blk.gpsimd(f_w1)

                for hc in range(4):
                    def f_pe_w1(t, bq=bq, e=e, hc=hc):
                        t.wait_ge(s_mu[0].h, pe_m[("w1l", bq, e)])
                        if hc == 0 and e == 0:
                            s_dve.wait(t, dve_m[("sev", bq, KC - 1)])
                        if (e, hc) != (0, 0):
                            s_act.wait(t, act_m.get(("gel", bq, e, hc - 1),
                                                    act_m.get(("gel", bq, e - 1, 3), 0)))
                        for kc in range(KC):
                            ins = t.matmul(pb[6][:, 0:128],
                                           W1B[:, kc * 512 + hc * 128: kc * 512 + (hc + 1) * 128],
                                           SLIN[:, kc * 256 + e * 128: kc * 256 + e * 128 + 128],
                                           start=(kc == 0), stop=(kc == KC - 1))
                        if hc == 3:
                            pe_m[("w1", bq, e)] = s_pe.inc(ins)
                            pe_m[("w1h", bq, e, hc)] = s_pe.n
                        else:
                            pe_m[("w1h", bq, e, hc)] = s_pe.inc(ins)
                    blk.tensor(f_pe_w1)

                    def f_act_gel(a, bq=bq, e=e, hc=hc):
                        s_pe.wait(a, pe_m[("w1h", bq, e, hc)])
                        ins = a.activation(HFM[:, (e * 4 + hc) * 128:(e * 4 + hc + 1) * 128],
                                           pb[6][:, 0:128], AF.Gelu,
                                           bias=B1C[:, e * 4 + hc:e * 4 + hc + 1])
                        act_m[("gel", bq, e, hc)] = C["s_act"].inc(ins)
                    blk.scalar(f_act_gel)

            for e in range(2):
                def f_w2(gp, bq=bq, e=e):
                    if bq == 0 and e == 0:
                        s_dve.wait(gp, dve_m["xn"])
                        gp.wait_ge(s_agx.h, 1)
                    else:
                        s_pe.wait(gp, pe_m.get(("w2", bq, e - 1, 3),
                                               pe_m.get(("w2", bq - 1, 1, 3), 0)))
                    src = W2s[e].rearrange("(hc p) d2 -> p hc d2", p=128)
                    dst = W2B.rearrange("p (hc d2) -> p hc d2", d2=2048)
                    pe_m[("w2l", bq, e)] = s_mu[1].inc(gp.dma_start(out=dst, in_=src), 16)
                blk.gpsimd(f_w2)

                def f_b2(sy, bq=bq, e=e):
                    if not (bq == 0 and e == 0):
                        s_dve.wait(sy, dve_m.get(("b2b", bq, e - 1, 3),
                                                 dve_m.get(("b2b", bq - 1, 1, 3), 0)))
                    pe_m[("b2r", bq, e)] = s_ld.inc(
                        sy.dma_start(out=b2row, in_=b2s[e:e + 1, :]), 16)
                blk.sync(f_b2)

                for db in range(4):
                    def f_pe_b2(t, bq=bq, e=e, db=db):
                        t.wait_ge(s_ld.h, pe_m[("b2r", bq, e)])
                        if (bq, e, db) != (0, 0, 0):
                            prev = (bq, e, db - 1) if db > 0 else \
                                ((bq, e - 1, 3) if e > 0 else (bq - 1, 1, 3))
                            s_dve.wait(t, dve_m[("b2b",) + prev])
                        ins = t.matmul(pb[7][:, 0:512], sb["ones_m"][:, :],
                                       b2row[0:1, db * 512:(db + 1) * 512],
                                       start=True, stop=True)
                        pe_m[("b2", bq, e, db)] = s_pe.inc(ins)
                    blk.tensor(f_pe_b2)

                    def f_dve_b2(v, bq=bq, e=e, db=db):
                        s_pe.wait(v, pe_m[("b2", bq, e, db)])
                        ins = v.tensor_copy(B2B[:, db * 512:(db + 1) * 512],
                                            pb[7][:, 0:512])
                        dve_m[("b2b", bq, e, db)] = C["s_dve"].inc(ins)
                    blk.vector(f_dve_b2)

                for db in range(4):
                    def f_pe_w2(t, bq=bq, e=e, db=db):
                        t.wait_ge(s_mu[1].h, pe_m[("w2l", bq, e)])
                        s_act.wait(t, act_m[("gel", bq, e, 3)])
                        if (bq, e, db) != (0, 0, 0):
                            prev = (bq, e, db - 1) if db > 0 else \
                                ((bq, e - 1, 3) if e > 0 else (bq - 1, 1, 3))
                            s_dve.wait(t, dve_m.get(("sov",) + prev, 0))
                        for hc in range(4):
                            ins = t.matmul(pb[6][:, 0:512],
                                           HFM[:, (e * 4 + hc) * 128:(e * 4 + hc + 1) * 128],
                                           W2B[:, hc * 2048 + db * 512: hc * 2048 + (db + 1) * 512],
                                           start=(hc == 0), stop=(hc == 3))
                        pe_m[("w2", bq, e, db)] = s_pe.inc(ins)
                    blk.tensor(f_pe_w2)

                    def f_dve_sov(v, bq=bq, e=e, db=db):
                        s_pe.wait(v, pe_m[("w2", bq, e, db)])
                        s_dve.wait(v, dve_m[("b2b", bq, e, 3)])
                        if not (bq == 0 and e == 0 and db == 0):
                            v.wait_ge(s_so.h, so_marks.get((bq, e, db), 0))
                        ins = v.tensor_tensor(STG[:, 0:512], pb[6][:, 0:512],
                                              B2B[:, db * 512:(db + 1) * 512], ALU.add)
                        dve_m[("sov", bq, e, db)] = C["s_dve"].inc(ins)
                    blk.vector(f_dve_sov)

                    def f_so(sy, bq=bq, e=e, db=db):
                        s_dve.wait(sy, dve_m[("sov", bq, e, db)])
                        dst = slag_i[bq * 256 + e * 128: bq * 256 + (e + 1) * 128,
                                     db * 512:(db + 1) * 512]
                        mk = s_so.inc(sy.dma_start(out=dst, in_=STG[:, 0:512]), 16)
                        nxt = (bq, e, db + 1) if db < 3 else \
                            ((bq, e + 1, 0) if e < 1 else (bq + 1, 0, 0))
                        so_marks[nxt] = mk
                    blk.sync(f_so)

        def f_ag2(gp):
            gp.wait_ge(s_so.h, s_so.n)
            ins = gp.collective_compute(
                "AllGather", ALU.bypass, replica_groups=[list(range(8))],
                ins=[slag_i], outs=[slag_o])
            s_agsl.inc(ins, 1)
        blk.gpsimd(f_ag2)

        for kc in range(KC):
            par = kc % 2

            def f_mur(gp, kc=kc, par=par):
                if kc == 0:
                    s_pe.wait(gp, pe_m[("w2", 1, 1, 3)])
                if kc >= 2:
                    s_pe.wait(gp, pe_m[("mq", kc - 2)])
                src = muf[kc * 128:(kc + 1) * 128, :]
                pe_m[("mul", kc)] = s_mu[par].inc(
                    gp.dma_start(out=MUC[par][:, :], in_=src), 16)
            blk.gpsimd(f_mur)

            def f_act_mq(a, kc=kc, par=par):
                a.wait_ge(s_mu[par].h, pe_m[("mul", kc)])
                if kc >= 1:
                    s_pe.wait(a, pe_m[("mq", kc - 1)])
                a.activation(SQ0[:, :], MUC[par][:, 0:1024], AF.Square)
                ins = a.activation(SQ1[:, :], MUC[par][:, 1024:2048], AF.Square)
                act_m[("mq", kc)] = C["s_act"].inc(ins)
            blk.scalar(f_act_mq)

            def f_pe_mq(t, kc=kc):
                s_act.wait(t, act_m[("mq", kc)])
                if kc == 0:
                    s_dve.wait(t, dve_m[("sov", 1, 1, 3)])
                for q4 in range(4):
                    srcq = (SQ0 if q4 < 2 else SQ1)[:, (q4 % 2) * 512:(q4 % 2 + 1) * 512]
                    ins = t.matmul(pb[q4][0:1, :], sb["ones_k"][:, :], srcq,
                                   start=(kc == 0), stop=(kc == KC - 1))
                pe_m[("mq", kc)] = s_pe.inc(ins)
            blk.tensor(f_pe_mq)

        def f_dve_cn(v):
            s_pe.wait(v, pe_m[("mq", KC - 1)])
            for q4 in range(4):
                v.tensor_copy(cn_row[:, q4 * 512:(q4 + 1) * 512], pb[q4][0:1, :])
            v.drain()
            v.tensor_scalar_max(cn_row[:, :], cn_row[:, :], 1e-24)
            dve_m["cn"] = C["s_dve"].inc(v.drain())
        blk.vector(f_dve_cn)

        def f_act_cn(a):
            s_dve.wait(a, dve_m["cn"])
            ins = a.activation(cn_row[:, :], cn_row[:, :], AF.Sqrt)
            act_m["cns"] = C["s_act"].inc(ins)
        blk.scalar(f_act_cn)

        def f_dve_cn2(v):
            s_act.wait(v, act_m["cns"])
            ins = v.reciprocal(cn_row[:, :], cn_row[:, :])
            dve_m["cnr"] = C["s_dve"].inc(ins)
        blk.vector(f_dve_cn2)

        for ec2 in range(2):
            def f_pe_rnc(t, ec2=ec2):
                s_dve.wait(t, dve_m["cnr"])
                if ec2 == 1:
                    s_dve.wait(t, dve_m[("rncc", 0)])
                for j in range(8):
                    ins = t.transpose(pb[4][0:128, j * 4:j * 4 + 1],
                                      cn_row[:, (ec2 * 8 + j) * 128:(ec2 * 8 + j + 1) * 128],
                                      sb["idn_f"][0:1, 0:1])
                pe_m[("rnc", ec2)] = s_pe.inc(ins)
            blk.tensor(f_pe_rnc)

            def f_dve_rnc(v, ec2=ec2):
                s_pe.wait(v, pe_m[("rnc", ec2)])
                for j in range(8):
                    ins = v.tensor_copy(RNC[:, ec2 * 8 + j:ec2 * 8 + j + 1],
                                        pb[4][0:128, j * 4:j * 4 + 1])
                dve_m[("rncc", ec2)] = C["s_dve"].inc(ins)
            blk.vector(f_dve_rnc)

        for ec in range(KC):
            par = ec % 2

            def f_muc(gp, ec=ec, par=par):
                s_pe.wait(gp, pe_m[("mq", KC - 1)])
                if ec >= 2:
                    s_pe.wait(gp, pe_m[("cs", ec - 2, 1)])
                src = muf[:, ec * 128:(ec + 1) * 128].rearrange(
                    "(kc p) e -> p kc e", p=128)
                dst = MUC[par][:, :].rearrange("p (kc e) -> p kc e", e=128)
                pe_m[("mucl", ec)] = s_mu[par].inc(gp.dma_start(out=dst, in_=src), 16)
            blk.gpsimd(f_muc)

            for th in range(2):
                def f_pe_cs(t, ec=ec, th=th, par=par):
                    t.wait_ge(s_mu[par].h, pe_m[("mucl", ec)])
                    s_dve.wait(t, dve_m[("rncc", 1)])
                    if (ec, th) != (0, 0):
                        prev = (ec, th - 1) if th == 1 else (ec - 1, 1)
                        s_act.wait(t, act_m.get(("csx",) + prev, 0))
                    for kc in range(KC):
                        ins = t.matmul(pb[5][:, :],
                                       MUC[par][:, kc * 128:(kc + 1) * 128],
                                       XN[:, kc * TOK + th * 512: kc * TOK + (th + 1) * 512],
                                       start=(kc == 0), stop=(kc == KC - 1))
                    pe_m[("cs", ec, th)] = s_pe.inc(ins)
                blk.tensor(f_pe_cs)

                def f_act_cs(a, ec=ec, th=th):
                    s_pe.wait(a, pe_m[("cs", ec, th)])
                    ins = a.activation(PALL[:, ec * TOK + th * 512: ec * TOK + (th + 1) * 512],
                                       pb[5][:, :], AF.Exp,
                                       bias=s_nb, scale=RNC[:, ec:ec + 1])
                    act_m[("csx", ec, th)] = C["s_act"].inc(ins)
                blk.scalar(f_act_cs)

                def f_pe_cz(t, ec=ec, th=th):
                    s_act.wait(t, act_m[("csx", ec, th)])
                    t.wait_ge(s_okb.h, okb_mark)
                    ins = t.matmul(pb[6 + th][0:1, :],
                                   onekb[:, :],
                                   PALL[:, ec * TOK + th * 512: ec * TOK + (th + 1) * 512],
                                   start=(ec == 0), stop=(ec == KC - 1))
                    pe_m[("cz", ec, th)] = s_pe.inc(ins)
                blk.tensor(f_pe_cz)

        def f_dve_cz(v):
            s_pe.wait(v, pe_m[("cz", KC - 1, 0)])
            s_pe.wait(v, pe_m[("cz", KC - 1, 1)])
            v.tensor_copy(zrow[:, 0:512], pb[6][0:1, :])
            v.tensor_copy(zrow[:, 512:1024], pb[7][0:1, :])
            v.drain()
            v.reciprocal(zrow[:, :], zrow[:, :])
            v.drain()
            ins = v.tensor_scalar_mul(zrow[:, :], zrow[:, :], 0.5)
            dve_m["cz"] = C["s_dve"].inc(ins)
        blk.vector(f_dve_cz)

        def f_pe_zib(t):
            s_dve.wait(t, dve_m["cz"])
            for th in range(2):
                ins = t.matmul(pb[6 + th][:, :], sb["ones_m"][:, :],
                               zrow[0:1, th * 512:(th + 1) * 512], start=True, stop=True)
            pe_m["zib"] = s_pe.inc(ins)
        blk.tensor(f_pe_zib)

        def f_dve_zib(v):
            s_pe.wait(v, pe_m["zib"])
            for th in range(2):
                v.tensor_copy(ZIB[:, th * 512:(th + 1) * 512], pb[6 + th][:, :])
            v.drain()
            for ec in range(KC):
                ins = v.tensor_tensor(PALL[:, ec * TOK:(ec + 1) * TOK],
                                      PALL[:, ec * TOK:(ec + 1) * TOK],
                                      ZIB[:, :], ALU.mult)
            dve_m["pall"] = C["s_dve"].inc(ins)
        blk.vector(f_dve_zib)

        if debug:
            def f_dbgf(sy):
                s_dve.wait(sy, dve_m["xn"])
                C["s_out"].inc(sy.dma_start(
                    out=io["dbg_xn"].rearrange("(kc p) t -> p kc t", p=128),
                    in_=XN[:, :].rearrange("p (kc t) -> p kc t", t=TOK)), 16)
                s_dve.wait(sy, dve_m["muns"])
                C["s_out"].inc(sy.dma_start(out=io["dbg_muns"], in_=MUNS[:, :]), 16)
                s_dve.wait(sy, dve_m["pall"])
                C["s_out"].inc(sy.dma_start(
                    out=io["dbg_pall"].rearrange("(ec p) t -> p ec t", p=128),
                    in_=PALL.rearrange("p (ec t) -> p ec t", t=TOK)), 16)
                C["s_out"].inc(sy.dma_start(out=io["dbg_cn"], in_=cn_row), 16)
                sy.wait_ge(s_agsl.h, 1)
                C["s_out"].inc(sy.dma_start(out=io["dbg_slag"], in_=slag_o), 16)
            blk.sync(f_dbgf)

        oc_mark = {}
        for hc in range(KC):
            for th in range(2):
                par = (hc * 2 + th) % 2

                def f_slt(sy, hc=hc, th=th, par=par):
                    sy.wait_ge(s_agsl.h, 1)
                    if hc * 2 + th >= 2:
                        s_pe.wait(sy, pe_m.get(("av", hc - 1, th), 0))

                    srcz = slag_o.rearrange("(ec2 z) d2 -> z ec2 d2", z=512)[
                        256 * th:256 * th + 256, :, hc * 128:(hc + 1) * 128]
                    srcz = srcz.rearrange("(ecl p) ec2 d2 -> p ecl ec2 d2", p=128)
                    dstz = SLT[par].rearrange("p (ec2 ecl d2) -> p ec2 ecl d2",
                                              ec2=8, d2=128)
                    mk = 0
                    for ecl in range(2):
                        mk = s_sl[par].inc(
                            sy.dma_start(out=dstz[:, :, ecl, :],
                                         in_=srcz[:, ecl, :, :]), 16)
                    oc_mark[("slt", hc, th)] = mk
                blk.sync(f_slt)

                def f_xr(sy, hc=hc, th=th):
                    if hc == 0:
                        s_pe.wait(sy, pe_m[("mq", KC - 1)])
                    if hc >= 1:
                        s_dve.wait(sy, dve_m.get(("yt", hc - 1, th), 0))
                    src = x_in.rearrange("(kc p) t -> p kc t", p=128)[
                        :, hc, th * 512:(th + 1) * 512]
                    oc_mark[("x", hc, th)] = s_x2[th].inc(
                        sy.dma_start(out=(SQ0 if th == 0 else SQ1)[:, 0:512],
                                     in_=src), 16)
                blk.sync(f_xr)

                def f_pe_av(t, hc=hc, th=th, par=par):
                    t.wait_ge(s_sl[par].h, oc_mark[("slt", hc, th)])
                    s_dve.wait(t, dve_m["pall"])
                    if hc * 2 + th >= 2:
                        s_dve.wait(t, dve_m.get(("yt", hc - 1, th), 0))
                    for ec in range(KC):
                        ins = t.matmul(pb[4 + th][:, :],
                                       SLT[par][:, ec * 128:(ec + 1) * 128],
                                       PALL[:, ec * TOK + th * 512: ec * TOK + (th + 1) * 512],
                                       start=(ec == 0), stop=(ec == KC - 1))
                    pe_m[("av", hc, th)] = s_pe.inc(ins)
                blk.tensor(f_pe_av)

                def f_dve_av(v, hc=hc, th=th):
                    s_pe.wait(v, pe_m[("av", hc, th)])
                    v.wait_ge(s_x2[th].h, oc_mark[("x", hc, th)])
                    if hc > 0:
                        v.wait_ge(s_y2[th].h, oc_mark.get(("y", hc - 1, th), 0))
                    xr = (SQ0 if th == 0 else SQ1)[:, 0:512]
                    yt = (SQ0 if th == 0 else SQ1)[:, 512:1024]
                    ins = v.scalar_tensor_tensor(yt, pb[4 + th][:, :], 1.0, xr,
                                                 ALU.mult, ALU.add)
                    dve_m[("yt", hc, th)] = C["s_dve"].inc(ins)
                blk.vector(f_dve_av)

                def f_yo(sy, hc=hc, th=th):
                    s_dve.wait(sy, dve_m[("yt", hc, th)])
                    yt = (SQ0 if th == 0 else SQ1)[:, 512:1024]
                    dst = y_out.rearrange("(kc p) t -> p kc t", p=128)[
                        :, hc, th * 512:(th + 1) * 512]
                    oc_mark[("y", hc, th)] = s_y2[th].inc(
                        sy.dma_start(out=dst, in_=yt), 16)
                    if debug and hc == 0 and th == 0:
                        C["s_out"].inc(sy.dma_start(out=io["dbg_sq0"], in_=SQ0[:, :]), 16)
                        C["s_out"].inc(sy.dma_start(out=io["dbg_slt"], in_=SLT[par][:, :]), 16)
                blk.sync(f_yo)

        def f_fin(sy):
            s_y2[0].wait(sy)
            s_y2[1].wait(sy)
            s_out.wait(sy)
        blk.sync(f_fin)


def ffn1_io(nc):
    return {
        "x_in": nc.dram_tensor("x_fm", [D, TOK], F32, kind="ExternalInput").ap(),
        "y_out": nc.dram_tensor("y_fm", [D, TOK], F32, kind="ExternalOutput").ap(),
        "gamma": nc.dram_tensor("gamma", [128, KC], F32, kind="ExternalInput").ap(),
        "beta": nc.dram_tensor("beta", [128, KC], F32, kind="ExternalInput").ap(),
        "mu_f": nc.dram_tensor("mu_f", [D, 2048], F32, kind="ExternalInput").ap(),
        "mu_sl": nc.dram_tensor("mu_sl", [D, 256], F32, kind="ExternalInput").ap(),
        "scale1": nc.dram_tensor("scale1", [1, 1], F32, kind="ExternalInput").ap(),
        "W1_sl": nc.dram_tensor("W1_sl", [2, D, 512], F32, kind="ExternalInput").ap(),
        "b1_sl": nc.dram_tensor("b1_sl", [128, 8], F32, kind="ExternalInput").ap(),
        "W2_sl": nc.dram_tensor("W2_sl", [2, 512, D], F32, kind="ExternalInput").ap(),
        "b2_sl": nc.dram_tensor("b2_sl", [2, D], F32, kind="ExternalInput").ap(),
    }


def build_ffn1_program(scale=0.5, debug=False):
    nc = bass.Bass("TRN2", target_bir_lowering=False)
    io = ffn1_io(nc)
    if debug:
        io["dbg_xn"] = nc.dram_tensor("dbg_xn", [D, TOK], BF16, kind="ExternalOutput").ap()
        io["dbg_muns"] = nc.dram_tensor("dbg_muns", [128, KC * 256], BF16, kind="ExternalOutput").ap()
        io["dbg_pall"] = nc.dram_tensor("dbg_pall", [D, TOK], BF16, kind="ExternalOutput").ap()
        io["dbg_cn"] = nc.dram_tensor("dbg_cn", [1, 2048], F32, kind="ExternalOutput").ap()
        io["dbg_slag"] = nc.dram_tensor("dbg_slag", [8 * 512, D], BF16, kind="ExternalOutput").ap()
        io["dbg_sq0"] = nc.dram_tensor("dbg_sq0", [128, 1024], F32, kind="ExternalOutput").ap()
        io["dbg_slt"] = nc.dram_tensor("dbg_slt", [128, 2048], BF16, kind="ExternalOutput").ap()
    C = common_setup(nc)
    pbk = [nc.alloc_psum_tensor(f"pbk{i}", [128, 512], F32) for i in range(8)]
    with nc.Block() as blk:
        sb = emit_load_consts(nc, blk, C)
        ar = new_arena(nc)
        emit_ffn1(nc, blk, C, sb, ar, pbk, io, "_f1", scale=scale, debug=debug)
    return nc



def prep_attn1_weights(d):
    return {
        "wqkv_T": np.ascontiguousarray(np.asarray(d["L1_attn_w_qkv"]).T),
        "wproj_T": np.ascontiguousarray(np.asarray(d["L1_attn_w_proj"]).T),
    }


def emit_attn1(nc, blk, C, sb, ar, pb, io, sfx):
    x_in = io["x_in"]
    y_out = io["y_out"]
    wqkv = io["wqkv_T"]
    wproj = io["wproj_T"]

    s_ws = [Sem(nc, f"s_ws{i}{sfx}") for i in range(2)]

    X = ar.alloc([128, KC * TOK], F32)
    XB = ar.alloc([128, KC * TOK], BF16)
    QKV = ar.alloc([128, 24 * TOK], BF16)
    CTX = ar.alloc([128, KC * TOK], BF16)
    WS = [ar.alloc([128, KC * 128], BF16) for i in range(2)]
    SCR2 = ar.alloc([128, 3072], F32)
    sc = {"MB": SCR2[:, 0:1024], "RB": SCR2[:, 1024:2048], "TMP": SCR2[:, 2048:3072]}
    SQ0 = ar.alloc([128, 1024], F32)
    SQ1 = ar.alloc([128, 1024], F32)
    sc["SQ0"] = SQ0[:, :]
    sc["SQ1"] = SQ1[:, :]
    sc["rows"] = CTX[:, :].bitcast(F32)[0:1, 0:6 * TOK]
    SA = SCR2[0:1, 0:512]
    DRow = SCR2[0:1, 512:1024]
    W0R = SCR2[0:1, 1024:1536]
    VD = SQ1[:, :].bitcast(BF16)
    PRD = SQ0[:, :]

    s_ld, s_gw, s_pe = C["s_ld"], C["s_gw"], C["s_pe"]
    s_act, s_dve, s_out = C["s_act"], C["s_dve"], C["s_out"]

    if True:
        emit_load_master(nc, blk, C, X[:, :], x_in)
        xb_mark, ln_bank = emit_ln(nc, blk, C, sb, X[:, :], XB[:, :],
                                   [pb[0], pb[1]], sc)

        pe_m = {}
        act_m = {}
        dve_m = {}

        for oc in range(24):
            par = oc % 2

            def f_gw(gp, oc=oc, par=par):
                if oc >= 2:
                    s_pe.wait(gp, pe_m[("p", oc - 2, 1)])
                src = wqkv[:, oc * 128:(oc + 1) * 128].rearrange(
                    "(kc p) o -> p kc o", p=128)
                dst = WS[par].rearrange("p (kc o) -> p kc o", o=128)
                pe_m[("w", oc)] = s_ws[par].inc(gp.dma_start(out=dst, in_=src), 16)
            blk.gpsimd(f_gw)

            for th in range(2):
                def f_pe(t, oc=oc, th=th, par=par):
                    t.wait_ge(s_ws[par].h, pe_m[("w", oc)])
                    if oc == 0 and th == 0:
                        s_dve.wait(t, xb_mark)
                        s_dve.wait(t, ln_bank)
                    prev = (oc, th - 1) if th == 1 else (oc - 1, 1)
                    if oc > 0 or th > 0:
                        s_act.wait(t, act_m.get(("e",) + prev, 0))
                    for kc in range(KC):
                        ins = t.matmul(pb[2 + th][:, :],
                                       WS[par][:, kc * 128:(kc + 1) * 128],
                                       XB[:, kc * TOK + th * 512:kc * TOK + (th + 1) * 512],
                                       start=(kc == 0), stop=(kc == KC - 1))
                    pe_m[("p", oc, th)] = s_pe.inc(ins)
                blk.tensor(f_pe)

                def f_act(a, oc=oc, th=th):
                    s_pe.wait(a, pe_m[("p", oc, th)])
                    ins = a.activation(QKV[:, oc * TOK + th * 512:oc * TOK + (th + 1) * 512],
                                       pb[2 + th][:, :], AF.Copy)
                    act_m[("e", oc, th)] = C["s_act"].inc(ins)
                blk.scalar(f_act)

        qkv_done = act_m[("e", 23, 1)]

        for gg in range(4):
            qc0 = 6 * gg
            kc_ = 6 * gg + 4
            vc_ = 6 * gg + 5

            def f_dve_vd(v, gg=gg, vc_=vc_):
                s_act.wait(v, qkv_done)
                ins = v.tensor_tensor(VD[:, 0:512], QKV[:, vc_ * TOK + 0:vc_ * TOK + 512],
                                      QKV[:, vc_ * TOK + 512:vc_ * TOK + 1024],
                                      ALU.subtract)
                dve_m[("vd", gg)] = C["s_dve"].inc(ins)
            blk.vector(f_dve_vd)

            for hp in range(4):
                qc = qc0 + hp
                h = 4 * gg + hp
                for s_ in range(2):
                    key = (gg, hp, s_)

                    def f_dve_pr(v, qc=qc, kc_=kc_, s_=s_, key=key):
                        s_act.wait(v, qkv_done)
                        if key != (0, 0, 0):
                            s_pe.wait(v, pe_m.get(("sc2",) + prevkey[0], 0))
                        v.tensor_tensor(PRD[:, 0:512],
                                        QKV[:, qc * TOK + s_ * 512:qc * TOK + (s_ + 1) * 512],
                                        QKV[:, kc_ * TOK + 0:kc_ * TOK + 512], ALU.mult)
                        ins = v.tensor_tensor(PRD[:, 512:1024],
                                              QKV[:, qc * TOK + s_ * 512:qc * TOK + (s_ + 1) * 512],
                                              QKV[:, kc_ * TOK + 512:kc_ * TOK + 1024],
                                              ALU.mult)
                        dve_m[("pr",) + key] = C["s_dve"].inc(ins)
                    blk.vector(f_dve_pr)

                    def f_pe_sc(t, key=key):
                        s_dve.wait(t, dve_m[("pr",) + key])
                        if key != (0, 0, 0):
                            s_dve.wait(t, dve_m.get(("sa",) + prevkey[0], 0))
                        t.matmul(pb[4][0:1, :], sb["ones_k"][:, :], PRD[:, 0:512],
                                 start=True, stop=True)
                        ins = t.matmul(pb[5][0:1, :], sb["ones_k"][:, :],
                                       PRD[:, 512:1024], start=True, stop=True)
                        pe_m[("sc2",) + key] = s_pe.inc(ins)
                    blk.tensor(f_pe_sc)

                    def f_dve_sa(v, key=key):
                        s_pe.wait(v, pe_m[("sc2",) + key])
                        if key != (0, 0, 0):
                            s_act.wait(v, act_m.get(("sg",) + prevkey[0], 0))
                        v.tensor_copy(SA, pb[4][0:1, :])
                        v.drain()
                        ins = v.scalar_tensor_tensor(DRow, pb[5][0:1, :], -1.0, SA,
                                                     ALU.mult, ALU.add)
                        dve_m[("sa",) + key] = C["s_dve"].inc(ins)
                    blk.vector(f_dve_sa)

                    def f_act_sg(a, key=key):
                        s_dve.wait(a, dve_m[("sa",) + key])
                        ins = a.activation(W0R, DRow, AF.Sigmoid,
                                           scale=float(1.0 / np.sqrt(128.0)))
                        act_m[("sg",) + key] = C["s_act"].inc(ins)
                    blk.scalar(f_act_sg)

                    def f_pe_w0(t, key=key):
                        s_act.wait(t, act_m[("sg",) + key])
                        if key != (0, 0, 0):
                            s_dve.wait(t, dve_m.get(("cx",) + prevkey[0], 0))
                        ins = t.matmul(pb[6][:, :], sb["ones_m"][:, :], W0R,
                                       start=True, stop=True)
                        pe_m[("w0",) + key] = s_pe.inc(ins)
                    blk.tensor(f_pe_w0)

                    def f_dve_cx(v, gg=gg, hp=hp, s_=s_, key=key, vc_=vc_, h=h):
                        s_pe.wait(v, pe_m[("w0",) + key])
                        s_dve.wait(v, dve_m[("vd", gg)])
                        v.tensor_tensor(sc["TMP"][:, 0:512], VD[:, 0:512], pb[6][:, :],
                                        ALU.mult)
                        v.drain()
                        ins = v.tensor_tensor(
                            CTX[:, h * TOK + s_ * 512: h * TOK + (s_ + 1) * 512],
                            sc["TMP"][:, 0:512],
                            QKV[:, vc_ * TOK + 512:vc_ * TOK + 1024], ALU.add)
                        dve_m[("cx",) + key] = C["s_dve"].inc(ins)
                    blk.vector(f_dve_cx)

                    prevkey = [key]

        ctx_done_mark = dve_m[("cx", 3, 3, 1)]

        for hc in range(KC):
            par = hc % 2

            def f_gw_o(gp, hc=hc, par=par):
                s_pe.wait(gp, pe_m[("p", 23, 1)])
                if hc >= 2:
                    s_pe.wait(gp, pe_m[("o", hc - 2, 1)])
                src = wproj[:, hc * 128:(hc + 1) * 128].rearrange(
                    "(ec p) o -> p ec o", p=128)
                dst = WS[par].rearrange("p (ec o) -> p ec o", o=128)
                pe_m[("wo", hc)] = s_ws[par].inc(gp.dma_start(out=dst, in_=src), 16)
            blk.gpsimd(f_gw_o)

            for th in range(2):
                def f_pe_o(t, hc=hc, th=th, par=par):
                    t.wait_ge(s_ws[par].h, pe_m[("wo", hc)])
                    if hc == 0 and th == 0:
                        s_dve.wait(t, ctx_done_mark)
                    prev = (hc, th - 1) if th == 1 else (hc - 1, 1)
                    if hc > 0 or th > 0:
                        s_dve.wait(t, dve_m.get(("yo",) + prev, 0))
                    for ec in range(KC):
                        ins = t.matmul(pb[th][:, :],
                                       WS[par][:, ec * 128:(ec + 1) * 128],
                                       CTX[:, ec * TOK + th * 512:ec * TOK + (th + 1) * 512],
                                       start=(ec == 0), stop=(ec == KC - 1))
                    pe_m[("o", hc, th)] = s_pe.inc(ins)
                blk.tensor(f_pe_o)

                def f_dve_o(v, hc=hc, th=th):
                    s_pe.wait(v, pe_m[("o", hc, th)])
                    ins = v.scalar_tensor_tensor(
                        X[:, hc * TOK + th * 512: hc * TOK + (th + 1) * 512],
                        pb[th][:, :], 1.0,
                        X[:, hc * TOK + th * 512: hc * TOK + (th + 1) * 512],
                        ALU.mult, ALU.add)
                    dve_m[("yo", hc, th)] = C["s_dve"].inc(ins)
                blk.vector(f_dve_o)

        emit_store_master(nc, blk, C, X[:, :], y_out,
                          [(C["s_dve"], dve_m[("yo", KC - 1, 1)])])


def attn1_io(nc):
    return {
        "x_in": nc.dram_tensor("x_fm", [D, TOK], F32, kind="ExternalInput").ap(),
        "y_out": nc.dram_tensor("y_fm", [D, TOK], F32, kind="ExternalOutput").ap(),
        "wqkv_T": nc.dram_tensor("wqkv_T", [D, 3072], F32, kind="ExternalInput").ap(),
        "wproj_T": nc.dram_tensor("wproj_T", [D, D], F32, kind="ExternalInput").ap(),
    }


def build_attn1_program():
    nc = bass.Bass("TRN2", target_bir_lowering=False)
    io = attn1_io(nc)
    C = common_setup(nc)
    pbk = [nc.alloc_psum_tensor(f"pbk{i}", [128, 512], F32) for i in range(8)]
    with nc.Block() as blk:
        sb = emit_load_consts(nc, blk, C)
        ar = new_arena(nc)
        emit_attn1(nc, blk, C, sb, ar, pbk, io, "_a1")
    return nc



def build_fused_program(debug=False):
    nc = bass.Bass("TRN2", target_bir_lowering=False)
    x0 = nc.dram_tensor("x_fm", [D, TOK], F32, kind="ExternalInput").ap()
    y = nc.dram_tensor("y_fm", [D, TOK], F32, kind="ExternalOutput").ap()
    kind = "ExternalOutput" if debug else "Internal"
    xs = [nc.dram_tensor(f"xs{i}", [D, TOK], F32, kind=kind).ap()
          for i in range(1, 6)]

    io_moe = {
        "router_T": nc.dram_tensor("router_T", [D, 8], F32, kind="ExternalInput").ap(),
        "gup_T": nc.dram_tensor("gup_T", [8, D, 2816], F32, kind="ExternalInput").ap(),
        "down_T": nc.dram_tensor("down_T", [8, 1408, D], F32, kind="ExternalInput").ap(),
        "shg_T": nc.dram_tensor("shg_T", [D, 5632], F32, kind="ExternalInput").ap(),
        "shu_T": nc.dram_tensor("shu_T", [D, 5632], F32, kind="ExternalInput").ap(),
        "shd_T": nc.dram_tensor("shd_T", [5632, D], F32, kind="ExternalInput").ap(),
        "sgw_T": nc.dram_tensor("sgw_T", [128, KC], F32, kind="ExternalInput").ap(),
    }
    io_a0 = {
        "Wq": nc.dram_tensor("Wq", [D, D], F32, kind="ExternalInput").ap(),
        "Wk": nc.dram_tensor("Wk", [D, 512], F32, kind="ExternalInput").ap(),
        "Wv": nc.dram_tensor("Wv", [D, 512], F32, kind="ExternalInput").ap(),
        "Wo": nc.dram_tensor("Wo", [D, D], F32, kind="ExternalInput").ap(),
        "cs_t": nc.dram_tensor("cs_t", [128, 2048], F32, kind="ExternalInput").ap(),
        "masks": nc.dram_tensor("masks", [8, 128, 512], BF16, kind="ExternalInput").ap(),
    }
    io_f1 = {
        "gamma": nc.dram_tensor("gamma", [128, KC], F32, kind="ExternalInput").ap(),
        "beta": nc.dram_tensor("beta", [128, KC], F32, kind="ExternalInput").ap(),
        "mu_f": nc.dram_tensor("mu_f", [D, 2048], F32, kind="ExternalInput").ap(),
        "mu_sl": nc.dram_tensor("mu_sl", [D, 256], F32, kind="ExternalInput").ap(),
        "scale1": nc.dram_tensor("scale1", [1, 1], F32, kind="ExternalInput").ap(),
        "W1_sl": nc.dram_tensor("W1_sl", [2, D, 512], F32, kind="ExternalInput").ap(),
        "b1_sl": nc.dram_tensor("b1_sl", [128, 8], F32, kind="ExternalInput").ap(),
        "W2_sl": nc.dram_tensor("W2_sl", [2, 512, D], F32, kind="ExternalInput").ap(),
        "b2_sl": nc.dram_tensor("b2_sl", [2, D], F32, kind="ExternalInput").ap(),
    }
    io_a1 = {
        "wqkv_T": nc.dram_tensor("wqkv_T", [D, 3072], F32, kind="ExternalInput").ap(),
        "wproj_T": nc.dram_tensor("wproj_T", [D, D], F32, kind="ExternalInput").ap(),
    }

    C = common_setup(nc)
    pbk = [nc.alloc_psum_tensor(f"pbk{i}", [128, 512], F32) for i in range(8)]
    with nc.Block() as blk:
        sb = emit_load_consts(nc, blk, C)
        ar = new_arena(nc)

        def sub(emit_fn, io_extra, x_in, y_out, sfx, **kw):
            ar.reset()
            emit_boundary(nc, blk, C)
            io = dict(io_extra)
            io["x_in"] = x_in
            io["y_out"] = y_out
            emit_fn(nc, blk, C, sb, ar, pbk, io, sfx, **kw)

        sub(emit_moe, io_moe, x0, xs[0], "_s1", scale=0.5)
        sub(emit_attn0, io_a0, xs[0], xs[1], "_s2")
        sub(emit_moe, io_moe, xs[1], xs[2], "_s3", scale=0.5)
        sub(emit_ffn1, io_f1, xs[2], xs[3], "_s4", scale=0.5)
        sub(emit_attn1, io_a1, xs[3], xs[4], "_s5")
        sub(emit_ffn1, io_f1, xs[4], y, "_s6", scale=0.5)
    return nc


def fused_in_maps(d):
    w_moe = prep_moe_weights(d)
    w_a0 = prep_attn0_weights(d)
    w_a1 = prep_attn1_weights(d)
    xsh = shard_x_fm(d["L0_attn_x"])
    ims = []
    for c in range(NCORES):
        m = {"x_fm": xsh[c]}
        m.update(w_moe)
        m.update(w_a0)
        m.update(w_a1)
        m.update(prep_ffn1_weights(d, c))
        m["cs_t"] = rope_tables_for_core(c)
        m["masks"] = attn_masks_for_core(c)
        ims.append(m)
    return ims



def kernel(**inputs):
    nc = build_fused_program()
    ims = fused_in_maps(inputs)
    res = run_bass_kernel_spmd(nc, ims, core_ids=list(range(NCORES)))
    return unshard_x_fm([res.results[c]["y_fm"] for c in range(NCORES)])


# version 05
# speedup vs baseline: 1.1906x; 1.1906x over previous
import numpy as np
import ml_dtypes
import concourse.bass as bass
import concourse.mybir as mybir
from concourse.bass_utils import run_bass_kernel_spmd

F32 = mybir.dt.float32
BF16 = mybir.dt.bfloat16
AF = mybir.ActivationFunctionType
ALU = mybir.AluOpType
AX = mybir.AxisListType
bfnp = ml_dtypes.bfloat16

NCORES = 8
D = 2048
T = 4096
B = 2
STRIP = 256
TOK = 1024
KC = D // 128
LN_EPS = 1e-5


class Sem:
    __slots__ = ("h", "n")

    def __init__(self, nc, name):
        self.h = nc.alloc_semaphore(name)
        self.n = 0

    def inc(self, instr, k=1):
        instr.then_inc(self.h, k)
        self.n += k
        return self.n

    def wait(self, eng, n=None):
        eng.wait_ge(self.h, self.n if n is None else n)


class Arena:

    def __init__(self, nc, name, cols_bf16):
        self.t = nc.alloc_sbuf_tensor(name, [128, cols_bf16], BF16)
        self.view_f = self.t[:, :].bitcast(F32)
        self.cols = cols_bf16
        self.off = 0

    def reset(self):
        self.off = 0

    def alloc(self, shape, dtype):
        p, n = shape
        esz = 4 if dtype == F32 else 2
        a = (self.off + 31) // 32 * 32
        self.off = a + n * esz
        assert self.off <= self.cols * 2, f"arena overflow: {self.off} > {self.cols*2}"
        if dtype == F32:
            return self.view_f[0:p, a // 4:a // 4 + n]
        return self.t[0:p, a // 2:a // 2 + n]


def emit_boundary(nc, blk, C):
    vals = [(C[k].h, C[k].n) for k in
            ["s_ld", "s_gw", "s_pe", "s_act", "s_dve", "s_out", "s_ag", "s_pool"]]
    for eng in ["sync", "gpsimd", "tensor", "scalar", "vector"]:
        def f(e, vals=vals):
            for h, n in vals:
                if n:
                    e.wait_ge(h, n)
        getattr(blk, eng)(f)



def shard_x_fm(x):
    x = np.ascontiguousarray(np.asarray(x), dtype=np.float32)
    shards = []
    for c in range(NCORES):
        cols = np.concatenate([x[0, c::8], x[1, c::8]], axis=0)
        shards.append(np.ascontiguousarray(cols.T))
    return shards


def unshard_x_fm(shards):
    x = np.empty((B, T, D), np.float32)
    for c in range(NCORES):
        cols = np.asarray(shards[c]).T
        x[0, c::8] = cols[0:512]
        x[1, c::8] = cols[512:1024]
    return x


def prep_moe_weights(d):
    w = {}
    w["router_T"] = np.ascontiguousarray(d["L0_ffn_router_weight"].T)
    gup = np.asarray(d["L0_ffn_gate_up_proj"])
    g = gup[:, :1408, :]
    u = gup[:, 1408:, :]
    g = g.reshape(8, 11, 128, 2048)
    u = u.reshape(8, 11, 128, 2048)
    inter = np.stack([g, u], axis=2)
    inter = inter.reshape(8, 2816, 2048)
    w["gup_T"] = np.ascontiguousarray(inter.transpose(0, 2, 1))
    w["down_T"] = np.ascontiguousarray(np.asarray(d["L0_ffn_down_proj"]).transpose(0, 2, 1))
    w["shg_T"] = np.ascontiguousarray(d["L0_ffn_sh_gate_w"].T)
    w["shu_T"] = np.ascontiguousarray(d["L0_ffn_sh_up_w"].T)
    w["shd_T"] = np.ascontiguousarray(d["L0_ffn_sh_down_w"].T)
    w["sgw_T"] = np.ascontiguousarray(
        np.asarray(d["L0_ffn_shared_gate_w"]).reshape(KC, 128).T)
    return w



def common_setup(nc):
    C = {}
    for nm in ["s_ld", "s_gw", "s_pe", "s_act", "s_dve", "s_pool", "s_out", "s_ag"]:
        C[nm] = Sem(nc, nm)
    C["t_ones_k"] = nc.inline_tensor(np.ones((128, 1), np.float32), "t_ones_k")
    C["t_ones_m"] = nc.inline_tensor(np.ones((1, 128), np.float32), "t_ones_m")
    C["t_idn_f"] = nc.inline_tensor(np.eye(128, dtype=np.float32), "t_idn_f")
    C["t_idn_b"] = nc.inline_tensor(np.eye(128, dtype=bfnp), "t_idn_b")
    C["t_eps"] = nc.inline_tensor(np.full((128, 1), LN_EPS, np.float32), "t_eps")
    C["t_onesb"] = nc.inline_tensor(np.ones((128, 1), bfnp), "t_onesb")
    return C


def emit_load_consts(nc, blk, C):
    sb = {
        "ones_k": nc.alloc_sbuf_tensor("sb_ones_k", [128, 1], F32),
        "ones_m": nc.alloc_sbuf_tensor("sb_ones_m", [1, 128], F32),
        "idn_f": nc.alloc_sbuf_tensor("sb_idn_f", [128, 128], F32),
        "eps": nc.alloc_sbuf_tensor("sb_eps", [128, 1], F32),
    }
    s_ld = C["s_ld"]

    def f(sy):
        s_ld.inc(sy.dma_start(out=sb["ones_k"][:], in_=C["t_ones_k"].ap()), 16)
        s_ld.inc(sy.dma_start(out=sb["ones_m"][:], in_=C["t_ones_m"].ap()), 16)
        s_ld.inc(sy.dma_start(out=sb["idn_f"][:], in_=C["t_idn_f"].ap()), 16)
        s_ld.inc(sy.dma_start(out=sb["eps"][:], in_=C["t_eps"].ap()), 16)
    blk.sync(f)
    return sb


def emit_load_master(nc, blk, C, X, x_fm_ap, ntok=TOK):
    s_ld = C["s_ld"]

    def f(sy):
        src = x_fm_ap.rearrange("(kc p) t -> p kc t", p=128)
        dst = X[:, :].rearrange("p (kc t) -> p kc t", t=ntok)
        s_ld.inc(sy.dma_start(out=dst, in_=src), 16)
    blk.sync(f)
    return s_ld.n


def emit_store_master(nc, blk, C, X, y_fm_ap, waits, ntok=TOK):
    s_out = C["s_out"]

    def f(sy):
        for s, v in waits:
            s.wait(sy, v)
        dst = y_fm_ap.rearrange("(kc p) t -> p kc t", p=128)
        src = X[:, :].rearrange("p (kc t) -> p kc t", t=ntok)
        s_out.inc(sy.dma_start(out=dst, in_=src), 16)
        s_out.wait(sy)
    blk.sync(f)


def emit_ln(nc, blk, C, sb, X, XB, pm, sc, ntok=TOK, x_ready=None, gamma_ap=None,
           beta_ap=None, XF=None):
    s_pe, s_act, s_dve, s_ld = C["s_pe"], C["s_act"], C["s_dve"], C["s_ld"]
    nh = ntok // 512
    SQ = [sc["SQ0"], sc["SQ1"]]
    rows = sc["rows"]
    r_sum = rows[:, 0 * ntok:1 * ntok]
    r_sq = rows[:, 1 * ntok:2 * ntok]
    r_mean = rows[:, 2 * ntok:3 * ntok]
    r_var = rows[:, 3 * ntok:4 * ntok]
    r_rstd = rows[:, 4 * ntok:5 * ntok]
    r_tmp = rows[:, 5 * ntok:6 * ntok]
    MB, RB, TMP = sc["MB"], sc["RB"], sc["TMP"]
    ones_k, ones_m = sb["ones_k"], sb["ones_m"]
    if x_ready is None:
        x_ready = [(s_ld, s_ld.n)]

    gb = None
    if gamma_ap is not None:
        gb = sc.get("gb")
        if gb is None:
            gb = nc.alloc_sbuf_tensor(None, [128, 2 * KC], F32)[:, :]

        def f_gb(sy):
            s_ld.inc(sy.dma_start(out=gb[:, 0:KC], in_=gamma_ap), 16)
            s_ld.inc(sy.dma_start(out=gb[:, KC:2 * KC], in_=beta_ap), 16)
        blk.sync(f_gb)
        gb_ready = s_ld.n

    pe_m = {}

    def f_pe_a(t):
        for s, v in x_ready:
            s.wait(t, v)
        for th in range(nh):
            for k in range(KC):
                ins = t.matmul(pm[th][0:1, :], ones_k[:, :],
                               X[:, k * ntok + th * 512: k * ntok + (th + 1) * 512],
                               start=(k == 0), stop=(k == KC - 1))
            pe_m[("sum", th)] = s_pe.inc(ins)
    blk.tensor(f_pe_a)

    def f_dve_a(v):
        for th in range(nh):
            s_pe.wait(v, pe_m[("sum", th)])
            ins = v.tensor_copy(r_sum[:, th * 512:(th + 1) * 512], pm[th][0:1, :])
        pe_m["sum_evac"] = s_dve.inc(ins)
    blk.vector(f_dve_a)

    act_m = {}
    for k in range(KC):
        def f_act_b(a, k=k):
            if k < 2:
                for s, v in x_ready:
                    s.wait(a, v)
            else:
                s_pe.wait(a, pe_m[("sq", k - 2)])
            if k == 0:
                s_dve.wait(a, pe_m["sum_evac"])
            act_m[k] = s_act.inc(a.activation(SQ[k % 2][:, :],
                                              X[:, k * ntok:(k + 1) * ntok], AF.Square))
        blk.scalar(f_act_b)

        def f_pe_b(t, k=k):
            s_act.wait(t, act_m[k])
            if k == 0:
                s_dve.wait(t, pe_m["sum_evac"])
            for th in range(nh):
                ins = t.matmul(pm[th][0:1, :], ones_k[:, :],
                               SQ[k % 2][:, th * 512:(th + 1) * 512],
                               start=(k == 0), stop=(k == KC - 1))
                if k == KC - 1:
                    pe_m[("sumsq", th)] = s_pe.inc(ins)
            if k < KC - 1:
                pe_m[("sq", k)] = s_pe.inc(ins)
            else:
                pe_m[("sq", k)] = pe_m[("sumsq", nh - 1)]
        blk.tensor(f_pe_b)

    dve_m = {}

    def f_dve_c(v):
        for th in range(nh):
            s_pe.wait(v, pe_m[("sumsq", th)])
            v.tensor_copy(r_sq[:, th * 512:(th + 1) * 512], pm[th][0:1, :])
        v.drain()
        v.tensor_scalar_mul(r_mean[:, :], r_sum[:, :], 1.0 / D)
        v.tensor_scalar_mul(r_var[:, :], r_sq[:, :], 1.0 / D)
        v.drain()
        v.tensor_tensor(r_tmp[:, :], r_mean[:, :], r_mean[:, :], ALU.mult)
        v.drain()
        ins = v.tensor_tensor(r_var[:, :], r_var[:, :], r_tmp[:, :], ALU.subtract)
        dve_m["var"] = s_dve.inc(ins)
    blk.vector(f_dve_c)

    def f_act_c(a):
        s_dve.wait(a, dve_m["var"])
        act_m["sd"] = s_act.inc(a.activation(r_var[:, :], r_var[:, :], AF.Sqrt,
                                             bias=sb["eps"][0:1, :]))
    blk.scalar(f_act_c)

    def f_dve_c2(v):
        s_act.wait(v, act_m["sd"])
        dve_m["rstd"] = s_dve.inc(v.reciprocal(r_rstd[:, :], r_var[:, :]))
    blk.vector(f_dve_c2)

    for th in range(nh):
        def f_pe_d(t, th=th):
            if th == 0:
                s_dve.wait(t, dve_m["rstd"])
            else:
                s_dve.wait(t, dve_m[("bc", th - 1)])
            t.matmul(pm[0][:, :], ones_m[:, :], r_mean[:, th * 512:(th + 1) * 512],
                     start=True, stop=True)
            ins = t.matmul(pm[1][:, :], ones_m[:, :], r_rstd[:, th * 512:(th + 1) * 512],
                           start=True, stop=True)
            pe_m[("bc", th)] = s_pe.inc(ins)
        blk.tensor(f_pe_d)

        def f_dve_d(v, th=th):
            s_pe.wait(v, pe_m[("bc", th)])
            v.tensor_copy(MB[:, th * 512:(th + 1) * 512], pm[0][:, :])
            ins = v.tensor_copy(RB[:, th * 512:(th + 1) * 512], pm[1][:, :])
            dve_m[("bc", th)] = s_dve.inc(ins)
        blk.vector(f_dve_d)

    def f_dve_d_all(v):
        dve_m["bc"] = dve_m[("bc", nh - 1)]
        if gamma_ap is not None:
            v.wait_ge(s_ld.h, gb_ready)
        for k in range(KC):
            v.tensor_tensor(TMP[:, :], X[:, k * ntok:(k + 1) * ntok], MB[:, :],
                            ALU.subtract)
            v.drain()
            if gamma_ap is None:
                if XF is not None:
                    ins = v.tensor_tensor(XF[:, k * ntok:(k + 1) * ntok], TMP[:, :],
                                          RB[:, :], ALU.mult)
                    if XB is not None:
                        v.drain()
                        ins = v.tensor_copy(XB[:, k * ntok:(k + 1) * ntok],
                                            XF[:, k * ntok:(k + 1) * ntok])
                else:
                    ins = v.tensor_tensor(XB[:, k * ntok:(k + 1) * ntok], TMP[:, :],
                                          RB[:, :], ALU.mult)
            else:
                v.tensor_tensor(TMP[:, :], TMP[:, :], RB[:, :], ALU.mult)
                v.drain()
                v.tensor_scalar(TMP[:, :], TMP[:, :], gb[:, k:k + 1],
                                gb[:, KC + k:KC + k + 1], ALU.mult, ALU.add)
                v.drain()
                if XF is not None:
                    v.tensor_copy(XF[:, k * ntok:(k + 1) * ntok], TMP[:, :])
                ins = v.tensor_copy(XB[:, k * ntok:(k + 1) * ntok], TMP[:, :])
        dve_m["xb"] = s_dve.inc(ins)
    blk.vector(f_dve_d_all)

    return dve_m["xb"], dve_m["bc"]


def emit_moe(nc, blk, C, sb, ar, pb, io, sfx, scale=0.5, debug=False):
    x_in = io["x_in"]
    y_out = io["y_out"]
    if debug:
        dbg_xb = io["dbg_xb"]
        dbg_cf = io["dbg_cf"]
        dbg_bg = io["dbg_bg"]
        dbg_a = io["dbg_a"]
        dbg_rs = io["dbg_rs"]
    rT = io["router_T"]
    gupT = io["gup_T"]
    dwnT = io["down_T"]
    shgT = io["shg_T"]
    shuT = io["shu_T"]
    shdT = io["shd_T"]
    sgwT = io["sgw_T"]

    s_wgu = [Sem(nc, f"s_wgu{i}{sfx}") for i in range(2)]
    s_wd = [Sem(nc, f"s_wd{i}{sfx}") for i in range(2)]
    s_crow = Sem(nc, f"s_crow{sfx}")

    X = ar.alloc([128, KC * TOK], F32)
    XB = ar.alloc([128, KC * TOK], BF16)
    A = ar.alloc([128, 11 * TOK], BF16)
    Wgu = [ar.alloc([128, KC * 256], BF16) for i in range(2)]
    Wd = [ar.alloc([128, 11 * 512], BF16) for i in range(2)]
    RTsb = ar.alloc([128, KC * 8], BF16)
    SGWsb = ar.alloc([128, KC], BF16)
    SCR = ar.alloc([128, 11264], F32)
    sc = {
        "SQ0": SCR[:, 0 * TOK:1 * TOK],
        "SQ1": SCR[:, 1 * TOK:2 * TOK],
        "MB": SCR[:, 2 * TOK:3 * TOK],
        "RB": SCR[:, 3 * TOK:4 * TOK],
        "TMP": SCR[:, 4 * TOK:5 * TOK],
        "rows": SCR[0:1, 5 * TOK:11 * TOK],
    }
    SG = SCR[:, 0 * TOK:1 * TOK]
    UB = SCR[:, 1 * TOK:2 * TOK]
    BE = SCR[:, 5 * TOK:6 * TOK]
    BG = SCR[:, 6 * TOK:7 * TOK]
    CF = SCR[0:8, 7 * TOK:8 * TOK]
    CROW = SCR[0:1, 8 * TOK:9 * TOK]
    GR = SCR[0:1, 9 * TOK:10 * TOK]
    RS = SCR[:, 10 * TOK:10 * TOK + 64]
    RCOL = SCR[:, 10 * TOK + 64:10 * TOK + 72]

    pg = pb[0:2]
    pu = pb[2:4]
    pd = pb[4:6]
    pm = pb[6:8]

    s_ld, s_gw, s_pe = C["s_ld"], C["s_gw"], C["s_pe"]
    s_act, s_dve, s_pool = C["s_act"], C["s_dve"], C["s_pool"]

    if True:
        emit_load_master(nc, blk, C, X, x_in)

        def f_rt(sy):
            s_ld.inc(sy.dma_start(out=RTsb[:, :].rearrange("p (kc o) -> p kc o", o=8),
                                  in_=rT.rearrange("(kc p) o -> p kc o", p=128)), 16)
            s_ld.inc(sy.dma_start(out=SGWsb[:, :], in_=sgwT), 16)
        blk.gpsimd(f_rt)
        rt_mark = s_ld.n

        xb_mark, ln_bank_mark = emit_ln(nc, blk, C, sb, X, XB, pm, sc)

        pe_m = {}
        dve_m = {}
        act_m = {}
        for tc in range(8):
            def f_pe_r(t, tc=tc):
                if tc == 0:
                    s_dve.wait(t, ln_bank_mark)
                    s_dve.wait(t, xb_mark)
                    t.wait_ge(s_ld.h, rt_mark)
                else:
                    s_dve.wait(t, dve_m[("rdone", tc - 1)])
                for k in range(KC):
                    ins = t.matmul(pm[tc % 2][:, 0:8],
                                   XB[:, k * TOK + tc * 128: k * TOK + (tc + 1) * 128],
                                   RTsb[:, k * 8:(k + 1) * 8],
                                   start=(k == 0), stop=(k == KC - 1))
                pe_m[("rl", tc)] = s_pe.inc(ins)
            blk.tensor(f_pe_r)

            def f_dve_r(v, tc=tc):
                if debug and tc == 1:
                    v.wait_ge(C["s_out"].h, rs_dump_mark)
                s_pe.wait(v, pe_m[("rl", tc)])
                l8 = RS[:, 0:8]
                eq = RS[:, 8:16]
                lm = RS[:, 16:24]
                ge = RS[:, 24:32]
                ex = RS[:, 32:40]
                gate = RS[:, 40:48]
                cmb = RS[:, 48:56]
                m1 = RCOL[:, 0:1]
                m2 = RCOL[:, 1:2]
                dn = RCOL[:, 2:3]
                di = RCOL[:, 3:4]
                v.tensor_copy(l8, pm[tc % 2][:, 0:8])
                v.drain()
                v.reduce_max(m1, l8, axis=AX.X)
                v.drain()
                v.tensor_scalar(eq, l8, m1, None, ALU.is_ge)
                v.drain()
                v.scalar_tensor_tensor(lm, eq, -1e30, l8, ALU.mult, ALU.add)
                v.drain()
                v.reduce_max(m2, lm, axis=AX.X)
                v.drain()
                v.tensor_scalar(ge, l8, m2, None, ALU.is_ge)
                v.tensor_scalar(ex, l8, m1, None, ALU.subtract)
                ins = v.drain()
                dve_m[("rsub", tc)] = s_dve.inc(ins)
            blk.vector(f_dve_r)

            def f_act_r(a, tc=tc):
                s_dve.wait(a, dve_m[("rsub", tc)])
                act_m[("rexp", tc)] = s_act.inc(
                    a.activation(RS[:, 32:40], RS[:, 32:40], AF.Exp))
            blk.scalar(f_act_r)

            def f_dve_r2(v, tc=tc):
                s_act.wait(v, act_m[("rexp", tc)])
                ex = RS[:, 32:40]
                ge = RS[:, 24:32]
                gate = RS[:, 40:48]
                cmb = RS[:, 48:56]
                dn = RCOL[:, 2:3]
                di = RCOL[:, 3:4]
                v.tensor_tensor(gate, ex, ge, ALU.mult)
                v.drain()
                v.reduce_sum(dn, gate, axis=AX.X)
                v.drain()
                v.reciprocal(di, dn)
                v.drain()
                v.tensor_scalar_mul(di, di, scale)
                v.drain()
                ins = v.tensor_scalar_mul(cmb, gate, di)
                dve_m[("cmb", tc)] = s_dve.inc(ins)
            blk.vector(f_dve_r2)

            if debug and tc == 0:
                def f_dbg_rs(sy):
                    s_dve.wait(sy, dve_m[("cmb", 0)])
                    dve_m["dbg_rs"] = True
                    C["s_out"].inc(sy.dma_start(out=dbg_rs[:, 0:64], in_=RS[:, :]), 16)
                    C["s_out"].inc(sy.dma_start(out=dbg_rs[:, 64:72], in_=RCOL[:, :]), 16)
                blk.sync(f_dbg_rs)
                rs_dump_mark = C["s_out"].n

            def f_pe_r2(t, tc=tc):
                s_dve.wait(t, dve_m[("cmb", tc)])
                ins = t.transpose(pm[tc % 2][0:8, 0:128], RS[:, 48:56], sb["idn_f"][:, :])
                pe_m[("rt", tc)] = s_pe.inc(ins)
            blk.tensor(f_pe_r2)

            def f_dve_r3(v, tc=tc):
                s_pe.wait(v, pe_m[("rt", tc)])
                ins = v.tensor_copy(CF[0:8, tc * 128:(tc + 1) * 128], pm[tc % 2][0:8, 0:128])
                dve_m[("rdone", tc)] = s_dve.inc(ins)
            blk.vector(f_dve_r3)

        def f_pe_g(t):
            s_dve.wait(t, dve_m[("rdone", 7)])
            for th in range(2):
                for k in range(KC):
                    ins = t.matmul(pm[th][0:1, :], SGWsb[:, k:k + 1],
                                   XB[:, k * TOK + th * 512: k * TOK + (th + 1) * 512],
                                   start=(k == 0), stop=(k == KC - 1))
                pe_m[("sg", th)] = s_pe.inc(ins)
        blk.tensor(f_pe_g)

        def f_act_g(a):
            for th in range(2):
                s_pe.wait(a, pe_m[("sg", th)])
                ins = a.activation(GR[:, th * 512:(th + 1) * 512], pm[th][0:1, :],
                                   AF.Sigmoid)
            act_m["sg"] = s_act.inc(ins)
        blk.scalar(f_act_g)

        def f_dve_g(v):
            s_act.wait(v, act_m["sg"])
            ins = v.tensor_scalar_mul(GR[:, :], GR[:, :], scale)
            dve_m["gr"] = s_dve.inc(ins)
        blk.vector(f_dve_g)

        def f_pe_g2(t):
            s_dve.wait(t, dve_m["gr"])
            for th in range(2):
                ins = t.matmul(pm[th][:, :], sb["ones_m"][:, :],
                               GR[:, th * 512:(th + 1) * 512], start=True, stop=True)
            pe_m["gb"] = s_pe.inc(ins)
        blk.tensor(f_pe_g2)

        def f_dve_g2(v):
            s_pe.wait(v, pe_m["gb"])
            for th in range(2):
                ins = v.tensor_copy(BG[:, th * 512:(th + 1) * 512], pm[th][:, :])
            dve_m["bg"] = s_dve.inc(ins)
        blk.vector(f_dve_g2)

        gw_mark = {}
        gu_consumed = {}
        dn_consumed = {}
        pe_gu = {}
        pe_dn = {}
        act_su = {}
        dve_ub = {}
        dve_a = {}
        dve_x = {}
        be_ready = {}
        pd_free = {}
        pd_use = 0

        dbg_a_mark = [0]
        for u in range(12):
            is_sh = u >= 8
            g = u - 8
            if debug and u == 1:
                def f_dbg_a(sy):
                    s_dve.wait(sy, dve_a[(0, 10, 1)])
                    dbg_a_mark[0] = C["s_out"].inc(
                        sy.dma_start(out=dbg_a.rearrange("(jc p) t -> p jc t", p=128),
                                     in_=A[:, :].rearrange("p (jc t) -> p jc t", t=TOK)),
                        16)
                blk.sync(f_dbg_a)
            if not is_sh:
                def f_pool_be(gp, u=u):
                    s_dve.wait(gp, dve_m[("rdone", 7)])
                    if u > 0:
                        s_dve.wait(gp, dve_a[(u - 1, 10, 1)])
                    gw_mark[("crow", u)] = s_crow.inc(
                        gp.dma_start(out=CROW[:, :], in_=CF[u:u + 1, :]), 16)
                blk.gpsimd(f_pool_be)

                def f_pe_be(t, u=u):
                    t.wait_ge(s_crow.h, gw_mark[("crow", u)])
                    if u > 0:
                        s_dve.wait(t, dve_m[("be", u - 1)])
                    else:
                        s_dve.wait(t, dve_m["bg"])
                    for th in range(2):
                        ins = t.matmul(pm[th][:, :], sb["ones_m"][:, :],
                                       CROW[:, th * 512:(th + 1) * 512],
                                       start=True, stop=True)
                    pe_m[("be", u)] = s_pe.inc(ins)
                blk.tensor(f_pe_be)

                def f_dve_be(v, u=u):
                    s_pe.wait(v, pe_m[("be", u)])
                    if u > 0:
                        pass
                    for th in range(2):
                        ins = v.tensor_copy(BE[:, th * 512:(th + 1) * 512], pm[th][:, :])
                    dve_m[("be", u)] = s_dve.inc(ins)
                    be_ready[u] = dve_m[("be", u)]
                blk.vector(f_dve_be)
            else:
                if u == 8:
                    be_ready[u] = dve_m["bg"]
                else:
                    be_ready[u] = be_ready[8]
            BEu = BG if is_sh else BE

            for m in range(11):
                gi = u * 11 + m
                par = gi % 2

                def f_gw(gp, u=u, m=m, gi=gi, par=par, is_sh=is_sh, g=g):
                    if gi >= 2:
                        s_pe.wait(gp, gu_consumed[gi - 2])
                    dst = Wgu[par][:, :].rearrange("p (kc j) -> p kc j", j=256)
                    sw = s_wgu[par]
                    if not is_sh:
                        src = gupT[u, :, m * 256:(m + 1) * 256] \
                            .rearrange("(kc p) j -> p kc j", p=128)
                        ins = gp.dma_start(out=dst, in_=src)
                    else:
                        jm = g * 11 + m
                        srcg = shgT[:, jm * 128:(jm + 1) * 128] \
                            .rearrange("(kc p) j -> p kc j", p=128)
                        srcu = shuT[:, jm * 128:(jm + 1) * 128] \
                            .rearrange("(kc p) j -> p kc j", p=128)
                        sw.inc(gp.dma_start(out=dst[:, :, 0:128], in_=srcg), 16)
                        ins = gp.dma_start(out=dst[:, :, 128:256], in_=srcu)
                    gw_mark[("gu", gi)] = sw.inc(ins, 16)
                blk.gpsimd(f_gw)

                def f_pe_gu(t, u=u, m=m, gi=gi, par=par):
                    t.wait_ge(s_wgu[par].h, gw_mark[("gu", gi)])
                    if u == 0 and m == 0:
                        s_dve.wait(t, xb_mark)
                    if gi >= 1:
                        s_act.wait(t, act_su[(gi - 1, 1)])
                        s_dve.wait(t, dve_ub[(gi - 1, 1)])
                    for th in range(2):
                        for k in range(KC):
                            ins = t.matmul(pg[th][:, :],
                                           Wgu[par][:, k * 256:k * 256 + 128],
                                           XB[:, k * TOK + th * 512:k * TOK + (th + 1) * 512],
                                           start=(k == 0), stop=(k == KC - 1))
                        pe_gu[(gi, "g", th)] = s_pe.inc(ins)
                    for th in range(2):
                        for k in range(KC):
                            ins = t.matmul(pu[th][:, :],
                                           Wgu[par][:, k * 256 + 128:k * 256 + 256],
                                           XB[:, k * TOK + th * 512:k * TOK + (th + 1) * 512],
                                           start=(k == 0), stop=(k == KC - 1))
                        pe_gu[(gi, "u", th)] = s_pe.inc(ins)
                    gu_consumed[gi] = pe_gu[(gi, "u", 1)]
                blk.tensor(f_pe_gu)

                def f_act_su(a, u=u, m=m, gi=gi):
                    for th in range(2):
                        s_pe.wait(a, pe_gu[(gi, "g", th)])
                        if gi >= 1:
                            s_dve.wait(a, dve_a[(u, m - 1, 1)] if m >= 1
                                       else dve_a[(u - 1, 10, 1)] if u >= 1 else 0)
                        ins = a.activation(SG[:, th * 512:(th + 1) * 512], pg[th][:, :],
                                           AF.Silu)
                        act_su[(gi, th)] = s_act.inc(ins)
                blk.scalar(f_act_su)

                def f_dve_gu(v, u=u, m=m, gi=gi):
                    if debug and u == 1 and m == 0:
                        v.wait_ge(C["s_out"].h, dbg_a_mark[0])
                    for th in range(2):
                        s_pe.wait(v, pe_gu[(gi, "u", th)])
                        ins = v.tensor_tensor(UB[:, th * 512:(th + 1) * 512], pu[th][:, :],
                                              BEu[:, th * 512:(th + 1) * 512], ALU.mult)
                        dve_ub[(gi, th)] = s_dve.inc(ins)
                        v.drain()
                        s_act.wait(v, act_su[(gi, th)])
                        ins = v.tensor_tensor(
                            A[:, m * TOK + th * 512:m * TOK + (th + 1) * 512],
                            SG[:, th * 512:(th + 1) * 512],
                            UB[:, th * 512:(th + 1) * 512], ALU.mult)
                        dve_a[(u, m, th)] = s_dve.inc(ins)
                blk.vector(f_dve_gu)

            for s4 in range(4):
                di = u * 4 + s4
                par = di % 2

                def f_gw_d(gp, u=u, s4=s4, di=di, par=par, is_sh=is_sh, g=g):
                    if di >= 2:
                        s_pe.wait(gp, dn_consumed[di - 2])
                    dst = Wd[par][:, :].rearrange("p (jc h) -> p jc h", h=512)
                    if not is_sh:
                        src = dwnT[u, :, s4 * 512:(s4 + 1) * 512] \
                            .rearrange("(jc p) h -> p jc h", p=128)
                    else:
                        src = shdT[g * 1408:(g + 1) * 1408, s4 * 512:(s4 + 1) * 512] \
                            .rearrange("(jc p) h -> p jc h", p=128)
                    gw_mark[("d", di)] = s_wd[par].inc(gp.dma_start(out=dst, in_=src), 16)
                blk.gpsimd(f_gw_d)

                for hl in range(4):
                    hc = s4 * 4 + hl
                    for th in range(2):
                        nonlocal_pd = pd_use
                        pdb = pd[pd_use % 2]
                        pd_use += 1

                        def f_pe_d(t, u=u, s4=s4, di=di, hl=hl, th=th, hc=hc,
                                   pdb=pdb, slot=nonlocal_pd, par=par):
                            if hl == 0 and th == 0:
                                t.wait_ge(s_wd[par].h, gw_mark[("d", di)])
                                if s4 == 0:
                                    s_dve.wait(t, dve_a[(u, 10, 1)])
                            if slot >= 2:
                                s_dve.wait(t, pd_free[slot - 2])
                            for jc in range(11):
                                ins = t.matmul(pdb[:, :],
                                               Wd[par][:, jc * 512 + hl * 128: jc * 512 + (hl + 1) * 128],
                                               A[:, jc * TOK + th * 512: jc * TOK + (th + 1) * 512],
                                               start=(jc == 0), stop=(jc == 10))
                            pe_dn[slot] = s_pe.inc(ins)
                            if hl == 3 and th == 1:
                                dn_consumed[di] = pe_dn[slot]
                        blk.tensor(f_pe_d)

                        def f_dve_d(v, u=u, hc=hc, th=th, pdb=pdb, slot=nonlocal_pd):
                            s_pe.wait(v, pe_dn[slot])
                            ins = v.scalar_tensor_tensor(
                                X[:, hc * TOK + th * 512: hc * TOK + (th + 1) * 512],
                                pdb[:, :], 1.0,
                                X[:, hc * TOK + th * 512: hc * TOK + (th + 1) * 512],
                                ALU.mult, ALU.add)
                            pd_free[slot] = s_dve.inc(ins)
                        blk.vector(f_dve_d)

        if debug:
            def f_dbg(sy):
                s_dve.wait(sy, xb_mark)
                C["s_out"].inc(sy.dma_start(
                    out=dbg_xb.rearrange("(kc p) t -> p kc t", p=128),
                    in_=XB[:, :].rearrange("p (kc t) -> p kc t", t=TOK)), 16)
                s_dve.wait(sy, dve_m[("rdone", 7)])
                C["s_out"].inc(sy.dma_start(out=dbg_cf, in_=CF), 16)
                s_dve.wait(sy, dve_m["bg"])
                C["s_out"].inc(sy.dma_start(out=dbg_bg, in_=BG[:, :]), 16)
            blk.sync(f_dbg)

        last_x = s_dve.n
        emit_store_master(nc, blk, C, X, y_out, [(s_dve, last_x)])


def moe_io(nc, debug=False):
    io = {
        "x_in": nc.dram_tensor("x_fm", [D, TOK], F32, kind="ExternalInput").ap(),
        "y_out": nc.dram_tensor("y_fm", [D, TOK], F32, kind="ExternalOutput").ap(),
        "router_T": nc.dram_tensor("router_T", [D, 8], F32, kind="ExternalInput").ap(),
        "gup_T": nc.dram_tensor("gup_T", [8, D, 2816], F32, kind="ExternalInput").ap(),
        "down_T": nc.dram_tensor("down_T", [8, 1408, D], F32, kind="ExternalInput").ap(),
        "shg_T": nc.dram_tensor("shg_T", [D, 5632], F32, kind="ExternalInput").ap(),
        "shu_T": nc.dram_tensor("shu_T", [D, 5632], F32, kind="ExternalInput").ap(),
        "shd_T": nc.dram_tensor("shd_T", [5632, D], F32, kind="ExternalInput").ap(),
        "sgw_T": nc.dram_tensor("sgw_T", [128, KC], F32, kind="ExternalInput").ap(),
    }
    if debug:
        io["dbg_xb"] = nc.dram_tensor("dbg_xb", [D, TOK], BF16, kind="ExternalOutput").ap()
        io["dbg_cf"] = nc.dram_tensor("dbg_cf", [8, TOK], F32, kind="ExternalOutput").ap()
        io["dbg_bg"] = nc.dram_tensor("dbg_bg", [128, TOK], F32, kind="ExternalOutput").ap()
        io["dbg_a"] = nc.dram_tensor("dbg_a", [1408, TOK], BF16, kind="ExternalOutput").ap()
        io["dbg_rs"] = nc.dram_tensor("dbg_rs", [128, 72], F32, kind="ExternalOutput").ap()
    return io


def new_arena(nc):
    return Arena(nc, "AR", (nc.sbuf_bytes_remaining - 64) // 2)


def build_moe_program(scale=0.5, debug=False):
    nc = bass.Bass("TRN2", target_bir_lowering=False)
    io = moe_io(nc, debug)
    C = common_setup(nc)
    pbk = [nc.alloc_psum_tensor(f"pbk{i}", [128, 512], F32) for i in range(8)]
    with nc.Block() as blk:
        sb = emit_load_consts(nc, blk, C)
        ar = new_arena(nc)
        emit_moe(nc, blk, C, sb, ar, pbk, io, "_m", scale=scale, debug=debug)
    return nc





NQH = 32
NKV = 8
HD = 64


def prep_attn0_weights(d):
    return {
        "Wq": np.ascontiguousarray(np.asarray(d["L0_attn_Wq"])),
        "Wk": np.ascontiguousarray(np.asarray(d["L0_attn_Wk"])),
        "Wv": np.ascontiguousarray(np.asarray(d["L0_attn_Wv"])),
        "Wo": np.ascontiguousarray(np.asarray(d["L0_attn_Wo"])),
    }


def rope_tables_for_core(c):
    pos = (np.arange(512) * 8 + c).astype(np.float32)
    pos = np.concatenate([pos, pos])
    theta = 1.0 / 10000.0 ** (np.arange(0, HD, 2, dtype=np.float32) / HD)
    fr = np.outer(theta, pos)
    co, si = np.cos(fr).astype(np.float32), np.sin(fr).astype(np.float32)
    cs1 = np.concatenate([co, si, co, si], axis=0)
    cs2 = np.concatenate([si, co, si, co], axis=0)
    return np.concatenate([cs1, cs2], axis=1)


def attn_masks_for_core(c):
    out = np.zeros((8, 128, 512), np.float32)
    io = np.arange(128)
    for r in range(8):
        m = (io[:, None] + (1 if r > c else 0) <= io[None, :]).astype(np.float32)
        out[r] = np.tile(m, (1, 4))
    return out.astype(bfnp)


def _selk_table():
    t = np.zeros((128, 16, 32), np.float32)
    for k in range(16):
        for dd in range(128):
            t[dd, k, (128 * k + dd) // 64 % 32] = 1.0
    return np.ascontiguousarray(t.reshape(128, 16 * 32))


def _exp32_table():
    t = np.zeros((8, 32), np.float32)
    for h in range(32):
        t[h // 4, h] = 1.0
    return t


def emit_attn0(nc, blk, C, sb, ar, pb, io, sfx, debug=False):
    if "t_selk" not in C:
        C["t_selk"] = nc.inline_tensor(_selk_table(), "t_selk")
        C["t_exp32"] = nc.inline_tensor(_exp32_table(), "t_exp32")
    x_in = io["x_in"]
    y_out = io["y_out"]
    Wq = io["Wq"]
    Wk = io["Wk"]
    Wv = io["Wv"]
    Wo = io["Wo"]
    cs_in = io["cs_t"]
    mk_in = io["masks"]

    kag_i = nc.dram_tensor("kag_i" + sfx, [521, TOK], BF16, kind="Internal").ap()
    kag_o = nc.dram_tensor("kag_o" + sfx, [8 * 521, TOK], BF16, kind="Internal").ap()
    vag_i = nc.dram_tensor("vag_i" + sfx, [TOK, 520], BF16, kind="Internal").ap()
    vag_o = nc.dram_tensor("vag_o" + sfx, [8 * TOK, 520], BF16, kind="Internal").ap()
    mub_d = nc.dram_tensor("mub_d" + sfx, [NQH, TOK], BF16, kind="Internal").ap()
    if debug:
        dbg_qp = io["dbg_qp"]
        dbg_wq = io["dbg_wq"]
        dbg_xb = io["dbg_xb"]
        dbg_kt = io["dbg_kt"]
        dbg_vt = io["dbg_vt"]
        dbg_qr = io["dbg_qr"]
        dbg_ctx = io["dbg_ctx"]

    s_wq2 = [Sem(nc, f"s_wq2{i}{sfx}") for i in range(2)]
    s_wo2 = [Sem(nc, f"s_wo2{i}{sfx}") for i in range(2)]
    s_kl2 = [Sem(nc, f"s_kl2{i}{sfx}") for i in range(2)]
    s_vl2 = Sem(nc, "s_vl2" + sfx)
    s_mk2 = Sem(nc, "s_mk2" + sfx)
    s_mub2 = Sem(nc, "s_mub2" + sfx)
    s_kv2 = Sem(nc, "s_kv2" + sfx)
    s_x2 = [Sem(nc, f"s_x2{i}{sfx}") for i in range(2)]
    s_y2 = [Sem(nc, f"s_y2{i}{sfx}") for i in range(2)]
    s_agk = Sem(nc, "s_agk" + sfx)
    s_agv = Sem(nc, "s_agv" + sfx)

    BIGA = ar.alloc([128, KC * TOK], F32)
    Xm = BIGA[:, :]
    BAb = BIGA[:, :].bitcast(BF16)
    XB = BAb[:, 0:16384]
    VT = BAb[:, 16384:16384 + 8 * 520]
    KT = BAb[0:65, 20544:20544 + 8 * TOK]
    CTX = BAb[:, 0:16384]
    KL = [BAb[0:65, 16384 + i * 4096: 16384 + (i + 1) * 4096] for i in range(2)]
    PT = [BAb[:, 24576 + i * 512: 24576 + (i + 1) * 512] for i in range(8)]

    WBUF = ar.alloc([128, 20736], BF16)
    Wk_sb = WBUF[:, 0:8192]
    Wv_sb = WBUF[:, 8192:16384]
    Wqs = [WBUF[:, 16384 + i * 2048:16384 + (i + 1) * 2048] for i in range(2)]
    VL = WBUF[:, 0:16640]
    Wos = [WBUF[:, 16640 + i * 2048:16640 + (i + 1) * 2048] for i in range(2)]

    QR = ar.alloc([128, NQH * TOK], BF16)[0:65, :]
    QRf = QR[:, :].bitcast(F32)
    rows_scr = QRf[0:1, 0:6 * TOK]

    MK = ar.alloc([128, 8 * 512], BF16)
    SCR2 = ar.alloc([128, 3072], F32)
    sc = {"MB": SCR2[:, 0:1024], "RB": SCR2[:, 1024:2048], "TMP": SCR2[:, 2048:3072],
          "rows": rows_scr}
    ctmp = SCR2[0:64, 2048:2560]
    zr = SCR2[0:1, 1024:1536]
    zr2 = SCR2[0:1, 1536:2048]
    SQ0 = ar.alloc([128, 1024], F32)
    SQ1 = ar.alloc([128, 1024], F32)
    sc["SQ0"] = SQ0[:, :]
    sc["SQ1"] = SQ1[:, :]
    QP = ar.alloc([128, 1024], F32)
    QN = ar.alloc([128, 1024], F32)[0:32, :]
    M32 = SQ0[0:32, :]
    M32b = SQ1[0:32, :].bitcast(BF16)[:, 0:1024]
    CS = MK[:, 0:4096].bitcast(F32)
    K8 = SCR2[0:8, 1024:1032]
    KV8 = SCR2[0:8, 1032:1033]
    KV32 = SCR2[0:32, 1040:1041]
    EXP32 = SCR2[0:8, 1056:1088]
    NRMX = SCR2[0:1, 1088:1096]
    SELK = ar.alloc([128, 512], F32)
    KRW = SCR2[0:1, 2048:3072]

    s_ld, s_gw, s_pe = C["s_ld"], C["s_gw"], C["s_pe"]
    s_act, s_dve, s_pool, s_ag = C["s_act"], C["s_dve"], C["s_pool"], C["s_ag"]
    s_out = C["s_out"]

    if True:
        emit_load_master(nc, blk, C, Xm, x_in)

        def f_ld2(sy):
            s_ld.inc(sy.dma_start(out=CS[:, :], in_=cs_in), 16)
            s_ld.inc(sy.dma_start(out=SELK[:, :], in_=C["t_selk"].ap()), 16)
            s_ld.inc(sy.dma_start(out=EXP32[:, :], in_=C["t_exp32"].ap()), 16)
        blk.sync(f_ld2)

        def f_ld3(gp):
            s_gw.inc(gp.dma_start(
                out=Wk_sb.rearrange("p (kc v) -> p kc v", v=512),
                in_=Wk.rearrange("(kc p) v -> p kc v", p=128)), 16)
            s_gw.inc(gp.dma_start(
                out=Wv_sb.rearrange("p (kc v) -> p kc v", v=512),
                in_=Wv.rearrange("(kc p) v -> p kc v", p=128)), 16)
        blk.gpsimd(f_ld3)
        wkv_mark = s_gw.n

        xb_mark, ln_bank = emit_ln(nc, blk, C, sb, Xm, XB, [pb[0], pb[1]], sc)

        pe_m = {}
        act_m = {}
        dve_m = {}

        def f_v_init(v):
            s_dve.wait(v, xb_mark)
            ins = v.memset(
                VT.rearrange("p (tc g e) -> p tc g e", g=8, e=65)[:, :, :, 64:65], 1.0)
            dve_m["vt_ones"] = s_dve.inc(ins)
        blk.vector(f_v_init)

        for tc in range(8):
            def f_pe_v(t, tc=tc):
                if tc == 0:
                    s_dve.wait(t, ln_bank)
                    t.wait_ge(s_gw.h, wkv_mark)
                    s_dve.wait(t, xb_mark)
                else:
                    s_dve.wait(t, dve_m[("vte", tc - 1)])
                for k in range(KC):
                    ins = t.matmul(pb[2 + tc % 2][:, :],
                                   XB[:, k * TOK + tc * 128:k * TOK + (tc + 1) * 128],
                                   Wv_sb[:, k * 512:(k + 1) * 512],
                                   start=(k == 0), stop=(k == KC - 1))
                pe_m[("v", tc)] = s_pe.inc(ins)
            blk.tensor(f_pe_v)

            def f_dve_v(v, tc=tc):
                s_pe.wait(v, pe_m[("v", tc)])
                if tc == 0:
                    pass
                dst = VT[:, tc * 520:(tc + 1) * 520] \
                    .rearrange("p (g e) -> p g e", e=65)[:, :, 0:64]
                src = pb[2 + tc % 2][:, :].rearrange("p (g e) -> p g e", e=64)
                ins = v.tensor_copy(dst, src)
                dve_m[("vte", tc)] = s_dve.inc(ins)
            blk.vector(f_dve_v)

        def f_vout(sy):
            s_dve.wait(sy, dve_m[("vte", 7)])
            s_dve.wait(sy, dve_m["vt_ones"])
            s_kv2.inc(sy.dma_start(
                out=vag_i.rearrange("(tc p) ge -> p tc ge", p=128),
                in_=VT.rearrange("p (tc ge) -> p tc ge", ge=520)), 16)
        blk.sync(f_vout)
        vout_mark = s_kv2.n

        def f_k_init(v):
            ins = v.memset(KT[64:65, :].bitcast(BF16), 1.0)
            dve_m["kt_ones"] = s_dve.inc(ins)
        blk.vector(f_k_init)

        for kc2 in range(4):
            def f_pe_k(t, kc2=kc2):
                if kc2 > 0:
                    s_act.wait(t, act_m[("kqp", kc2 - 1)])
                for th in range(2):
                    for k in range(KC):
                        ins = t.matmul(pb[4 + th][:, :],
                                       Wk_sb[:, k * 512 + kc2 * 128:k * 512 + kc2 * 128 + 128],
                                       XB[:, k * TOK + th * 512:k * TOK + (th + 1) * 512],
                                       start=(k == 0), stop=(k == KC - 1))
                    pe_m[("k", kc2, th)] = s_pe.inc(ins)
            blk.tensor(f_pe_k)

            def f_act_k(a, kc2=kc2):
                for th in range(2):
                    s_pe.wait(a, pe_m[("k", kc2, th)])
                    if kc2 > 0:
                        s_dve.wait(a, dve_m[("krope", kc2 - 1)])
                        s_act.wait(a, act_m[("ksq", kc2 - 1)])
                    ins = a.activation(QP[:, th * 512:(th + 1) * 512],
                                       pb[4 + th][:, :], AF.Copy)
                act_m[("kqp", kc2)] = s_act.inc(ins)
            blk.scalar(f_act_k)

            def f_dve_k(v, kc2=kc2):
                s_act.wait(v, act_m[("kqp", kc2)])
                t1 = sc["TMP"][0:32, :]
                t2 = SCR2[0:32, 0:1024]
                for e in range(2):
                    gidx = 2 * kc2 + e
                    qa = QP[64 * e:64 * e + 32, :]
                    qb = QP[64 * e + 32:64 * e + 64, :]
                    cs_c = CS[64 * e:64 * e + 32, 0:1024]
                    cs_s = CS[64 * e + 32:64 * e + 64, 0:1024]
                    cs_s2 = CS[64 * e:64 * e + 32, 1024:2048]
                    cs_c2 = CS[64 * e + 32:64 * e + 64, 1024:2048]
                    kt_a = KT[0:32, gidx * TOK:(gidx + 1) * TOK]
                    kt_b = KT[32:64, gidx * TOK:(gidx + 1) * TOK]
                    v.tensor_tensor(t1, qa, cs_c, ALU.mult)
                    v.tensor_tensor(t2, qb, cs_s, ALU.mult)
                    v.drain()
                    v.tensor_tensor(kt_a, t1, t2, ALU.subtract)
                    v.tensor_tensor(t1, qb, cs_c2, ALU.mult)
                    v.tensor_tensor(t2, qa, cs_s2, ALU.mult)
                    v.drain()
                    ins = v.tensor_tensor(kt_b, t1, t2, ALU.add)
                dve_m[("krope", kc2)] = s_dve.inc(ins)
            blk.vector(f_dve_k)

            def f_act_k2(a, kc2=kc2):
                s_act.wait(a, act_m[("kqp", kc2)])
                if kc2 > 0:
                    s_pe.wait(a, pe_m[("knrm", kc2 - 1, 1, 1)])
                ins = a.activation(SQ0[:, :], QP[:, :], AF.Square)
                act_m[("ksq", kc2)] = s_act.inc(ins)
            blk.scalar(f_act_k2)

            for e in range(2):
                def f_pe_k2(t, kc2=kc2, e=e):
                    s_act.wait(t, act_m[("ksq", kc2)])
                    if (kc2, e) != (0, 0):
                        prev = (kc2, e - 1) if e == 1 else (kc2 - 1, 1)
                        s_dve.wait(t, dve_m[("knc", prev[0], prev[1])])
                    ins = t.matmul(pb[6][0:1, :], SELK[:, e:e + 1],
                                   SQ0[:, 0:512], start=True, stop=True)
                    pe_m[("knrm", kc2, e, 0)] = s_pe.inc(ins)
                    ins = t.matmul(pb[7][0:1, :], SELK[:, e:e + 1],
                                   SQ0[:, 512:1024], start=True, stop=True)
                    pe_m[("knrm", kc2, e, 1)] = s_pe.inc(ins)
                blk.tensor(f_pe_k2)

                def f_dve_k2(v, kc2=kc2, e=e):
                    gidx = 2 * kc2 + e
                    s_pe.wait(v, pe_m[("knrm", kc2, e, 0)])
                    v.tensor_copy(KRW[:, 0:512], pb[6][0:1, :])
                    s_pe.wait(v, pe_m[("knrm", kc2, e, 1)])
                    v.tensor_copy(KRW[:, 512:1024], pb[7][0:1, :])
                    v.drain()
                    ins = v.reduce_max(NRMX[:, gidx:gidx + 1], KRW[:, :], axis=AX.X)
                    dve_m[("knc", kc2, e)] = s_dve.inc(ins)
                blk.vector(f_dve_k2)

        def f_kout(sy):
            s_dve.wait(sy, dve_m[("krope", 3)])
            s_dve.wait(sy, dve_m["kt_ones"])
            s_kv2.inc(sy.dma_start(
                out=kag_i[0:520, :].rearrange("(g dd) t -> dd g t", dd=65),
                in_=KT.rearrange("dd (g t) -> dd g t", t=TOK)), 16)
        blk.sync(f_kout)
        kout_mark = s_kv2.n

        def f_nout(gp):
            s_dve.wait(gp, dve_m[("knc", 3, 1)])
            s_gw.inc(gp.dma_start(out=kag_i[520:521, 0:8], in_=NRMX[:, :]), 16)
        blk.gpsimd(f_nout)
        nout_mark = s_gw.n

        dbg_mark = [0]
        dbg_m2 = [0]
        if debug:
            def f_dbg1(sy):
                s_dve.wait(sy, dve_m[("krope", 3)])
                s_dve.wait(sy, dve_m[("vte", 7)])
                s_out.inc(sy.dma_start(out=dbg_xb.rearrange("(kc p) t -> p kc t", p=128),
                                       in_=XB.rearrange("p (kc t) -> p kc t", t=TOK)), 16)
                s_out.inc(sy.dma_start(out=dbg_kt, in_=KT), 16)
                dbg_mark[0] = s_out.inc(sy.dma_start(out=dbg_vt, in_=VT), 16)
            blk.sync(f_dbg1)

        def f_ag(gp):
            gp.wait_ge(s_kv2.h, kout_mark)
            gp.wait_ge(s_gw.h, nout_mark)
            ins = gp.collective_compute(
                "AllGather", ALU.bypass, replica_groups=[list(range(8))],
                ins=[kag_i], outs=[kag_o])
            s_agk.inc(ins, 1)
            ins = gp.collective_compute(
                "AllGather", ALU.bypass, replica_groups=[list(range(8))],
                ins=[vag_i], outs=[vag_o])
            s_agv.inc(ins, 1)
        blk.gpsimd(f_ag)

        for qc in range(16):
            par = qc % 2

            def f_gw_q(gp, qc=qc, par=par):
                if qc >= 2:
                    s_pe.wait(gp, pe_m[("q", qc - 2, 1)])
                    if debug and qc == 2:
                        gp.wait_ge(s_out.h, dbg_m2[0])
                src = Wq[:, qc * 128:(qc + 1) * 128].rearrange(
                    "(kc p) o -> p kc o", p=128)
                dst = Wqs[par].rearrange("p (kc o) -> p kc o", o=128)
                pe_m[("qw", qc)] = s_wq2[par].inc(gp.dma_start(out=dst, in_=src), 16)
            blk.gpsimd(f_gw_q)

            def f_pe_q(t, qc=qc, par=par):
                t.wait_ge(s_wq2[par].h, pe_m[("qw", qc)])
                if qc == 0:
                    s_dve.wait(t, dve_m[("vte", 7)])
                else:
                    s_act.wait(t, act_m[("qqp", qc - 1)])
                for th in range(2):
                    for k in range(KC):
                        ins = t.matmul(pb[2 + th][:, :],
                                       Wqs[par][:, k * 128:(k + 1) * 128],
                                       XB[:, k * TOK + th * 512:k * TOK + (th + 1) * 512],
                                       start=(k == 0), stop=(k == KC - 1))
                    pe_m[("q", qc, th)] = s_pe.inc(ins)
            blk.tensor(f_pe_q)

            def f_act_q(a, qc=qc):
                for th in range(2):
                    s_pe.wait(a, pe_m[("q", qc, th)])
                    if qc > 0:
                        s_dve.wait(a, dve_m[("qrope", qc - 1)])
                        s_act.wait(a, act_m[("qsq", qc - 1)])
                    ins = a.activation(QP[:, th * 512:(th + 1) * 512],
                                       pb[2 + th][:, :], AF.Copy, scale=0.125)
                act_m[("qqp", qc)] = s_act.inc(ins)
            blk.scalar(f_act_q)

            if debug and qc == 0:
                def f_dbg_qp(sy):
                    s_act.wait(sy, act_m[("qqp", 0)])
                    s_out.inc(sy.dma_start(out=dbg_wq, in_=Wqs[0][:, :]), 16)
                    dbg_m2[0] = s_out.inc(sy.dma_start(out=dbg_qp, in_=QP[:, :]), 16)
                blk.sync(f_dbg_qp)

            def f_dve_q(v, qc=qc):
                if debug and qc == 0:
                    v.wait_ge(s_out.h, dbg_m2[0])
                s_act.wait(v, act_m[("qqp", qc)])
                t1 = sc["TMP"][0:32, :]
                t2 = SCR2[0:32, 0:1024]
                for e in range(2):
                    h = 2 * qc + e
                    qa = QP[64 * e:64 * e + 32, :]
                    qb = QP[64 * e + 32:64 * e + 64, :]
                    cs_c = CS[64 * e:64 * e + 32, 0:1024]
                    cs_s = CS[64 * e + 32:64 * e + 64, 0:1024]
                    cs_s2 = CS[64 * e:64 * e + 32, 1024:2048]
                    cs_c2 = CS[64 * e + 32:64 * e + 64, 1024:2048]
                    qt_a = QR[0:32, h * TOK:(h + 1) * TOK]
                    qt_b = QR[32:64, h * TOK:(h + 1) * TOK]
                    v.tensor_tensor(t1, qa, cs_c, ALU.mult)
                    v.tensor_tensor(t2, qb, cs_s, ALU.mult)
                    v.drain()
                    v.tensor_tensor(qt_a, t1, t2, ALU.subtract)
                    v.tensor_tensor(t1, qb, cs_c2, ALU.mult)
                    v.tensor_tensor(t2, qa, cs_s2, ALU.mult)
                    v.drain()
                    ins = v.tensor_tensor(qt_b, t1, t2, ALU.add)
                dve_m[("qrope", qc)] = s_dve.inc(ins)
            blk.vector(f_dve_q)

            def f_act_q2(a, qc=qc):
                s_act.wait(a, act_m[("qqp", qc)])
                if qc >= 2:
                    s_pe.wait(a, pe_m[("qn", qc - 2)])
                SQx = SQ0 if qc % 2 == 0 else SQ1
                ins = a.activation(SQx[:, :], QP[:, :], AF.Square)
                act_m[("qsq", qc)] = s_act.inc(ins)
            blk.scalar(f_act_q2)

            def f_pe_q2(t, qc=qc):
                s_act.wait(t, act_m[("qsq", qc)])
                if qc == 0:
                    s_dve.wait(t, dve_m[("knc", 3, 1)])
                SQx = SQ0 if qc % 2 == 0 else SQ1
                for th in range(2):
                    ins = t.matmul(pb[6 + th][0:32, :],
                                   SELK[:, qc * 32:(qc + 1) * 32],
                                   SQx[:, th * 512:(th + 1) * 512],
                                   start=(qc == 0), stop=(qc == KC - 1))
                pe_m[("qn", qc)] = s_pe.inc(ins)
            blk.tensor(f_pe_q2)

        qproj_done = pe_m[("q", 15, 1)]

        def f_dve_qn(v):
            s_pe.wait(v, pe_m[("qn", 15)])
            v.tensor_copy(QN[:, 0:512], pb[6][0:32, :])
            ins = v.tensor_copy(QN[:, 512:1024], pb[7][0:32, :])
            dve_m["qn"] = s_dve.inc(ins)
        blk.vector(f_dve_qn)

        def f_mkld(sy):
            s_dve.wait(sy, dve_m[("qrope", 15)])
            s_mk2.inc(sy.dma_start(out=MK[:, :].rearrange("p (r q) -> p r q", q=512),
                                   in_=mk_in.rearrange("r p q -> p r q")), 16)
        blk.sync(f_mkld)
        mk_mark = s_mk2.n

        def f_km1(gp):
            gp.wait_ge(s_agk.h, 1)
            src = kag_o[:, 0:8].rearrange("(r q) h -> r q h", q=521)[:, 520, :]
            s_gw.inc(gp.dma_start(out=K8[:, :], in_=src), 16)
        blk.gpsimd(f_km1)
        km1_mark = s_gw.n

        def f_pe_km(t):
            t.wait_ge(s_gw.h, km1_mark)
            ins = t.transpose(pb[4][0:8, 0:8], K8[:, :], sb["idn_f"][0:8, 0:8])
            pe_m["k8t"] = s_pe.inc(ins)
        blk.tensor(f_pe_km)

        def f_dve_km(v):
            s_pe.wait(v, pe_m["k8t"])
            v.tensor_copy(K8[:, :], pb[4][0:8, 0:8])
            v.drain()
            ins = v.reduce_max(KV8[:, :], K8[:, :], axis=AX.X)
            dve_m["kv8"] = s_dve.inc(ins)
        blk.vector(f_dve_km)

        def f_pe_km2(t):
            s_dve.wait(t, dve_m["kv8"])
            ins = t.matmul(pb[5][0:32, 0:1], EXP32[:, :], KV8[:, :],
                           start=True, stop=True)
            pe_m["kv32"] = s_pe.inc(ins)
        blk.tensor(f_pe_km2)

        def f_dve_km2(v):
            s_pe.wait(v, pe_m["kv32"])
            v.tensor_copy(KV32[:, :], pb[5][0:32, 0:1])
            s_dve.wait(v, dve_m["qn"])
            s_pe.wait(v, pe_m[("qn", 15)])
            v.drain()
            ins = v.tensor_scalar_mul(M32[:, :], QN[:, :], KV32[:, :])
            dve_m["m32a"] = s_dve.inc(ins)
        blk.vector(f_dve_km2)

        def f_act_km(a):
            s_dve.wait(a, dve_m["m32a"])
            ins = a.activation(M32[:, :], M32[:, :], AF.Sqrt)
            act_m["m32s"] = s_act.inc(ins)
        blk.scalar(f_act_km)

        def f_dve_km3(v):
            s_act.wait(v, act_m["m32s"])
            v.tensor_scalar_mul(M32[:, :], M32[:, :], -1.0)
            v.drain()
            ins = v.tensor_copy(M32b[:, :], M32[:, :])
            dve_m["m32b"] = s_dve.inc(ins)
        blk.vector(f_dve_km3)

        def f_mub(sy):
            s_dve.wait(sy, dve_m["m32b"])
            s_mub2.inc(sy.dma_start(out=mub_d, in_=M32b[:, :]), 16)
            sy.wait_ge(s_mub2.h, s_mub2.n)
            s_mub2.inc(sy.dma_start(out=QR[64:65, :],
                                    in_=mub_d.rearrange("h t -> (h t)")), 16)
        blk.sync(f_mub)
        mub_mark = s_mub2.n

        kl_mark = {}
        vl_mark = {}

        groups = []
        _ci = 0
        for b in range(2):
            for g in range(8):
                for qt in range(4):
                    blocks = [(a_, r) for a_ in range(qt + 1) for r in range(8)]
                    for j in range(0, len(blocks), 4):
                        grp = blocks[j:j + 4]
                        groups.append(dict(
                            b=b, g=g, qt=qt, par=(b * 8 + g) % 2,
                            blocks=[(a_, r, _ci + jj)
                                    for jj, (a_, r) in enumerate(grp)],
                            qt_first=(j == 0), qt_last=(j + 4 == len(blocks)),
                            masked=(grp[0][0] == qt)))
                        _ci += 4
        ngrp = len(groups)

        def emit_vl(b):
            def f_vl(sy, b=b):
                sy.wait_ge(s_agv.h, 1)
                s_pe.wait(sy, qproj_done)
                if b > 0:
                    s_pe.wait(sy, pe_m[("avl", b - 1)])
                for aa in range(4):
                    src = vag_o.rearrange("(r aa p) e -> p r aa e", p=128, aa=8)[
                        :, :, 4 * b + aa, :]
                    dst = VL.rearrange("p (r aa e) -> p r aa e", r=8, e=520)[
                        :, :, aa, :]
                    vl_mark[b] = s_vl2.inc(sy.dma_start(out=dst, in_=src), 16)
            blk.sync(f_vl)

        def emit_kl(b, g):
            def f_kl(sy, b=b, g=g):
                sy.wait_ge(s_agk.h, 1)
                bg = b * 8 + g
                if bg < 2:
                    sy.wait_ge(s_mub2.h, mub_mark)
                    if debug:
                        sy.wait_ge(s_out.h, dbg_mark[0])
                if bg >= 2:
                    s_pe.wait(sy, pe_m[("kl_done", bg - 2)])
                src = kag_o.rearrange("(r z) i -> z r i", z=521)[
                    65 * g:65 * g + 65, :, b * 512:(b + 1) * 512]
                dst = KL[(b * 8 + g) % 2].rearrange("dd (r i) -> dd r i", i=512)
                kl_mark[(b, g)] = s_kl2[(b * 8 + g) % 2].inc(
                    sy.dma_start(out=dst, in_=src), 16)
            blk.sync(f_kl)

        def emit_S(k):
            G = groups[k]
            def f_pe_s(t, G=G, k=k):
                if k > 0:
                    s_act.wait(t, act_m[("eg", k - 1)])
                t.wait_ge(s_kl2[G["par"]].h, kl_mark[(G["b"], G["g"])])
                for (a_, r, ci) in G["blocks"]:
                    if ci == 0:
                        t.wait_ge(s_mub2.h, mub_mark)
                    rhs = QR[0:65, :].rearrange("dd (h i) -> dd h i", i=TOK)[
                        :, 4 * G["g"]:4 * G["g"] + 4,
                        G["b"] * 512 + G["qt"] * 128:
                        G["b"] * 512 + (G["qt"] + 1) * 128]
                    ins = t.matmul(pb[4 + ci % 4][:, :],
                                   KL[G["par"]][:, r * 512 + a_ * 128:
                                                r * 512 + (a_ + 1) * 128],
                                   rhs, start=True, stop=True)
                pe_m[("sg", k)] = s_pe.inc(ins)
            blk.tensor(f_pe_s)

        def emit_E(k):
            G = groups[k]
            def f_act_e(a, G=G, k=k):
                s_pe.wait(a, pe_m[("sg", k)])
                if k >= 2:
                    s_pe.wait(a, pe_m[("ag", k - 2)])
                for (a_, r, ci) in G["blocks"]:
                    ins = a.activation(PT[ci % 8][:, :], pb[4 + ci % 4][:, :],
                                       AF.Exp)
                act_m[("eg", k)] = s_act.inc(ins)
            blk.scalar(f_act_e)
            if G["masked"]:
                def f_dve_mk(v, G=G, k=k):
                    v.wait_ge(s_mk2.h, mk_mark)
                    s_act.wait(v, act_m[("eg", k)])
                    for (a_, r, ci) in G["blocks"]:
                        ins = v.tensor_tensor(
                            PT[ci % 8][:, :], PT[ci % 8][:, :],
                            MK[:, r * 512:(r + 1) * 512], ALU.mult)
                    dve_m[("mg", k)] = s_dve.inc(ins)
                blk.vector(f_dve_mk)

        def emit_A(k):
            G = groups[k]
            qn = (G["b"] * 8 + G["g"]) * 4 + G["qt"]
            def f_pe_av(t, G=G, k=k, qn=qn):
                if G["masked"]:
                    s_dve.wait(t, dve_m[("mg", k)])
                else:
                    s_act.wait(t, act_m[("eg", k)])
                t.wait_ge(s_vl2.h, vl_mark[G["b"]])
                if G["qt_first"] and qn >= 1:
                    s_dve.wait(t, dve_m[("zi", qn - 1)])
                for idx, (a_, r, ci) in enumerate(G["blocks"]):
                    ins = t.matmul(pb[2][0:65, :],
                                   VL[:, (r * 4 + a_) * 520 + 65 * G["g"]:
                                      (r * 4 + a_) * 520 + 65 * G["g"] + 65],
                                   PT[ci % 8][:, :],
                                   start=(G["qt_first"] and idx == 0),
                                   stop=(G["qt_last"] and idx == 3))
                pe_m[("ag", k)] = s_pe.inc(ins)
            blk.tensor(f_pe_av)
            if G["qt_last"]:
                def f_dve_ev(v, qn=qn, k=k):
                    s_pe.wait(v, pe_m[("ag", k)])
                    v.tensor_copy(ctmp[:, :], pb[2][0:64, :])
                    ins = v.tensor_copy(zr[:, :], pb[2][64:65, :])
                    dve_m[("zi", qn)] = s_dve.inc(ins)
                    v.drain()
                    ins = v.reciprocal(zr2[:, :], zr[:, :])
                    dve_m[("zrr", qn)] = s_dve.inc(ins)
                blk.vector(f_dve_ev)

        def emit_Z(qn):
            b, g, qt = qn // 32, (qn // 4) % 8, qn % 4
            def f_pe_z(t, qn=qn):
                s_dve.wait(t, dve_m[("zrr", qn)])
                if qn >= 1:
                    s_dve.wait(t, dve_m[("ctxw", qn - 1)])
                ins = t.matmul(pb[3][0:64, :], sb["ones_m"][:, 0:64],
                               zr2[:, :], start=True, stop=True)
                pe_m[("z", qn)] = s_pe.inc(ins)
            blk.tensor(f_pe_z)

            def f_dve_ev2(v, qn=qn, b=b, g=g, qt=qt):
                s_pe.wait(v, pe_m[("z", qn)])
                for hp in range(4):
                    h = 4 * g + hp
                    dst = CTX[64 * (h % 2):64 * (h % 2) + 64,
                              (h // 2) * TOK + b * 512 + qt * 128:
                              (h // 2) * TOK + b * 512 + (qt + 1) * 128]
                    ins = v.tensor_tensor(
                        dst, ctmp[:, hp * 128:(hp + 1) * 128],
                        pb[3][0:64, hp * 128:(hp + 1) * 128], ALU.mult)
                dve_m[("ctxw", qn)] = s_dve.inc(ins)
            blk.vector(f_dve_ev2)

        emit_vl(0)
        cur_bg = None
        for k in range(ngrp + 1):
            if k < ngrp:
                G = groups[k]
                if (G["b"], G["g"]) != cur_bg:
                    cur_bg = (G["b"], G["g"])
                    emit_kl(*cur_bg)
                emit_S(k)
                emit_E(k)
                nxt = groups[k + 1] if k + 1 < ngrp else None
                if nxt is None or (nxt["b"], nxt["g"]) != (G["b"], G["g"]):
                    pe_m[("kl_done", G["b"] * 8 + G["g"])] = pe_m[("sg", k)]
            if k >= 1:
                Gp = groups[k - 1]
                qn_p = (Gp["b"] * 8 + Gp["g"]) * 4 + Gp["qt"]
                if (not Gp["qt_first"]) and k >= 2 and groups[k - 2]["qt_first"] \
                        and qn_p >= 1:
                    emit_Z(qn_p - 1)
                emit_A(k - 1)
                if Gp["qt_last"] and qn_p == 31:
                    pe_m[("avl", 0)] = pe_m[("ag", k - 1)]
                    emit_vl(1)
        pe_m[("avl", 1)] = pe_m[("ag", ngrp - 1)]
        emit_Z(63)

        ctx_all = dve_m[("ctxw", 63)]

        oc_mark = {}
        for hc in range(16):
            par = hc % 2

            def f_gw_o(gp, hc=hc, par=par):
                s_pe.wait(gp, qproj_done)
                if hc >= 2:
                    s_pe.wait(gp, pe_m[("o", hc - 2, 1)])
                src = Wo[:, hc * 128:(hc + 1) * 128].rearrange(
                    "(ec p) o -> p ec o", p=128)
                dst = Wos[par].rearrange("p (ec o) -> p ec o", o=128)
                oc_mark[hc] = s_wo2[par].inc(gp.dma_start(out=dst, in_=src), 16)
            blk.gpsimd(f_gw_o)

            def f_xr(sy, hc=hc):
                if hc >= 1:
                    s_dve.wait(sy, dve_m.get(("yt", hc - 1, 1), 0))
                for th in range(2):
                    src = x_in.rearrange("(kc p) t -> p kc t", p=128)[
                        :, hc, th * 512:(th + 1) * 512]
                    oc_mark[("x", hc, th)] = s_x2[th].inc(
                        sy.dma_start(out=(SQ0 if th == 0 else SQ1)[:, 0:512],
                                     in_=src), 16)
            blk.sync(f_xr)

            for th in range(2):
                def f_pe_o(t, hc=hc, th=th, par=par):
                    t.wait_ge(s_wo2[par].h, oc_mark[hc])
                    if hc == 0 and th == 0:
                        s_dve.wait(t, ctx_all)
                    prev = (hc, th - 1) if th == 1 else (hc - 1, 1)
                    if hc > 0 or th > 0:
                        s_dve.wait(t, dve_m.get(("yt",) + prev, 0))
                    for ec in range(KC):
                        ins = t.matmul(pb[4 + th][:, :],
                                       Wos[par][:, ec * 128:(ec + 1) * 128],
                                       CTX[:, ec * TOK + th * 512:
                                           ec * TOK + (th + 1) * 512],
                                       start=(ec == 0), stop=(ec == KC - 1))
                    pe_m[("o", hc, th)] = s_pe.inc(ins)
                blk.tensor(f_pe_o)

                def f_dve_o(v, hc=hc, th=th):
                    s_pe.wait(v, pe_m[("o", hc, th)])
                    v.wait_ge(s_x2[th].h, oc_mark[("x", hc, th)])
                    if hc > 0:
                        v.wait_ge(s_y2[th].h, oc_mark.get(("y", hc - 1, th), 0))
                    xr = (SQ0 if th == 0 else SQ1)[:, 0:512]
                    yt = (SQ0 if th == 0 else SQ1)[:, 512:1024]
                    ins = v.scalar_tensor_tensor(yt, pb[4 + th][:, :], 1.0, xr,
                                                 ALU.mult, ALU.add)
                    dve_m[("yt", hc, th)] = s_dve.inc(ins)
                blk.vector(f_dve_o)

                def f_yo(sy, hc=hc, th=th):
                    s_dve.wait(sy, dve_m[("yt", hc, th)])
                    yt = (SQ0 if th == 0 else SQ1)[:, 512:1024]
                    dst = y_out.rearrange("(kc p) t -> p kc t", p=128)[
                        :, hc, th * 512:(th + 1) * 512]
                    oc_mark[("y", hc, th)] = s_y2[th].inc(sy.dma_start(out=dst, in_=yt), 16)
                    s_out.inc(sy.nop(), 0) if False else None
                blk.sync(f_yo)

        if debug:
            def f_dbg2(sy):
                sy.wait_ge(s_mub2.h, mub_mark)
                s_out.inc(sy.dma_start(out=dbg_qr, in_=QR[:, :]), 16)
                s_dve.wait(sy, ctx_all)
                s_out.inc(sy.dma_start(out=dbg_ctx, in_=CTX), 16)
            blk.sync(f_dbg2)

        def f_fin(sy):
            s_y2[0].wait(sy)
            s_y2[1].wait(sy)
            s_out.wait(sy)
        blk.sync(f_fin)


def attn0_io(nc, debug=False):
    io = {
        "x_in": nc.dram_tensor("x_fm", [D, TOK], F32, kind="ExternalInput").ap(),
        "y_out": nc.dram_tensor("y_fm", [D, TOK], F32, kind="ExternalOutput").ap(),
        "Wq": nc.dram_tensor("Wq", [D, D], F32, kind="ExternalInput").ap(),
        "Wk": nc.dram_tensor("Wk", [D, 512], F32, kind="ExternalInput").ap(),
        "Wv": nc.dram_tensor("Wv", [D, 512], F32, kind="ExternalInput").ap(),
        "Wo": nc.dram_tensor("Wo", [D, D], F32, kind="ExternalInput").ap(),
        "cs_t": nc.dram_tensor("cs_t", [128, 2048], F32, kind="ExternalInput").ap(),
        "masks": nc.dram_tensor("masks", [8, 128, 512], BF16, kind="ExternalInput").ap(),
    }
    if debug:
        io["dbg_qp"] = nc.dram_tensor("dbg_qp", [128, TOK], F32, kind="ExternalOutput").ap()
        io["dbg_wq"] = nc.dram_tensor("dbg_wq", [128, 2048], BF16, kind="ExternalOutput").ap()
        io["dbg_xb"] = nc.dram_tensor("dbg_xb", [D, TOK], BF16, kind="ExternalOutput").ap()
        io["dbg_kt"] = nc.dram_tensor("dbg_kt", [65, 8 * TOK], BF16, kind="ExternalOutput").ap()
        io["dbg_vt"] = nc.dram_tensor("dbg_vt", [128, 4160], BF16, kind="ExternalOutput").ap()
        io["dbg_qr"] = nc.dram_tensor("dbg_qr", [65, NQH * TOK], BF16, kind="ExternalOutput").ap()
        io["dbg_ctx"] = nc.dram_tensor("dbg_ctx", [128, 16384], BF16, kind="ExternalOutput").ap()
    return io


def build_attn0_program(debug=False):
    nc = bass.Bass("TRN2", target_bir_lowering=False)
    io = attn0_io(nc, debug)
    C = common_setup(nc)
    C["t_selk"] = nc.inline_tensor(_selk_table(), "t_selk")
    C["t_exp32"] = nc.inline_tensor(_exp32_table(), "t_exp32")
    pbk = [nc.alloc_psum_tensor(f"pbk{i}", [128, 512], F32) for i in range(8)]
    with nc.Block() as blk:
        sb = emit_load_consts(nc, blk, C)
        ar = new_arena(nc)
        emit_attn0(nc, blk, C, sb, ar, pbk, io, "_a0", debug=debug)
    return nc



def prep_ffn1_weights(d, c):
    mu = np.ascontiguousarray(np.asarray(d["L1_ffn_mu"]).reshape(D, 2048))
    w = {
        "gamma": np.ascontiguousarray(np.asarray(d["L1_ffn_gamma"]).reshape(KC, 128).T),
        "beta": np.ascontiguousarray(np.asarray(d["L1_ffn_beta"]).reshape(KC, 128).T),
        "mu_f": mu,
        "mu_sl": np.ascontiguousarray(mu[:, 256 * c:256 * (c + 1)]),
        "scale1": np.asarray(d["L1_ffn_scale"]).reshape(1, 1).astype(np.float32),
        "W1_sl": np.ascontiguousarray(np.asarray(d["L1_ffn_W1"])[2 * c:2 * c + 2]),
        "b1_sl": np.ascontiguousarray(
            np.asarray(d["L1_ffn_b1"])[2 * c:2 * c + 2].reshape(2, 4, 128).transpose(2, 0, 1).reshape(128, 8)),
        "W2_sl": np.ascontiguousarray(np.asarray(d["L1_ffn_W2"])[2 * c:2 * c + 2]),
        "b2_sl": np.ascontiguousarray(np.asarray(d["L1_ffn_b2"])[2 * c:2 * c + 2]),
    }
    return w


def emit_ffn1(nc, blk, C, sb, ar, pb, io, sfx, scale=0.5, debug=False):
    x_in = io["x_in"]
    y_out = io["y_out"]
    gam = io["gamma"]
    bet = io["beta"]
    muf = io["mu_f"]
    musl = io["mu_sl"]
    sc1 = io["scale1"]
    W1s = io["W1_sl"]
    b1s = io["b1_sl"]
    W2s = io["W2_sl"]
    b2s = io["b2_sl"]

    xag_i = nc.dram_tensor("xag_i" + sfx, [2 * D, TOK], BF16, kind="Internal").ap()
    xag_o = nc.dram_tensor("xag_o" + sfx, [8 * 2 * D, TOK], BF16, kind="Internal").ap()
    slag_i = nc.dram_tensor("slag_i" + sfx, [512, D], BF16, kind="Internal").ap()
    slag_o = nc.dram_tensor("slag_o" + sfx, [8 * 512, D], BF16, kind="Internal").ap()

    s_xl = [Sem(nc, f"s_xl{i}{sfx}") for i in range(2)]
    s_sl = [Sem(nc, f"s_sl{i}{sfx}") for i in range(2)]
    s_mu = [Sem(nc, f"s_mu{i}{sfx}") for i in range(2)]
    s_x2 = [Sem(nc, f"s_x2{i}{sfx}") for i in range(2)]
    s_y2 = [Sem(nc, f"s_y2{i}{sfx}") for i in range(2)]
    s_so = Sem(nc, "s_so" + sfx)
    s_agx = Sem(nc, "s_agx" + sfx)
    s_agsl = Sem(nc, "s_agsl" + sfx)

    BIGA = ar.alloc([128, KC * TOK], F32)
    Xm = BIGA[:, :]
    BAb = BIGA[:, :].bitcast(BF16)
    XB2 = BAb[:, 0:16384]
    XN = ar.alloc([128, KC * TOK], BF16)
    EXA = ar.alloc([128, 32 * 256], BF16)
    XLT = [ar.alloc([128, 2048], BF16) for i in range(2)]
    MUNS = ar.alloc([128, KC * 256], BF16)
    W1B = BAb[:, 0:8192]
    W2B = BAb[:, 8192:16384]
    B2B = BIGA[:, 8192:10240]
    PALL = BAb[:, 0:16384]
    SLT = [BAb[:, 16384 + i * 2048:16384 + (i + 1) * 2048] for i in range(2)]
    MUC = [BAb[:, 20480 + i * 2048:20480 + (i + 1) * 2048] for i in range(2)]
    ZIB = BIGA[:, 11264 + 1024:11264 + 2048]
    SCR2 = ar.alloc([128, 3072], F32)
    sc = {"MB": SCR2[:, 0:1024], "RB": SCR2[:, 1024:2048], "TMP": SCR2[:, 2048:3072]}
    SQ0 = ar.alloc([128, 1024], F32)
    SQ1 = ar.alloc([128, 1024], F32)
    sc["SQ0"] = SQ0[:, :]
    sc["SQ1"] = SQ1[:, :]
    XNf = XN[:, :].bitcast(F32)
    sc["rows"] = XNf[0:1, 0:6 * TOK]
    r_ss = SCR2[0:1, 1024:2048]
    r_z = SCR2[0:1, 2048:3072]
    cn_row = SCR2[0:1, 0:2048]
    zrow = SCR2[0:1, 2048:3072]
    SMALL = ar.alloc([128, 96], F32)
    sc_gb = SMALL[:, 32:64]
    s_scale = SMALL[0:1, 0:1]
    s_negabs = SMALL[0:1, 1:2]
    s_nb = SMALL[:, 2:3]
    ZC = SMALL[:, 4:6]
    ZBT = SCR2[:, 0:256]
    b2row = SCR2[0:1, 256:2304]
    B1C = SMALL[:, 8:16]
    RNC = SMALL[:, 16:32]
    HFM = ar.alloc([128, 1024], BF16)
    SLIN = ar.alloc([128, KC * 256], BF16)
    STG = ar.alloc([128, 2 * 512], BF16)

    s_ld, s_gw, s_pe = C["s_ld"], C["s_gw"], C["s_pe"]
    s_act, s_dve, s_out = C["s_act"], C["s_dve"], C["s_out"]

    if True:
        emit_load_master(nc, blk, C, Xm, x_in)

        def f_ld2(gp):
            s_gw.inc(gp.dma_start(
                out=MUNS[:, :].rearrange("p (kc e) -> p kc e", e=256),
                in_=musl.rearrange("(kc p) e -> p kc e", p=128)), 16)
            s_gw.inc(gp.dma_start(out=SMALL[0:1, 0:1], in_=sc1), 16)
            s_gw.inc(gp.dma_start(out=B1C[:, :], in_=b1s), 16)
        blk.gpsimd(f_ld2)
        ld2_mark = s_gw.n

        xb_mark, bank1 = emit_ln(nc, blk, C, sb, Xm, None, [pb[0], pb[1]], sc,
                                 XF=Xm)
        sc["gb"] = sc_gb
        xb2_mark, bank2 = emit_ln(nc, blk, C, sb, Xm, XB2, [pb[0], pb[1]], sc,
                                  x_ready=[(C["s_dve"], xb_mark)],
                                  gamma_ap=gam, beta_ap=bet, XF=None)

        pe_m = {}
        act_m = {}
        dve_m = {}

        for k in range(KC):
            def f_act_l2(a, k=k):
                if k == 0:
                    s_dve.wait(a, xb2_mark)
                else:
                    s_pe.wait(a, pe_m[("ss", k - 1)])
                act_m[("sq", k)] = C["s_act"].inc(
                    a.activation(SQ0[:, :], XB2[:, k * TOK:(k + 1) * TOK], AF.Square))
            blk.scalar(f_act_l2)

            def f_pe_l2(t, k=k):
                s_act.wait(t, act_m[("sq", k)])
                if k == 0:
                    s_dve.wait(t, bank2)
                for th in range(2):
                    ins = t.matmul(pb[th][0:1, :], sb["ones_k"][:, :],
                                   SQ0[:, th * 512:(th + 1) * 512],
                                   start=(k == 0), stop=(k == KC - 1))
                pe_m[("ss", k)] = s_pe.inc(ins)
            blk.tensor(f_pe_l2)

        def f_dve_l2(v):
            s_pe.wait(v, pe_m[("ss", KC - 1)])
            v.tensor_copy(r_ss[:, 0:512], pb[0][0:1, :])
            v.tensor_copy(r_ss[:, 512:1024], pb[1][0:1, :])
            v.drain()
            dve_m["ss"] = C["s_dve"].inc(v.engine_nop())
        blk.vector(f_dve_l2)

        def f_act_l2b(a):
            s_dve.wait(a, dve_m["ss"])
            act_m["sd"] = C["s_act"].inc(
                a.activation(r_ss[:, :], r_ss[:, :], AF.Sqrt))
        blk.scalar(f_act_l2b)

        def f_dve_l2b(v):
            s_act.wait(v, act_m["sd"])
            v.wait_ge(s_gw.h, ld2_mark)
            v.tensor_scalar_max(r_ss[:, :], r_ss[:, :], 1e-12)
            v.drain()
            v.reciprocal(r_z[:, :], r_ss[:, :])
            v.drain()
            ins = v.tensor_scalar_mul(r_z[:, :], r_z[:, :], s_scale)
            dve_m["rsc"] = C["s_dve"].inc(ins)
        blk.vector(f_dve_l2b)

        def f_pe_rb(t):
            s_dve.wait(t, dve_m["rsc"])
            for th in range(2):
                ins = t.matmul(pb[2 + th][:, :], sb["ones_m"][:, :],
                               r_z[:, th * 512:(th + 1) * 512], start=True, stop=True)
            pe_m["rb"] = s_pe.inc(ins)
        blk.tensor(f_pe_rb)

        def f_dve_rb(v):
            s_pe.wait(v, pe_m["rb"])
            for th in range(2):
                v.tensor_copy(sc["RB"][:, th * 512:(th + 1) * 512], pb[2 + th][:, :])
            v.drain()
            for k in range(KC):
                ins = v.tensor_tensor(XN[:, k * TOK:(k + 1) * TOK],
                                      XB2[:, k * TOK:(k + 1) * TOK], sc["RB"][:, :],
                                      ALU.mult)
            dve_m["xn"] = C["s_dve"].inc(ins)
        blk.vector(f_dve_rb)

        def f_act_na(a):
            a.wait_ge(s_gw.h, ld2_mark)
            act_m["abs"] = C["s_act"].inc(
                a.activation(s_negabs, s_scale, AF.Abs))
        blk.scalar(f_act_na)

        def f_dve_na(v):
            s_act.wait(v, act_m["abs"])
            ins = v.tensor_scalar_mul(s_negabs, s_negabs, -1.0)
            dve_m["na"] = C["s_dve"].inc(ins)
        blk.vector(f_dve_na)

        def f_pe_na(t):
            s_dve.wait(t, dve_m["na"])
            ins = t.matmul(pb[4][:, 0:1], sb["ones_m"][:, :], s_negabs,
                           start=True, stop=True)
            pe_m["nab"] = s_pe.inc(ins)
        blk.tensor(f_pe_na)

        def f_dve_nab(v):
            s_pe.wait(v, pe_m["nab"])
            ins = v.tensor_copy(s_nb, pb[4][:, 0:1])
            dve_m["nab"] = C["s_dve"].inc(ins)
        blk.vector(f_dve_nab)

        def f_xout(sy):
            s_dve.wait(sy, dve_m["xn"])
            s_ld.inc(sy.dma_start(
                out=xag_i[0:D, :].rearrange("(kc p) t -> p kc t", p=128),
                in_=XB2.rearrange("p (kc t) -> p kc t", t=TOK)), 16)
            s_ld.inc(sy.dma_start(
                out=xag_i[D:2 * D, :].rearrange("(kc p) t -> p kc t", p=128),
                in_=XN[:, :].rearrange("p (kc t) -> p kc t", t=TOK)), 16)
        blk.sync(f_xout)
        xout_mark = s_ld.n

        def f_ag(gp):
            gp.wait_ge(s_ld.h, xout_mark)
            ins = gp.collective_compute(
                "AllGather", ALU.bypass, replica_groups=[list(range(8))],
                ins=[xag_i], outs=[xag_o])
            s_agx.inc(ins, 1)
        blk.gpsimd(f_ag)

        for k in range(KC):
            def f_act_mn(a, k=k):
                a.wait_ge(s_gw.h, ld2_mark)
                if k > 0:
                    s_pe.wait(a, pe_m[("mn", k - 1)])
                s_act.wait(a, act_m.get(("mnp", k - 2), 0))
                act_m[("mnp", k)] = C["s_act"].inc(
                    a.activation(SQ1[:, 0:256], MUNS[:, k * 256:(k + 1) * 256],
                                 AF.Square))
            blk.scalar(f_act_mn)

            def f_pe_mn(t, k=k):
                s_act.wait(t, act_m[("mnp", k)])
                if k == 0:
                    s_dve.wait(t, dve_m["nab"])
                ins = t.matmul(pb[4][0:1, 0:256], sb["ones_k"][:, :],
                               SQ1[:, 0:256], start=(k == 0), stop=(k == KC - 1))
                pe_m[("mn", k)] = s_pe.inc(ins)
            blk.tensor(f_pe_mn)

        def f_dve_mn(v):
            s_pe.wait(v, pe_m[("mn", KC - 1)])
            v.tensor_copy(SQ1[0:1, 0:256], pb[4][0:1, 0:256])
            v.drain()
            v.tensor_scalar_max(SQ1[0:1, 0:256], SQ1[0:1, 0:256], 1e-24)
            dve_m["mnc"] = C["s_dve"].inc(v.drain())
        blk.vector(f_dve_mn)

        def f_act_mn2(a):
            s_dve.wait(a, dve_m["mnc"])
            act_m["mns"] = C["s_act"].inc(
                a.activation(SQ1[0:1, 0:256], SQ1[0:1, 0:256], AF.Sqrt))
        blk.scalar(f_act_mn2)

        def f_dve_mn2(v):
            s_act.wait(v, act_m["mns"])
            ins = v.reciprocal(SQ1[0:1, 0:256], SQ1[0:1, 0:256])
            dve_m["mnr"] = C["s_dve"].inc(ins)
        blk.vector(f_dve_mn2)

        def f_pe_mn2(t):
            s_dve.wait(t, dve_m["mnr"])
            ins = t.matmul(pb[4][:, 0:256], sb["ones_m"][:, :], SQ1[0:1, 0:256],
                           start=True, stop=True)
            pe_m["mnb"] = s_pe.inc(ins)
        blk.tensor(f_pe_mn2)

        def f_dve_mn3(v):
            s_pe.wait(v, pe_m["mnb"])
            v.tensor_copy(SQ1[:, 0:256], pb[4][:, 0:256])
            v.drain()
            for k in range(KC):
                ins = v.tensor_tensor(MUNS[:, k * 256:(k + 1) * 256],
                                      MUNS[:, k * 256:(k + 1) * 256],
                                      SQ1[:, 0:256], ALU.mult)
            dve_m["muns"] = C["s_dve"].inc(ins)
        blk.vector(f_dve_mn3)

        onekb = ar.alloc([128, 1], BF16)

        s_okb = Sem(nc, "s_okb" + sfx)

        def f_okb(sy):
            s_okb.inc(sy.dma_start(out=onekb[:], in_=C["t_onesb"].ap()), 16)
        blk.sync(f_okb)
        okb_mark = s_okb.n

        so_marks = {}
        for bq in range(2):
            for nb in range(32):
                r_, a_ = nb // 4, nb % 4
                par = nb % 2

                def f_xnt(gp, bq=bq, nb=nb, r_=r_, a_=a_, par=par):
                    gp.wait_ge(s_agx.h, 1)
                    if bq == 1 and nb < 2:
                        s_pe.wait(gp, pe_m[("si", 0, 30 + nb)])
                    if nb >= 2:
                        s_pe.wait(gp, pe_m[("lg", bq, nb - 2)])
                    src = xag_o.rearrange("(r z) t -> z r t", z=2 * D)[
                        D + 0:2 * D, r_, bq * 512 + a_ * 128: bq * 512 + (a_ + 1) * 128]
                    src = src.rearrange("(kc p) i -> p kc i", p=128)
                    dst = XLT[par][:, :].rearrange("p (kc i) -> p kc i", i=128)
                    pe_m[("xnt", bq, nb)] = s_xl[par].inc(
                        gp.dma_start(out=dst, in_=src), 16)
                blk.gpsimd(f_xnt)

                def f_pe_lg(t, bq=bq, nb=nb, par=par):
                    t.wait_ge(s_xl[par].h, pe_m[("xnt", bq, nb)])
                    if bq == 0 and nb == 0:
                        s_dve.wait(t, dve_m["muns"])
                        s_dve.wait(t, dve_m["nab"])
                    if bq == 1 and nb in (0, 1):
                        s_dve.wait(t, dve_m[("sov", 0, 1, 3)])
                    if (bq, nb) != (0, 0) and (bq, nb) != (0, 1):
                        pnb = (bq, nb - 2) if nb >= 2 else (bq - 1, 30 + nb)
                        s_act.wait(t, act_m[("exp",) + pnb])
                    for kc in range(KC):
                        ins = t.matmul(pb[6 + par][:, 0:256],
                                       XLT[par][:, kc * 128:(kc + 1) * 128],
                                       MUNS[:, kc * 256:(kc + 1) * 256],
                                       start=(kc == 0), stop=(kc == KC - 1))
                    pe_m[("lg", bq, nb)] = s_pe.inc(ins)
                blk.tensor(f_pe_lg)

                def f_act_exp(a, bq=bq, nb=nb, par=par):
                    s_pe.wait(a, pe_m[("lg", bq, nb)])
                    if bq == 1 and nb == 0:
                        s_pe.wait(a, pe_m[("si", 0, 31)])
                    ins = a.activation(EXA[:, nb * 256:(nb + 1) * 256],
                                       pb[6 + par][:, 0:256], AF.Exp, bias=s_nb)
                    act_m[("exp", bq, nb)] = C["s_act"].inc(ins)
                blk.scalar(f_act_exp)

                def f_pe_z(t, bq=bq, nb=nb):
                    s_act.wait(t, act_m[("exp", bq, nb)])
                    t.wait_ge(s_okb.h, okb_mark)
                    if bq == 1 and nb == 0:
                        s_dve.wait(t, dve_m[("zin", 0)])
                    ins = t.matmul(pb[5][0:1, 0:256], onekb[:, :],
                                   EXA[:, nb * 256:(nb + 1) * 256],
                                   start=(nb == 0), stop=(nb == 31))
                    pe_m[("zz", bq, nb)] = s_pe.inc(ins)
                blk.tensor(f_pe_z)

            def f_dve_zi(v, bq=bq):
                s_pe.wait(v, pe_m[("zz", bq, 31)])
                v.tensor_copy(SQ0[0:1, 0:256], pb[5][0:1, 0:256])
                v.drain()
                ins = v.reciprocal(SQ0[0:1, 0:256], SQ0[0:1, 0:256])
                dve_m[("zin", bq)] = C["s_dve"].inc(ins)
            blk.vector(f_dve_zi)

            def f_pe_zb(t, bq=bq):
                s_dve.wait(t, dve_m[("zin", bq)])
                ins = t.matmul(pb[4][:, 0:256], sb["ones_m"][:, :],
                               SQ0[0:1, 0:256], start=True, stop=True)
                pe_m[("zb", bq)] = s_pe.inc(ins)
            blk.tensor(f_pe_zb)

            def f_dve_zb(v, bq=bq):
                s_pe.wait(v, pe_m[("zb", bq)])
                ins = v.tensor_copy(ZBT[:, :], pb[4][:, 0:256])
                dve_m[("zc", bq)] = C["s_dve"].inc(ins)
            blk.vector(f_dve_zb)

            for nb in range(32):
                r_, a_ = nb // 4, nb % 4
                par = nb % 2

                def f_xlt(sy, bq=bq, nb=nb, r_=r_, a_=a_, par=par):
                    sy.wait_ge(s_agx.h, 1)
                    if nb < 2:
                        s_pe.wait(sy, pe_m[("lg", bq, 30 + nb)])
                    if (bq, nb) >= (0, 2):
                        prev = ("si", bq, nb - 2) if nb >= 2 else ("si", bq - 1, 30 + nb)
                        s_pe.wait(sy, pe_m.get(prev, 0))
                    src = xag_o[r_ * 2 * D: r_ * 2 * D + D,
                                bq * 512 + a_ * 128: bq * 512 + (a_ + 1) * 128]
                    pe_m[("xlt", bq, nb)] = s_xl[par].inc(
                        sy.dma_start(out=XLT[par][:, :], in_=src, transpose=True), 16)
                blk.sync(f_xlt)

                def f_pe_si(t, bq=bq, nb=nb, par=par):
                    t.wait_ge(s_xl[par].h, pe_m[("xlt", bq, nb)])
                    if nb == 0:
                        s_dve.wait(t, dve_m[("zc", bq)])
                        if bq == 1:
                            s_dve.wait(t, dve_m[("sov", 0, 1, 3)])
                    for kc in range(KC):
                        bank = kc // 2
                        half = (kc % 2) * 256
                        ins = t.matmul(pb[bank][:, half:half + 256],
                                       XLT[par][:, kc * 128:(kc + 1) * 128],
                                       EXA[:, nb * 256:(nb + 1) * 256],
                                       start=(nb == 0 and kc % 2 == 0),
                                       stop=(nb == 31))
                    pe_m[("si", bq, nb)] = s_pe.inc(ins)
                blk.tensor(f_pe_si)

            for kc in range(KC):
                def f_dve_sev(v, bq=bq, kc=kc):
                    if kc == 0:
                        s_pe.wait(v, pe_m[("si", bq, 31)])
                    bank = kc // 2
                    half = (kc % 2) * 256
                    ins = v.tensor_tensor(SLIN[:, kc * 256:(kc + 1) * 256],
                                          pb[bank][:, half:half + 256],
                                          ZBT[:, :], ALU.mult)
                    dve_m[("sev", bq, kc)] = C["s_dve"].inc(ins)
                blk.vector(f_dve_sev)

            for e in range(2):
                def f_w1(gp, bq=bq, e=e):
                    if bq == 0 and e == 0:
                        s_dve.wait(gp, dve_m["xn"])
                        gp.wait_ge(s_agx.h, 1)
                    else:
                        s_pe.wait(gp, pe_m.get(("w1", bq, e - 1),
                                               pe_m.get(("w1", bq - 1, 1), 0)))
                    src = W1s[e].rearrange("(kc p) h -> p kc h", p=128)
                    dst = W1B.rearrange("p (kc h) -> p kc h", h=512)
                    pe_m[("w1l", bq, e)] = s_mu[0].inc(gp.dma_start(out=dst, in_=src), 16)
                blk.gpsimd(f_w1)

                for hc in range(4):
                    def f_pe_w1(t, bq=bq, e=e, hc=hc):
                        t.wait_ge(s_mu[0].h, pe_m[("w1l", bq, e)])
                        if hc == 0 and e == 0:
                            s_dve.wait(t, dve_m[("sev", bq, KC - 1)])
                        if (e, hc) != (0, 0):
                            s_act.wait(t, act_m.get(("gel", bq, e, hc - 1),
                                                    act_m.get(("gel", bq, e - 1, 3), 0)))
                        for kc in range(KC):
                            ins = t.matmul(pb[6][:, 0:128],
                                           W1B[:, kc * 512 + hc * 128: kc * 512 + (hc + 1) * 128],
                                           SLIN[:, kc * 256 + e * 128: kc * 256 + e * 128 + 128],
                                           start=(kc == 0), stop=(kc == KC - 1))
                        if hc == 3:
                            pe_m[("w1", bq, e)] = s_pe.inc(ins)
                            pe_m[("w1h", bq, e, hc)] = s_pe.n
                        else:
                            pe_m[("w1h", bq, e, hc)] = s_pe.inc(ins)
                    blk.tensor(f_pe_w1)

                    def f_act_gel(a, bq=bq, e=e, hc=hc):
                        s_pe.wait(a, pe_m[("w1h", bq, e, hc)])
                        ins = a.activation(HFM[:, (e * 4 + hc) * 128:(e * 4 + hc + 1) * 128],
                                           pb[6][:, 0:128], AF.Gelu,
                                           bias=B1C[:, e * 4 + hc:e * 4 + hc + 1])
                        act_m[("gel", bq, e, hc)] = C["s_act"].inc(ins)
                    blk.scalar(f_act_gel)

            for e in range(2):
                def f_w2(gp, bq=bq, e=e):
                    if bq == 0 and e == 0:
                        s_dve.wait(gp, dve_m["xn"])
                        gp.wait_ge(s_agx.h, 1)
                    else:
                        s_pe.wait(gp, pe_m.get(("w2", bq, e - 1, 3),
                                               pe_m.get(("w2", bq - 1, 1, 3), 0)))
                    src = W2s[e].rearrange("(hc p) d2 -> p hc d2", p=128)
                    dst = W2B.rearrange("p (hc d2) -> p hc d2", d2=2048)
                    pe_m[("w2l", bq, e)] = s_mu[1].inc(gp.dma_start(out=dst, in_=src), 16)
                blk.gpsimd(f_w2)

                def f_b2(sy, bq=bq, e=e):
                    if not (bq == 0 and e == 0):
                        s_dve.wait(sy, dve_m.get(("b2b", bq, e - 1, 3),
                                                 dve_m.get(("b2b", bq - 1, 1, 3), 0)))
                    pe_m[("b2r", bq, e)] = s_ld.inc(
                        sy.dma_start(out=b2row, in_=b2s[e:e + 1, :]), 16)
                blk.sync(f_b2)

                for db in range(4):
                    def f_pe_b2(t, bq=bq, e=e, db=db):
                        t.wait_ge(s_ld.h, pe_m[("b2r", bq, e)])
                        if (bq, e, db) != (0, 0, 0):
                            prev = (bq, e, db - 1) if db > 0 else \
                                ((bq, e - 1, 3) if e > 0 else (bq - 1, 1, 3))
                            s_dve.wait(t, dve_m[("b2b",) + prev])
                        ins = t.matmul(pb[7][:, 0:512], sb["ones_m"][:, :],
                                       b2row[0:1, db * 512:(db + 1) * 512],
                                       start=True, stop=True)
                        pe_m[("b2", bq, e, db)] = s_pe.inc(ins)
                    blk.tensor(f_pe_b2)

                    def f_dve_b2(v, bq=bq, e=e, db=db):
                        s_pe.wait(v, pe_m[("b2", bq, e, db)])
                        ins = v.tensor_copy(B2B[:, db * 512:(db + 1) * 512],
                                            pb[7][:, 0:512])
                        dve_m[("b2b", bq, e, db)] = C["s_dve"].inc(ins)
                    blk.vector(f_dve_b2)

                for db in range(4):
                    def f_pe_w2(t, bq=bq, e=e, db=db):
                        t.wait_ge(s_mu[1].h, pe_m[("w2l", bq, e)])
                        s_act.wait(t, act_m[("gel", bq, e, 3)])
                        if (bq, e, db) != (0, 0, 0):
                            prev = (bq, e, db - 1) if db > 0 else \
                                ((bq, e - 1, 3) if e > 0 else (bq - 1, 1, 3))
                            s_dve.wait(t, dve_m.get(("sov",) + prev, 0))
                        for hc in range(4):
                            ins = t.matmul(pb[6][:, 0:512],
                                           HFM[:, (e * 4 + hc) * 128:(e * 4 + hc + 1) * 128],
                                           W2B[:, hc * 2048 + db * 512: hc * 2048 + (db + 1) * 512],
                                           start=(hc == 0), stop=(hc == 3))
                        pe_m[("w2", bq, e, db)] = s_pe.inc(ins)
                    blk.tensor(f_pe_w2)

                    def f_dve_sov(v, bq=bq, e=e, db=db):
                        s_pe.wait(v, pe_m[("w2", bq, e, db)])
                        s_dve.wait(v, dve_m[("b2b", bq, e, 3)])
                        if not (bq == 0 and e == 0 and db == 0):
                            v.wait_ge(s_so.h, so_marks.get((bq, e, db), 0))
                        ins = v.tensor_tensor(STG[:, 0:512], pb[6][:, 0:512],
                                              B2B[:, db * 512:(db + 1) * 512], ALU.add)
                        dve_m[("sov", bq, e, db)] = C["s_dve"].inc(ins)
                    blk.vector(f_dve_sov)

                    def f_so(sy, bq=bq, e=e, db=db):
                        s_dve.wait(sy, dve_m[("sov", bq, e, db)])
                        dst = slag_i[bq * 256 + e * 128: bq * 256 + (e + 1) * 128,
                                     db * 512:(db + 1) * 512]
                        mk = s_so.inc(sy.dma_start(out=dst, in_=STG[:, 0:512]), 16)
                        nxt = (bq, e, db + 1) if db < 3 else \
                            ((bq, e + 1, 0) if e < 1 else (bq + 1, 0, 0))
                        so_marks[nxt] = mk
                    blk.sync(f_so)

        def f_ag2(gp):
            gp.wait_ge(s_so.h, s_so.n)
            ins = gp.collective_compute(
                "AllGather", ALU.bypass, replica_groups=[list(range(8))],
                ins=[slag_i], outs=[slag_o])
            s_agsl.inc(ins, 1)
        blk.gpsimd(f_ag2)

        for kc in range(KC):
            par = kc % 2

            def f_mur(gp, kc=kc, par=par):
                if kc == 0:
                    s_pe.wait(gp, pe_m[("w2", 1, 1, 3)])
                if kc >= 2:
                    s_pe.wait(gp, pe_m[("mq", kc - 2)])
                src = muf[kc * 128:(kc + 1) * 128, :]
                pe_m[("mul", kc)] = s_mu[par].inc(
                    gp.dma_start(out=MUC[par][:, :], in_=src), 16)
            blk.gpsimd(f_mur)

            def f_act_mq(a, kc=kc, par=par):
                a.wait_ge(s_mu[par].h, pe_m[("mul", kc)])
                if kc >= 1:
                    s_pe.wait(a, pe_m[("mq", kc - 1)])
                a.activation(SQ0[:, :], MUC[par][:, 0:1024], AF.Square)
                ins = a.activation(SQ1[:, :], MUC[par][:, 1024:2048], AF.Square)
                act_m[("mq", kc)] = C["s_act"].inc(ins)
            blk.scalar(f_act_mq)

            def f_pe_mq(t, kc=kc):
                s_act.wait(t, act_m[("mq", kc)])
                if kc == 0:
                    s_dve.wait(t, dve_m[("sov", 1, 1, 3)])
                for q4 in range(4):
                    srcq = (SQ0 if q4 < 2 else SQ1)[:, (q4 % 2) * 512:(q4 % 2 + 1) * 512]
                    ins = t.matmul(pb[q4][0:1, :], sb["ones_k"][:, :], srcq,
                                   start=(kc == 0), stop=(kc == KC - 1))
                pe_m[("mq", kc)] = s_pe.inc(ins)
            blk.tensor(f_pe_mq)

        def f_dve_cn(v):
            s_pe.wait(v, pe_m[("mq", KC - 1)])
            for q4 in range(4):
                v.tensor_copy(cn_row[:, q4 * 512:(q4 + 1) * 512], pb[q4][0:1, :])
            v.drain()
            v.tensor_scalar_max(cn_row[:, :], cn_row[:, :], 1e-24)
            dve_m["cn"] = C["s_dve"].inc(v.drain())
        blk.vector(f_dve_cn)

        def f_act_cn(a):
            s_dve.wait(a, dve_m["cn"])
            ins = a.activation(cn_row[:, :], cn_row[:, :], AF.Sqrt)
            act_m["cns"] = C["s_act"].inc(ins)
        blk.scalar(f_act_cn)

        def f_dve_cn2(v):
            s_act.wait(v, act_m["cns"])
            ins = v.reciprocal(cn_row[:, :], cn_row[:, :])
            dve_m["cnr"] = C["s_dve"].inc(ins)
        blk.vector(f_dve_cn2)

        for ec2 in range(2):
            def f_pe_rnc(t, ec2=ec2):
                s_dve.wait(t, dve_m["cnr"])
                if ec2 == 1:
                    s_dve.wait(t, dve_m[("rncc", 0)])
                for j in range(8):
                    ins = t.transpose(pb[4][0:128, j * 4:j * 4 + 1],
                                      cn_row[:, (ec2 * 8 + j) * 128:(ec2 * 8 + j + 1) * 128],
                                      sb["idn_f"][0:1, 0:1])
                pe_m[("rnc", ec2)] = s_pe.inc(ins)
            blk.tensor(f_pe_rnc)

            def f_dve_rnc(v, ec2=ec2):
                s_pe.wait(v, pe_m[("rnc", ec2)])
                for j in range(8):
                    ins = v.tensor_copy(RNC[:, ec2 * 8 + j:ec2 * 8 + j + 1],
                                        pb[4][0:128, j * 4:j * 4 + 1])
                dve_m[("rncc", ec2)] = C["s_dve"].inc(ins)
            blk.vector(f_dve_rnc)

        for ec in range(KC):
            par = ec % 2

            def f_muc(gp, ec=ec, par=par):
                s_pe.wait(gp, pe_m[("mq", KC - 1)])
                if ec >= 2:
                    s_pe.wait(gp, pe_m[("cs", ec - 2, 1)])
                src = muf[:, ec * 128:(ec + 1) * 128].rearrange(
                    "(kc p) e -> p kc e", p=128)
                dst = MUC[par][:, :].rearrange("p (kc e) -> p kc e", e=128)
                pe_m[("mucl", ec)] = s_mu[par].inc(gp.dma_start(out=dst, in_=src), 16)
            blk.gpsimd(f_muc)

            for th in range(2):
                def f_pe_cs(t, ec=ec, th=th, par=par):
                    t.wait_ge(s_mu[par].h, pe_m[("mucl", ec)])
                    s_dve.wait(t, dve_m[("rncc", 1)])
                    if (ec, th) != (0, 0):
                        prev = (ec, th - 1) if th == 1 else (ec - 1, 1)
                        s_act.wait(t, act_m.get(("csx",) + prev, 0))
                    for kc in range(KC):
                        ins = t.matmul(pb[5][:, :],
                                       MUC[par][:, kc * 128:(kc + 1) * 128],
                                       XN[:, kc * TOK + th * 512: kc * TOK + (th + 1) * 512],
                                       start=(kc == 0), stop=(kc == KC - 1))
                    pe_m[("cs", ec, th)] = s_pe.inc(ins)
                blk.tensor(f_pe_cs)

                def f_act_cs(a, ec=ec, th=th):
                    s_pe.wait(a, pe_m[("cs", ec, th)])
                    ins = a.activation(PALL[:, ec * TOK + th * 512: ec * TOK + (th + 1) * 512],
                                       pb[5][:, :], AF.Exp,
                                       bias=s_nb, scale=RNC[:, ec:ec + 1])
                    act_m[("csx", ec, th)] = C["s_act"].inc(ins)
                blk.scalar(f_act_cs)

                def f_pe_cz(t, ec=ec, th=th):
                    s_act.wait(t, act_m[("csx", ec, th)])
                    t.wait_ge(s_okb.h, okb_mark)
                    ins = t.matmul(pb[6 + th][0:1, :],
                                   onekb[:, :],
                                   PALL[:, ec * TOK + th * 512: ec * TOK + (th + 1) * 512],
                                   start=(ec == 0), stop=(ec == KC - 1))
                    pe_m[("cz", ec, th)] = s_pe.inc(ins)
                blk.tensor(f_pe_cz)

        def f_dve_cz(v):
            s_pe.wait(v, pe_m[("cz", KC - 1, 0)])
            s_pe.wait(v, pe_m[("cz", KC - 1, 1)])
            v.tensor_copy(zrow[:, 0:512], pb[6][0:1, :])
            v.tensor_copy(zrow[:, 512:1024], pb[7][0:1, :])
            v.drain()
            v.reciprocal(zrow[:, :], zrow[:, :])
            v.drain()
            ins = v.tensor_scalar_mul(zrow[:, :], zrow[:, :], 0.5)
            dve_m["cz"] = C["s_dve"].inc(ins)
        blk.vector(f_dve_cz)

        def f_pe_zib(t):
            s_dve.wait(t, dve_m["cz"])
            for th in range(2):
                ins = t.matmul(pb[6 + th][:, :], sb["ones_m"][:, :],
                               zrow[0:1, th * 512:(th + 1) * 512], start=True, stop=True)
            pe_m["zib"] = s_pe.inc(ins)
        blk.tensor(f_pe_zib)

        def f_dve_zib(v):
            s_pe.wait(v, pe_m["zib"])
            for th in range(2):
                v.tensor_copy(ZIB[:, th * 512:(th + 1) * 512], pb[6 + th][:, :])
            v.drain()
            for ec in range(KC):
                ins = v.tensor_tensor(PALL[:, ec * TOK:(ec + 1) * TOK],
                                      PALL[:, ec * TOK:(ec + 1) * TOK],
                                      ZIB[:, :], ALU.mult)
            dve_m["pall"] = C["s_dve"].inc(ins)
        blk.vector(f_dve_zib)

        if debug:
            def f_dbgf(sy):
                s_dve.wait(sy, dve_m["xn"])
                C["s_out"].inc(sy.dma_start(
                    out=io["dbg_xn"].rearrange("(kc p) t -> p kc t", p=128),
                    in_=XN[:, :].rearrange("p (kc t) -> p kc t", t=TOK)), 16)
                s_dve.wait(sy, dve_m["muns"])
                C["s_out"].inc(sy.dma_start(out=io["dbg_muns"], in_=MUNS[:, :]), 16)
                s_dve.wait(sy, dve_m["pall"])
                C["s_out"].inc(sy.dma_start(
                    out=io["dbg_pall"].rearrange("(ec p) t -> p ec t", p=128),
                    in_=PALL.rearrange("p (ec t) -> p ec t", t=TOK)), 16)
                C["s_out"].inc(sy.dma_start(out=io["dbg_cn"], in_=cn_row), 16)
                sy.wait_ge(s_agsl.h, 1)
                C["s_out"].inc(sy.dma_start(out=io["dbg_slag"], in_=slag_o), 16)
            blk.sync(f_dbgf)

        oc_mark = {}
        for hc in range(KC):
            for th in range(2):
                par = (hc * 2 + th) % 2

                def f_slt(sy, hc=hc, th=th, par=par):
                    sy.wait_ge(s_agsl.h, 1)
                    if hc * 2 + th >= 2:
                        s_pe.wait(sy, pe_m.get(("av", hc - 1, th), 0))

                    srcz = slag_o.rearrange("(ec2 z) d2 -> z ec2 d2", z=512)[
                        256 * th:256 * th + 256, :, hc * 128:(hc + 1) * 128]
                    srcz = srcz.rearrange("(ecl p) ec2 d2 -> p ecl ec2 d2", p=128)
                    dstz = SLT[par].rearrange("p (ec2 ecl d2) -> p ec2 ecl d2",
                                              ec2=8, d2=128)
                    mk = 0
                    for ecl in range(2):
                        mk = s_sl[par].inc(
                            sy.dma_start(out=dstz[:, :, ecl, :],
                                         in_=srcz[:, ecl, :, :]), 16)
                    oc_mark[("slt", hc, th)] = mk
                blk.sync(f_slt)

                def f_xr(sy, hc=hc, th=th):
                    if hc == 0:
                        s_pe.wait(sy, pe_m[("mq", KC - 1)])
                    if hc >= 1:
                        s_dve.wait(sy, dve_m.get(("yt", hc - 1, th), 0))
                    src = x_in.rearrange("(kc p) t -> p kc t", p=128)[
                        :, hc, th * 512:(th + 1) * 512]
                    oc_mark[("x", hc, th)] = s_x2[th].inc(
                        sy.dma_start(out=(SQ0 if th == 0 else SQ1)[:, 0:512],
                                     in_=src), 16)
                blk.sync(f_xr)

                def f_pe_av(t, hc=hc, th=th, par=par):
                    t.wait_ge(s_sl[par].h, oc_mark[("slt", hc, th)])
                    s_dve.wait(t, dve_m["pall"])
                    if hc * 2 + th >= 2:
                        s_dve.wait(t, dve_m.get(("yt", hc - 1, th), 0))
                    for ec in range(KC):
                        ins = t.matmul(pb[4 + th][:, :],
                                       SLT[par][:, ec * 128:(ec + 1) * 128],
                                       PALL[:, ec * TOK + th * 512: ec * TOK + (th + 1) * 512],
                                       start=(ec == 0), stop=(ec == KC - 1))
                    pe_m[("av", hc, th)] = s_pe.inc(ins)
                blk.tensor(f_pe_av)

                def f_dve_av(v, hc=hc, th=th):
                    s_pe.wait(v, pe_m[("av", hc, th)])
                    v.wait_ge(s_x2[th].h, oc_mark[("x", hc, th)])
                    if hc > 0:
                        v.wait_ge(s_y2[th].h, oc_mark.get(("y", hc - 1, th), 0))
                    xr = (SQ0 if th == 0 else SQ1)[:, 0:512]
                    yt = (SQ0 if th == 0 else SQ1)[:, 512:1024]
                    ins = v.scalar_tensor_tensor(yt, pb[4 + th][:, :], 1.0, xr,
                                                 ALU.mult, ALU.add)
                    dve_m[("yt", hc, th)] = C["s_dve"].inc(ins)
                blk.vector(f_dve_av)

                def f_yo(sy, hc=hc, th=th):
                    s_dve.wait(sy, dve_m[("yt", hc, th)])
                    yt = (SQ0 if th == 0 else SQ1)[:, 512:1024]
                    dst = y_out.rearrange("(kc p) t -> p kc t", p=128)[
                        :, hc, th * 512:(th + 1) * 512]
                    oc_mark[("y", hc, th)] = s_y2[th].inc(
                        sy.dma_start(out=dst, in_=yt), 16)
                    if debug and hc == 0 and th == 0:
                        C["s_out"].inc(sy.dma_start(out=io["dbg_sq0"], in_=SQ0[:, :]), 16)
                        C["s_out"].inc(sy.dma_start(out=io["dbg_slt"], in_=SLT[par][:, :]), 16)
                blk.sync(f_yo)

        def f_fin(sy):
            s_y2[0].wait(sy)
            s_y2[1].wait(sy)
            s_out.wait(sy)
        blk.sync(f_fin)


def ffn1_io(nc):
    return {
        "x_in": nc.dram_tensor("x_fm", [D, TOK], F32, kind="ExternalInput").ap(),
        "y_out": nc.dram_tensor("y_fm", [D, TOK], F32, kind="ExternalOutput").ap(),
        "gamma": nc.dram_tensor("gamma", [128, KC], F32, kind="ExternalInput").ap(),
        "beta": nc.dram_tensor("beta", [128, KC], F32, kind="ExternalInput").ap(),
        "mu_f": nc.dram_tensor("mu_f", [D, 2048], F32, kind="ExternalInput").ap(),
        "mu_sl": nc.dram_tensor("mu_sl", [D, 256], F32, kind="ExternalInput").ap(),
        "scale1": nc.dram_tensor("scale1", [1, 1], F32, kind="ExternalInput").ap(),
        "W1_sl": nc.dram_tensor("W1_sl", [2, D, 512], F32, kind="ExternalInput").ap(),
        "b1_sl": nc.dram_tensor("b1_sl", [128, 8], F32, kind="ExternalInput").ap(),
        "W2_sl": nc.dram_tensor("W2_sl", [2, 512, D], F32, kind="ExternalInput").ap(),
        "b2_sl": nc.dram_tensor("b2_sl", [2, D], F32, kind="ExternalInput").ap(),
    }


def build_ffn1_program(scale=0.5, debug=False):
    nc = bass.Bass("TRN2", target_bir_lowering=False)
    io = ffn1_io(nc)
    if debug:
        io["dbg_xn"] = nc.dram_tensor("dbg_xn", [D, TOK], BF16, kind="ExternalOutput").ap()
        io["dbg_muns"] = nc.dram_tensor("dbg_muns", [128, KC * 256], BF16, kind="ExternalOutput").ap()
        io["dbg_pall"] = nc.dram_tensor("dbg_pall", [D, TOK], BF16, kind="ExternalOutput").ap()
        io["dbg_cn"] = nc.dram_tensor("dbg_cn", [1, 2048], F32, kind="ExternalOutput").ap()
        io["dbg_slag"] = nc.dram_tensor("dbg_slag", [8 * 512, D], BF16, kind="ExternalOutput").ap()
        io["dbg_sq0"] = nc.dram_tensor("dbg_sq0", [128, 1024], F32, kind="ExternalOutput").ap()
        io["dbg_slt"] = nc.dram_tensor("dbg_slt", [128, 2048], BF16, kind="ExternalOutput").ap()
    C = common_setup(nc)
    pbk = [nc.alloc_psum_tensor(f"pbk{i}", [128, 512], F32) for i in range(8)]
    with nc.Block() as blk:
        sb = emit_load_consts(nc, blk, C)
        ar = new_arena(nc)
        emit_ffn1(nc, blk, C, sb, ar, pbk, io, "_f1", scale=scale, debug=debug)
    return nc



def prep_attn1_weights(d):
    return {
        "wqkv_T": np.ascontiguousarray(np.asarray(d["L1_attn_w_qkv"]).T),
        "wproj_T": np.ascontiguousarray(np.asarray(d["L1_attn_w_proj"]).T),
    }


def emit_attn1(nc, blk, C, sb, ar, pb, io, sfx):
    x_in = io["x_in"]
    y_out = io["y_out"]
    wqkv = io["wqkv_T"]
    wproj = io["wproj_T"]

    s_ws = [Sem(nc, f"s_ws{i}{sfx}") for i in range(2)]

    X = ar.alloc([128, KC * TOK], F32)
    XB = ar.alloc([128, KC * TOK], BF16)
    QKV = ar.alloc([128, 24 * TOK], BF16)
    CTX = ar.alloc([128, KC * TOK], BF16)
    WS = [ar.alloc([128, KC * 128], BF16) for i in range(2)]
    SCR2 = ar.alloc([128, 3072], F32)
    sc = {"MB": SCR2[:, 0:1024], "RB": SCR2[:, 1024:2048], "TMP": SCR2[:, 2048:3072]}
    SQ0 = ar.alloc([128, 1024], F32)
    SQ1 = ar.alloc([128, 1024], F32)
    sc["SQ0"] = SQ0[:, :]
    sc["SQ1"] = SQ1[:, :]
    sc["rows"] = CTX[:, :].bitcast(F32)[0:1, 0:6 * TOK]
    SA = SCR2[0:1, 0:512]
    DRow = SCR2[0:1, 512:1024]
    W0R = SCR2[0:1, 1024:1536]
    VD = SQ1[:, :].bitcast(BF16)
    PRD = SQ0[:, :]

    s_ld, s_gw, s_pe = C["s_ld"], C["s_gw"], C["s_pe"]
    s_act, s_dve, s_out = C["s_act"], C["s_dve"], C["s_out"]

    if True:
        emit_load_master(nc, blk, C, X[:, :], x_in)
        xb_mark, ln_bank = emit_ln(nc, blk, C, sb, X[:, :], XB[:, :],
                                   [pb[0], pb[1]], sc)

        pe_m = {}
        act_m = {}
        dve_m = {}

        for oc in range(24):
            par = oc % 2

            def f_gw(gp, oc=oc, par=par):
                if oc >= 2:
                    s_pe.wait(gp, pe_m[("p", oc - 2, 1)])
                src = wqkv[:, oc * 128:(oc + 1) * 128].rearrange(
                    "(kc p) o -> p kc o", p=128)
                dst = WS[par].rearrange("p (kc o) -> p kc o", o=128)
                pe_m[("w", oc)] = s_ws[par].inc(gp.dma_start(out=dst, in_=src), 16)
            blk.gpsimd(f_gw)

            for th in range(2):
                def f_pe(t, oc=oc, th=th, par=par):
                    t.wait_ge(s_ws[par].h, pe_m[("w", oc)])
                    if oc == 0 and th == 0:
                        s_dve.wait(t, xb_mark)
                        s_dve.wait(t, ln_bank)
                    prev = (oc, th - 1) if th == 1 else (oc - 1, 1)
                    if oc > 0 or th > 0:
                        s_act.wait(t, act_m.get(("e",) + prev, 0))
                    for kc in range(KC):
                        ins = t.matmul(pb[2 + th][:, :],
                                       WS[par][:, kc * 128:(kc + 1) * 128],
                                       XB[:, kc * TOK + th * 512:kc * TOK + (th + 1) * 512],
                                       start=(kc == 0), stop=(kc == KC - 1))
                    pe_m[("p", oc, th)] = s_pe.inc(ins)
                blk.tensor(f_pe)

                def f_act(a, oc=oc, th=th):
                    s_pe.wait(a, pe_m[("p", oc, th)])
                    ins = a.activation(QKV[:, oc * TOK + th * 512:oc * TOK + (th + 1) * 512],
                                       pb[2 + th][:, :], AF.Copy)
                    act_m[("e", oc, th)] = C["s_act"].inc(ins)
                blk.scalar(f_act)

        qkv_done = act_m[("e", 23, 1)]

        for gg in range(4):
            qc0 = 6 * gg
            kc_ = 6 * gg + 4
            vc_ = 6 * gg + 5

            def f_dve_vd(v, gg=gg, vc_=vc_):
                s_act.wait(v, qkv_done)
                ins = v.tensor_tensor(VD[:, 0:512], QKV[:, vc_ * TOK + 0:vc_ * TOK + 512],
                                      QKV[:, vc_ * TOK + 512:vc_ * TOK + 1024],
                                      ALU.subtract)
                dve_m[("vd", gg)] = C["s_dve"].inc(ins)
            blk.vector(f_dve_vd)

            for hp in range(4):
                qc = qc0 + hp
                h = 4 * gg + hp
                for s_ in range(2):
                    key = (gg, hp, s_)

                    def f_dve_pr(v, qc=qc, kc_=kc_, s_=s_, key=key):
                        s_act.wait(v, qkv_done)
                        if key != (0, 0, 0):
                            s_pe.wait(v, pe_m.get(("sc2",) + prevkey[0], 0))
                        v.tensor_tensor(PRD[:, 0:512],
                                        QKV[:, qc * TOK + s_ * 512:qc * TOK + (s_ + 1) * 512],
                                        QKV[:, kc_ * TOK + 0:kc_ * TOK + 512], ALU.mult)
                        ins = v.tensor_tensor(PRD[:, 512:1024],
                                              QKV[:, qc * TOK + s_ * 512:qc * TOK + (s_ + 1) * 512],
                                              QKV[:, kc_ * TOK + 512:kc_ * TOK + 1024],
                                              ALU.mult)
                        dve_m[("pr",) + key] = C["s_dve"].inc(ins)
                    blk.vector(f_dve_pr)

                    def f_pe_sc(t, key=key):
                        s_dve.wait(t, dve_m[("pr",) + key])
                        if key != (0, 0, 0):
                            s_dve.wait(t, dve_m.get(("sa",) + prevkey[0], 0))
                        t.matmul(pb[4][0:1, :], sb["ones_k"][:, :], PRD[:, 0:512],
                                 start=True, stop=True)
                        ins = t.matmul(pb[5][0:1, :], sb["ones_k"][:, :],
                                       PRD[:, 512:1024], start=True, stop=True)
                        pe_m[("sc2",) + key] = s_pe.inc(ins)
                    blk.tensor(f_pe_sc)

                    def f_dve_sa(v, key=key):
                        s_pe.wait(v, pe_m[("sc2",) + key])
                        if key != (0, 0, 0):
                            s_act.wait(v, act_m.get(("sg",) + prevkey[0], 0))
                        v.tensor_copy(SA, pb[4][0:1, :])
                        v.drain()
                        ins = v.scalar_tensor_tensor(DRow, pb[5][0:1, :], -1.0, SA,
                                                     ALU.mult, ALU.add)
                        dve_m[("sa",) + key] = C["s_dve"].inc(ins)
                    blk.vector(f_dve_sa)

                    def f_act_sg(a, key=key):
                        s_dve.wait(a, dve_m[("sa",) + key])
                        ins = a.activation(W0R, DRow, AF.Sigmoid,
                                           scale=float(1.0 / np.sqrt(128.0)))
                        act_m[("sg",) + key] = C["s_act"].inc(ins)
                    blk.scalar(f_act_sg)

                    def f_pe_w0(t, key=key):
                        s_act.wait(t, act_m[("sg",) + key])
                        if key != (0, 0, 0):
                            s_dve.wait(t, dve_m.get(("cx",) + prevkey[0], 0))
                        ins = t.matmul(pb[6][:, :], sb["ones_m"][:, :], W0R,
                                       start=True, stop=True)
                        pe_m[("w0",) + key] = s_pe.inc(ins)
                    blk.tensor(f_pe_w0)

                    def f_dve_cx(v, gg=gg, hp=hp, s_=s_, key=key, vc_=vc_, h=h):
                        s_pe.wait(v, pe_m[("w0",) + key])
                        s_dve.wait(v, dve_m[("vd", gg)])
                        v.tensor_tensor(sc["TMP"][:, 0:512], VD[:, 0:512], pb[6][:, :],
                                        ALU.mult)
                        v.drain()
                        ins = v.tensor_tensor(
                            CTX[:, h * TOK + s_ * 512: h * TOK + (s_ + 1) * 512],
                            sc["TMP"][:, 0:512],
                            QKV[:, vc_ * TOK + 512:vc_ * TOK + 1024], ALU.add)
                        dve_m[("cx",) + key] = C["s_dve"].inc(ins)
                    blk.vector(f_dve_cx)

                    prevkey = [key]

        ctx_done_mark = dve_m[("cx", 3, 3, 1)]

        for hc in range(KC):
            par = hc % 2

            def f_gw_o(gp, hc=hc, par=par):
                s_pe.wait(gp, pe_m[("p", 23, 1)])
                if hc >= 2:
                    s_pe.wait(gp, pe_m[("o", hc - 2, 1)])
                src = wproj[:, hc * 128:(hc + 1) * 128].rearrange(
                    "(ec p) o -> p ec o", p=128)
                dst = WS[par].rearrange("p (ec o) -> p ec o", o=128)
                pe_m[("wo", hc)] = s_ws[par].inc(gp.dma_start(out=dst, in_=src), 16)
            blk.gpsimd(f_gw_o)

            for th in range(2):
                def f_pe_o(t, hc=hc, th=th, par=par):
                    t.wait_ge(s_ws[par].h, pe_m[("wo", hc)])
                    if hc == 0 and th == 0:
                        s_dve.wait(t, ctx_done_mark)
                    prev = (hc, th - 1) if th == 1 else (hc - 1, 1)
                    if hc > 0 or th > 0:
                        s_dve.wait(t, dve_m.get(("yo",) + prev, 0))
                    for ec in range(KC):
                        ins = t.matmul(pb[th][:, :],
                                       WS[par][:, ec * 128:(ec + 1) * 128],
                                       CTX[:, ec * TOK + th * 512:ec * TOK + (th + 1) * 512],
                                       start=(ec == 0), stop=(ec == KC - 1))
                    pe_m[("o", hc, th)] = s_pe.inc(ins)
                blk.tensor(f_pe_o)

                def f_dve_o(v, hc=hc, th=th):
                    s_pe.wait(v, pe_m[("o", hc, th)])
                    ins = v.scalar_tensor_tensor(
                        X[:, hc * TOK + th * 512: hc * TOK + (th + 1) * 512],
                        pb[th][:, :], 1.0,
                        X[:, hc * TOK + th * 512: hc * TOK + (th + 1) * 512],
                        ALU.mult, ALU.add)
                    dve_m[("yo", hc, th)] = C["s_dve"].inc(ins)
                blk.vector(f_dve_o)

        emit_store_master(nc, blk, C, X[:, :], y_out,
                          [(C["s_dve"], dve_m[("yo", KC - 1, 1)])])


def attn1_io(nc):
    return {
        "x_in": nc.dram_tensor("x_fm", [D, TOK], F32, kind="ExternalInput").ap(),
        "y_out": nc.dram_tensor("y_fm", [D, TOK], F32, kind="ExternalOutput").ap(),
        "wqkv_T": nc.dram_tensor("wqkv_T", [D, 3072], F32, kind="ExternalInput").ap(),
        "wproj_T": nc.dram_tensor("wproj_T", [D, D], F32, kind="ExternalInput").ap(),
    }


def build_attn1_program():
    nc = bass.Bass("TRN2", target_bir_lowering=False)
    io = attn1_io(nc)
    C = common_setup(nc)
    pbk = [nc.alloc_psum_tensor(f"pbk{i}", [128, 512], F32) for i in range(8)]
    with nc.Block() as blk:
        sb = emit_load_consts(nc, blk, C)
        ar = new_arena(nc)
        emit_attn1(nc, blk, C, sb, ar, pbk, io, "_a1")
    return nc



def build_fused_program(debug=False):
    nc = bass.Bass("TRN2", target_bir_lowering=False)
    x0 = nc.dram_tensor("x_fm", [D, TOK], F32, kind="ExternalInput").ap()
    y = nc.dram_tensor("y_fm", [D, TOK], F32, kind="ExternalOutput").ap()
    kind = "ExternalOutput" if debug else "Internal"
    xs = [nc.dram_tensor(f"xs{i}", [D, TOK], F32, kind=kind).ap()
          for i in range(1, 6)]

    io_moe = {
        "router_T": nc.dram_tensor("router_T", [D, 8], F32, kind="ExternalInput").ap(),
        "gup_T": nc.dram_tensor("gup_T", [8, D, 2816], F32, kind="ExternalInput").ap(),
        "down_T": nc.dram_tensor("down_T", [8, 1408, D], F32, kind="ExternalInput").ap(),
        "shg_T": nc.dram_tensor("shg_T", [D, 5632], F32, kind="ExternalInput").ap(),
        "shu_T": nc.dram_tensor("shu_T", [D, 5632], F32, kind="ExternalInput").ap(),
        "shd_T": nc.dram_tensor("shd_T", [5632, D], F32, kind="ExternalInput").ap(),
        "sgw_T": nc.dram_tensor("sgw_T", [128, KC], F32, kind="ExternalInput").ap(),
    }
    io_a0 = {
        "Wq": nc.dram_tensor("Wq", [D, D], F32, kind="ExternalInput").ap(),
        "Wk": nc.dram_tensor("Wk", [D, 512], F32, kind="ExternalInput").ap(),
        "Wv": nc.dram_tensor("Wv", [D, 512], F32, kind="ExternalInput").ap(),
        "Wo": nc.dram_tensor("Wo", [D, D], F32, kind="ExternalInput").ap(),
        "cs_t": nc.dram_tensor("cs_t", [128, 2048], F32, kind="ExternalInput").ap(),
        "masks": nc.dram_tensor("masks", [8, 128, 512], BF16, kind="ExternalInput").ap(),
    }
    io_f1 = {
        "gamma": nc.dram_tensor("gamma", [128, KC], F32, kind="ExternalInput").ap(),
        "beta": nc.dram_tensor("beta", [128, KC], F32, kind="ExternalInput").ap(),
        "mu_f": nc.dram_tensor("mu_f", [D, 2048], F32, kind="ExternalInput").ap(),
        "mu_sl": nc.dram_tensor("mu_sl", [D, 256], F32, kind="ExternalInput").ap(),
        "scale1": nc.dram_tensor("scale1", [1, 1], F32, kind="ExternalInput").ap(),
        "W1_sl": nc.dram_tensor("W1_sl", [2, D, 512], F32, kind="ExternalInput").ap(),
        "b1_sl": nc.dram_tensor("b1_sl", [128, 8], F32, kind="ExternalInput").ap(),
        "W2_sl": nc.dram_tensor("W2_sl", [2, 512, D], F32, kind="ExternalInput").ap(),
        "b2_sl": nc.dram_tensor("b2_sl", [2, D], F32, kind="ExternalInput").ap(),
    }
    io_a1 = {
        "wqkv_T": nc.dram_tensor("wqkv_T", [D, 3072], F32, kind="ExternalInput").ap(),
        "wproj_T": nc.dram_tensor("wproj_T", [D, D], F32, kind="ExternalInput").ap(),
    }

    C = common_setup(nc)
    pbk = [nc.alloc_psum_tensor(f"pbk{i}", [128, 512], F32) for i in range(8)]
    with nc.Block() as blk:
        sb = emit_load_consts(nc, blk, C)
        ar = new_arena(nc)

        def sub(emit_fn, io_extra, x_in, y_out, sfx, **kw):
            ar.reset()
            emit_boundary(nc, blk, C)
            io = dict(io_extra)
            io["x_in"] = x_in
            io["y_out"] = y_out
            emit_fn(nc, blk, C, sb, ar, pbk, io, sfx, **kw)

        sub(emit_moe, io_moe, x0, xs[0], "_s1", scale=0.5)
        sub(emit_attn0, io_a0, xs[0], xs[1], "_s2")
        sub(emit_moe, io_moe, xs[1], xs[2], "_s3", scale=0.5)
        sub(emit_ffn1, io_f1, xs[2], xs[3], "_s4", scale=0.5)
        sub(emit_attn1, io_a1, xs[3], xs[4], "_s5")
        sub(emit_ffn1, io_f1, xs[4], y, "_s6", scale=0.5)
    return nc


def fused_in_maps(d):
    w_moe = prep_moe_weights(d)
    w_a0 = prep_attn0_weights(d)
    w_a1 = prep_attn1_weights(d)
    xsh = shard_x_fm(d["L0_attn_x"])
    ims = []
    for c in range(NCORES):
        m = {"x_fm": xsh[c]}
        m.update(w_moe)
        m.update(w_a0)
        m.update(w_a1)
        m.update(prep_ffn1_weights(d, c))
        m["cs_t"] = rope_tables_for_core(c)
        m["masks"] = attn_masks_for_core(c)
        ims.append(m)
    return ims



def kernel(**inputs):
    nc = build_fused_program()
    ims = fused_in_maps(inputs)
    res = run_bass_kernel_spmd(nc, ims, core_ids=list(range(NCORES)))
    return unshard_x_fm([res.results[c]["y_fm"] for c in range(NCORES)])


# version 06 (final)
# speedup vs baseline: 1.2317x; 1.0345x over previous
import numpy as np
import ml_dtypes
import concourse.bass as bass
import concourse.mybir as mybir
from concourse.bass_utils import run_bass_kernel_spmd

F32 = mybir.dt.float32
BF16 = mybir.dt.bfloat16
AF = mybir.ActivationFunctionType
ALU = mybir.AluOpType
AX = mybir.AxisListType
bfnp = ml_dtypes.bfloat16

NCORES = 8
D = 2048
T = 4096
B = 2
STRIP = 256
TOK = 1024
KC = D // 128
LN_EPS = 1e-5


class Sem:
    __slots__ = ("h", "n")

    def __init__(self, nc, name):
        self.h = nc.alloc_semaphore(name)
        self.n = 0

    def inc(self, instr, k=1):
        instr.then_inc(self.h, k)
        self.n += k
        return self.n

    def wait(self, eng, n=None):
        eng.wait_ge(self.h, self.n if n is None else n)


class Arena:

    def __init__(self, nc, name, cols_bf16):
        self.t = nc.alloc_sbuf_tensor(name, [128, cols_bf16], BF16)
        self.view_f = self.t[:, :].bitcast(F32)
        self.cols = cols_bf16
        self.off = 0

    def reset(self):
        self.off = 0

    def alloc(self, shape, dtype):
        p, n = shape
        esz = 4 if dtype == F32 else 2
        a = (self.off + 31) // 32 * 32
        self.off = a + n * esz
        assert self.off <= self.cols * 2, f"arena overflow: {self.off} > {self.cols*2}"
        if dtype == F32:
            return self.view_f[0:p, a // 4:a // 4 + n]
        return self.t[0:p, a // 2:a // 2 + n]


def emit_boundary(nc, blk, C):
    vals = [(C[k].h, C[k].n) for k in
            ["s_ld", "s_gw", "s_pe", "s_act", "s_dve", "s_out", "s_ag", "s_pool"]]
    for eng in ["sync", "gpsimd", "tensor", "scalar", "vector"]:
        def f(e, vals=vals):
            for h, n in vals:
                if n:
                    e.wait_ge(h, n)
        getattr(blk, eng)(f)



def shard_x_fm(x):
    x = np.ascontiguousarray(np.asarray(x), dtype=np.float32)
    shards = []
    for c in range(NCORES):
        cols = np.concatenate([x[0, c::8], x[1, c::8]], axis=0)
        shards.append(np.ascontiguousarray(cols.T))
    return shards


def unshard_x_fm(shards):
    x = np.empty((B, T, D), np.float32)
    for c in range(NCORES):
        cols = np.asarray(shards[c]).T
        x[0, c::8] = cols[0:512]
        x[1, c::8] = cols[512:1024]
    return x


def prep_moe_weights(d):
    w = {}
    w["router_T"] = np.ascontiguousarray(d["L0_ffn_router_weight"].T)
    gup = np.asarray(d["L0_ffn_gate_up_proj"])
    g = gup[:, :1408, :]
    u = gup[:, 1408:, :]
    g = g.reshape(8, 11, 128, 2048)
    u = u.reshape(8, 11, 128, 2048)
    inter = np.stack([g, u], axis=2)
    inter = inter.reshape(8, 2816, 2048)
    w["gup_T"] = np.ascontiguousarray(inter.transpose(0, 2, 1))
    w["down_T"] = np.ascontiguousarray(np.asarray(d["L0_ffn_down_proj"]).transpose(0, 2, 1))
    w["shg_T"] = np.ascontiguousarray(d["L0_ffn_sh_gate_w"].T)
    w["shu_T"] = np.ascontiguousarray(d["L0_ffn_sh_up_w"].T)
    w["shd_T"] = np.ascontiguousarray(d["L0_ffn_sh_down_w"].T)
    w["sgw_T"] = np.ascontiguousarray(
        np.asarray(d["L0_ffn_shared_gate_w"]).reshape(KC, 128).T)
    return w



def common_setup(nc):
    C = {}
    for nm in ["s_ld", "s_gw", "s_pe", "s_act", "s_dve", "s_pool", "s_out", "s_ag"]:
        C[nm] = Sem(nc, nm)
    C["t_ones_k"] = nc.inline_tensor(np.ones((128, 1), np.float32), "t_ones_k")
    C["t_ones_m"] = nc.inline_tensor(np.ones((1, 128), np.float32), "t_ones_m")
    C["t_idn_f"] = nc.inline_tensor(np.eye(128, dtype=np.float32), "t_idn_f")
    C["t_idn_b"] = nc.inline_tensor(np.eye(128, dtype=bfnp), "t_idn_b")
    C["t_eps"] = nc.inline_tensor(np.full((128, 1), LN_EPS, np.float32), "t_eps")
    C["t_onesb"] = nc.inline_tensor(np.ones((128, 1), bfnp), "t_onesb")
    return C


def emit_load_consts(nc, blk, C):
    sb = {
        "ones_k": nc.alloc_sbuf_tensor("sb_ones_k", [128, 1], F32),
        "ones_m": nc.alloc_sbuf_tensor("sb_ones_m", [1, 128], F32),
        "idn_f": nc.alloc_sbuf_tensor("sb_idn_f", [128, 128], F32),
        "eps": nc.alloc_sbuf_tensor("sb_eps", [128, 1], F32),
    }
    s_ld = C["s_ld"]

    def f(sy):
        s_ld.inc(sy.dma_start(out=sb["ones_k"][:], in_=C["t_ones_k"].ap()), 16)
        s_ld.inc(sy.dma_start(out=sb["ones_m"][:], in_=C["t_ones_m"].ap()), 16)
        s_ld.inc(sy.dma_start(out=sb["idn_f"][:], in_=C["t_idn_f"].ap()), 16)
        s_ld.inc(sy.dma_start(out=sb["eps"][:], in_=C["t_eps"].ap()), 16)
    blk.sync(f)
    return sb


def emit_load_master(nc, blk, C, X, x_fm_ap, ntok=TOK):
    s_ld = C["s_ld"]

    def f(sy):
        src = x_fm_ap.rearrange("(kc p) t -> p kc t", p=128)
        dst = X[:, :].rearrange("p (kc t) -> p kc t", t=ntok)
        s_ld.inc(sy.dma_start(out=dst, in_=src), 16)
    blk.sync(f)
    return s_ld.n


def emit_store_master(nc, blk, C, X, y_fm_ap, waits, ntok=TOK):
    s_out = C["s_out"]

    def f(sy):
        for s, v in waits:
            s.wait(sy, v)
        dst = y_fm_ap.rearrange("(kc p) t -> p kc t", p=128)
        src = X[:, :].rearrange("p (kc t) -> p kc t", t=ntok)
        s_out.inc(sy.dma_start(out=dst, in_=src), 16)
        s_out.wait(sy)
    blk.sync(f)


def emit_ln(nc, blk, C, sb, X, XB, pm, sc, ntok=TOK, x_ready=None, gamma_ap=None,
           beta_ap=None, XF=None):
    s_pe, s_act, s_dve, s_ld = C["s_pe"], C["s_act"], C["s_dve"], C["s_ld"]
    nh = ntok // 512
    SQ = [sc["SQ0"], sc["SQ1"]]
    rows = sc["rows"]
    r_sum = rows[:, 0 * ntok:1 * ntok]
    r_sq = rows[:, 1 * ntok:2 * ntok]
    r_mean = rows[:, 2 * ntok:3 * ntok]
    r_var = rows[:, 3 * ntok:4 * ntok]
    r_rstd = rows[:, 4 * ntok:5 * ntok]
    r_tmp = rows[:, 5 * ntok:6 * ntok]
    MB, RB, TMP = sc["MB"], sc["RB"], sc["TMP"]
    ones_k, ones_m = sb["ones_k"], sb["ones_m"]
    if x_ready is None:
        x_ready = [(s_ld, s_ld.n)]

    gb = None
    if gamma_ap is not None:
        gb = sc.get("gb")
        if gb is None:
            gb = nc.alloc_sbuf_tensor(None, [128, 2 * KC], F32)[:, :]

        def f_gb(sy):
            s_ld.inc(sy.dma_start(out=gb[:, 0:KC], in_=gamma_ap), 16)
            s_ld.inc(sy.dma_start(out=gb[:, KC:2 * KC], in_=beta_ap), 16)
        blk.sync(f_gb)
        gb_ready = s_ld.n

    pe_m = {}

    def f_pe_a(t):
        for s, v in x_ready:
            s.wait(t, v)
        for th in range(nh):
            for k in range(KC):
                ins = t.matmul(pm[th][0:1, :], ones_k[:, :],
                               X[:, k * ntok + th * 512: k * ntok + (th + 1) * 512],
                               start=(k == 0), stop=(k == KC - 1))
            pe_m[("sum", th)] = s_pe.inc(ins)
    blk.tensor(f_pe_a)

    def f_dve_a(v):
        for th in range(nh):
            s_pe.wait(v, pe_m[("sum", th)])
            ins = v.tensor_copy(r_sum[:, th * 512:(th + 1) * 512], pm[th][0:1, :])
        pe_m["sum_evac"] = s_dve.inc(ins)
    blk.vector(f_dve_a)

    act_m = {}
    for k in range(KC):
        def f_act_b(a, k=k):
            if k < 2:
                for s, v in x_ready:
                    s.wait(a, v)
            else:
                s_pe.wait(a, pe_m[("sq", k - 2)])
            if k == 0:
                s_dve.wait(a, pe_m["sum_evac"])
            act_m[k] = s_act.inc(a.activation(SQ[k % 2][:, :],
                                              X[:, k * ntok:(k + 1) * ntok], AF.Square))
        blk.scalar(f_act_b)

        def f_pe_b(t, k=k):
            s_act.wait(t, act_m[k])
            if k == 0:
                s_dve.wait(t, pe_m["sum_evac"])
            for th in range(nh):
                ins = t.matmul(pm[th][0:1, :], ones_k[:, :],
                               SQ[k % 2][:, th * 512:(th + 1) * 512],
                               start=(k == 0), stop=(k == KC - 1))
                if k == KC - 1:
                    pe_m[("sumsq", th)] = s_pe.inc(ins)
            if k < KC - 1:
                pe_m[("sq", k)] = s_pe.inc(ins)
            else:
                pe_m[("sq", k)] = pe_m[("sumsq", nh - 1)]
        blk.tensor(f_pe_b)

    dve_m = {}

    def f_dve_c(v):
        for th in range(nh):
            s_pe.wait(v, pe_m[("sumsq", th)])
            v.tensor_copy(r_sq[:, th * 512:(th + 1) * 512], pm[th][0:1, :])
        v.drain()
        v.tensor_scalar_mul(r_mean[:, :], r_sum[:, :], 1.0 / D)
        v.tensor_scalar_mul(r_var[:, :], r_sq[:, :], 1.0 / D)
        v.drain()
        v.tensor_tensor(r_tmp[:, :], r_mean[:, :], r_mean[:, :], ALU.mult)
        v.drain()
        ins = v.tensor_tensor(r_var[:, :], r_var[:, :], r_tmp[:, :], ALU.subtract)
        dve_m["var"] = s_dve.inc(ins)
    blk.vector(f_dve_c)

    def f_act_c(a):
        s_dve.wait(a, dve_m["var"])
        act_m["sd"] = s_act.inc(a.activation(r_var[:, :], r_var[:, :], AF.Sqrt,
                                             bias=sb["eps"][0:1, :]))
    blk.scalar(f_act_c)

    def f_dve_c2(v):
        s_act.wait(v, act_m["sd"])
        dve_m["rstd"] = s_dve.inc(v.reciprocal(r_rstd[:, :], r_var[:, :]))
    blk.vector(f_dve_c2)

    for th in range(nh):
        def f_pe_d(t, th=th):
            if th == 0:
                s_dve.wait(t, dve_m["rstd"])
            else:
                s_dve.wait(t, dve_m[("bc", th - 1)])
            t.matmul(pm[0][:, :], ones_m[:, :], r_mean[:, th * 512:(th + 1) * 512],
                     start=True, stop=True)
            ins = t.matmul(pm[1][:, :], ones_m[:, :], r_rstd[:, th * 512:(th + 1) * 512],
                           start=True, stop=True)
            pe_m[("bc", th)] = s_pe.inc(ins)
        blk.tensor(f_pe_d)

        def f_dve_d(v, th=th):
            s_pe.wait(v, pe_m[("bc", th)])
            v.tensor_copy(MB[:, th * 512:(th + 1) * 512], pm[0][:, :])
            ins = v.tensor_copy(RB[:, th * 512:(th + 1) * 512], pm[1][:, :])
            dve_m[("bc", th)] = s_dve.inc(ins)
        blk.vector(f_dve_d)

    def f_dve_d_all(v):
        dve_m["bc"] = dve_m[("bc", nh - 1)]
        if gamma_ap is not None:
            v.wait_ge(s_ld.h, gb_ready)
        for k in range(KC):
            v.tensor_tensor(TMP[:, :], X[:, k * ntok:(k + 1) * ntok], MB[:, :],
                            ALU.subtract)
            v.drain()
            if gamma_ap is None:
                if XF is not None:
                    ins = v.tensor_tensor(XF[:, k * ntok:(k + 1) * ntok], TMP[:, :],
                                          RB[:, :], ALU.mult)
                    if XB is not None:
                        v.drain()
                        ins = v.tensor_copy(XB[:, k * ntok:(k + 1) * ntok],
                                            XF[:, k * ntok:(k + 1) * ntok])
                else:
                    ins = v.tensor_tensor(XB[:, k * ntok:(k + 1) * ntok], TMP[:, :],
                                          RB[:, :], ALU.mult)
            else:
                v.tensor_tensor(TMP[:, :], TMP[:, :], RB[:, :], ALU.mult)
                v.drain()
                v.tensor_scalar(TMP[:, :], TMP[:, :], gb[:, k:k + 1],
                                gb[:, KC + k:KC + k + 1], ALU.mult, ALU.add)
                v.drain()
                if XF is not None:
                    v.tensor_copy(XF[:, k * ntok:(k + 1) * ntok], TMP[:, :])
                ins = v.tensor_copy(XB[:, k * ntok:(k + 1) * ntok], TMP[:, :])
        dve_m["xb"] = s_dve.inc(ins)
    blk.vector(f_dve_d_all)

    return dve_m["xb"], dve_m["bc"]


def emit_moe(nc, blk, C, sb, ar, pb, io, sfx, scale=0.5, debug=False):
    x_in = io["x_in"]
    y_out = io["y_out"]
    if debug:
        dbg_xb = io["dbg_xb"]
        dbg_cf = io["dbg_cf"]
        dbg_bg = io["dbg_bg"]
        dbg_a = io["dbg_a"]
        dbg_rs = io["dbg_rs"]
    rT = io["router_T"]
    gupT = io["gup_T"]
    dwnT = io["down_T"]
    shgT = io["shg_T"]
    shuT = io["shu_T"]
    shdT = io["shd_T"]
    sgwT = io["sgw_T"]

    s_wgu = [Sem(nc, f"s_wgu{i}{sfx}") for i in range(2)]
    s_wd = [Sem(nc, f"s_wd{i}{sfx}") for i in range(2)]
    s_crow = Sem(nc, f"s_crow{sfx}")

    X = ar.alloc([128, KC * TOK], F32)
    XB = ar.alloc([128, KC * TOK], BF16)
    A = ar.alloc([128, 11 * TOK], BF16)
    Wgu = [ar.alloc([128, KC * 256], BF16) for i in range(2)]
    Wd = [ar.alloc([128, 11 * 512], BF16) for i in range(2)]
    RTsb = ar.alloc([128, KC * 8], BF16)
    SGWsb = ar.alloc([128, KC], BF16)
    SCR = ar.alloc([128, 11264], F32)
    sc = {
        "SQ0": SCR[:, 0 * TOK:1 * TOK],
        "SQ1": SCR[:, 1 * TOK:2 * TOK],
        "MB": SCR[:, 2 * TOK:3 * TOK],
        "RB": SCR[:, 3 * TOK:4 * TOK],
        "TMP": SCR[:, 4 * TOK:5 * TOK],
        "rows": SCR[0:1, 5 * TOK:11 * TOK],
    }
    SG = SCR[:, 0 * TOK:1 * TOK]
    UB = SCR[:, 1 * TOK:2 * TOK]
    BE = SCR[:, 5 * TOK:6 * TOK]
    BG = SCR[:, 6 * TOK:7 * TOK]
    CF = SCR[0:8, 7 * TOK:8 * TOK]
    CROW = SCR[0:1, 8 * TOK:9 * TOK]
    GR = SCR[0:1, 9 * TOK:10 * TOK]
    RS = SCR[:, 10 * TOK:10 * TOK + 64]
    RCOL = SCR[:, 10 * TOK + 64:10 * TOK + 72]

    pg = pb[0:2]
    pu = pb[2:4]
    pd = pb[4:6]
    pm = pb[6:8]

    s_ld, s_gw, s_pe = C["s_ld"], C["s_gw"], C["s_pe"]
    s_act, s_dve, s_pool = C["s_act"], C["s_dve"], C["s_pool"]

    if True:
        emit_load_master(nc, blk, C, X, x_in)

        def f_rt(sy):
            s_ld.inc(sy.dma_start(out=RTsb[:, :].rearrange("p (kc o) -> p kc o", o=8),
                                  in_=rT.rearrange("(kc p) o -> p kc o", p=128)), 16)
            s_ld.inc(sy.dma_start(out=SGWsb[:, :], in_=sgwT), 16)
        blk.gpsimd(f_rt)
        rt_mark = s_ld.n

        xb_mark, ln_bank_mark = emit_ln(nc, blk, C, sb, X, XB, pm, sc)

        pe_m = {}
        dve_m = {}
        act_m = {}
        for tc in range(8):
            def f_pe_r(t, tc=tc):
                if tc == 0:
                    s_dve.wait(t, ln_bank_mark)
                    s_dve.wait(t, xb_mark)
                    t.wait_ge(s_ld.h, rt_mark)
                else:
                    s_dve.wait(t, dve_m[("rdone", tc - 1)])
                for k in range(KC):
                    ins = t.matmul(pm[tc % 2][:, 0:8],
                                   XB[:, k * TOK + tc * 128: k * TOK + (tc + 1) * 128],
                                   RTsb[:, k * 8:(k + 1) * 8],
                                   start=(k == 0), stop=(k == KC - 1))
                pe_m[("rl", tc)] = s_pe.inc(ins)
            blk.tensor(f_pe_r)

            def f_dve_r(v, tc=tc):
                if debug and tc == 1:
                    v.wait_ge(C["s_out"].h, rs_dump_mark)
                s_pe.wait(v, pe_m[("rl", tc)])
                l8 = RS[:, 0:8]
                eq = RS[:, 8:16]
                lm = RS[:, 16:24]
                ge = RS[:, 24:32]
                ex = RS[:, 32:40]
                gate = RS[:, 40:48]
                cmb = RS[:, 48:56]
                m1 = RCOL[:, 0:1]
                m2 = RCOL[:, 1:2]
                dn = RCOL[:, 2:3]
                di = RCOL[:, 3:4]
                v.tensor_copy(l8, pm[tc % 2][:, 0:8])
                v.drain()
                v.reduce_max(m1, l8, axis=AX.X)
                v.drain()
                v.tensor_scalar(eq, l8, m1, None, ALU.is_ge)
                v.drain()
                v.scalar_tensor_tensor(lm, eq, -1e30, l8, ALU.mult, ALU.add)
                v.drain()
                v.reduce_max(m2, lm, axis=AX.X)
                v.drain()
                v.tensor_scalar(ge, l8, m2, None, ALU.is_ge)
                v.tensor_scalar(ex, l8, m1, None, ALU.subtract)
                ins = v.drain()
                dve_m[("rsub", tc)] = s_dve.inc(ins)
            blk.vector(f_dve_r)

            def f_act_r(a, tc=tc):
                s_dve.wait(a, dve_m[("rsub", tc)])
                act_m[("rexp", tc)] = s_act.inc(
                    a.activation(RS[:, 32:40], RS[:, 32:40], AF.Exp))
            blk.scalar(f_act_r)

            def f_dve_r2(v, tc=tc):
                s_act.wait(v, act_m[("rexp", tc)])
                ex = RS[:, 32:40]
                ge = RS[:, 24:32]
                gate = RS[:, 40:48]
                cmb = RS[:, 48:56]
                dn = RCOL[:, 2:3]
                di = RCOL[:, 3:4]
                v.tensor_tensor(gate, ex, ge, ALU.mult)
                v.drain()
                v.reduce_sum(dn, gate, axis=AX.X)
                v.drain()
                v.reciprocal(di, dn)
                v.drain()
                v.tensor_scalar_mul(di, di, scale)
                v.drain()
                ins = v.tensor_scalar_mul(cmb, gate, di)
                dve_m[("cmb", tc)] = s_dve.inc(ins)
            blk.vector(f_dve_r2)

            if debug and tc == 0:
                def f_dbg_rs(sy):
                    s_dve.wait(sy, dve_m[("cmb", 0)])
                    dve_m["dbg_rs"] = True
                    C["s_out"].inc(sy.dma_start(out=dbg_rs[:, 0:64], in_=RS[:, :]), 16)
                    C["s_out"].inc(sy.dma_start(out=dbg_rs[:, 64:72], in_=RCOL[:, :]), 16)
                blk.sync(f_dbg_rs)
                rs_dump_mark = C["s_out"].n

            def f_pe_r2(t, tc=tc):
                s_dve.wait(t, dve_m[("cmb", tc)])
                ins = t.transpose(pm[tc % 2][0:8, 0:128], RS[:, 48:56], sb["idn_f"][:, :])
                pe_m[("rt", tc)] = s_pe.inc(ins)
            blk.tensor(f_pe_r2)

            def f_dve_r3(v, tc=tc):
                s_pe.wait(v, pe_m[("rt", tc)])
                ins = v.tensor_copy(CF[0:8, tc * 128:(tc + 1) * 128], pm[tc % 2][0:8, 0:128])
                dve_m[("rdone", tc)] = s_dve.inc(ins)
            blk.vector(f_dve_r3)

        def f_pe_g(t):
            s_dve.wait(t, dve_m[("rdone", 7)])
            for th in range(2):
                for k in range(KC):
                    ins = t.matmul(pm[th][0:1, :], SGWsb[:, k:k + 1],
                                   XB[:, k * TOK + th * 512: k * TOK + (th + 1) * 512],
                                   start=(k == 0), stop=(k == KC - 1))
                pe_m[("sg", th)] = s_pe.inc(ins)
        blk.tensor(f_pe_g)

        def f_act_g(a):
            for th in range(2):
                s_pe.wait(a, pe_m[("sg", th)])
                ins = a.activation(GR[:, th * 512:(th + 1) * 512], pm[th][0:1, :],
                                   AF.Sigmoid)
            act_m["sg"] = s_act.inc(ins)
        blk.scalar(f_act_g)

        def f_dve_g(v):
            s_act.wait(v, act_m["sg"])
            ins = v.tensor_scalar_mul(GR[:, :], GR[:, :], scale)
            dve_m["gr"] = s_dve.inc(ins)
        blk.vector(f_dve_g)

        def f_pe_g2(t):
            s_dve.wait(t, dve_m["gr"])
            for th in range(2):
                ins = t.matmul(pm[th][:, :], sb["ones_m"][:, :],
                               GR[:, th * 512:(th + 1) * 512], start=True, stop=True)
            pe_m["gb"] = s_pe.inc(ins)
        blk.tensor(f_pe_g2)

        def f_dve_g2(v):
            s_pe.wait(v, pe_m["gb"])
            for th in range(2):
                ins = v.tensor_copy(BG[:, th * 512:(th + 1) * 512], pm[th][:, :])
            dve_m["bg"] = s_dve.inc(ins)
        blk.vector(f_dve_g2)

        gw_mark = {}
        gu_consumed = {}
        dn_consumed = {}
        pe_gu = {}
        pe_dn = {}
        act_su = {}
        dve_ub = {}
        dve_a = {}
        dve_x = {}
        be_ready = {}
        pd_free = {}
        pd_use = 0

        dbg_a_mark = [0]
        for u in range(12):
            is_sh = u >= 8
            g = u - 8
            if debug and u == 1:
                def f_dbg_a(sy):
                    s_dve.wait(sy, dve_a[(0, 10, 1)])
                    dbg_a_mark[0] = C["s_out"].inc(
                        sy.dma_start(out=dbg_a.rearrange("(jc p) t -> p jc t", p=128),
                                     in_=A[:, :].rearrange("p (jc t) -> p jc t", t=TOK)),
                        16)
                blk.sync(f_dbg_a)
            if not is_sh:
                def f_pool_be(gp, u=u):
                    s_dve.wait(gp, dve_m[("rdone", 7)])
                    if u > 0:
                        s_dve.wait(gp, dve_a[(u - 1, 10, 1)])
                    gw_mark[("crow", u)] = s_crow.inc(
                        gp.dma_start(out=CROW[:, :], in_=CF[u:u + 1, :]), 16)
                blk.gpsimd(f_pool_be)

                def f_pe_be(t, u=u):
                    t.wait_ge(s_crow.h, gw_mark[("crow", u)])
                    if u > 0:
                        s_dve.wait(t, dve_m[("be", u - 1)])
                    else:
                        s_dve.wait(t, dve_m["bg"])
                    for th in range(2):
                        ins = t.matmul(pm[th][:, :], sb["ones_m"][:, :],
                                       CROW[:, th * 512:(th + 1) * 512],
                                       start=True, stop=True)
                    pe_m[("be", u)] = s_pe.inc(ins)
                blk.tensor(f_pe_be)

                def f_dve_be(v, u=u):
                    s_pe.wait(v, pe_m[("be", u)])
                    if u > 0:
                        pass
                    for th in range(2):
                        ins = v.tensor_copy(BE[:, th * 512:(th + 1) * 512], pm[th][:, :])
                    dve_m[("be", u)] = s_dve.inc(ins)
                    be_ready[u] = dve_m[("be", u)]
                blk.vector(f_dve_be)
            else:
                if u == 8:
                    be_ready[u] = dve_m["bg"]
                else:
                    be_ready[u] = be_ready[8]
            BEu = BG if is_sh else BE

            for m in range(11):
                gi = u * 11 + m
                par = gi % 2

                def f_gw(gp, u=u, m=m, gi=gi, par=par, is_sh=is_sh, g=g):
                    if gi >= 2:
                        s_pe.wait(gp, gu_consumed[gi - 2])
                    dst = Wgu[par][:, :].rearrange("p (kc j) -> p kc j", j=256)
                    sw = s_wgu[par]
                    if not is_sh:
                        src = gupT[u, :, m * 256:(m + 1) * 256] \
                            .rearrange("(kc p) j -> p kc j", p=128)
                        ins = gp.dma_start(out=dst, in_=src)
                    else:
                        jm = g * 11 + m
                        srcg = shgT[:, jm * 128:(jm + 1) * 128] \
                            .rearrange("(kc p) j -> p kc j", p=128)
                        srcu = shuT[:, jm * 128:(jm + 1) * 128] \
                            .rearrange("(kc p) j -> p kc j", p=128)
                        sw.inc(gp.dma_start(out=dst[:, :, 0:128], in_=srcg), 16)
                        ins = gp.dma_start(out=dst[:, :, 128:256], in_=srcu)
                    gw_mark[("gu", gi)] = sw.inc(ins, 16)
                blk.gpsimd(f_gw)

                def f_pe_gu(t, u=u, m=m, gi=gi, par=par):
                    t.wait_ge(s_wgu[par].h, gw_mark[("gu", gi)])
                    if u == 0 and m == 0:
                        s_dve.wait(t, xb_mark)
                    if gi >= 1:
                        s_act.wait(t, act_su[(gi - 1, 1)])
                        s_dve.wait(t, dve_ub[(gi - 1, 1)])
                    for th in range(2):
                        for k in range(KC):
                            ins = t.matmul(pg[th][:, :],
                                           Wgu[par][:, k * 256:k * 256 + 128],
                                           XB[:, k * TOK + th * 512:k * TOK + (th + 1) * 512],
                                           start=(k == 0), stop=(k == KC - 1))
                        pe_gu[(gi, "g", th)] = s_pe.inc(ins)
                    for th in range(2):
                        for k in range(KC):
                            ins = t.matmul(pu[th][:, :],
                                           Wgu[par][:, k * 256 + 128:k * 256 + 256],
                                           XB[:, k * TOK + th * 512:k * TOK + (th + 1) * 512],
                                           start=(k == 0), stop=(k == KC - 1))
                        pe_gu[(gi, "u", th)] = s_pe.inc(ins)
                    gu_consumed[gi] = pe_gu[(gi, "u", 1)]
                blk.tensor(f_pe_gu)

                def f_act_su(a, u=u, m=m, gi=gi):
                    for th in range(2):
                        s_pe.wait(a, pe_gu[(gi, "g", th)])
                        if gi >= 1:
                            s_dve.wait(a, dve_a[(u, m - 1, 1)] if m >= 1
                                       else dve_a[(u - 1, 10, 1)] if u >= 1 else 0)
                        ins = a.activation(SG[:, th * 512:(th + 1) * 512], pg[th][:, :],
                                           AF.Silu)
                        act_su[(gi, th)] = s_act.inc(ins)
                blk.scalar(f_act_su)

                def f_dve_gu(v, u=u, m=m, gi=gi):
                    if debug and u == 1 and m == 0:
                        v.wait_ge(C["s_out"].h, dbg_a_mark[0])
                    for th in range(2):
                        s_pe.wait(v, pe_gu[(gi, "u", th)])
                        ins = v.tensor_tensor(UB[:, th * 512:(th + 1) * 512], pu[th][:, :],
                                              BEu[:, th * 512:(th + 1) * 512], ALU.mult)
                        dve_ub[(gi, th)] = s_dve.inc(ins)
                        v.drain()
                        s_act.wait(v, act_su[(gi, th)])
                        ins = v.tensor_tensor(
                            A[:, m * TOK + th * 512:m * TOK + (th + 1) * 512],
                            SG[:, th * 512:(th + 1) * 512],
                            UB[:, th * 512:(th + 1) * 512], ALU.mult)
                        dve_a[(u, m, th)] = s_dve.inc(ins)
                blk.vector(f_dve_gu)

            for s4 in range(4):
                di = u * 4 + s4
                par = di % 2

                def f_gw_d(gp, u=u, s4=s4, di=di, par=par, is_sh=is_sh, g=g):
                    if di >= 2:
                        s_pe.wait(gp, dn_consumed[di - 2])
                    dst = Wd[par][:, :].rearrange("p (jc h) -> p jc h", h=512)
                    if not is_sh:
                        src = dwnT[u, :, s4 * 512:(s4 + 1) * 512] \
                            .rearrange("(jc p) h -> p jc h", p=128)
                    else:
                        src = shdT[g * 1408:(g + 1) * 1408, s4 * 512:(s4 + 1) * 512] \
                            .rearrange("(jc p) h -> p jc h", p=128)
                    gw_mark[("d", di)] = s_wd[par].inc(gp.dma_start(out=dst, in_=src), 16)
                blk.gpsimd(f_gw_d)

                for hl in range(4):
                    hc = s4 * 4 + hl
                    for th in range(2):
                        nonlocal_pd = pd_use
                        pdb = pd[pd_use % 2]
                        pd_use += 1

                        def f_pe_d(t, u=u, s4=s4, di=di, hl=hl, th=th, hc=hc,
                                   pdb=pdb, slot=nonlocal_pd, par=par):
                            if hl == 0 and th == 0:
                                t.wait_ge(s_wd[par].h, gw_mark[("d", di)])
                                if s4 == 0:
                                    s_dve.wait(t, dve_a[(u, 10, 1)])
                            if slot >= 2:
                                s_dve.wait(t, pd_free[slot - 2])
                            for jc in range(11):
                                ins = t.matmul(pdb[:, :],
                                               Wd[par][:, jc * 512 + hl * 128: jc * 512 + (hl + 1) * 128],
                                               A[:, jc * TOK + th * 512: jc * TOK + (th + 1) * 512],
                                               start=(jc == 0), stop=(jc == 10))
                            pe_dn[slot] = s_pe.inc(ins)
                            if hl == 3 and th == 1:
                                dn_consumed[di] = pe_dn[slot]
                        blk.tensor(f_pe_d)

                        def f_dve_d(v, u=u, hc=hc, th=th, pdb=pdb, slot=nonlocal_pd):
                            s_pe.wait(v, pe_dn[slot])
                            ins = v.scalar_tensor_tensor(
                                X[:, hc * TOK + th * 512: hc * TOK + (th + 1) * 512],
                                pdb[:, :], 1.0,
                                X[:, hc * TOK + th * 512: hc * TOK + (th + 1) * 512],
                                ALU.mult, ALU.add)
                            pd_free[slot] = s_dve.inc(ins)
                        blk.vector(f_dve_d)

        if debug:
            def f_dbg(sy):
                s_dve.wait(sy, xb_mark)
                C["s_out"].inc(sy.dma_start(
                    out=dbg_xb.rearrange("(kc p) t -> p kc t", p=128),
                    in_=XB[:, :].rearrange("p (kc t) -> p kc t", t=TOK)), 16)
                s_dve.wait(sy, dve_m[("rdone", 7)])
                C["s_out"].inc(sy.dma_start(out=dbg_cf, in_=CF), 16)
                s_dve.wait(sy, dve_m["bg"])
                C["s_out"].inc(sy.dma_start(out=dbg_bg, in_=BG[:, :]), 16)
            blk.sync(f_dbg)

        last_x = s_dve.n
        emit_store_master(nc, blk, C, X, y_out, [(s_dve, last_x)])


def moe_io(nc, debug=False):
    io = {
        "x_in": nc.dram_tensor("x_fm", [D, TOK], F32, kind="ExternalInput").ap(),
        "y_out": nc.dram_tensor("y_fm", [D, TOK], F32, kind="ExternalOutput").ap(),
        "router_T": nc.dram_tensor("router_T", [D, 8], F32, kind="ExternalInput").ap(),
        "gup_T": nc.dram_tensor("gup_T", [8, D, 2816], F32, kind="ExternalInput").ap(),
        "down_T": nc.dram_tensor("down_T", [8, 1408, D], F32, kind="ExternalInput").ap(),
        "shg_T": nc.dram_tensor("shg_T", [D, 5632], F32, kind="ExternalInput").ap(),
        "shu_T": nc.dram_tensor("shu_T", [D, 5632], F32, kind="ExternalInput").ap(),
        "shd_T": nc.dram_tensor("shd_T", [5632, D], F32, kind="ExternalInput").ap(),
        "sgw_T": nc.dram_tensor("sgw_T", [128, KC], F32, kind="ExternalInput").ap(),
    }
    if debug:
        io["dbg_xb"] = nc.dram_tensor("dbg_xb", [D, TOK], BF16, kind="ExternalOutput").ap()
        io["dbg_cf"] = nc.dram_tensor("dbg_cf", [8, TOK], F32, kind="ExternalOutput").ap()
        io["dbg_bg"] = nc.dram_tensor("dbg_bg", [128, TOK], F32, kind="ExternalOutput").ap()
        io["dbg_a"] = nc.dram_tensor("dbg_a", [1408, TOK], BF16, kind="ExternalOutput").ap()
        io["dbg_rs"] = nc.dram_tensor("dbg_rs", [128, 72], F32, kind="ExternalOutput").ap()
    return io


def new_arena(nc):
    return Arena(nc, "AR", (nc.sbuf_bytes_remaining - 64) // 2)


def build_moe_program(scale=0.5, debug=False):
    nc = bass.Bass("TRN2", target_bir_lowering=False)
    io = moe_io(nc, debug)
    C = common_setup(nc)
    pbk = [nc.alloc_psum_tensor(f"pbk{i}", [128, 512], F32) for i in range(8)]
    with nc.Block() as blk:
        sb = emit_load_consts(nc, blk, C)
        ar = new_arena(nc)
        emit_moe(nc, blk, C, sb, ar, pbk, io, "_m", scale=scale, debug=debug)
    return nc





NQH = 32
NKV = 8
HD = 64


def prep_attn0_weights(d):
    return {
        "Wq": np.ascontiguousarray(np.asarray(d["L0_attn_Wq"])),
        "Wk": np.ascontiguousarray(np.asarray(d["L0_attn_Wk"])),
        "Wv": np.ascontiguousarray(np.asarray(d["L0_attn_Wv"])),
        "Wo": np.ascontiguousarray(np.asarray(d["L0_attn_Wo"])),
    }


def rope_tables_for_core(c):
    pos = (np.arange(512) * 8 + c).astype(np.float32)
    pos = np.concatenate([pos, pos])
    theta = 1.0 / 10000.0 ** (np.arange(0, HD, 2, dtype=np.float32) / HD)
    fr = np.outer(theta, pos)
    co, si = np.cos(fr).astype(np.float32), np.sin(fr).astype(np.float32)
    cs1 = np.concatenate([co, si, co, si], axis=0)
    cs2 = np.concatenate([si, co, si, co], axis=0)
    return np.concatenate([cs1, cs2], axis=1)


def attn_masks_for_core(c):
    out = np.zeros((8, 128, 512), np.float32)
    io = np.arange(128)
    for r in range(8):
        m = (io[:, None] + (1 if r > c else 0) <= io[None, :]).astype(np.float32)
        out[r] = np.tile(m, (1, 4))
    return out.astype(bfnp)


def _selk_table():
    t = np.zeros((128, 16, 32), np.float32)
    for k in range(16):
        for dd in range(128):
            t[dd, k, (128 * k + dd) // 64 % 32] = 1.0
    return np.ascontiguousarray(t.reshape(128, 16 * 32))


def _exp32_table():
    t = np.zeros((8, 32), np.float32)
    for h in range(32):
        t[h // 4, h] = 1.0
    return t


def emit_attn0(nc, blk, C, sb, ar, pb, io, sfx, debug=False):
    if "t_selk" not in C:
        C["t_selk"] = nc.inline_tensor(_selk_table(), "t_selk")
        C["t_exp32"] = nc.inline_tensor(_exp32_table(), "t_exp32")
    x_in = io["x_in"]
    y_out = io["y_out"]
    Wq = io["Wq"]
    Wk = io["Wk"]
    Wv = io["Wv"]
    Wo = io["Wo"]
    cs_in = io["cs_t"]
    mk_in = io["masks"]

    kag_i = nc.dram_tensor("kag_i" + sfx, [521, TOK], BF16, kind="Internal").ap()
    kag_o = nc.dram_tensor("kag_o" + sfx, [8 * 521, TOK], BF16, kind="Internal", addr_space="Shared").ap()
    vag_i = nc.dram_tensor("vag_i" + sfx, [TOK, 520], BF16, kind="Internal").ap()
    vag_o = nc.dram_tensor("vag_o" + sfx, [8 * TOK, 520], BF16, kind="Internal", addr_space="Shared").ap()
    mub_d = nc.dram_tensor("mub_d" + sfx, [NQH, TOK], BF16, kind="Internal").ap()
    if debug:
        dbg_qp = io["dbg_qp"]
        dbg_wq = io["dbg_wq"]
        dbg_xb = io["dbg_xb"]
        dbg_kt = io["dbg_kt"]
        dbg_vt = io["dbg_vt"]
        dbg_qr = io["dbg_qr"]
        dbg_ctx = io["dbg_ctx"]

    s_wq2 = [Sem(nc, f"s_wq2{i}{sfx}") for i in range(2)]
    s_wo2 = [Sem(nc, f"s_wo2{i}{sfx}") for i in range(2)]
    s_kl2 = [Sem(nc, f"s_kl2{i}{sfx}") for i in range(2)]
    s_vl2 = Sem(nc, "s_vl2" + sfx)
    s_mk2 = Sem(nc, "s_mk2" + sfx)
    s_mub2 = Sem(nc, "s_mub2" + sfx)
    s_kv2 = Sem(nc, "s_kv2" + sfx)
    s_x2 = [Sem(nc, f"s_x2{i}{sfx}") for i in range(2)]
    s_y2 = [Sem(nc, f"s_y2{i}{sfx}") for i in range(2)]
    s_agk = Sem(nc, "s_agk" + sfx)
    s_agv = Sem(nc, "s_agv" + sfx)

    BIGA = ar.alloc([128, KC * TOK], F32)
    Xm = BIGA[:, :]
    BAb = BIGA[:, :].bitcast(BF16)
    XB = BAb[:, 0:16384]
    VT = BAb[:, 16384:16384 + 8 * 520]
    KT = BAb[0:65, 20544:20544 + 8 * TOK]
    CTX = BAb[:, 0:16384]
    KL = [BAb[0:65, 16384 + i * 4096: 16384 + (i + 1) * 4096] for i in range(2)]
    PT = [BAb[:, 24576 + i * 512: 24576 + (i + 1) * 512] for i in range(8)]

    WBUF = ar.alloc([128, 20736], BF16)
    Wk_sb = WBUF[:, 0:8192]
    Wv_sb = WBUF[:, 8192:16384]
    Wqs = [WBUF[:, 16384 + i * 2048:16384 + (i + 1) * 2048] for i in range(2)]
    VL = WBUF[:, 0:16640]
    Wos = [WBUF[:, 16640 + i * 2048:16640 + (i + 1) * 2048] for i in range(2)]

    QR = ar.alloc([128, NQH * TOK], BF16)[0:65, :]
    QRf = QR[:, :].bitcast(F32)
    rows_scr = QRf[0:1, 0:6 * TOK]

    MK = ar.alloc([128, 8 * 512], BF16)
    SCR2 = ar.alloc([128, 3072], F32)
    sc = {"MB": SCR2[:, 0:1024], "RB": SCR2[:, 1024:2048], "TMP": SCR2[:, 2048:3072],
          "rows": rows_scr}
    ctmp = SCR2[0:64, 2048:2560]
    zr = SCR2[0:1, 1024:1536]
    zr2 = SCR2[0:1, 1536:2048]
    SQ0 = ar.alloc([128, 1024], F32)
    SQ1 = ar.alloc([128, 1024], F32)
    sc["SQ0"] = SQ0[:, :]
    sc["SQ1"] = SQ1[:, :]
    QP = ar.alloc([128, 1024], F32)
    QN = ar.alloc([128, 1024], F32)[0:32, :]
    M32 = SQ0[0:32, :]
    M32b = SQ1[0:32, :].bitcast(BF16)[:, 0:1024]
    CS = MK[:, 0:4096].bitcast(F32)
    K8 = SCR2[0:8, 1024:1032]
    KV8 = SCR2[0:8, 1032:1033]
    KV32 = SCR2[0:32, 1040:1041]
    EXP32 = SCR2[0:8, 1056:1088]
    NRMX = SCR2[0:1, 1088:1096]
    SELK = ar.alloc([128, 512], F32)
    KRW = SCR2[0:1, 2048:3072]

    s_ld, s_gw, s_pe = C["s_ld"], C["s_gw"], C["s_pe"]
    s_act, s_dve, s_pool, s_ag = C["s_act"], C["s_dve"], C["s_pool"], C["s_ag"]
    s_out = C["s_out"]

    if True:
        emit_load_master(nc, blk, C, Xm, x_in)

        def f_ld2(sy):
            s_ld.inc(sy.dma_start(out=CS[:, :], in_=cs_in), 16)
            s_ld.inc(sy.dma_start(out=SELK[:, :], in_=C["t_selk"].ap()), 16)
            s_ld.inc(sy.dma_start(out=EXP32[:, :], in_=C["t_exp32"].ap()), 16)
        blk.sync(f_ld2)

        def f_ld3(gp):
            s_gw.inc(gp.dma_start(
                out=Wk_sb.rearrange("p (kc v) -> p kc v", v=512),
                in_=Wk.rearrange("(kc p) v -> p kc v", p=128)), 16)
            s_gw.inc(gp.dma_start(
                out=Wv_sb.rearrange("p (kc v) -> p kc v", v=512),
                in_=Wv.rearrange("(kc p) v -> p kc v", p=128)), 16)
        blk.gpsimd(f_ld3)
        wkv_mark = s_gw.n

        xb_mark, ln_bank = emit_ln(nc, blk, C, sb, Xm, XB, [pb[0], pb[1]], sc)

        pe_m = {}
        act_m = {}
        dve_m = {}

        def f_v_init(v):
            s_dve.wait(v, xb_mark)
            ins = v.memset(
                VT.rearrange("p (tc g e) -> p tc g e", g=8, e=65)[:, :, :, 64:65], 1.0)
            dve_m["vt_ones"] = s_dve.inc(ins)
        blk.vector(f_v_init)

        for tc in range(8):
            def f_pe_v(t, tc=tc):
                if tc == 0:
                    s_dve.wait(t, ln_bank)
                    t.wait_ge(s_gw.h, wkv_mark)
                    s_dve.wait(t, xb_mark)
                else:
                    s_dve.wait(t, dve_m[("vte", tc - 1)])
                for k in range(KC):
                    ins = t.matmul(pb[2 + tc % 2][:, :],
                                   XB[:, k * TOK + tc * 128:k * TOK + (tc + 1) * 128],
                                   Wv_sb[:, k * 512:(k + 1) * 512],
                                   start=(k == 0), stop=(k == KC - 1))
                pe_m[("v", tc)] = s_pe.inc(ins)
            blk.tensor(f_pe_v)

            def f_dve_v(v, tc=tc):
                s_pe.wait(v, pe_m[("v", tc)])
                if tc == 0:
                    pass
                dst = VT[:, tc * 520:(tc + 1) * 520] \
                    .rearrange("p (g e) -> p g e", e=65)[:, :, 0:64]
                src = pb[2 + tc % 2][:, :].rearrange("p (g e) -> p g e", e=64)
                ins = v.tensor_copy(dst, src)
                dve_m[("vte", tc)] = s_dve.inc(ins)
            blk.vector(f_dve_v)

        def f_vout(sy):
            s_dve.wait(sy, dve_m[("vte", 7)])
            s_dve.wait(sy, dve_m["vt_ones"])
            s_kv2.inc(sy.dma_start(
                out=vag_i.rearrange("(tc p) ge -> p tc ge", p=128),
                in_=VT.rearrange("p (tc ge) -> p tc ge", ge=520)), 16)
        blk.sync(f_vout)
        vout_mark = s_kv2.n

        def f_k_init(v):
            ins = v.memset(KT[64:65, :].bitcast(BF16), 1.0)
            dve_m["kt_ones"] = s_dve.inc(ins)
        blk.vector(f_k_init)

        for kc2 in range(4):
            def f_pe_k(t, kc2=kc2):
                if kc2 > 0:
                    s_act.wait(t, act_m[("kqp", kc2 - 1)])
                for th in range(2):
                    for k in range(KC):
                        ins = t.matmul(pb[4 + th][:, :],
                                       Wk_sb[:, k * 512 + kc2 * 128:k * 512 + kc2 * 128 + 128],
                                       XB[:, k * TOK + th * 512:k * TOK + (th + 1) * 512],
                                       start=(k == 0), stop=(k == KC - 1))
                    pe_m[("k", kc2, th)] = s_pe.inc(ins)
            blk.tensor(f_pe_k)

            def f_act_k(a, kc2=kc2):
                for th in range(2):
                    s_pe.wait(a, pe_m[("k", kc2, th)])
                    if kc2 > 0:
                        s_dve.wait(a, dve_m[("krope", kc2 - 1)])
                        s_act.wait(a, act_m[("ksq", kc2 - 1)])
                    ins = a.activation(QP[:, th * 512:(th + 1) * 512],
                                       pb[4 + th][:, :], AF.Copy)
                act_m[("kqp", kc2)] = s_act.inc(ins)
            blk.scalar(f_act_k)

            def f_dve_k(v, kc2=kc2):
                s_act.wait(v, act_m[("kqp", kc2)])
                t1 = sc["TMP"][0:32, :]
                t2 = SCR2[0:32, 0:1024]
                for e in range(2):
                    gidx = 2 * kc2 + e
                    qa = QP[64 * e:64 * e + 32, :]
                    qb = QP[64 * e + 32:64 * e + 64, :]
                    cs_c = CS[64 * e:64 * e + 32, 0:1024]
                    cs_s = CS[64 * e + 32:64 * e + 64, 0:1024]
                    cs_s2 = CS[64 * e:64 * e + 32, 1024:2048]
                    cs_c2 = CS[64 * e + 32:64 * e + 64, 1024:2048]
                    kt_a = KT[0:32, gidx * TOK:(gidx + 1) * TOK]
                    kt_b = KT[32:64, gidx * TOK:(gidx + 1) * TOK]
                    v.tensor_tensor(t1, qa, cs_c, ALU.mult)
                    v.tensor_tensor(t2, qb, cs_s, ALU.mult)
                    v.drain()
                    v.tensor_tensor(kt_a, t1, t2, ALU.subtract)
                    v.tensor_tensor(t1, qb, cs_c2, ALU.mult)
                    v.tensor_tensor(t2, qa, cs_s2, ALU.mult)
                    v.drain()
                    ins = v.tensor_tensor(kt_b, t1, t2, ALU.add)
                dve_m[("krope", kc2)] = s_dve.inc(ins)
            blk.vector(f_dve_k)

            def f_act_k2(a, kc2=kc2):
                s_act.wait(a, act_m[("kqp", kc2)])
                if kc2 > 0:
                    s_pe.wait(a, pe_m[("knrm", kc2 - 1, 1, 1)])
                ins = a.activation(SQ0[:, :], QP[:, :], AF.Square)
                act_m[("ksq", kc2)] = s_act.inc(ins)
            blk.scalar(f_act_k2)

            for e in range(2):
                def f_pe_k2(t, kc2=kc2, e=e):
                    s_act.wait(t, act_m[("ksq", kc2)])
                    if (kc2, e) != (0, 0):
                        prev = (kc2, e - 1) if e == 1 else (kc2 - 1, 1)
                        s_dve.wait(t, dve_m[("knc", prev[0], prev[1])])
                    ins = t.matmul(pb[6][0:1, :], SELK[:, e:e + 1],
                                   SQ0[:, 0:512], start=True, stop=True)
                    pe_m[("knrm", kc2, e, 0)] = s_pe.inc(ins)
                    ins = t.matmul(pb[7][0:1, :], SELK[:, e:e + 1],
                                   SQ0[:, 512:1024], start=True, stop=True)
                    pe_m[("knrm", kc2, e, 1)] = s_pe.inc(ins)
                blk.tensor(f_pe_k2)

                def f_dve_k2(v, kc2=kc2, e=e):
                    gidx = 2 * kc2 + e
                    s_pe.wait(v, pe_m[("knrm", kc2, e, 0)])
                    v.tensor_copy(KRW[:, 0:512], pb[6][0:1, :])
                    s_pe.wait(v, pe_m[("knrm", kc2, e, 1)])
                    v.tensor_copy(KRW[:, 512:1024], pb[7][0:1, :])
                    v.drain()
                    ins = v.reduce_max(NRMX[:, gidx:gidx + 1], KRW[:, :], axis=AX.X)
                    dve_m[("knc", kc2, e)] = s_dve.inc(ins)
                blk.vector(f_dve_k2)

        def f_kout(sy):
            s_dve.wait(sy, dve_m[("krope", 3)])
            s_dve.wait(sy, dve_m["kt_ones"])
            s_kv2.inc(sy.dma_start(
                out=kag_i[0:520, :].rearrange("(g dd) t -> dd g t", dd=65),
                in_=KT.rearrange("dd (g t) -> dd g t", t=TOK)), 16)
        blk.sync(f_kout)
        kout_mark = s_kv2.n

        def f_nout(gp):
            s_dve.wait(gp, dve_m[("knc", 3, 1)])
            s_gw.inc(gp.dma_start(out=kag_i[520:521, 0:8], in_=NRMX[:, :]), 16)
        blk.gpsimd(f_nout)
        nout_mark = s_gw.n

        dbg_mark = [0]
        dbg_m2 = [0]
        if debug:
            def f_dbg1(sy):
                s_dve.wait(sy, dve_m[("krope", 3)])
                s_dve.wait(sy, dve_m[("vte", 7)])
                s_out.inc(sy.dma_start(out=dbg_xb.rearrange("(kc p) t -> p kc t", p=128),
                                       in_=XB.rearrange("p (kc t) -> p kc t", t=TOK)), 16)
                s_out.inc(sy.dma_start(out=dbg_kt, in_=KT), 16)
                dbg_mark[0] = s_out.inc(sy.dma_start(out=dbg_vt, in_=VT), 16)
            blk.sync(f_dbg1)

        def f_ag(gp):
            gp.wait_ge(s_kv2.h, kout_mark)
            gp.wait_ge(s_gw.h, nout_mark)
            ins = gp.collective_compute(
                "AllGather", ALU.bypass, replica_groups=[list(range(8))],
                ins=[kag_i], outs=[kag_o])
            s_agk.inc(ins, 1)
            ins = gp.collective_compute(
                "AllGather", ALU.bypass, replica_groups=[list(range(8))],
                ins=[vag_i], outs=[vag_o])
            s_agv.inc(ins, 1)
        blk.gpsimd(f_ag)

        for qc in range(16):
            par = qc % 2

            def f_gw_q(gp, qc=qc, par=par):
                if qc >= 2:
                    s_pe.wait(gp, pe_m[("q", qc - 2, 1)])
                    if debug and qc == 2:
                        gp.wait_ge(s_out.h, dbg_m2[0])
                src = Wq[:, qc * 128:(qc + 1) * 128].rearrange(
                    "(kc p) o -> p kc o", p=128)
                dst = Wqs[par].rearrange("p (kc o) -> p kc o", o=128)
                pe_m[("qw", qc)] = s_wq2[par].inc(gp.dma_start(out=dst, in_=src), 16)
            blk.gpsimd(f_gw_q)

            def f_pe_q(t, qc=qc, par=par):
                t.wait_ge(s_wq2[par].h, pe_m[("qw", qc)])
                if qc == 0:
                    s_dve.wait(t, dve_m[("vte", 7)])
                else:
                    s_act.wait(t, act_m[("qqp", qc - 1)])
                for th in range(2):
                    for k in range(KC):
                        ins = t.matmul(pb[2 + th][:, :],
                                       Wqs[par][:, k * 128:(k + 1) * 128],
                                       XB[:, k * TOK + th * 512:k * TOK + (th + 1) * 512],
                                       start=(k == 0), stop=(k == KC - 1))
                    pe_m[("q", qc, th)] = s_pe.inc(ins)
            blk.tensor(f_pe_q)

            def f_act_q(a, qc=qc):
                for th in range(2):
                    s_pe.wait(a, pe_m[("q", qc, th)])
                    if qc > 0:
                        s_dve.wait(a, dve_m[("qrope", qc - 1)])
                        s_act.wait(a, act_m[("qsq", qc - 1)])
                    ins = a.activation(QP[:, th * 512:(th + 1) * 512],
                                       pb[2 + th][:, :], AF.Copy, scale=0.125)
                act_m[("qqp", qc)] = s_act.inc(ins)
            blk.scalar(f_act_q)

            if debug and qc == 0:
                def f_dbg_qp(sy):
                    s_act.wait(sy, act_m[("qqp", 0)])
                    s_out.inc(sy.dma_start(out=dbg_wq, in_=Wqs[0][:, :]), 16)
                    dbg_m2[0] = s_out.inc(sy.dma_start(out=dbg_qp, in_=QP[:, :]), 16)
                blk.sync(f_dbg_qp)

            def f_dve_q(v, qc=qc):
                if debug and qc == 0:
                    v.wait_ge(s_out.h, dbg_m2[0])
                s_act.wait(v, act_m[("qqp", qc)])
                t1 = sc["TMP"][0:32, :]
                t2 = SCR2[0:32, 0:1024]
                for e in range(2):
                    h = 2 * qc + e
                    qa = QP[64 * e:64 * e + 32, :]
                    qb = QP[64 * e + 32:64 * e + 64, :]
                    cs_c = CS[64 * e:64 * e + 32, 0:1024]
                    cs_s = CS[64 * e + 32:64 * e + 64, 0:1024]
                    cs_s2 = CS[64 * e:64 * e + 32, 1024:2048]
                    cs_c2 = CS[64 * e + 32:64 * e + 64, 1024:2048]
                    qt_a = QR[0:32, h * TOK:(h + 1) * TOK]
                    qt_b = QR[32:64, h * TOK:(h + 1) * TOK]
                    v.tensor_tensor(t1, qa, cs_c, ALU.mult)
                    v.tensor_tensor(t2, qb, cs_s, ALU.mult)
                    v.drain()
                    v.tensor_tensor(qt_a, t1, t2, ALU.subtract)
                    v.tensor_tensor(t1, qb, cs_c2, ALU.mult)
                    v.tensor_tensor(t2, qa, cs_s2, ALU.mult)
                    v.drain()
                    ins = v.tensor_tensor(qt_b, t1, t2, ALU.add)
                dve_m[("qrope", qc)] = s_dve.inc(ins)
            blk.vector(f_dve_q)

            def f_act_q2(a, qc=qc):
                s_act.wait(a, act_m[("qqp", qc)])
                if qc >= 2:
                    s_pe.wait(a, pe_m[("qn", qc - 2)])
                SQx = SQ0 if qc % 2 == 0 else SQ1
                ins = a.activation(SQx[:, :], QP[:, :], AF.Square)
                act_m[("qsq", qc)] = s_act.inc(ins)
            blk.scalar(f_act_q2)

            def f_pe_q2(t, qc=qc):
                s_act.wait(t, act_m[("qsq", qc)])
                if qc == 0:
                    s_dve.wait(t, dve_m[("knc", 3, 1)])
                SQx = SQ0 if qc % 2 == 0 else SQ1
                for th in range(2):
                    ins = t.matmul(pb[6 + th][0:32, :],
                                   SELK[:, qc * 32:(qc + 1) * 32],
                                   SQx[:, th * 512:(th + 1) * 512],
                                   start=(qc == 0), stop=(qc == KC - 1))
                pe_m[("qn", qc)] = s_pe.inc(ins)
            blk.tensor(f_pe_q2)

        qproj_done = pe_m[("q", 15, 1)]

        def f_dve_qn(v):
            s_pe.wait(v, pe_m[("qn", 15)])
            v.tensor_copy(QN[:, 0:512], pb[6][0:32, :])
            ins = v.tensor_copy(QN[:, 512:1024], pb[7][0:32, :])
            dve_m["qn"] = s_dve.inc(ins)
        blk.vector(f_dve_qn)

        def f_mkld(sy):
            s_dve.wait(sy, dve_m[("qrope", 15)])
            s_mk2.inc(sy.dma_start(out=MK[:, :].rearrange("p (r q) -> p r q", q=512),
                                   in_=mk_in.rearrange("r p q -> p r q")), 16)
        blk.sync(f_mkld)
        mk_mark = s_mk2.n

        def f_km1(gp):
            gp.wait_ge(s_agk.h, 1)
            src = kag_o[:, 0:8].rearrange("(r q) h -> r q h", q=521)[:, 520, :]
            s_gw.inc(gp.dma_start(out=K8[:, :], in_=src), 16)
        blk.gpsimd(f_km1)
        km1_mark = s_gw.n

        def f_pe_km(t):
            t.wait_ge(s_gw.h, km1_mark)
            ins = t.transpose(pb[4][0:8, 0:8], K8[:, :], sb["idn_f"][0:8, 0:8])
            pe_m["k8t"] = s_pe.inc(ins)
        blk.tensor(f_pe_km)

        def f_dve_km(v):
            s_pe.wait(v, pe_m["k8t"])
            v.tensor_copy(K8[:, :], pb[4][0:8, 0:8])
            v.drain()
            ins = v.reduce_max(KV8[:, :], K8[:, :], axis=AX.X)
            dve_m["kv8"] = s_dve.inc(ins)
        blk.vector(f_dve_km)

        def f_pe_km2(t):
            s_dve.wait(t, dve_m["kv8"])
            ins = t.matmul(pb[5][0:32, 0:1], EXP32[:, :], KV8[:, :],
                           start=True, stop=True)
            pe_m["kv32"] = s_pe.inc(ins)
        blk.tensor(f_pe_km2)

        def f_dve_km2(v):
            s_pe.wait(v, pe_m["kv32"])
            v.tensor_copy(KV32[:, :], pb[5][0:32, 0:1])
            s_dve.wait(v, dve_m["qn"])
            s_pe.wait(v, pe_m[("qn", 15)])
            v.drain()
            ins = v.tensor_scalar_mul(M32[:, :], QN[:, :], KV32[:, :])
            dve_m["m32a"] = s_dve.inc(ins)
        blk.vector(f_dve_km2)

        def f_act_km(a):
            s_dve.wait(a, dve_m["m32a"])
            ins = a.activation(M32[:, :], M32[:, :], AF.Sqrt)
            act_m["m32s"] = s_act.inc(ins)
        blk.scalar(f_act_km)

        def f_dve_km3(v):
            s_act.wait(v, act_m["m32s"])
            v.tensor_scalar_mul(M32[:, :], M32[:, :], -1.0)
            v.drain()
            ins = v.tensor_copy(M32b[:, :], M32[:, :])
            dve_m["m32b"] = s_dve.inc(ins)
        blk.vector(f_dve_km3)

        def f_mub(sy):
            s_dve.wait(sy, dve_m["m32b"])
            s_mub2.inc(sy.dma_start(out=mub_d, in_=M32b[:, :]), 16)
            sy.wait_ge(s_mub2.h, s_mub2.n)
            s_mub2.inc(sy.dma_start(out=QR[64:65, :],
                                    in_=mub_d.rearrange("h t -> (h t)")), 16)
        blk.sync(f_mub)
        mub_mark = s_mub2.n

        kl_mark = {}
        vl_mark = {}

        groups = []
        _ci = 0
        for b in range(2):
            for g in range(8):
                for qt in range(4):
                    blocks = [(a_, r) for a_ in range(qt + 1) for r in range(8)]
                    for j in range(0, len(blocks), 4):
                        grp = blocks[j:j + 4]
                        groups.append(dict(
                            b=b, g=g, qt=qt, par=(b * 8 + g) % 2,
                            blocks=[(a_, r, _ci + jj)
                                    for jj, (a_, r) in enumerate(grp)],
                            qt_first=(j == 0), qt_last=(j + 4 == len(blocks)),
                            masked=(grp[0][0] == qt)))
                        _ci += 4
        ngrp = len(groups)

        def emit_vl(b):
            def f_vl(sy, b=b):
                sy.wait_ge(s_agv.h, 1)
                s_pe.wait(sy, qproj_done)
                if b > 0:
                    s_pe.wait(sy, pe_m[("avl", b - 1)])
                for aa in range(4):
                    src = vag_o.rearrange("(r aa p) e -> p r aa e", p=128, aa=8)[
                        :, :, 4 * b + aa, :]
                    dst = VL.rearrange("p (r aa e) -> p r aa e", r=8, e=520)[
                        :, :, aa, :]
                    vl_mark[b] = s_vl2.inc(sy.dma_start(out=dst, in_=src), 16)
            blk.sync(f_vl)

        def emit_kl(b, g):
            def f_kl(sy, b=b, g=g):
                sy.wait_ge(s_agk.h, 1)
                bg = b * 8 + g
                if bg < 2:
                    sy.wait_ge(s_mub2.h, mub_mark)
                    if debug:
                        sy.wait_ge(s_out.h, dbg_mark[0])
                if bg >= 2:
                    s_pe.wait(sy, pe_m[("kl_done", bg - 2)])
                src = kag_o.rearrange("(r z) i -> z r i", z=521)[
                    65 * g:65 * g + 65, :, b * 512:(b + 1) * 512]
                dst = KL[(b * 8 + g) % 2].rearrange("dd (r i) -> dd r i", i=512)
                kl_mark[(b, g)] = s_kl2[(b * 8 + g) % 2].inc(
                    sy.dma_start(out=dst, in_=src), 16)
            blk.sync(f_kl)

        def emit_S(k):
            G = groups[k]
            def f_pe_s(t, G=G, k=k):
                if k > 0:
                    s_act.wait(t, act_m[("eg", k - 1)])
                t.wait_ge(s_kl2[G["par"]].h, kl_mark[(G["b"], G["g"])])
                for (a_, r, ci) in G["blocks"]:
                    if ci == 0:
                        t.wait_ge(s_mub2.h, mub_mark)
                    rhs = QR[0:65, :].rearrange("dd (h i) -> dd h i", i=TOK)[
                        :, 4 * G["g"]:4 * G["g"] + 4,
                        G["b"] * 512 + G["qt"] * 128:
                        G["b"] * 512 + (G["qt"] + 1) * 128]
                    ins = t.matmul(pb[4 + ci % 4][:, :],
                                   KL[G["par"]][:, r * 512 + a_ * 128:
                                                r * 512 + (a_ + 1) * 128],
                                   rhs, start=True, stop=True)
                pe_m[("sg", k)] = s_pe.inc(ins)
            blk.tensor(f_pe_s)

        def emit_E(k):
            G = groups[k]
            def f_act_e(a, G=G, k=k):
                s_pe.wait(a, pe_m[("sg", k)])
                if k >= 2:
                    s_pe.wait(a, pe_m[("ag", k - 2)])
                for (a_, r, ci) in G["blocks"]:
                    ins = a.activation(PT[ci % 8][:, :], pb[4 + ci % 4][:, :],
                                       AF.Exp)
                act_m[("eg", k)] = s_act.inc(ins)
            blk.scalar(f_act_e)
            if G["masked"]:
                def f_dve_mk(v, G=G, k=k):
                    v.wait_ge(s_mk2.h, mk_mark)
                    s_act.wait(v, act_m[("eg", k)])
                    for (a_, r, ci) in G["blocks"]:
                        ins = v.tensor_tensor(
                            PT[ci % 8][:, :], PT[ci % 8][:, :],
                            MK[:, r * 512:(r + 1) * 512], ALU.mult)
                    dve_m[("mg", k)] = s_dve.inc(ins)
                blk.vector(f_dve_mk)

        def emit_A(k):
            G = groups[k]
            qn = (G["b"] * 8 + G["g"]) * 4 + G["qt"]
            def f_pe_av(t, G=G, k=k, qn=qn):
                if G["masked"]:
                    s_dve.wait(t, dve_m[("mg", k)])
                else:
                    s_act.wait(t, act_m[("eg", k)])
                t.wait_ge(s_vl2.h, vl_mark[G["b"]])
                if G["qt_first"] and qn >= 1:
                    s_dve.wait(t, dve_m[("zi", qn - 1)])
                for idx, (a_, r, ci) in enumerate(G["blocks"]):
                    ins = t.matmul(pb[2][0:65, :],
                                   VL[:, (r * 4 + a_) * 520 + 65 * G["g"]:
                                      (r * 4 + a_) * 520 + 65 * G["g"] + 65],
                                   PT[ci % 8][:, :],
                                   start=(G["qt_first"] and idx == 0),
                                   stop=(G["qt_last"] and idx == 3))
                pe_m[("ag", k)] = s_pe.inc(ins)
            blk.tensor(f_pe_av)
            if G["qt_last"]:
                def f_dve_ev(v, qn=qn, k=k):
                    s_pe.wait(v, pe_m[("ag", k)])
                    v.tensor_copy(ctmp[:, :], pb[2][0:64, :])
                    ins = v.tensor_copy(zr[:, :], pb[2][64:65, :])
                    dve_m[("zi", qn)] = s_dve.inc(ins)
                    v.drain()
                    ins = v.reciprocal(zr2[:, :], zr[:, :])
                    dve_m[("zrr", qn)] = s_dve.inc(ins)
                blk.vector(f_dve_ev)

        def emit_Z(qn):
            b, g, qt = qn // 32, (qn // 4) % 8, qn % 4
            def f_pe_z(t, qn=qn):
                s_dve.wait(t, dve_m[("zrr", qn)])
                if qn >= 1:
                    s_dve.wait(t, dve_m[("ctxw", qn - 1)])
                ins = t.matmul(pb[3][0:64, :], sb["ones_m"][:, 0:64],
                               zr2[:, :], start=True, stop=True)
                pe_m[("z", qn)] = s_pe.inc(ins)
            blk.tensor(f_pe_z)

            def f_dve_ev2(v, qn=qn, b=b, g=g, qt=qt):
                s_pe.wait(v, pe_m[("z", qn)])
                for hp in range(4):
                    h = 4 * g + hp
                    dst = CTX[64 * (h % 2):64 * (h % 2) + 64,
                              (h // 2) * TOK + b * 512 + qt * 128:
                              (h // 2) * TOK + b * 512 + (qt + 1) * 128]
                    ins = v.tensor_tensor(
                        dst, ctmp[:, hp * 128:(hp + 1) * 128],
                        pb[3][0:64, hp * 128:(hp + 1) * 128], ALU.mult)
                dve_m[("ctxw", qn)] = s_dve.inc(ins)
            blk.vector(f_dve_ev2)

        emit_vl(0)
        cur_bg = None
        for k in range(ngrp + 1):
            if k < ngrp:
                G = groups[k]
                if (G["b"], G["g"]) != cur_bg:
                    cur_bg = (G["b"], G["g"])
                    emit_kl(*cur_bg)
                emit_S(k)
                emit_E(k)
                nxt = groups[k + 1] if k + 1 < ngrp else None
                if nxt is None or (nxt["b"], nxt["g"]) != (G["b"], G["g"]):
                    pe_m[("kl_done", G["b"] * 8 + G["g"])] = pe_m[("sg", k)]
            if k >= 1:
                Gp = groups[k - 1]
                qn_p = (Gp["b"] * 8 + Gp["g"]) * 4 + Gp["qt"]
                if (not Gp["qt_first"]) and k >= 2 and groups[k - 2]["qt_first"] \
                        and qn_p >= 1:
                    emit_Z(qn_p - 1)
                emit_A(k - 1)
                if Gp["qt_last"] and qn_p == 31:
                    pe_m[("avl", 0)] = pe_m[("ag", k - 1)]
                    emit_vl(1)
        pe_m[("avl", 1)] = pe_m[("ag", ngrp - 1)]
        emit_Z(63)

        ctx_all = dve_m[("ctxw", 63)]

        oc_mark = {}
        for hc in range(16):
            par = hc % 2

            def f_gw_o(gp, hc=hc, par=par):
                s_pe.wait(gp, qproj_done)
                if hc >= 2:
                    s_pe.wait(gp, pe_m[("o", hc - 2, 1)])
                src = Wo[:, hc * 128:(hc + 1) * 128].rearrange(
                    "(ec p) o -> p ec o", p=128)
                dst = Wos[par].rearrange("p (ec o) -> p ec o", o=128)
                oc_mark[hc] = s_wo2[par].inc(gp.dma_start(out=dst, in_=src), 16)
            blk.gpsimd(f_gw_o)

            def f_xr(sy, hc=hc):
                if hc >= 1:
                    s_dve.wait(sy, dve_m.get(("yt", hc - 1, 1), 0))
                for th in range(2):
                    src = x_in.rearrange("(kc p) t -> p kc t", p=128)[
                        :, hc, th * 512:(th + 1) * 512]
                    oc_mark[("x", hc, th)] = s_x2[th].inc(
                        sy.dma_start(out=(SQ0 if th == 0 else SQ1)[:, 0:512],
                                     in_=src), 16)
            blk.sync(f_xr)

            for th in range(2):
                def f_pe_o(t, hc=hc, th=th, par=par):
                    t.wait_ge(s_wo2[par].h, oc_mark[hc])
                    if hc == 0 and th == 0:
                        s_dve.wait(t, ctx_all)
                    prev = (hc, th - 1) if th == 1 else (hc - 1, 1)
                    if hc > 0 or th > 0:
                        s_dve.wait(t, dve_m.get(("yt",) + prev, 0))
                    for ec in range(KC):
                        ins = t.matmul(pb[4 + th][:, :],
                                       Wos[par][:, ec * 128:(ec + 1) * 128],
                                       CTX[:, ec * TOK + th * 512:
                                           ec * TOK + (th + 1) * 512],
                                       start=(ec == 0), stop=(ec == KC - 1))
                    pe_m[("o", hc, th)] = s_pe.inc(ins)
                blk.tensor(f_pe_o)

                def f_dve_o(v, hc=hc, th=th):
                    s_pe.wait(v, pe_m[("o", hc, th)])
                    v.wait_ge(s_x2[th].h, oc_mark[("x", hc, th)])
                    if hc > 0:
                        v.wait_ge(s_y2[th].h, oc_mark.get(("y", hc - 1, th), 0))
                    xr = (SQ0 if th == 0 else SQ1)[:, 0:512]
                    yt = (SQ0 if th == 0 else SQ1)[:, 512:1024]
                    ins = v.scalar_tensor_tensor(yt, pb[4 + th][:, :], 1.0, xr,
                                                 ALU.mult, ALU.add)
                    dve_m[("yt", hc, th)] = s_dve.inc(ins)
                blk.vector(f_dve_o)

                def f_yo(sy, hc=hc, th=th):
                    s_dve.wait(sy, dve_m[("yt", hc, th)])
                    yt = (SQ0 if th == 0 else SQ1)[:, 512:1024]
                    dst = y_out.rearrange("(kc p) t -> p kc t", p=128)[
                        :, hc, th * 512:(th + 1) * 512]
                    oc_mark[("y", hc, th)] = s_y2[th].inc(sy.dma_start(out=dst, in_=yt), 16)
                    s_out.inc(sy.nop(), 0) if False else None
                blk.sync(f_yo)

        if debug:
            def f_dbg2(sy):
                sy.wait_ge(s_mub2.h, mub_mark)
                s_out.inc(sy.dma_start(out=dbg_qr, in_=QR[:, :]), 16)
                s_dve.wait(sy, ctx_all)
                s_out.inc(sy.dma_start(out=dbg_ctx, in_=CTX), 16)
            blk.sync(f_dbg2)

        def f_fin(sy):
            s_y2[0].wait(sy)
            s_y2[1].wait(sy)
            s_out.wait(sy)
        blk.sync(f_fin)


def attn0_io(nc, debug=False):
    io = {
        "x_in": nc.dram_tensor("x_fm", [D, TOK], F32, kind="ExternalInput").ap(),
        "y_out": nc.dram_tensor("y_fm", [D, TOK], F32, kind="ExternalOutput").ap(),
        "Wq": nc.dram_tensor("Wq", [D, D], F32, kind="ExternalInput").ap(),
        "Wk": nc.dram_tensor("Wk", [D, 512], F32, kind="ExternalInput").ap(),
        "Wv": nc.dram_tensor("Wv", [D, 512], F32, kind="ExternalInput").ap(),
        "Wo": nc.dram_tensor("Wo", [D, D], F32, kind="ExternalInput").ap(),
        "cs_t": nc.dram_tensor("cs_t", [128, 2048], F32, kind="ExternalInput").ap(),
        "masks": nc.dram_tensor("masks", [8, 128, 512], BF16, kind="ExternalInput").ap(),
    }
    if debug:
        io["dbg_qp"] = nc.dram_tensor("dbg_qp", [128, TOK], F32, kind="ExternalOutput").ap()
        io["dbg_wq"] = nc.dram_tensor("dbg_wq", [128, 2048], BF16, kind="ExternalOutput").ap()
        io["dbg_xb"] = nc.dram_tensor("dbg_xb", [D, TOK], BF16, kind="ExternalOutput").ap()
        io["dbg_kt"] = nc.dram_tensor("dbg_kt", [65, 8 * TOK], BF16, kind="ExternalOutput").ap()
        io["dbg_vt"] = nc.dram_tensor("dbg_vt", [128, 4160], BF16, kind="ExternalOutput").ap()
        io["dbg_qr"] = nc.dram_tensor("dbg_qr", [65, NQH * TOK], BF16, kind="ExternalOutput").ap()
        io["dbg_ctx"] = nc.dram_tensor("dbg_ctx", [128, 16384], BF16, kind="ExternalOutput").ap()
    return io


def build_attn0_program(debug=False):
    nc = bass.Bass("TRN2", target_bir_lowering=False)
    io = attn0_io(nc, debug)
    C = common_setup(nc)
    C["t_selk"] = nc.inline_tensor(_selk_table(), "t_selk")
    C["t_exp32"] = nc.inline_tensor(_exp32_table(), "t_exp32")
    pbk = [nc.alloc_psum_tensor(f"pbk{i}", [128, 512], F32) for i in range(8)]
    with nc.Block() as blk:
        sb = emit_load_consts(nc, blk, C)
        ar = new_arena(nc)
        emit_attn0(nc, blk, C, sb, ar, pbk, io, "_a0", debug=debug)
    return nc



def prep_ffn1_weights(d, c):
    mu = np.ascontiguousarray(np.asarray(d["L1_ffn_mu"]).reshape(D, 2048))
    w = {
        "gamma": np.ascontiguousarray(np.asarray(d["L1_ffn_gamma"]).reshape(KC, 128).T),
        "beta": np.ascontiguousarray(np.asarray(d["L1_ffn_beta"]).reshape(KC, 128).T),
        "mu_f": mu,
        "mu_sl": np.ascontiguousarray(mu[:, 256 * c:256 * (c + 1)]),
        "scale1": np.asarray(d["L1_ffn_scale"]).reshape(1, 1).astype(np.float32),
        "W1_sl": np.ascontiguousarray(np.asarray(d["L1_ffn_W1"])[2 * c:2 * c + 2]),
        "b1_sl": np.ascontiguousarray(
            np.asarray(d["L1_ffn_b1"])[2 * c:2 * c + 2].reshape(2, 4, 128).transpose(2, 0, 1).reshape(128, 8)),
        "W2_sl": np.ascontiguousarray(np.asarray(d["L1_ffn_W2"])[2 * c:2 * c + 2]),
        "b2_sl": np.ascontiguousarray(np.asarray(d["L1_ffn_b2"])[2 * c:2 * c + 2]),
    }
    return w


def emit_ffn1(nc, blk, C, sb, ar, pb, io, sfx, scale=0.5, debug=False):
    x_in = io["x_in"]
    y_out = io["y_out"]
    gam = io["gamma"]
    bet = io["beta"]
    muf = io["mu_f"]
    musl = io["mu_sl"]
    sc1 = io["scale1"]
    W1s = io["W1_sl"]
    b1s = io["b1_sl"]
    W2s = io["W2_sl"]
    b2s = io["b2_sl"]

    xag_i = nc.dram_tensor("xag_i" + sfx, [2 * D, TOK], BF16, kind="Internal").ap()
    xag_o = nc.dram_tensor("xag_o" + sfx, [8 * 2 * D, TOK], BF16, kind="Internal", addr_space="Shared").ap()
    slag_i = nc.dram_tensor("slag_i" + sfx, [512, D], BF16, kind="Internal").ap()
    slag_o = nc.dram_tensor("slag_o" + sfx, [8 * 512, D], BF16, kind="Internal", addr_space="Shared").ap()

    s_xl = [Sem(nc, f"s_xl{i}{sfx}") for i in range(2)]
    s_sl = [Sem(nc, f"s_sl{i}{sfx}") for i in range(2)]
    s_mu = [Sem(nc, f"s_mu{i}{sfx}") for i in range(2)]
    s_x2 = [Sem(nc, f"s_x2{i}{sfx}") for i in range(2)]
    s_y2 = [Sem(nc, f"s_y2{i}{sfx}") for i in range(2)]
    s_so = Sem(nc, "s_so" + sfx)
    s_agx = Sem(nc, "s_agx" + sfx)
    s_agsl = Sem(nc, "s_agsl" + sfx)

    BIGA = ar.alloc([128, KC * TOK], F32)
    Xm = BIGA[:, :]
    BAb = BIGA[:, :].bitcast(BF16)
    XB2 = BAb[:, 0:16384]
    XN = ar.alloc([128, KC * TOK], BF16)
    EXA = ar.alloc([128, 32 * 256], BF16)
    XLT = [ar.alloc([128, 2048], BF16) for i in range(2)]
    MUNS = ar.alloc([128, KC * 256], BF16)
    W1B = BAb[:, 0:8192]
    W2B = BAb[:, 8192:16384]
    B2B = BIGA[:, 8192:10240]
    PALL = BAb[:, 0:16384]
    SLT = [BAb[:, 16384 + i * 2048:16384 + (i + 1) * 2048] for i in range(2)]
    MUC = [BAb[:, 20480 + i * 2048:20480 + (i + 1) * 2048] for i in range(2)]
    ZIB = BIGA[:, 11264 + 1024:11264 + 2048]
    SCR2 = ar.alloc([128, 3072], F32)
    sc = {"MB": SCR2[:, 0:1024], "RB": SCR2[:, 1024:2048], "TMP": SCR2[:, 2048:3072]}
    SQ0 = ar.alloc([128, 1024], F32)
    SQ1 = ar.alloc([128, 1024], F32)
    sc["SQ0"] = SQ0[:, :]
    sc["SQ1"] = SQ1[:, :]
    XNf = XN[:, :].bitcast(F32)
    sc["rows"] = XNf[0:1, 0:6 * TOK]
    r_ss = SCR2[0:1, 1024:2048]
    r_z = SCR2[0:1, 2048:3072]
    cn_row = SCR2[0:1, 0:2048]
    zrow = SCR2[0:1, 2048:3072]
    SMALL = ar.alloc([128, 96], F32)
    sc_gb = SMALL[:, 32:64]
    s_scale = SMALL[0:1, 0:1]
    s_negabs = SMALL[0:1, 1:2]
    s_nb = SMALL[:, 2:3]
    ZC = SMALL[:, 4:6]
    ZBT = SCR2[:, 0:256]
    b2row = SCR2[0:1, 256:2304]
    B1C = SMALL[:, 8:16]
    RNC = SMALL[:, 16:32]
    HFM = ar.alloc([128, 1024], BF16)
    SLIN = ar.alloc([128, KC * 256], BF16)
    STG = ar.alloc([128, 2 * 512], BF16)

    s_ld, s_gw, s_pe = C["s_ld"], C["s_gw"], C["s_pe"]
    s_act, s_dve, s_out = C["s_act"], C["s_dve"], C["s_out"]

    if True:
        emit_load_master(nc, blk, C, Xm, x_in)

        def f_ld2(gp):
            s_gw.inc(gp.dma_start(
                out=MUNS[:, :].rearrange("p (kc e) -> p kc e", e=256),
                in_=musl.rearrange("(kc p) e -> p kc e", p=128)), 16)
            s_gw.inc(gp.dma_start(out=SMALL[0:1, 0:1], in_=sc1), 16)
            s_gw.inc(gp.dma_start(out=B1C[:, :], in_=b1s), 16)
        blk.gpsimd(f_ld2)
        ld2_mark = s_gw.n

        xb_mark, bank1 = emit_ln(nc, blk, C, sb, Xm, None, [pb[0], pb[1]], sc,
                                 XF=Xm)
        sc["gb"] = sc_gb
        xb2_mark, bank2 = emit_ln(nc, blk, C, sb, Xm, XB2, [pb[0], pb[1]], sc,
                                  x_ready=[(C["s_dve"], xb_mark)],
                                  gamma_ap=gam, beta_ap=bet, XF=None)

        pe_m = {}
        act_m = {}
        dve_m = {}

        for k in range(KC):
            def f_act_l2(a, k=k):
                if k == 0:
                    s_dve.wait(a, xb2_mark)
                else:
                    s_pe.wait(a, pe_m[("ss", k - 1)])
                act_m[("sq", k)] = C["s_act"].inc(
                    a.activation(SQ0[:, :], XB2[:, k * TOK:(k + 1) * TOK], AF.Square))
            blk.scalar(f_act_l2)

            def f_pe_l2(t, k=k):
                s_act.wait(t, act_m[("sq", k)])
                if k == 0:
                    s_dve.wait(t, bank2)
                for th in range(2):
                    ins = t.matmul(pb[th][0:1, :], sb["ones_k"][:, :],
                                   SQ0[:, th * 512:(th + 1) * 512],
                                   start=(k == 0), stop=(k == KC - 1))
                pe_m[("ss", k)] = s_pe.inc(ins)
            blk.tensor(f_pe_l2)

        def f_dve_l2(v):
            s_pe.wait(v, pe_m[("ss", KC - 1)])
            v.tensor_copy(r_ss[:, 0:512], pb[0][0:1, :])
            v.tensor_copy(r_ss[:, 512:1024], pb[1][0:1, :])
            v.drain()
            dve_m["ss"] = C["s_dve"].inc(v.engine_nop())
        blk.vector(f_dve_l2)

        def f_act_l2b(a):
            s_dve.wait(a, dve_m["ss"])
            act_m["sd"] = C["s_act"].inc(
                a.activation(r_ss[:, :], r_ss[:, :], AF.Sqrt))
        blk.scalar(f_act_l2b)

        def f_dve_l2b(v):
            s_act.wait(v, act_m["sd"])
            v.wait_ge(s_gw.h, ld2_mark)
            v.tensor_scalar_max(r_ss[:, :], r_ss[:, :], 1e-12)
            v.drain()
            v.reciprocal(r_z[:, :], r_ss[:, :])
            v.drain()
            ins = v.tensor_scalar_mul(r_z[:, :], r_z[:, :], s_scale)
            dve_m["rsc"] = C["s_dve"].inc(ins)
        blk.vector(f_dve_l2b)

        def f_pe_rb(t):
            s_dve.wait(t, dve_m["rsc"])
            for th in range(2):
                ins = t.matmul(pb[2 + th][:, :], sb["ones_m"][:, :],
                               r_z[:, th * 512:(th + 1) * 512], start=True, stop=True)
            pe_m["rb"] = s_pe.inc(ins)
        blk.tensor(f_pe_rb)

        def f_dve_rb(v):
            s_pe.wait(v, pe_m["rb"])
            for th in range(2):
                v.tensor_copy(sc["RB"][:, th * 512:(th + 1) * 512], pb[2 + th][:, :])
            v.drain()
            for k in range(KC):
                ins = v.tensor_tensor(XN[:, k * TOK:(k + 1) * TOK],
                                      XB2[:, k * TOK:(k + 1) * TOK], sc["RB"][:, :],
                                      ALU.mult)
            dve_m["xn"] = C["s_dve"].inc(ins)
        blk.vector(f_dve_rb)

        def f_act_na(a):
            a.wait_ge(s_gw.h, ld2_mark)
            act_m["abs"] = C["s_act"].inc(
                a.activation(s_negabs, s_scale, AF.Abs))
        blk.scalar(f_act_na)

        def f_dve_na(v):
            s_act.wait(v, act_m["abs"])
            ins = v.tensor_scalar_mul(s_negabs, s_negabs, -1.0)
            dve_m["na"] = C["s_dve"].inc(ins)
        blk.vector(f_dve_na)

        def f_pe_na(t):
            s_dve.wait(t, dve_m["na"])
            ins = t.matmul(pb[4][:, 0:1], sb["ones_m"][:, :], s_negabs,
                           start=True, stop=True)
            pe_m["nab"] = s_pe.inc(ins)
        blk.tensor(f_pe_na)

        def f_dve_nab(v):
            s_pe.wait(v, pe_m["nab"])
            ins = v.tensor_copy(s_nb, pb[4][:, 0:1])
            dve_m["nab"] = C["s_dve"].inc(ins)
        blk.vector(f_dve_nab)

        def f_xout(sy):
            s_dve.wait(sy, dve_m["xn"])
            s_ld.inc(sy.dma_start(
                out=xag_i[0:D, :].rearrange("(kc p) t -> p kc t", p=128),
                in_=XB2.rearrange("p (kc t) -> p kc t", t=TOK)), 16)
            s_ld.inc(sy.dma_start(
                out=xag_i[D:2 * D, :].rearrange("(kc p) t -> p kc t", p=128),
                in_=XN[:, :].rearrange("p (kc t) -> p kc t", t=TOK)), 16)
        blk.sync(f_xout)
        xout_mark = s_ld.n

        def f_ag(gp):
            gp.wait_ge(s_ld.h, xout_mark)
            ins = gp.collective_compute(
                "AllGather", ALU.bypass, replica_groups=[list(range(8))],
                ins=[xag_i], outs=[xag_o])
            s_agx.inc(ins, 1)
        blk.gpsimd(f_ag)

        for k in range(KC):
            def f_act_mn(a, k=k):
                a.wait_ge(s_gw.h, ld2_mark)
                if k > 0:
                    s_pe.wait(a, pe_m[("mn", k - 1)])
                s_act.wait(a, act_m.get(("mnp", k - 2), 0))
                act_m[("mnp", k)] = C["s_act"].inc(
                    a.activation(SQ1[:, 0:256], MUNS[:, k * 256:(k + 1) * 256],
                                 AF.Square))
            blk.scalar(f_act_mn)

            def f_pe_mn(t, k=k):
                s_act.wait(t, act_m[("mnp", k)])
                if k == 0:
                    s_dve.wait(t, dve_m["nab"])
                ins = t.matmul(pb[4][0:1, 0:256], sb["ones_k"][:, :],
                               SQ1[:, 0:256], start=(k == 0), stop=(k == KC - 1))
                pe_m[("mn", k)] = s_pe.inc(ins)
            blk.tensor(f_pe_mn)

        def f_dve_mn(v):
            s_pe.wait(v, pe_m[("mn", KC - 1)])
            v.tensor_copy(SQ1[0:1, 0:256], pb[4][0:1, 0:256])
            v.drain()
            v.tensor_scalar_max(SQ1[0:1, 0:256], SQ1[0:1, 0:256], 1e-24)
            dve_m["mnc"] = C["s_dve"].inc(v.drain())
        blk.vector(f_dve_mn)

        def f_act_mn2(a):
            s_dve.wait(a, dve_m["mnc"])
            act_m["mns"] = C["s_act"].inc(
                a.activation(SQ1[0:1, 0:256], SQ1[0:1, 0:256], AF.Sqrt))
        blk.scalar(f_act_mn2)

        def f_dve_mn2(v):
            s_act.wait(v, act_m["mns"])
            ins = v.reciprocal(SQ1[0:1, 0:256], SQ1[0:1, 0:256])
            dve_m["mnr"] = C["s_dve"].inc(ins)
        blk.vector(f_dve_mn2)

        def f_pe_mn2(t):
            s_dve.wait(t, dve_m["mnr"])
            ins = t.matmul(pb[4][:, 0:256], sb["ones_m"][:, :], SQ1[0:1, 0:256],
                           start=True, stop=True)
            pe_m["mnb"] = s_pe.inc(ins)
        blk.tensor(f_pe_mn2)

        def f_dve_mn3(v):
            s_pe.wait(v, pe_m["mnb"])
            v.tensor_copy(SQ1[:, 0:256], pb[4][:, 0:256])
            v.drain()
            for k in range(KC):
                ins = v.tensor_tensor(MUNS[:, k * 256:(k + 1) * 256],
                                      MUNS[:, k * 256:(k + 1) * 256],
                                      SQ1[:, 0:256], ALU.mult)
            dve_m["muns"] = C["s_dve"].inc(ins)
        blk.vector(f_dve_mn3)

        onekb = ar.alloc([128, 1], BF16)

        s_okb = Sem(nc, "s_okb" + sfx)

        def f_okb(sy):
            s_okb.inc(sy.dma_start(out=onekb[:], in_=C["t_onesb"].ap()), 16)
        blk.sync(f_okb)
        okb_mark = s_okb.n

        so_marks = {}
        for bq in range(2):
            for nb in range(32):
                r_, a_ = nb // 4, nb % 4
                par = nb % 2

                def f_xnt(gp, bq=bq, nb=nb, r_=r_, a_=a_, par=par):
                    gp.wait_ge(s_agx.h, 1)
                    if bq == 1 and nb < 2:
                        s_pe.wait(gp, pe_m[("si", 0, 30 + nb)])
                    if nb >= 2:
                        s_pe.wait(gp, pe_m[("lg", bq, nb - 2)])
                    src = xag_o.rearrange("(r z) t -> z r t", z=2 * D)[
                        D + 0:2 * D, r_, bq * 512 + a_ * 128: bq * 512 + (a_ + 1) * 128]
                    src = src.rearrange("(kc p) i -> p kc i", p=128)
                    dst = XLT[par][:, :].rearrange("p (kc i) -> p kc i", i=128)
                    pe_m[("xnt", bq, nb)] = s_xl[par].inc(
                        gp.dma_start(out=dst, in_=src), 16)
                blk.gpsimd(f_xnt)

                def f_pe_lg(t, bq=bq, nb=nb, par=par):
                    t.wait_ge(s_xl[par].h, pe_m[("xnt", bq, nb)])
                    if bq == 0 and nb == 0:
                        s_dve.wait(t, dve_m["muns"])
                        s_dve.wait(t, dve_m["nab"])
                    if bq == 1 and nb in (0, 1):
                        s_dve.wait(t, dve_m[("sov", 0, 1, 3)])
                    if (bq, nb) != (0, 0) and (bq, nb) != (0, 1):
                        pnb = (bq, nb - 2) if nb >= 2 else (bq - 1, 30 + nb)
                        s_act.wait(t, act_m[("exp",) + pnb])
                    for kc in range(KC):
                        ins = t.matmul(pb[6 + par][:, 0:256],
                                       XLT[par][:, kc * 128:(kc + 1) * 128],
                                       MUNS[:, kc * 256:(kc + 1) * 256],
                                       start=(kc == 0), stop=(kc == KC - 1))
                    pe_m[("lg", bq, nb)] = s_pe.inc(ins)
                blk.tensor(f_pe_lg)

                def f_act_exp(a, bq=bq, nb=nb, par=par):
                    s_pe.wait(a, pe_m[("lg", bq, nb)])
                    if bq == 1 and nb == 0:
                        s_pe.wait(a, pe_m[("si", 0, 31)])
                    ins = a.activation(EXA[:, nb * 256:(nb + 1) * 256],
                                       pb[6 + par][:, 0:256], AF.Exp, bias=s_nb)
                    act_m[("exp", bq, nb)] = C["s_act"].inc(ins)
                blk.scalar(f_act_exp)

                def f_pe_z(t, bq=bq, nb=nb):
                    s_act.wait(t, act_m[("exp", bq, nb)])
                    t.wait_ge(s_okb.h, okb_mark)
                    if bq == 1 and nb == 0:
                        s_dve.wait(t, dve_m[("zin", 0)])
                    ins = t.matmul(pb[5][0:1, 0:256], onekb[:, :],
                                   EXA[:, nb * 256:(nb + 1) * 256],
                                   start=(nb == 0), stop=(nb == 31))
                    pe_m[("zz", bq, nb)] = s_pe.inc(ins)
                blk.tensor(f_pe_z)

            def f_dve_zi(v, bq=bq):
                s_pe.wait(v, pe_m[("zz", bq, 31)])
                v.tensor_copy(SQ0[0:1, 0:256], pb[5][0:1, 0:256])
                v.drain()
                ins = v.reciprocal(SQ0[0:1, 0:256], SQ0[0:1, 0:256])
                dve_m[("zin", bq)] = C["s_dve"].inc(ins)
            blk.vector(f_dve_zi)

            def f_pe_zb(t, bq=bq):
                s_dve.wait(t, dve_m[("zin", bq)])
                ins = t.matmul(pb[4][:, 0:256], sb["ones_m"][:, :],
                               SQ0[0:1, 0:256], start=True, stop=True)
                pe_m[("zb", bq)] = s_pe.inc(ins)
            blk.tensor(f_pe_zb)

            def f_dve_zb(v, bq=bq):
                s_pe.wait(v, pe_m[("zb", bq)])
                ins = v.tensor_copy(ZBT[:, :], pb[4][:, 0:256])
                dve_m[("zc", bq)] = C["s_dve"].inc(ins)
            blk.vector(f_dve_zb)

            for nb in range(32):
                r_, a_ = nb // 4, nb % 4
                par = nb % 2

                def f_xlt(sy, bq=bq, nb=nb, r_=r_, a_=a_, par=par):
                    sy.wait_ge(s_agx.h, 1)
                    if nb < 2:
                        s_pe.wait(sy, pe_m[("lg", bq, 30 + nb)])
                    if (bq, nb) >= (0, 2):
                        prev = ("si", bq, nb - 2) if nb >= 2 else ("si", bq - 1, 30 + nb)
                        s_pe.wait(sy, pe_m.get(prev, 0))
                    src = xag_o[r_ * 2 * D: r_ * 2 * D + D,
                                bq * 512 + a_ * 128: bq * 512 + (a_ + 1) * 128]
                    pe_m[("xlt", bq, nb)] = s_xl[par].inc(
                        sy.dma_start(out=XLT[par][:, :], in_=src, transpose=True), 16)
                blk.sync(f_xlt)

                def f_pe_si(t, bq=bq, nb=nb, par=par):
                    t.wait_ge(s_xl[par].h, pe_m[("xlt", bq, nb)])
                    if nb == 0:
                        s_dve.wait(t, dve_m[("zc", bq)])
                        if bq == 1:
                            s_dve.wait(t, dve_m[("sov", 0, 1, 3)])
                    for kc in range(KC):
                        bank = kc // 2
                        half = (kc % 2) * 256
                        ins = t.matmul(pb[bank][:, half:half + 256],
                                       XLT[par][:, kc * 128:(kc + 1) * 128],
                                       EXA[:, nb * 256:(nb + 1) * 256],
                                       start=(nb == 0 and kc % 2 == 0),
                                       stop=(nb == 31))
                    pe_m[("si", bq, nb)] = s_pe.inc(ins)
                blk.tensor(f_pe_si)

            for kc in range(KC):
                def f_dve_sev(v, bq=bq, kc=kc):
                    if kc == 0:
                        s_pe.wait(v, pe_m[("si", bq, 31)])
                    bank = kc // 2
                    half = (kc % 2) * 256
                    ins = v.tensor_tensor(SLIN[:, kc * 256:(kc + 1) * 256],
                                          pb[bank][:, half:half + 256],
                                          ZBT[:, :], ALU.mult)
                    dve_m[("sev", bq, kc)] = C["s_dve"].inc(ins)
                blk.vector(f_dve_sev)

            for e in range(2):
                def f_w1(gp, bq=bq, e=e):
                    if bq == 0 and e == 0:
                        s_dve.wait(gp, dve_m["xn"])
                        gp.wait_ge(s_agx.h, 1)
                    else:
                        s_pe.wait(gp, pe_m.get(("w1", bq, e - 1),
                                               pe_m.get(("w1", bq - 1, 1), 0)))
                    src = W1s[e].rearrange("(kc p) h -> p kc h", p=128)
                    dst = W1B.rearrange("p (kc h) -> p kc h", h=512)
                    pe_m[("w1l", bq, e)] = s_mu[0].inc(gp.dma_start(out=dst, in_=src), 16)
                blk.gpsimd(f_w1)

                for hc in range(4):
                    def f_pe_w1(t, bq=bq, e=e, hc=hc):
                        t.wait_ge(s_mu[0].h, pe_m[("w1l", bq, e)])
                        if hc == 0 and e == 0:
                            s_dve.wait(t, dve_m[("sev", bq, KC - 1)])
                        if (e, hc) != (0, 0):
                            s_act.wait(t, act_m.get(("gel", bq, e, hc - 1),
                                                    act_m.get(("gel", bq, e - 1, 3), 0)))
                        for kc in range(KC):
                            ins = t.matmul(pb[6][:, 0:128],
                                           W1B[:, kc * 512 + hc * 128: kc * 512 + (hc + 1) * 128],
                                           SLIN[:, kc * 256 + e * 128: kc * 256 + e * 128 + 128],
                                           start=(kc == 0), stop=(kc == KC - 1))
                        if hc == 3:
                            pe_m[("w1", bq, e)] = s_pe.inc(ins)
                            pe_m[("w1h", bq, e, hc)] = s_pe.n
                        else:
                            pe_m[("w1h", bq, e, hc)] = s_pe.inc(ins)
                    blk.tensor(f_pe_w1)

                    def f_act_gel(a, bq=bq, e=e, hc=hc):
                        s_pe.wait(a, pe_m[("w1h", bq, e, hc)])
                        ins = a.activation(HFM[:, (e * 4 + hc) * 128:(e * 4 + hc + 1) * 128],
                                           pb[6][:, 0:128], AF.Gelu,
                                           bias=B1C[:, e * 4 + hc:e * 4 + hc + 1])
                        act_m[("gel", bq, e, hc)] = C["s_act"].inc(ins)
                    blk.scalar(f_act_gel)

            for e in range(2):
                def f_w2(gp, bq=bq, e=e):
                    if bq == 0 and e == 0:
                        s_dve.wait(gp, dve_m["xn"])
                        gp.wait_ge(s_agx.h, 1)
                    else:
                        s_pe.wait(gp, pe_m.get(("w2", bq, e - 1, 3),
                                               pe_m.get(("w2", bq - 1, 1, 3), 0)))
                    src = W2s[e].rearrange("(hc p) d2 -> p hc d2", p=128)
                    dst = W2B.rearrange("p (hc d2) -> p hc d2", d2=2048)
                    pe_m[("w2l", bq, e)] = s_mu[1].inc(gp.dma_start(out=dst, in_=src), 16)
                blk.gpsimd(f_w2)

                def f_b2(sy, bq=bq, e=e):
                    if not (bq == 0 and e == 0):
                        s_dve.wait(sy, dve_m.get(("b2b", bq, e - 1, 3),
                                                 dve_m.get(("b2b", bq - 1, 1, 3), 0)))
                    pe_m[("b2r", bq, e)] = s_ld.inc(
                        sy.dma_start(out=b2row, in_=b2s[e:e + 1, :]), 16)
                blk.sync(f_b2)

                for db in range(4):
                    def f_pe_b2(t, bq=bq, e=e, db=db):
                        t.wait_ge(s_ld.h, pe_m[("b2r", bq, e)])
                        if (bq, e, db) != (0, 0, 0):
                            prev = (bq, e, db - 1) if db > 0 else \
                                ((bq, e - 1, 3) if e > 0 else (bq - 1, 1, 3))
                            s_dve.wait(t, dve_m[("b2b",) + prev])
                        ins = t.matmul(pb[7][:, 0:512], sb["ones_m"][:, :],
                                       b2row[0:1, db * 512:(db + 1) * 512],
                                       start=True, stop=True)
                        pe_m[("b2", bq, e, db)] = s_pe.inc(ins)
                    blk.tensor(f_pe_b2)

                    def f_dve_b2(v, bq=bq, e=e, db=db):
                        s_pe.wait(v, pe_m[("b2", bq, e, db)])
                        ins = v.tensor_copy(B2B[:, db * 512:(db + 1) * 512],
                                            pb[7][:, 0:512])
                        dve_m[("b2b", bq, e, db)] = C["s_dve"].inc(ins)
                    blk.vector(f_dve_b2)

                for db in range(4):
                    def f_pe_w2(t, bq=bq, e=e, db=db):
                        t.wait_ge(s_mu[1].h, pe_m[("w2l", bq, e)])
                        s_act.wait(t, act_m[("gel", bq, e, 3)])
                        if (bq, e, db) != (0, 0, 0):
                            prev = (bq, e, db - 1) if db > 0 else \
                                ((bq, e - 1, 3) if e > 0 else (bq - 1, 1, 3))
                            s_dve.wait(t, dve_m.get(("sov",) + prev, 0))
                        for hc in range(4):
                            ins = t.matmul(pb[6][:, 0:512],
                                           HFM[:, (e * 4 + hc) * 128:(e * 4 + hc + 1) * 128],
                                           W2B[:, hc * 2048 + db * 512: hc * 2048 + (db + 1) * 512],
                                           start=(hc == 0), stop=(hc == 3))
                        pe_m[("w2", bq, e, db)] = s_pe.inc(ins)
                    blk.tensor(f_pe_w2)

                    def f_dve_sov(v, bq=bq, e=e, db=db):
                        s_pe.wait(v, pe_m[("w2", bq, e, db)])
                        s_dve.wait(v, dve_m[("b2b", bq, e, 3)])
                        if not (bq == 0 and e == 0 and db == 0):
                            v.wait_ge(s_so.h, so_marks.get((bq, e, db), 0))
                        ins = v.tensor_tensor(STG[:, 0:512], pb[6][:, 0:512],
                                              B2B[:, db * 512:(db + 1) * 512], ALU.add)
                        dve_m[("sov", bq, e, db)] = C["s_dve"].inc(ins)
                    blk.vector(f_dve_sov)

                    def f_so(sy, bq=bq, e=e, db=db):
                        s_dve.wait(sy, dve_m[("sov", bq, e, db)])
                        dst = slag_i[bq * 256 + e * 128: bq * 256 + (e + 1) * 128,
                                     db * 512:(db + 1) * 512]
                        mk = s_so.inc(sy.dma_start(out=dst, in_=STG[:, 0:512]), 16)
                        nxt = (bq, e, db + 1) if db < 3 else \
                            ((bq, e + 1, 0) if e < 1 else (bq + 1, 0, 0))
                        so_marks[nxt] = mk
                    blk.sync(f_so)

        def f_ag2(gp):
            gp.wait_ge(s_so.h, s_so.n)
            ins = gp.collective_compute(
                "AllGather", ALU.bypass, replica_groups=[list(range(8))],
                ins=[slag_i], outs=[slag_o])
            s_agsl.inc(ins, 1)
        blk.gpsimd(f_ag2)

        for kc in range(KC):
            par = kc % 2

            def f_mur(gp, kc=kc, par=par):
                if kc == 0:
                    s_pe.wait(gp, pe_m[("w2", 1, 1, 3)])
                if kc >= 2:
                    s_pe.wait(gp, pe_m[("mq", kc - 2)])
                src = muf[kc * 128:(kc + 1) * 128, :]
                pe_m[("mul", kc)] = s_mu[par].inc(
                    gp.dma_start(out=MUC[par][:, :], in_=src), 16)
            blk.gpsimd(f_mur)

            def f_act_mq(a, kc=kc, par=par):
                a.wait_ge(s_mu[par].h, pe_m[("mul", kc)])
                if kc >= 1:
                    s_pe.wait(a, pe_m[("mq", kc - 1)])
                a.activation(SQ0[:, :], MUC[par][:, 0:1024], AF.Square)
                ins = a.activation(SQ1[:, :], MUC[par][:, 1024:2048], AF.Square)
                act_m[("mq", kc)] = C["s_act"].inc(ins)
            blk.scalar(f_act_mq)

            def f_pe_mq(t, kc=kc):
                s_act.wait(t, act_m[("mq", kc)])
                if kc == 0:
                    s_dve.wait(t, dve_m[("sov", 1, 1, 3)])
                for q4 in range(4):
                    srcq = (SQ0 if q4 < 2 else SQ1)[:, (q4 % 2) * 512:(q4 % 2 + 1) * 512]
                    ins = t.matmul(pb[q4][0:1, :], sb["ones_k"][:, :], srcq,
                                   start=(kc == 0), stop=(kc == KC - 1))
                pe_m[("mq", kc)] = s_pe.inc(ins)
            blk.tensor(f_pe_mq)

        def f_dve_cn(v):
            s_pe.wait(v, pe_m[("mq", KC - 1)])
            for q4 in range(4):
                v.tensor_copy(cn_row[:, q4 * 512:(q4 + 1) * 512], pb[q4][0:1, :])
            v.drain()
            v.tensor_scalar_max(cn_row[:, :], cn_row[:, :], 1e-24)
            dve_m["cn"] = C["s_dve"].inc(v.drain())
        blk.vector(f_dve_cn)

        def f_act_cn(a):
            s_dve.wait(a, dve_m["cn"])
            ins = a.activation(cn_row[:, :], cn_row[:, :], AF.Sqrt)
            act_m["cns"] = C["s_act"].inc(ins)
        blk.scalar(f_act_cn)

        def f_dve_cn2(v):
            s_act.wait(v, act_m["cns"])
            ins = v.reciprocal(cn_row[:, :], cn_row[:, :])
            dve_m["cnr"] = C["s_dve"].inc(ins)
        blk.vector(f_dve_cn2)

        for ec2 in range(2):
            def f_pe_rnc(t, ec2=ec2):
                s_dve.wait(t, dve_m["cnr"])
                if ec2 == 1:
                    s_dve.wait(t, dve_m[("rncc", 0)])
                for j in range(8):
                    ins = t.transpose(pb[4][0:128, j * 4:j * 4 + 1],
                                      cn_row[:, (ec2 * 8 + j) * 128:(ec2 * 8 + j + 1) * 128],
                                      sb["idn_f"][0:1, 0:1])
                pe_m[("rnc", ec2)] = s_pe.inc(ins)
            blk.tensor(f_pe_rnc)

            def f_dve_rnc(v, ec2=ec2):
                s_pe.wait(v, pe_m[("rnc", ec2)])
                for j in range(8):
                    ins = v.tensor_copy(RNC[:, ec2 * 8 + j:ec2 * 8 + j + 1],
                                        pb[4][0:128, j * 4:j * 4 + 1])
                dve_m[("rncc", ec2)] = C["s_dve"].inc(ins)
            blk.vector(f_dve_rnc)

        for ec in range(KC):
            par = ec % 2

            def f_muc(gp, ec=ec, par=par):
                s_pe.wait(gp, pe_m[("mq", KC - 1)])
                if ec >= 2:
                    s_pe.wait(gp, pe_m[("cs", ec - 2, 1)])
                src = muf[:, ec * 128:(ec + 1) * 128].rearrange(
                    "(kc p) e -> p kc e", p=128)
                dst = MUC[par][:, :].rearrange("p (kc e) -> p kc e", e=128)
                pe_m[("mucl", ec)] = s_mu[par].inc(gp.dma_start(out=dst, in_=src), 16)
            blk.gpsimd(f_muc)

            for th in range(2):
                def f_pe_cs(t, ec=ec, th=th, par=par):
                    t.wait_ge(s_mu[par].h, pe_m[("mucl", ec)])
                    s_dve.wait(t, dve_m[("rncc", 1)])
                    if (ec, th) != (0, 0):
                        prev = (ec, th - 1) if th == 1 else (ec - 1, 1)
                        s_act.wait(t, act_m.get(("csx",) + prev, 0))
                    for kc in range(KC):
                        ins = t.matmul(pb[5][:, :],
                                       MUC[par][:, kc * 128:(kc + 1) * 128],
                                       XN[:, kc * TOK + th * 512: kc * TOK + (th + 1) * 512],
                                       start=(kc == 0), stop=(kc == KC - 1))
                    pe_m[("cs", ec, th)] = s_pe.inc(ins)
                blk.tensor(f_pe_cs)

                def f_act_cs(a, ec=ec, th=th):
                    s_pe.wait(a, pe_m[("cs", ec, th)])
                    ins = a.activation(PALL[:, ec * TOK + th * 512: ec * TOK + (th + 1) * 512],
                                       pb[5][:, :], AF.Exp,
                                       bias=s_nb, scale=RNC[:, ec:ec + 1])
                    act_m[("csx", ec, th)] = C["s_act"].inc(ins)
                blk.scalar(f_act_cs)

                def f_pe_cz(t, ec=ec, th=th):
                    s_act.wait(t, act_m[("csx", ec, th)])
                    t.wait_ge(s_okb.h, okb_mark)
                    ins = t.matmul(pb[6 + th][0:1, :],
                                   onekb[:, :],
                                   PALL[:, ec * TOK + th * 512: ec * TOK + (th + 1) * 512],
                                   start=(ec == 0), stop=(ec == KC - 1))
                    pe_m[("cz", ec, th)] = s_pe.inc(ins)
                blk.tensor(f_pe_cz)

        def f_dve_cz(v):
            s_pe.wait(v, pe_m[("cz", KC - 1, 0)])
            s_pe.wait(v, pe_m[("cz", KC - 1, 1)])
            v.tensor_copy(zrow[:, 0:512], pb[6][0:1, :])
            v.tensor_copy(zrow[:, 512:1024], pb[7][0:1, :])
            v.drain()
            v.reciprocal(zrow[:, :], zrow[:, :])
            v.drain()
            ins = v.tensor_scalar_mul(zrow[:, :], zrow[:, :], 0.5)
            dve_m["cz"] = C["s_dve"].inc(ins)
        blk.vector(f_dve_cz)

        def f_pe_zib(t):
            s_dve.wait(t, dve_m["cz"])
            for th in range(2):
                ins = t.matmul(pb[6 + th][:, :], sb["ones_m"][:, :],
                               zrow[0:1, th * 512:(th + 1) * 512], start=True, stop=True)
            pe_m["zib"] = s_pe.inc(ins)
        blk.tensor(f_pe_zib)

        def f_dve_zib(v):
            s_pe.wait(v, pe_m["zib"])
            for th in range(2):
                v.tensor_copy(ZIB[:, th * 512:(th + 1) * 512], pb[6 + th][:, :])
            v.drain()
            for ec in range(KC):
                ins = v.tensor_tensor(PALL[:, ec * TOK:(ec + 1) * TOK],
                                      PALL[:, ec * TOK:(ec + 1) * TOK],
                                      ZIB[:, :], ALU.mult)
            dve_m["pall"] = C["s_dve"].inc(ins)
        blk.vector(f_dve_zib)

        if debug:
            def f_dbgf(sy):
                s_dve.wait(sy, dve_m["xn"])
                C["s_out"].inc(sy.dma_start(
                    out=io["dbg_xn"].rearrange("(kc p) t -> p kc t", p=128),
                    in_=XN[:, :].rearrange("p (kc t) -> p kc t", t=TOK)), 16)
                s_dve.wait(sy, dve_m["muns"])
                C["s_out"].inc(sy.dma_start(out=io["dbg_muns"], in_=MUNS[:, :]), 16)
                s_dve.wait(sy, dve_m["pall"])
                C["s_out"].inc(sy.dma_start(
                    out=io["dbg_pall"].rearrange("(ec p) t -> p ec t", p=128),
                    in_=PALL.rearrange("p (ec t) -> p ec t", t=TOK)), 16)
                C["s_out"].inc(sy.dma_start(out=io["dbg_cn"], in_=cn_row), 16)
                sy.wait_ge(s_agsl.h, 1)
                C["s_out"].inc(sy.dma_start(out=io["dbg_slag"], in_=slag_o), 16)
            blk.sync(f_dbgf)

        oc_mark = {}
        for hc in range(KC):
            for th in range(2):
                par = (hc * 2 + th) % 2

                def f_slt(sy, hc=hc, th=th, par=par):
                    sy.wait_ge(s_agsl.h, 1)
                    if hc * 2 + th >= 2:
                        s_pe.wait(sy, pe_m.get(("av", hc - 1, th), 0))

                    srcz = slag_o.rearrange("(ec2 z) d2 -> z ec2 d2", z=512)[
                        256 * th:256 * th + 256, :, hc * 128:(hc + 1) * 128]
                    srcz = srcz.rearrange("(ecl p) ec2 d2 -> p ecl ec2 d2", p=128)
                    dstz = SLT[par].rearrange("p (ec2 ecl d2) -> p ec2 ecl d2",
                                              ec2=8, d2=128)
                    mk = 0
                    for ecl in range(2):
                        mk = s_sl[par].inc(
                            sy.dma_start(out=dstz[:, :, ecl, :],
                                         in_=srcz[:, ecl, :, :]), 16)
                    oc_mark[("slt", hc, th)] = mk
                blk.sync(f_slt)

                def f_xr(sy, hc=hc, th=th):
                    if hc == 0:
                        s_pe.wait(sy, pe_m[("mq", KC - 1)])
                    if hc >= 1:
                        s_dve.wait(sy, dve_m.get(("yt", hc - 1, th), 0))
                    src = x_in.rearrange("(kc p) t -> p kc t", p=128)[
                        :, hc, th * 512:(th + 1) * 512]
                    oc_mark[("x", hc, th)] = s_x2[th].inc(
                        sy.dma_start(out=(SQ0 if th == 0 else SQ1)[:, 0:512],
                                     in_=src), 16)
                blk.sync(f_xr)

                def f_pe_av(t, hc=hc, th=th, par=par):
                    t.wait_ge(s_sl[par].h, oc_mark[("slt", hc, th)])
                    s_dve.wait(t, dve_m["pall"])
                    if hc * 2 + th >= 2:
                        s_dve.wait(t, dve_m.get(("yt", hc - 1, th), 0))
                    for ec in range(KC):
                        ins = t.matmul(pb[4 + th][:, :],
                                       SLT[par][:, ec * 128:(ec + 1) * 128],
                                       PALL[:, ec * TOK + th * 512: ec * TOK + (th + 1) * 512],
                                       start=(ec == 0), stop=(ec == KC - 1))
                    pe_m[("av", hc, th)] = s_pe.inc(ins)
                blk.tensor(f_pe_av)

                def f_dve_av(v, hc=hc, th=th):
                    s_pe.wait(v, pe_m[("av", hc, th)])
                    v.wait_ge(s_x2[th].h, oc_mark[("x", hc, th)])
                    if hc > 0:
                        v.wait_ge(s_y2[th].h, oc_mark.get(("y", hc - 1, th), 0))
                    xr = (SQ0 if th == 0 else SQ1)[:, 0:512]
                    yt = (SQ0 if th == 0 else SQ1)[:, 512:1024]
                    ins = v.scalar_tensor_tensor(yt, pb[4 + th][:, :], 1.0, xr,
                                                 ALU.mult, ALU.add)
                    dve_m[("yt", hc, th)] = C["s_dve"].inc(ins)
                blk.vector(f_dve_av)

                def f_yo(sy, hc=hc, th=th):
                    s_dve.wait(sy, dve_m[("yt", hc, th)])
                    yt = (SQ0 if th == 0 else SQ1)[:, 512:1024]
                    dst = y_out.rearrange("(kc p) t -> p kc t", p=128)[
                        :, hc, th * 512:(th + 1) * 512]
                    oc_mark[("y", hc, th)] = s_y2[th].inc(
                        sy.dma_start(out=dst, in_=yt), 16)
                    if debug and hc == 0 and th == 0:
                        C["s_out"].inc(sy.dma_start(out=io["dbg_sq0"], in_=SQ0[:, :]), 16)
                        C["s_out"].inc(sy.dma_start(out=io["dbg_slt"], in_=SLT[par][:, :]), 16)
                blk.sync(f_yo)

        def f_fin(sy):
            s_y2[0].wait(sy)
            s_y2[1].wait(sy)
            s_out.wait(sy)
        blk.sync(f_fin)


def ffn1_io(nc):
    return {
        "x_in": nc.dram_tensor("x_fm", [D, TOK], F32, kind="ExternalInput").ap(),
        "y_out": nc.dram_tensor("y_fm", [D, TOK], F32, kind="ExternalOutput").ap(),
        "gamma": nc.dram_tensor("gamma", [128, KC], F32, kind="ExternalInput").ap(),
        "beta": nc.dram_tensor("beta", [128, KC], F32, kind="ExternalInput").ap(),
        "mu_f": nc.dram_tensor("mu_f", [D, 2048], F32, kind="ExternalInput").ap(),
        "mu_sl": nc.dram_tensor("mu_sl", [D, 256], F32, kind="ExternalInput").ap(),
        "scale1": nc.dram_tensor("scale1", [1, 1], F32, kind="ExternalInput").ap(),
        "W1_sl": nc.dram_tensor("W1_sl", [2, D, 512], F32, kind="ExternalInput").ap(),
        "b1_sl": nc.dram_tensor("b1_sl", [128, 8], F32, kind="ExternalInput").ap(),
        "W2_sl": nc.dram_tensor("W2_sl", [2, 512, D], F32, kind="ExternalInput").ap(),
        "b2_sl": nc.dram_tensor("b2_sl", [2, D], F32, kind="ExternalInput").ap(),
    }


def build_ffn1_program(scale=0.5, debug=False):
    nc = bass.Bass("TRN2", target_bir_lowering=False)
    io = ffn1_io(nc)
    if debug:
        io["dbg_xn"] = nc.dram_tensor("dbg_xn", [D, TOK], BF16, kind="ExternalOutput").ap()
        io["dbg_muns"] = nc.dram_tensor("dbg_muns", [128, KC * 256], BF16, kind="ExternalOutput").ap()
        io["dbg_pall"] = nc.dram_tensor("dbg_pall", [D, TOK], BF16, kind="ExternalOutput").ap()
        io["dbg_cn"] = nc.dram_tensor("dbg_cn", [1, 2048], F32, kind="ExternalOutput").ap()
        io["dbg_slag"] = nc.dram_tensor("dbg_slag", [8 * 512, D], BF16, kind="ExternalOutput").ap()
        io["dbg_sq0"] = nc.dram_tensor("dbg_sq0", [128, 1024], F32, kind="ExternalOutput").ap()
        io["dbg_slt"] = nc.dram_tensor("dbg_slt", [128, 2048], BF16, kind="ExternalOutput").ap()
    C = common_setup(nc)
    pbk = [nc.alloc_psum_tensor(f"pbk{i}", [128, 512], F32) for i in range(8)]
    with nc.Block() as blk:
        sb = emit_load_consts(nc, blk, C)
        ar = new_arena(nc)
        emit_ffn1(nc, blk, C, sb, ar, pbk, io, "_f1", scale=scale, debug=debug)
    return nc



def prep_attn1_weights(d):
    return {
        "wqkv_T": np.ascontiguousarray(np.asarray(d["L1_attn_w_qkv"]).T),
        "wproj_T": np.ascontiguousarray(np.asarray(d["L1_attn_w_proj"]).T),
    }


def emit_attn1(nc, blk, C, sb, ar, pb, io, sfx):
    x_in = io["x_in"]
    y_out = io["y_out"]
    wqkv = io["wqkv_T"]
    wproj = io["wproj_T"]

    s_ws = [Sem(nc, f"s_ws{i}{sfx}") for i in range(2)]

    X = ar.alloc([128, KC * TOK], F32)
    XB = ar.alloc([128, KC * TOK], BF16)
    QKV = ar.alloc([128, 24 * TOK], BF16)
    CTX = ar.alloc([128, KC * TOK], BF16)
    WS = [ar.alloc([128, KC * 128], BF16) for i in range(2)]
    SCR2 = ar.alloc([128, 3072], F32)
    sc = {"MB": SCR2[:, 0:1024], "RB": SCR2[:, 1024:2048], "TMP": SCR2[:, 2048:3072]}
    SQ0 = ar.alloc([128, 1024], F32)
    SQ1 = ar.alloc([128, 1024], F32)
    sc["SQ0"] = SQ0[:, :]
    sc["SQ1"] = SQ1[:, :]
    sc["rows"] = CTX[:, :].bitcast(F32)[0:1, 0:6 * TOK]
    SA = SCR2[0:1, 0:512]
    DRow = SCR2[0:1, 512:1024]
    W0R = SCR2[0:1, 1024:1536]
    VD = SQ1[:, :].bitcast(BF16)
    PRD = SQ0[:, :]

    s_ld, s_gw, s_pe = C["s_ld"], C["s_gw"], C["s_pe"]
    s_act, s_dve, s_out = C["s_act"], C["s_dve"], C["s_out"]

    if True:
        emit_load_master(nc, blk, C, X[:, :], x_in)
        xb_mark, ln_bank = emit_ln(nc, blk, C, sb, X[:, :], XB[:, :],
                                   [pb[0], pb[1]], sc)

        pe_m = {}
        act_m = {}
        dve_m = {}

        for oc in range(24):
            par = oc % 2

            def f_gw(gp, oc=oc, par=par):
                if oc >= 2:
                    s_pe.wait(gp, pe_m[("p", oc - 2, 1)])
                src = wqkv[:, oc * 128:(oc + 1) * 128].rearrange(
                    "(kc p) o -> p kc o", p=128)
                dst = WS[par].rearrange("p (kc o) -> p kc o", o=128)
                pe_m[("w", oc)] = s_ws[par].inc(gp.dma_start(out=dst, in_=src), 16)
            blk.gpsimd(f_gw)

            for th in range(2):
                def f_pe(t, oc=oc, th=th, par=par):
                    t.wait_ge(s_ws[par].h, pe_m[("w", oc)])
                    if oc == 0 and th == 0:
                        s_dve.wait(t, xb_mark)
                        s_dve.wait(t, ln_bank)
                    prev = (oc, th - 1) if th == 1 else (oc - 1, 1)
                    if oc > 0 or th > 0:
                        s_act.wait(t, act_m.get(("e",) + prev, 0))
                    for kc in range(KC):
                        ins = t.matmul(pb[2 + th][:, :],
                                       WS[par][:, kc * 128:(kc + 1) * 128],
                                       XB[:, kc * TOK + th * 512:kc * TOK + (th + 1) * 512],
                                       start=(kc == 0), stop=(kc == KC - 1))
                    pe_m[("p", oc, th)] = s_pe.inc(ins)
                blk.tensor(f_pe)

                def f_act(a, oc=oc, th=th):
                    s_pe.wait(a, pe_m[("p", oc, th)])
                    ins = a.activation(QKV[:, oc * TOK + th * 512:oc * TOK + (th + 1) * 512],
                                       pb[2 + th][:, :], AF.Copy)
                    act_m[("e", oc, th)] = C["s_act"].inc(ins)
                blk.scalar(f_act)

        qkv_done = act_m[("e", 23, 1)]

        for gg in range(4):
            qc0 = 6 * gg
            kc_ = 6 * gg + 4
            vc_ = 6 * gg + 5

            def f_dve_vd(v, gg=gg, vc_=vc_):
                s_act.wait(v, qkv_done)
                ins = v.tensor_tensor(VD[:, 0:512], QKV[:, vc_ * TOK + 0:vc_ * TOK + 512],
                                      QKV[:, vc_ * TOK + 512:vc_ * TOK + 1024],
                                      ALU.subtract)
                dve_m[("vd", gg)] = C["s_dve"].inc(ins)
            blk.vector(f_dve_vd)

            for hp in range(4):
                qc = qc0 + hp
                h = 4 * gg + hp
                for s_ in range(2):
                    key = (gg, hp, s_)

                    def f_dve_pr(v, qc=qc, kc_=kc_, s_=s_, key=key):
                        s_act.wait(v, qkv_done)
                        if key != (0, 0, 0):
                            s_pe.wait(v, pe_m.get(("sc2",) + prevkey[0], 0))
                        v.tensor_tensor(PRD[:, 0:512],
                                        QKV[:, qc * TOK + s_ * 512:qc * TOK + (s_ + 1) * 512],
                                        QKV[:, kc_ * TOK + 0:kc_ * TOK + 512], ALU.mult)
                        ins = v.tensor_tensor(PRD[:, 512:1024],
                                              QKV[:, qc * TOK + s_ * 512:qc * TOK + (s_ + 1) * 512],
                                              QKV[:, kc_ * TOK + 512:kc_ * TOK + 1024],
                                              ALU.mult)
                        dve_m[("pr",) + key] = C["s_dve"].inc(ins)
                    blk.vector(f_dve_pr)

                    def f_pe_sc(t, key=key):
                        s_dve.wait(t, dve_m[("pr",) + key])
                        if key != (0, 0, 0):
                            s_dve.wait(t, dve_m.get(("sa",) + prevkey[0], 0))
                        t.matmul(pb[4][0:1, :], sb["ones_k"][:, :], PRD[:, 0:512],
                                 start=True, stop=True)
                        ins = t.matmul(pb[5][0:1, :], sb["ones_k"][:, :],
                                       PRD[:, 512:1024], start=True, stop=True)
                        pe_m[("sc2",) + key] = s_pe.inc(ins)
                    blk.tensor(f_pe_sc)

                    def f_dve_sa(v, key=key):
                        s_pe.wait(v, pe_m[("sc2",) + key])
                        if key != (0, 0, 0):
                            s_act.wait(v, act_m.get(("sg",) + prevkey[0], 0))
                        v.tensor_copy(SA, pb[4][0:1, :])
                        v.drain()
                        ins = v.scalar_tensor_tensor(DRow, pb[5][0:1, :], -1.0, SA,
                                                     ALU.mult, ALU.add)
                        dve_m[("sa",) + key] = C["s_dve"].inc(ins)
                    blk.vector(f_dve_sa)

                    def f_act_sg(a, key=key):
                        s_dve.wait(a, dve_m[("sa",) + key])
                        ins = a.activation(W0R, DRow, AF.Sigmoid,
                                           scale=float(1.0 / np.sqrt(128.0)))
                        act_m[("sg",) + key] = C["s_act"].inc(ins)
                    blk.scalar(f_act_sg)

                    def f_pe_w0(t, key=key):
                        s_act.wait(t, act_m[("sg",) + key])
                        if key != (0, 0, 0):
                            s_dve.wait(t, dve_m.get(("cx",) + prevkey[0], 0))
                        ins = t.matmul(pb[6][:, :], sb["ones_m"][:, :], W0R,
                                       start=True, stop=True)
                        pe_m[("w0",) + key] = s_pe.inc(ins)
                    blk.tensor(f_pe_w0)

                    def f_dve_cx(v, gg=gg, hp=hp, s_=s_, key=key, vc_=vc_, h=h):
                        s_pe.wait(v, pe_m[("w0",) + key])
                        s_dve.wait(v, dve_m[("vd", gg)])
                        v.tensor_tensor(sc["TMP"][:, 0:512], VD[:, 0:512], pb[6][:, :],
                                        ALU.mult)
                        v.drain()
                        ins = v.tensor_tensor(
                            CTX[:, h * TOK + s_ * 512: h * TOK + (s_ + 1) * 512],
                            sc["TMP"][:, 0:512],
                            QKV[:, vc_ * TOK + 512:vc_ * TOK + 1024], ALU.add)
                        dve_m[("cx",) + key] = C["s_dve"].inc(ins)
                    blk.vector(f_dve_cx)

                    prevkey = [key]

        ctx_done_mark = dve_m[("cx", 3, 3, 1)]

        for hc in range(KC):
            par = hc % 2

            def f_gw_o(gp, hc=hc, par=par):
                s_pe.wait(gp, pe_m[("p", 23, 1)])
                if hc >= 2:
                    s_pe.wait(gp, pe_m[("o", hc - 2, 1)])
                src = wproj[:, hc * 128:(hc + 1) * 128].rearrange(
                    "(ec p) o -> p ec o", p=128)
                dst = WS[par].rearrange("p (ec o) -> p ec o", o=128)
                pe_m[("wo", hc)] = s_ws[par].inc(gp.dma_start(out=dst, in_=src), 16)
            blk.gpsimd(f_gw_o)

            for th in range(2):
                def f_pe_o(t, hc=hc, th=th, par=par):
                    t.wait_ge(s_ws[par].h, pe_m[("wo", hc)])
                    if hc == 0 and th == 0:
                        s_dve.wait(t, ctx_done_mark)
                    prev = (hc, th - 1) if th == 1 else (hc - 1, 1)
                    if hc > 0 or th > 0:
                        s_dve.wait(t, dve_m.get(("yo",) + prev, 0))
                    for ec in range(KC):
                        ins = t.matmul(pb[th][:, :],
                                       WS[par][:, ec * 128:(ec + 1) * 128],
                                       CTX[:, ec * TOK + th * 512:ec * TOK + (th + 1) * 512],
                                       start=(ec == 0), stop=(ec == KC - 1))
                    pe_m[("o", hc, th)] = s_pe.inc(ins)
                blk.tensor(f_pe_o)

                def f_dve_o(v, hc=hc, th=th):
                    s_pe.wait(v, pe_m[("o", hc, th)])
                    ins = v.scalar_tensor_tensor(
                        X[:, hc * TOK + th * 512: hc * TOK + (th + 1) * 512],
                        pb[th][:, :], 1.0,
                        X[:, hc * TOK + th * 512: hc * TOK + (th + 1) * 512],
                        ALU.mult, ALU.add)
                    dve_m[("yo", hc, th)] = C["s_dve"].inc(ins)
                blk.vector(f_dve_o)

        emit_store_master(nc, blk, C, X[:, :], y_out,
                          [(C["s_dve"], dve_m[("yo", KC - 1, 1)])])


def attn1_io(nc):
    return {
        "x_in": nc.dram_tensor("x_fm", [D, TOK], F32, kind="ExternalInput").ap(),
        "y_out": nc.dram_tensor("y_fm", [D, TOK], F32, kind="ExternalOutput").ap(),
        "wqkv_T": nc.dram_tensor("wqkv_T", [D, 3072], F32, kind="ExternalInput").ap(),
        "wproj_T": nc.dram_tensor("wproj_T", [D, D], F32, kind="ExternalInput").ap(),
    }


def build_attn1_program():
    nc = bass.Bass("TRN2", target_bir_lowering=False)
    io = attn1_io(nc)
    C = common_setup(nc)
    pbk = [nc.alloc_psum_tensor(f"pbk{i}", [128, 512], F32) for i in range(8)]
    with nc.Block() as blk:
        sb = emit_load_consts(nc, blk, C)
        ar = new_arena(nc)
        emit_attn1(nc, blk, C, sb, ar, pbk, io, "_a1")
    return nc



def build_fused_program(debug=False):
    nc = bass.Bass("TRN2", target_bir_lowering=False)
    x0 = nc.dram_tensor("x_fm", [D, TOK], F32, kind="ExternalInput").ap()
    y = nc.dram_tensor("y_fm", [D, TOK], F32, kind="ExternalOutput").ap()
    kind = "ExternalOutput" if debug else "Internal"
    xs = [nc.dram_tensor(f"xs{i}", [D, TOK], F32, kind=kind).ap()
          for i in range(1, 6)]

    io_moe = {
        "router_T": nc.dram_tensor("router_T", [D, 8], F32, kind="ExternalInput").ap(),
        "gup_T": nc.dram_tensor("gup_T", [8, D, 2816], F32, kind="ExternalInput").ap(),
        "down_T": nc.dram_tensor("down_T", [8, 1408, D], F32, kind="ExternalInput").ap(),
        "shg_T": nc.dram_tensor("shg_T", [D, 5632], F32, kind="ExternalInput").ap(),
        "shu_T": nc.dram_tensor("shu_T", [D, 5632], F32, kind="ExternalInput").ap(),
        "shd_T": nc.dram_tensor("shd_T", [5632, D], F32, kind="ExternalInput").ap(),
        "sgw_T": nc.dram_tensor("sgw_T", [128, KC], F32, kind="ExternalInput").ap(),
    }
    io_a0 = {
        "Wq": nc.dram_tensor("Wq", [D, D], F32, kind="ExternalInput").ap(),
        "Wk": nc.dram_tensor("Wk", [D, 512], F32, kind="ExternalInput").ap(),
        "Wv": nc.dram_tensor("Wv", [D, 512], F32, kind="ExternalInput").ap(),
        "Wo": nc.dram_tensor("Wo", [D, D], F32, kind="ExternalInput").ap(),
        "cs_t": nc.dram_tensor("cs_t", [128, 2048], F32, kind="ExternalInput").ap(),
        "masks": nc.dram_tensor("masks", [8, 128, 512], BF16, kind="ExternalInput").ap(),
    }
    io_f1 = {
        "gamma": nc.dram_tensor("gamma", [128, KC], F32, kind="ExternalInput").ap(),
        "beta": nc.dram_tensor("beta", [128, KC], F32, kind="ExternalInput").ap(),
        "mu_f": nc.dram_tensor("mu_f", [D, 2048], F32, kind="ExternalInput").ap(),
        "mu_sl": nc.dram_tensor("mu_sl", [D, 256], F32, kind="ExternalInput").ap(),
        "scale1": nc.dram_tensor("scale1", [1, 1], F32, kind="ExternalInput").ap(),
        "W1_sl": nc.dram_tensor("W1_sl", [2, D, 512], F32, kind="ExternalInput").ap(),
        "b1_sl": nc.dram_tensor("b1_sl", [128, 8], F32, kind="ExternalInput").ap(),
        "W2_sl": nc.dram_tensor("W2_sl", [2, 512, D], F32, kind="ExternalInput").ap(),
        "b2_sl": nc.dram_tensor("b2_sl", [2, D], F32, kind="ExternalInput").ap(),
    }
    io_a1 = {
        "wqkv_T": nc.dram_tensor("wqkv_T", [D, 3072], F32, kind="ExternalInput").ap(),
        "wproj_T": nc.dram_tensor("wproj_T", [D, D], F32, kind="ExternalInput").ap(),
    }

    C = common_setup(nc)
    pbk = [nc.alloc_psum_tensor(f"pbk{i}", [128, 512], F32) for i in range(8)]
    with nc.Block() as blk:
        sb = emit_load_consts(nc, blk, C)
        ar = new_arena(nc)

        def sub(emit_fn, io_extra, x_in, y_out, sfx, **kw):
            ar.reset()
            emit_boundary(nc, blk, C)
            io = dict(io_extra)
            io["x_in"] = x_in
            io["y_out"] = y_out
            emit_fn(nc, blk, C, sb, ar, pbk, io, sfx, **kw)

        sub(emit_moe, io_moe, x0, xs[0], "_s1", scale=0.5)
        sub(emit_attn0, io_a0, xs[0], xs[1], "_s2")
        sub(emit_moe, io_moe, xs[1], xs[2], "_s3", scale=0.5)
        sub(emit_ffn1, io_f1, xs[2], xs[3], "_s4", scale=0.5)
        sub(emit_attn1, io_a1, xs[3], xs[4], "_s5")
        sub(emit_ffn1, io_f1, xs[4], y, "_s6", scale=0.5)
    return nc


def fused_in_maps(d):
    w_moe = prep_moe_weights(d)
    w_a0 = prep_attn0_weights(d)
    w_a1 = prep_attn1_weights(d)
    xsh = shard_x_fm(d["L0_attn_x"])
    ims = []
    for c in range(NCORES):
        m = {"x_fm": xsh[c]}
        m.update(w_moe)
        m.update(w_a0)
        m.update(w_a1)
        m.update(prep_ffn1_weights(d, c))
        m["cs_t"] = rope_tables_for_core(c)
        m["masks"] = attn_masks_for_core(c)
        ims.append(m)
    return ims



def kernel(**inputs):
    nc = build_fused_program()
    ims = fused_in_maps(inputs)
    res = run_bass_kernel_spmd(nc, ims, core_ids=list(range(NCORES)))
    return unshard_x_fm([res.results[c]["y_fm"] for c in range(NCORES)])
